# version 14
# baseline (speedup 1.0000x reference)
import numpy as np
import concourse.bass as bass
import concourse.mybir as mybir
from concourse.bass_utils import run_bass_kernel_spmd

F32 = mybir.dt.float32
BF16 = mybir.dt.bfloat16
AF = mybir.ActivationFunctionType
ALU = mybir.AluOpType
AX = mybir.AxisListType

NCORES = 8
B, T, D, H, DH = 4, 4096, 1024, 16, 64
TL = T // 2
NDT = D // 128
EPS_LN = 1e-5

DEBUG_STAGE = 0


ENGS = ("tensor", "vector", "scalar", "gpsimd", "sync")


class Prog:
    def __init__(self, nc):
        self.nc = nc
        self.stages = []
        self.cnt = {}
        self.sems = {}
        self.cur = None

    def sem(self, name):
        if name not in self.sems:
            self.sems[name] = self.nc.alloc_semaphore(name)
            self.cnt[name] = 0
        return name

    def stage(self, name):
        self.cur = {e: [] for e in ENGS}
        self.cur["_name"] = name
        self.stages.append(self.cur)

    ENG_SEM = {"vector": "v", "scalar": "a", "gpsimd": "pl"}

    def emit(self, eng, fn, waits=(), inc=None, amt=1, dma=False, chain=True):
        for s, v in waits:
            assert v <= self.cnt[s], (s, v, self.cnt[s])
        waits = list(waits)
        if not dma and eng in self.ENG_SEM:
            es = self.ENG_SEM[eng]
            self.sem(es)
            if chain and self.cnt[es]:
                waits.append((es, self.cnt[es]))
            if inc is None:
                inc = es
            assert inc == es, (eng, inc)
        rec = (fn, tuple(waits), inc, amt)
        self.cur[eng].append(rec)
        if inc is not None:
            self.cnt[inc] += amt
            return self.cnt[inc]
        return None

    def dma(self, eng, out, in_, waits=(), inc=None, **kw):
        if inc is None:
            inc = eng[0]
        self.sem(inc)
        w = list(waits)
        if self.cnt[inc]:
            w.append((inc, self.cnt[inc]))
        return self.emit(eng, lambda e: e.dma_start(out, in_, **kw), w, inc, 16,
                         dma=True)

    def replay(self):
        nc = self.nc
        ctx = nc.allow_low_precision(reason="bf16 matmul operands per problem tolerance")
        ctx.__enter__()
        for st in self.stages:
            with nc.Block(st["_name"]) as block:
                for eng in ENGS:
                    recs = st[eng]
                    if not recs:
                        continue

                    def body(e, recs=recs):
                        for fn, waits, inc, amt in recs:
                            for s, v in waits:
                                e.wait_ge(self.sems[s], v)
                            inst = fn(e)
                            if inc is not None and inst is not None:
                                inst.then_inc(self.sems[inc], amt)

                    getattr(block, eng)(body)


class Arena:

    def __init__(self, nc, slack=5120):
        self.nc = nc
        size = nc.sbuf_bytes_remaining - slack
        base, _ = nc.bump_sbuf(size)
        self.base = base
        self.off = base
        self.limit = base + size
        self.mark_off = None
        self.n = 0

    def alloc(self, shape, dtype, name=None):
        assert len(shape) == 2 and shape[0] <= 128
        itm = 2 if dtype == BF16 else 4
        sz = shape[1] * itm
        off = (self.off + 31) & ~31
        assert off + sz <= self.limit, (
            f"SBUF arena overflow: {name} needs {sz} at {off}, limit {self.limit}")
        self.off = off + sz
        self.n += 1
        nm = name or f"ar{self.n}"
        if not hasattr(self, "offs"):
            self.offs = {}
        self.offs[nm] = off
        return self.nc.alloc_sbuf_tensor_at(nm, list(shape), dtype, offset=off)

    def overlay(self, base_name, delta, shape, dtype, name):
        off = self.offs[base_name] + delta
        self.n += 1
        return self.nc.alloc_sbuf_tensor_at(name, list(shape), dtype, offset=off)

    def mark(self, key="base"):
        if self.mark_off is None:
            self.mark_off = {}
        self.mark_off[key] = self.off

    def reset(self, key="base"):
        self.off = self.mark_off[key]


def _rope_tables():
    base = 10000.0
    inv = 1.0 / base ** (np.arange(0, DH, 2, dtype=np.float64) / DH)
    ang = np.arange(T, dtype=np.float64)[:, None] * inv[None, :]
    cos = np.cos(ang)
    sin = np.sin(ang)
    cs = np.empty((128, T), np.float32)
    ss = np.empty((128, T), np.float32)
    for p in range(128):
        d = p % 64
        cs[p] = cos[:, d % 32].astype(np.float32)
        sgn = -1.0 if (d % 64) < 32 else 1.0
        ss[p] = (sgn * sin[:, d % 32]).astype(np.float32)
    return cs, ss


def _prep_core(inputs, c):
    b, half = c // 2, c % 2
    t0, o0 = half * TL, (1 - half) * TL
    order = np.r_[t0:t0 + TL, o0:o0 + TL]

    x = np.asarray(inputs["L0_attn_hidden_states"])
    cs, ss = _rope_tables()

    d = {}
    d["x"] = np.ascontiguousarray(x[b][order].T)
    d["wqT"] = np.ascontiguousarray(np.asarray(inputs["L0_attn_Wq"]).T)
    d["wkT"] = np.ascontiguousarray(np.asarray(inputs["L0_attn_Wk"]).T)
    wv = np.asarray(inputs["L0_attn_Wv"])
    wv65 = np.zeros((D, H * 65), np.float32)
    for h in range(H):
        wv65[:, h * 65:h * 65 + 64] = wv[h * 64:(h + 1) * 64, :].T
    d["wv65"] = wv65
    bv = np.asarray(inputs["L0_attn_bv"])
    bv_row = np.zeros((1, H * 65), np.float32)
    for h in range(H):
        bv_row[0, h * 65:h * 65 + 64] = bv[h * 64:(h + 1) * 64]
    d["bv_row"] = bv_row
    d["bq_p"] = np.ascontiguousarray(np.asarray(inputs["L0_attn_bq"]).reshape(8, 128).T)
    d["bk_p"] = np.ascontiguousarray(np.asarray(inputs["L0_attn_bk"]).reshape(8, 128).T)
    d["bo_c"] = np.ascontiguousarray(np.asarray(inputs["L0_attn_bo"]).reshape(8, 128).T)
    d["woT"] = np.ascontiguousarray(np.asarray(inputs["L0_attn_Wo"]).T)
    d["cs"] = np.ascontiguousarray(cs[:, order])
    d["ss"] = np.ascontiguousarray(ss[:, order])

    sel128 = np.zeros((128, 128), np.float32)
    sel128[0, 0:64] = 1.0
    sel128[64, 64:128] = 1.0
    d["sel128"] = sel128
    d["ident"] = np.eye(128, dtype=np.float32)
    perm = np.r_[32:64, 0:32, 96:128, 64:96]
    P = np.zeros((128, 128), np.float32)
    P[perm, np.arange(128)] = 1.0
    d["permm"] = P

    d["w1t"] = np.ascontiguousarray(np.asarray(inputs["L0_ffn_weight"]).T)
    d["disha_p"] = np.ascontiguousarray(np.asarray(inputs["L0_ffn_disha"]))
    psel = np.zeros((128, 64), np.float32)
    for pp in range(128):
        psel[pp, pp % 64] = 1.0
    d["psel"] = psel

    wq1 = np.asarray(inputs["L1_attn_W_Q"])[:, :, 0]
    wk1 = np.asarray(inputs["L1_attn_W_K"])[:, :, 0]
    wqk1 = np.zeros((D, 64), np.float32)
    wqk1[:, 0:16] = wq1.T
    wqk1[:, 32:48] = wk1.T
    d["wqk1"] = wqk1
    wv1 = np.asarray(inputs["L1_attn_W_V"])
    d["wv1"] = np.ascontiguousarray(wv1.transpose(1, 0, 2).reshape(D, D))
    wo1 = np.asarray(inputs["L1_attn_W_O"])
    d["wo1"] = np.ascontiguousarray(wo1.transpose(2, 1, 0).reshape(D, D))
    sel16 = np.zeros((16, 8 * 128), np.float32)
    for kt in range(8):
        for pp in range(128):
            sel16[2 * kt + (pp >= 64), kt * 128 + pp] = 1.0
    d["sel16"] = sel16
    d["oddmask"] = np.full((128, 1), float(half), np.float32)

    se_raw = np.asarray(inputs["L1_ffn_slot_embeds"]).reshape(1024, D)
    d["se_raw"] = np.ascontiguousarray(se_raw.T)
    d["ng"] = np.ascontiguousarray(np.asarray(inputs["L1_ffn_norm_g"]).reshape(8, 128).T)
    d["nb"] = np.ascontiguousarray(np.asarray(inputs["L1_ffn_norm_b"]).reshape(8, 128).T)
    d["sg"] = np.ascontiguousarray(np.asarray(inputs["L1_ffn_slot_g"]).reshape(8, 128).T)
    d["sbb"] = np.ascontiguousarray(np.asarray(inputs["L1_ffn_slot_b"]).reshape(8, 128).T)
    d["scale1"] = np.asarray(inputs["L1_ffn_scale"]).reshape(1, 1).astype(np.float32)
    es = (2 * c, 2 * c + 1)
    d["wg"] = np.ascontiguousarray(np.asarray(inputs["L1_ffn_Wg"])[list(es)])
    d["wu"] = np.ascontiguousarray(np.asarray(inputs["L1_ffn_Wu"])[list(es)])
    d["wd"] = np.ascontiguousarray(np.asarray(inputs["L1_ffn_Wd"])[list(es)])
    bg = np.asarray(inputs["L1_ffn_bg"])
    bu = np.asarray(inputs["L1_ffn_bu"])
    bgc = np.zeros((128, 32), np.float32)
    buc = np.zeros((128, 32), np.float32)
    for j, eg in enumerate(es):
        bgc[:, j * 16:(j + 1) * 16] = bg[eg].reshape(16, 128).T
        buc[:, j * 16:(j + 1) * 16] = bu[eg].reshape(16, 128).T
    d["bgc"] = bgc
    d["buc"] = buc
    d["bdr"] = np.ascontiguousarray(np.asarray(inputs["L1_ffn_bd"])[list(es)])
    return d


def _prep(inputs):
    return [_prep_core(inputs, c) for c in range(NCORES)]


def build():
    nc = bass.Bass("TRN2", target_bir_lowering=False)
    p = Prog(nc)
    for s in ("g", "s", "t", "v", "a", "pl", "cc", "o"):
        p.sem(s)

    dx = nc.dram_tensor("x", [D, T], F32, kind="ExternalInput").ap()
    dwqT = nc.dram_tensor("wqT", [D, D], F32, kind="ExternalInput").ap()
    dwkT = nc.dram_tensor("wkT", [D, D], F32, kind="ExternalInput").ap()
    dwv65 = nc.dram_tensor("wv65", [D, H * 65], F32, kind="ExternalInput").ap()
    dbv = nc.dram_tensor("bv_row", [1, H * 65], F32, kind="ExternalInput").ap()
    dbq = nc.dram_tensor("bq_p", [128, 8], F32, kind="ExternalInput").ap()
    dbk = nc.dram_tensor("bk_p", [128, 8], F32, kind="ExternalInput").ap()
    dbo = nc.dram_tensor("bo_c", [128, 8], F32, kind="ExternalInput").ap()
    dwoT = nc.dram_tensor("woT", [D, D], F32, kind="ExternalInput").ap()
    dcs = nc.dram_tensor("cs", [128, T], F32, kind="ExternalInput").ap()
    dss = nc.dram_tensor("ss", [128, T], F32, kind="ExternalInput").ap()
    dsel = nc.dram_tensor("sel128", [128, 128], F32, kind="ExternalInput").ap()
    dpermm = nc.dram_tensor("permm", [128, 128], F32, kind="ExternalInput").ap()

    dout = nc.dram_tensor("out_fm", [D, TL], F32, kind="ExternalOutput").ap()

    dw1t = nc.dram_tensor("w1t", [D, D], F32, kind="ExternalInput").ap()
    ddisha = nc.dram_tensor("disha_p", [64, D], F32, kind="ExternalInput").ap()
    dpsel = nc.dram_tensor("psel", [128, 64], F32, kind="ExternalInput").ap()
    dwqk1 = nc.dram_tensor("wqk1", [D, 64], F32, kind="ExternalInput").ap()
    dwv1 = nc.dram_tensor("wv1", [D, D], F32, kind="ExternalInput").ap()
    dwo1 = nc.dram_tensor("wo1", [D, D], F32, kind="ExternalInput").ap()
    dsel16 = nc.dram_tensor("sel16", [16, 8 * 128], F32, kind="ExternalInput").ap()
    dmask = nc.dram_tensor("oddmask", [128, 1], F32, kind="ExternalInput").ap()
    dident = nc.dram_tensor("ident", [128, 128], F32, kind="ExternalInput").ap()

    dln0 = nc.dram_tensor("ln0d", [D, T], BF16, kind="Internal").ap()
    dx1 = nc.dram_tensor("x1d", [D, TL], F32, kind="Internal").ap()
    dx2 = nc.dram_tensor("x2d", [D, TL], F32, kind="Internal").ap()
    dx3 = nc.dram_tensor("x3d", [D, TL], F32, kind="Internal").ap()
    dln1 = nc.dram_tensor("ln1d", [D, TL], BF16, kind="Internal").ap()
    dln2 = nc.dram_tensor("ln2d", [D, TL], BF16, kind="Internal").ap()
    dse = nc.dram_tensor("se_raw", [D, 1024], F32, kind="ExternalInput").ap()
    dng = nc.dram_tensor("ng", [128, 8], F32, kind="ExternalInput").ap()
    dnb = nc.dram_tensor("nb", [128, 8], F32, kind="ExternalInput").ap()
    dsg = nc.dram_tensor("sg", [128, 8], F32, kind="ExternalInput").ap()
    dsbb = nc.dram_tensor("sbb", [128, 8], F32, kind="ExternalInput").ap()
    dscale = nc.dram_tensor("scale1", [1, 1], F32, kind="ExternalInput").ap()
    dwg = nc.dram_tensor("wg", [2, D, 2048], F32, kind="ExternalInput").ap()
    dwu = nc.dram_tensor("wu", [2, D, 2048], F32, kind="ExternalInput").ap()
    dwd = nc.dram_tensor("wd", [2, 2048, D], F32, kind="ExternalInput").ap()
    dbgc = nc.dram_tensor("bgc", [128, 32], F32, kind="ExternalInput").ap()
    dbuc = nc.dram_tensor("buc", [128, 32], F32, kind="ExternalInput").ap()
    dbdr = nc.dram_tensor("bdr", [2, 1024], F32, kind="ExternalInput").ap()
    dln3 = nc.dram_tensor("ln3d", [D, TL], BF16, kind="Internal").ap()
    dse_ln = nc.dram_tensor("se_lnd", [D, 1024], BF16, kind="Internal").ap()
    dsl_in = nc.dram_tensor("sl_in", [1024, 1032], BF16, kind="Internal").ap()
    dsl_out = nc.dram_tensor("sl_out", [1024 * NCORES, 1032], BF16, kind="Internal").ap()
    deo_in = nc.dram_tensor("eo_in", [512, 1024], BF16, kind="Internal").ap()
    deo_out = nc.dram_tensor("eo_out", [512 * NCORES, 1024], BF16, kind="Internal").ap()
    dcar_in = nc.dram_tensor("car_in", [128, 24], F32, kind="Internal").ap()
    dcar_out = nc.dram_tensor("car_out", [128 * 2, 24], F32, kind="Internal").ap()

    ar = Arena(nc)
    ones128_bf = ar.alloc([128, 1], BF16, "ones128b")
    ones1x128 = ar.alloc([1, 128], F32, "ones1x128")
    ones1x128b = ar.alloc([1, 128], BF16, "ones1x128b")
    consts = ar.alloc([128, 4], F32, "consts")
    sel_sb = ar.alloc([128, 128], BF16, "sel_sb")
    id_sb = ar.alloc([128, 128], BF16, "id_sb")
    bq_sb = ar.alloc([128, 8], F32, "bq_sb")
    bk_sb = ar.alloc([128, 8], F32, "bk_sb")
    bo_sb = ar.alloc([128, 8], F32, "bo_sb")
    ar.mark()

    psAB = nc.alloc_psum_tensor("psAB", [128, 2048], F32)
    psA = psAB[:, 0:1024]
    psB = psAB[:, 1024:2048]
    psC = nc.alloc_psum_tensor("psC", [128, 512], F32)
    psD = nc.alloc_psum_tensor("psD", [128, 512], F32)
    psE = nc.alloc_psum_tensor("psE", [128, 512], F32)
    psT = nc.alloc_psum_tensor("psT", [128, 1024], BF16)

    cnt = p.cnt

    def MM(out, lhsT, rhs, start, stop, waits=(), inc=None, tp=None):
        return p.emit("tensor",
                      (lambda e, o=out, l=lhsT, r=rhs, s0=start, s1=stop, t=tp:
                       e.matmul(o, l, r, start=s0, stop=s1, tile_position=t)),
                      waits, inc)

    def CP(eng, out, in_, waits=(), inc=None):
        return p.emit(eng, (lambda e, o=out, i=in_: e.tensor_copy(o, i)), waits, inc)

    def TT(out, a, b, op, waits=(), inc=None, eng="vector"):
        return p.emit(eng, (lambda e, o=out, x=a, y=b, f=op: e.tensor_tensor(o, x, y, f)),
                      waits, inc)

    def TS(out, in_, s1, s2, op0, op1=None, waits=(), inc=None, eng="vector"):
        if op1 is None:
            op1 = ALU.bypass
        return p.emit(eng, (lambda e, o=out, i=in_, a=s1, b=s2, f0=op0, f1=op1:
                            e.tensor_scalar(o, i, a, b, f0, f1)), waits, inc)

    def STT(out, in0, scalar, in1, op0, op1, waits=(), inc=None):
        return p.emit("vector", (lambda e, o=out, i=in0, s=scalar, j=in1, f0=op0, f1=op1:
                                 e.scalar_tensor_tensor(o, i, s, j, f0, f1)), waits, inc)

    def ACTV(out, in_, func, bias=None, scale=None, waits=(), inc=None,
             chain=True):
        kw = {}
        if bias is not None:
            kw["bias"] = bias
        if scale is not None:
            kw["scale"] = scale
        return p.emit("scalar", (lambda e, o=out, i=in_, f=func, kw=kw:
                                 e.activation(o, i, f, **kw)), waits, inc,
                      chain=chain)

    def RECIP(out, in_, waits=(), inc=None):
        return p.emit("vector", (lambda e, o=out, i=in_: e.reciprocal(o, i)), waits, inc)

    def RED(out, in_, op, waits=(), inc=None, negate=False):
        return p.emit("vector", (lambda e, o=out, i=in_, f=op, n=negate:
                                 e.tensor_reduce(o, i, AX.X, f, negate=n)), waits, inc)

    def MS(ap, val, waits=(), inc=None):
        return p.emit("gpsimd", (lambda e, a=ap, v=val: e.memset(a, v)), waits, inc)

    def SCAN(out, d0, d1, init, waits=(), inc=None):
        return p.emit("vector", (lambda e, o=out, a=d0, b=d1, i=init:
                                 e.tensor_tensor_scan(o, a, b, i, ALU.add, ALU.add)),
                      waits, inc)

    p.stage("init")
    MS(ones128_bf[:, :], 1.0)
    MS(ones1x128[:, :], 1.0)
    MS(ones1x128b[:, :], 1.0)
    MS(consts[:, 0:1], 1.0 / 1024.0)
    MS(consts[:, 1:2], EPS_LN)
    MS(consts[:, 2:3], 1e-6)
    MS(consts[:, 3:4], float(np.pi / 2))
    p.dma("gpsimd", sel_sb[:, :], dsel, inc="g_tab")
    g_id = p.dma("gpsimd", id_sb[:, :], dident, inc="g_tab")
    p.dma("sync", bq_sb[:, :], dbq, inc="s_tab")
    p.dma("sync", bk_sb[:, :], dbk, inc="s_tab")
    tab_s = p.dma("sync", bo_sb[:, :], dbo, inc="s_tab")

    p.stage("ln0")
    ar.reset()
    _lnbuf_n = [0]

    def mk_ln_bufs():
        i = _lnbuf_n[0]
        _lnbuf_n[0] += 2
        return dict(
            xcf=[ar.alloc([128, NDT * 512], F32, f"xcf{i+j}") for j in range(2)],
            xcb=[ar.alloc([128, NDT * 512], BF16, f"xcb{i+j}") for j in range(2)],
            sqb=ar.alloc([128, NDT * 512], BF16, f"sqb{i}"),
            lnc=ar.alloc([128, NDT * 512], BF16, f"lnc{i}"),
            tmpf=ar.alloc([128, 512], F32, f"lntmp{i}"),
            rows=ar.alloc([1, 2048], F32, f"rows{i}"),
            rows2=ar.alloc([1, 2048], F32, f"rows2_{i}"))

    def emit_ln(src_dram, n_tok, dst_dram, bufs, gten=None, bten=None,
                postscale=None, src_ready=(), src_ready_fn=None):
        xcf, xcb = bufs["xcf"], bufs["xcb"]
        sqb, lnc, tmpf = bufs["sqb"], bufs["lnc"], bufs["tmpf"]
        rows, rows2 = bufs["rows"], bufs["rows2"]
        """Feature-major LN over d=1024, pipelined over 512-token chunks."""
        nch = n_tok // 512
        st_chunks = []
        base = _lnbuf_n[0] - 2
        rel_xcb = [None, None]
        rel_xcf = [None, None]
        rel_stats = [None, None]
        rel_sq = None
        rel_bcast = None
        for ch in range(nch):
            bi = ch % 2
            mean_bk = psE[0:1, 0:512] if bi == 0 else psD[0:1, 0:512]
            sq_bk = psC[0:1, 0:512] if bi == 0 else psB[0:1, 0:512]
            rset = rows if bi == 0 else rows2
            sl = slice(ch * 512, (ch + 1) * 512)
            src3 = src_dram.rearrange("(dt p) t -> p dt t", p=128)[:, :, sl]
            src_ready = src_ready_fn(ch) if src_ready_fn is not None else src_ready
            w = list(src_ready)
            if rel_xcf[bi]:
                w.append(("v", rel_xcf[bi]))
            xcf3 = xcf[bi].rearrange("p (dt t) -> p dt t", dt=NDT)
            ldf = p.dma("sync", xcf3[:, 0:4, :], src3[:, 0:4, :], waits=w,
                        inc=f"s_xcf{base + bi}")
            ldf2 = p.dma("scalar", xcf3[:, 4:8, :], src3[:, 4:8, :], waits=w,
                         inc=f"a_xcf{base + bi}")
            w = list(src_ready)
            if rel_xcb[bi]:
                w.append(("t", rel_xcb[bi]))
            ldb = p.dma("gpsimd", xcb[bi].rearrange("p (dt t) -> p dt t", dt=NDT),
                        src3, waits=w, inc=f"g_xcb{base + bi}")
            aq = None
            for dt in range(NDT):
                wq = []
                if dt == 0:
                    wq = [(f"g_xcb{base + bi}", ldb)]
                    if rel_sq:
                        wq.append(("t", rel_sq))
                aq = ACTV(sqb[:, dt * 512:(dt + 1) * 512],
                          xcb[bi][:, dt * 512:(dt + 1) * 512], AF.Square,
                          waits=wq, inc="a")
            for dt in range(NDT):
                wm = []
                if dt == 0:
                    wm = [(f"g_xcb{base + bi}", ldb)]
                    if rel_stats[bi]:
                        wm.append(("v", rel_stats[bi]))
                MM(mean_bk, ones128_bf[:, :], xcb[bi][:, dt * 512:(dt + 1) * 512],
                   start=(dt == 0), stop=(dt == NDT - 1), waits=wm)
            tmm = None
            for dt in range(NDT):
                wm = [("a", aq)] if dt == 0 else []
                tmm = MM(sq_bk, ones128_bf[:, :], sqb[:, dt * 512:(dt + 1) * 512],
                         start=(dt == 0), stop=(dt == NDT - 1), waits=wm,
                         inc=("t" if dt == NDT - 1 else None))
            rel_sq = tmm
            rel_xcb[bi] = tmm
            r_mean, r_srow, r_m2, r_var = (rset[0:1, 0:512], rset[0:1, 512:1024],
                                           rset[0:1, 1024:1536], rset[0:1, 1536:2048])
            TS(r_mean, mean_bk, consts[0:1, 0:1], None, ALU.mult, waits=[("t", tmm)])
            TS(r_srow, sq_bk, consts[0:1, 0:1], None, ALU.mult)
            TT(r_m2, r_mean, r_mean, ALU.mult)
            vv = p.emit("vector", (lambda e, a=r_var, b=r_srow, c=r_m2:
                                   e.tensor_sub(a, b, c)), inc="v")
            rel_stats[bi] = vv
            ACTV(r_m2, r_var, AF.Ln, bias=consts[0:1, 1:2], waits=[("v", vv)])
            av = ACTV(r_var, r_m2, AF.Exp, scale=-0.5, inc="a")
            vm = TT(r_m2, r_mean, r_var, ALU.mult, waits=[("a", av)], inc="v")
            wb = [("v", vm)]
            if rel_bcast:
                wb.append(("v", rel_bcast))
            MM(psA[:, 0:512], ones1x128[:, :], r_var, start=True, stop=True, waits=wb)
            tb = MM(psA[:, 512:1024], ones1x128[:, :], r_m2, start=True, stop=True,
                    inc="t")
            vo = None
            for dt in range(NDT):
                wv_ = [("t", tb)]
                if dt == 0:
                    wv_.append((f"s_xcf{base + bi}", ldf))
                    if p.cnt.get("s_lnst"):
                        wv_.append(("s_lnst", p.cnt["s_lnst"]))
                if dt == 4:
                    wv_.append((f"a_xcf{base + bi}", ldf2))
                TT(tmpf[:, :], xcf[bi][:, dt * 512:(dt + 1) * 512], psA[:, 0:512],
                   ALU.mult, waits=wv_)
                if gten is None and postscale is None:
                    vo = TT(lnc[:, dt * 512:(dt + 1) * 512], tmpf[:, :],
                            psA[:, 512:1024], ALU.subtract, inc="v")
                else:
                    TT(tmpf[:, :], tmpf[:, :], psA[:, 512:1024], ALU.subtract)
                    if gten is not None and postscale is None:
                        vo = TS(lnc[:, dt * 512:(dt + 1) * 512], tmpf[:, :],
                                gten[:, dt:dt + 1], bten[:, dt:dt + 1],
                                ALU.mult, ALU.add, inc="v")
                    else:
                        TS(tmpf[:, :], tmpf[:, :], gten[:, dt:dt + 1],
                           bten[:, dt:dt + 1], ALU.mult, ALU.add)
                        vo = TS(lnc[:, dt * 512:(dt + 1) * 512], tmpf[:, :],
                                postscale, None, ALU.mult, inc="v")
            rel_bcast = vo
            rel_xcf[bi] = vo
            st_chunks.append(p.dma("sync",
                             dst_dram.rearrange("(dt p) t -> p dt t", p=128)[:, :, sl],
                             lnc.rearrange("p (dt t) -> p dt t", dt=NDT),
                             waits=[("v", vo)], inc="s_lnst"))
        return st_chunks

    b0 = mk_ln_bufs()
    ln0_ch = emit_ln(dx, T, dln0, b0)
    ln0_done = ln0_ch[-1]
    LN0S = "s_lnst"

    NKT = T // 128
    qc_n = TL // 512

    p.stage("l0v")
    ar.reset()
    v_sb = ar.alloc([128, NKT * 1040], BF16, "v_sb")
    cs_sb = ar.alloc([128, T], BF16, "cs_sb")
    ss_sb = ar.alloc([128, T], BF16, "ss_sb")
    wo_sb = ar.alloc([128, NDT * 1024], BF16, "wo_sb")
    ao_sb = ar.alloc([128, 8 * TL], BF16, "ao_sb")
    r2_sb = ar.alloc([128, 1024], BF16, "r2_sb")
    perm_sb = ar.alloc([128, 128], BF16, "perm_sb")
    ar.mark("attn")
    wv_sb = ar.alloc([128, NDT * 1040], BF16, "wv_sb")
    lnk = [ar.alloc([128, NDT * 128], BF16, f"lnk{i}") for i in range(4)]
    bvr_sb = ar.alloc([1, H * 65], BF16, "bvr_sb")

    pl_r2 = MS(r2_sb[:, :], 0.0, inc="pl")
    g_wv = p.dma("gpsimd", wv_sb.rearrange("p (dt c) -> p dt c", dt=NDT),
                 dwv65.rearrange("(dt p) c -> p dt c", p=128), inc="g_wv")
    g_bv = p.dma("gpsimd", bvr_sb[:, :], dbv, inc="g_tab")
    g_pm = p.dma("gpsimd", perm_sb[:, :], dpermm, inc="g_tab")
    g_cs = p.dma("gpsimd", cs_sb[:, :], dcs, inc="g_cs")
    a_ss = p.dma("gpsimd", ss_sb[:, :], dss, inc="g_ss")
    a_wo = p.dma("gpsimd", wo_sb.rearrange("p (kt d) -> p kt d", kt=NDT),
                 dwoT.rearrange("(kt p) d -> p kt d", p=128), inc="g_wo")

    ve = None
    rel_lnk = [None] * 4
    cp_hist = [None, None]
    for kt in range(NKT):
        bi = kt % 4
        pb = psA if kt % 2 == 0 else psB
        pc = psC if kt % 2 == 0 else psD
        qd = "sync" if bi % 2 == 0 else "gpsimd"
        w = [(LN0S, ln0_ch[kt // 4])]
        if rel_lnk[bi]:
            w.append(("t", rel_lnk[bi]))
        lk = p.dma(qd, lnk[bi].rearrange("p (dt t) -> p dt t", dt=NDT),
                   dln0.rearrange("(dt p) t -> p dt t", p=128)[:, :, kt * 128:(kt + 1) * 128],
                   waits=w, inc=f"lk{bi}")
        for dt in range(NDT):
            w0 = []
            if dt == 0:
                w0 = [(f"lk{bi}", lk), ("g_wv", g_wv), ("g_tab", g_bv)]
                if kt < 2:
                    w0.append((LN0S, ln0_done))
                if cp_hist[kt % 2]:
                    w0.append(("v", cp_hist[kt % 2]))
            MM(pb[:, 0:512], lnk[bi][:, dt * 128:(dt + 1) * 128],
               wv_sb[:, dt * 1040:dt * 1040 + 512], start=(dt == 0), stop=False,
               waits=w0)
            MM(pb[:, 512:1024], lnk[bi][:, dt * 128:(dt + 1) * 128],
               wv_sb[:, dt * 1040 + 512:dt * 1040 + 1024], start=(dt == 0), stop=False)
            MM(pc[:, 0:16], lnk[bi][:, dt * 128:(dt + 1) * 128],
               wv_sb[:, dt * 1040 + 1024:dt * 1040 + 1040], start=(dt == 0), stop=False)
        MM(pb[:, 0:512], ones1x128b[:, :], bvr_sb[:, 0:512], start=False, stop=True)
        MM(pb[:, 512:1024], ones1x128b[:, :], bvr_sb[:, 512:1024], start=False, stop=True)
        tv = MM(pc[:, 0:16], ones1x128b[:, :], bvr_sb[:, 1024:1040],
                start=False, stop=True, inc="t")
        rel_lnk[bi] = tv
        off = kt * 1040
        CP("vector", v_sb[:, off:off + 512], pb[:, 0:512], waits=[("t", tv)])
        CP("vector", v_sb[:, off + 512:off + 1024], pb[:, 512:1024])
        ve = CP("vector", v_sb[:, off + 1024:off + 1040], pc[:, 0:16], inc="v")
        cp_hist[kt % 2] = ve
    pl_ones = MS(v_sb.rearrange("p (kt h e) -> p kt h e", kt=NKT, h=H)[:, :, :, 64:65],
                 1.0, waits=[("v", ve)], inc="pl")
    b1_t = cnt["t"]

    p.stage("l0attn")
    ar.reset("attn")
    wq_sb = [ar.alloc([128, NDT * 128], BF16, f"wq_sb{i}") for i in range(2)]
    wk_sb = [ar.alloc([128, NDT * 128], BF16, f"wk_sb{i}") for i in range(2)]
    lnqk = [ar.alloc([128, NDT * 512], BF16, f"lnqk{i}") for i in range(4)]
    q_sb = ar.alloc([128, TL], BF16, "q_sb")
    k_sb = ar.alloc([128, T], BF16, "k_sb")
    qr_sb = q_sb
    kr_sb = k_sb
    rtmp = ar.alloc([128, 512], BF16, "rtmp")
    p2_sb = [ar.alloc([128, 2048], BF16, f"p2_sb{i}") for i in range(2)]
    rb_sb = ar.alloc([128, 512], F32, "rb_sb")
    xr_sb = [ar.overlay("wq_sb0", 0, [128, 512], F32, "xr_sb0"),
             ar.overlay("wq_sb1", 0, [128, 512], F32, "xr_sb1")]
    o_sb = [ar.overlay("wk_sb0", 0, [128, 512], F32, "o_sb0"),
            ar.overlay("wk_sb1", 0, [128, 512], F32, "o_sb1")]

    v_ready = [("pl", pl_ones)]
    rel_wq = [None, None]
    rel_wk = [None, None]
    rel_lnqk = [None, None, None, None]
    rtts = []
    nrm_q = []
    for hp in range(8):
        bi = hp % 2
        w = [("t", rel_wq[bi] if rel_wq[bi] else b1_t)]
        g_wq = p.dma("gpsimd", wq_sb[bi].rearrange("p (dt m) -> p dt m", dt=NDT),
                     dwqT.rearrange("(dt p) m -> p dt m", p=128)[:, :, hp * 128:(hp + 1) * 128],
                     waits=w, inc=f"g_wq{bi}")
        w = [("t", rel_wk[bi] if rel_wk[bi] else b1_t)]
        g_wk = p.dma("gpsimd", wk_sb[bi].rearrange("p (dt m) -> p dt m", dt=NDT),
                     dwkT.rearrange("(dt p) m -> p dt m", p=128)[:, :, hp * 128:(hp + 1) * 128],
                     waits=w, inc=f"g_wk{bi}")
        vq_ts = [None] * qc_n
        vk_ts = [None] * (T // 512)
        pbanks = [psC[:, 0:512], psD[:, 0:512], psA[:, 0:512], psB[:, 0:512]]
        hv = cnt["v"]
        ts_hist = [hv, hv, hv, hv]
        gi = 0
        for tc in range(T // 512):
            cbi = tc % 4
            qd = "sync" if cbi % 2 == 0 else "gpsimd"
            wl = [(LN0S, ln0_ch[tc]),
                  ("t", rel_lnqk[cbi] if rel_lnqk[cbi] else b1_t)]
            s_ln = p.dma(qd, lnqk[cbi].rearrange("p (dt t) -> p dt t", dt=NDT),
                         dln0.rearrange("(dt p) t -> p dt t", p=128)[:, :, tc * 512:(tc + 1) * 512],
                         waits=wl, inc=f"lq{cbi}")
            if tc < qc_n:
                bk = gi % 4
                for dt in range(NDT):
                    w0 = []
                    if dt == 0:
                        w0 = [(f"g_wq{bi}", g_wq), (f"lq{cbi}", s_ln),
                              ("v", ts_hist[bk])]
                    MM(pbanks[bk], wq_sb[bi][:, dt * 128:(dt + 1) * 128],
                       lnqk[cbi][:, dt * 512:(dt + 1) * 512],
                       start=(dt == 0), stop=(dt == NDT - 1), waits=w0,
                       inc=("t" if dt == NDT - 1 else None))
                vq_ts[tc] = TS(q_sb[:, tc * 512:(tc + 1) * 512], pbanks[bk],
                               bq_sb[:, hp:hp + 1], None, ALU.add,
                               waits=[("t", cnt["t"]), ("s_tab", tab_s)], inc="v")
                ts_hist[bk] = vq_ts[tc]
                gi += 1
            bk = gi % 4
            for dt in range(NDT):
                w0 = []
                if dt == 0:
                    w0 = [(f"g_wk{bi}", g_wk), (f"lq{cbi}", s_ln),
                          ("v", ts_hist[bk])]
                MM(pbanks[bk], wk_sb[bi][:, dt * 128:(dt + 1) * 128],
                   lnqk[cbi][:, dt * 512:(dt + 1) * 512],
                   start=(dt == 0), stop=(dt == NDT - 1), waits=w0,
                   inc=("t" if dt == NDT - 1 else None))
            vk_ts[tc] = TS(k_sb[:, tc * 512:(tc + 1) * 512], pbanks[bk],
                           bk_sb[:, hp:hp + 1], None, ALU.add,
                           waits=[("t", cnt["t"]), ("s_tab", tab_s)], inc="v")
            ts_hist[bk] = vk_ts[tc]
            gi += 1
            rel_lnqk[cbi] = cnt["t"]
            if tc == qc_n - 1:
                rel_wq[bi] = cnt["t"]
            if tc == T // 512 - 1:
                rel_wk[bi] = cnt["t"]
        rope_k = [None] * (T // 512)
        pi = 0
        for (srcb, dstb, nch, vsrc) in ((q_sb, qr_sb, qc_n, vq_ts),
                                        (k_sb, kr_sb, T // 512, vk_ts)):
            for tc in range(nch):
                sl = slice(tc * 512, (tc + 1) * 512)
                pbank = psC if pi % 2 == 0 else psD
                w0 = [("v", vk_ts[-1] if pi < 2 else rtts[-2]), ("g_tab", g_pm)]
                tpm = MM(pbank[:, 0:512], perm_sb[:, :], srcb[:, sl],
                         start=True, stop=True, waits=w0, inc="t")
                t1 = TT(rtmp[:, :], pbank[:, 0:512], ss_sb[:, sl], ALU.mult,
                        waits=[("t", tpm), ("g_ss", a_ss), ("g_cs", g_cs)])
                rtts.append(t1)
                TT(dstb[:, sl], srcb[:, sl], cs_sb[:, sl], ALU.mult)
                vr = TT(dstb[:, sl], dstb[:, sl], rtmp[:, :], ALU.add, inc="v")
                if srcb is k_sb:
                    rope_k[tc] = vr
                pi += 1

        evac_prev = rope_k[T // 512 - 1]
        for qc in range(qc_n):
            a_hist = []
            tpv = None

            def emit_S(kt):
                sbuf = psA if kt % 2 == 0 else psB
                w0 = [("v", rope_k[kt // 4])] + v_ready
                if kt >= 2:
                    w0.append(("a", a_hist[kt - 2]))
                MM(sbuf[:, 0:512], kr_sb[0:64, kt * 128:(kt + 1) * 128],
                   qr_sb[0:64, qc * 512:(qc + 1) * 512], start=True, stop=True,
                   waits=w0, tp=(0, 0))
                ts = MM(sbuf[:, 512:1024], kr_sb[64:128, kt * 128:(kt + 1) * 128],
                        qr_sb[64:128, qc * 512:(qc + 1) * 512], start=True, stop=True,
                        tp=(64, 0), inc="t")
                ae = ACTV(p2_sb[kt % 2][:, 0:1024], sbuf[:, :], AF.Exp, scale=0.125,
                          waits=[("t", ts)], inc="a")
                a_hist.append(ae)

            def emit_PV(kt):
                nonlocal tpv
                pb = p2_sb[kt % 2]
                vofs = kt * 1040 + 130 * hp
                wpv = [("a", a_hist[kt])]
                if kt == 0:
                    wpv.append(("v", evac_prev))
                MM(psC[0:65, 0:512], v_sb[:, vofs:vofs + 65], pb[:, 0:512],
                   start=(kt == 0), stop=(kt == NKT - 1), waits=wpv)
                tpv = MM(psD[0:65, 0:512], v_sb[:, vofs + 65:vofs + 130],
                         pb[:, 512:1024], start=(kt == 0), stop=(kt == NKT - 1),
                         inc=("t" if kt == NKT - 1 else None))

            while nrm_q:
                aofs_n, sl2_n, ev_n = nrm_q.pop(0)
                tbN = MM(psE[:, 0:512], sel_sb[:, :], r2_sb[:, sl2_n],
                         start=True, stop=True,
                         waits=[("v", ev_n), ("pl", pl_r2), ("g_tab", g_bv)],
                         inc="t")
                RECIP(rb_sb[:, :], psE[:, 0:512], waits=[("t", tbN)])
                TT(ao_sb[0:64, aofs_n:aofs_n + 512],
                   ao_sb[0:64, aofs_n:aofs_n + 512], rb_sb[0:64, :], ALU.mult)
                TT(ao_sb[64:128, aofs_n:aofs_n + 512],
                   ao_sb[64:128, aofs_n:aofs_n + 512], rb_sb[64:128, :],
                   ALU.mult, inc="v")

            emit_S(0)
            for kt in range(1, NKT):
                emit_S(kt)
                emit_PV(kt - 1)
            emit_PV(NKT - 1)
            sl2 = slice((qc % 2) * 512, (qc % 2) * 512 + 512)
            aofs = hp * TL + qc * 512
            CP("vector", r2_sb[0:1, sl2], psC[64:65, 0:512], waits=[("t", tpv)])
            CP("vector", r2_sb[64:65, sl2], psD[64:65, 0:512])
            CP("vector", ao_sb[0:64, aofs:aofs + 512], psC[0:64, 0:512])
            evac_prev = CP("vector", ao_sb[64:128, aofs:aofs + 512],
                           psD[0:64, 0:512], inc="v")
            nrm_q.append((aofs, sl2, evac_prev))

    while nrm_q:
        aofs_n, sl2_n, ev_n = nrm_q.pop(0)
        tbN = MM(psE[:, 0:512], sel_sb[:, :], r2_sb[:, sl2_n], start=True,
                 stop=True, waits=[("v", ev_n), ("pl", pl_r2), ("g_tab", g_bv)],
                 inc="t")
        RECIP(rb_sb[:, :], psE[:, 0:512], waits=[("t", tbN)])
        TT(ao_sb[0:64, aofs_n:aofs_n + 512], ao_sb[0:64, aofs_n:aofs_n + 512],
           rb_sb[0:64, :], ALU.mult)
        TT(ao_sb[64:128, aofs_n:aofs_n + 512],
           ao_sb[64:128, aofs_n:aofs_n + 512], rb_sb[64:128, :],
           ALU.mult, inc="v")

    ao_done = cnt["v"]
    stt_h = [None, None]
    st_h = [None, None]
    x1_tc = {}
    for i, (dc, tc) in enumerate([(d, t) for d in range(NDT)
                                  for t in range(qc_n)]):
        b = i % 2
        sx = p.dma("scalar", xr_sb[b][:, :],
                   dx[dc * 128:(dc + 1) * 128, tc * 512:(tc + 1) * 512],
                   waits=[("v", stt_h[b] if stt_h[b] else ao_done)],
                   inc=f"a_xr{b}")
        for kt in range(NDT):
            w0 = []
            if kt == 0:
                w0 = [("v", ao_done), ("v", cnt["v"]), ("g_wo", a_wo)]
            MM(psC[:, 0:512],
               wo_sb[:, kt * 1024 + dc * 128: kt * 1024 + (dc + 1) * 128],
               ao_sb[:, kt * TL + tc * 512: kt * TL + (tc + 1) * 512],
               start=(kt == 0), stop=(kt == NDT - 1), waits=w0,
               inc=("t" if kt == NDT - 1 else None))
        vo = STT(o_sb[b][:, :], psC[:, 0:512], bo_sb[:, dc:dc + 1], xr_sb[b][:, :],
                 ALU.add, ALU.add,
                 waits=[("t", cnt["t"]), (f"a_xr{b}", sx), ("s_tab", tab_s)] +
                       ([("s_x1", st_h[b])] if st_h[b] else []),
                 inc="v")
        stt_h[b] = vo
        st_h[b] = p.dma("sync", dx1[dc * 128:(dc + 1) * 128, tc * 512:(tc + 1) * 512],
                        o_sb[b][:, :], waits=[("v", vo)], inc="s_x1")
        if dc == NDT - 1:
            x1_tc[tc] = st_h[b]

    p.stage("ffn0")
    ar.reset()
    bC = mk_ln_bufs()
    ln1_ch = emit_ln(dx1, TL, dln1, bC,
                     src_ready_fn=lambda ch: [("s_x1", x1_tc[ch])])
    ln1_done = ln1_ch[-1]
    w1_sb = ar.alloc([128, NDT * 1024], BF16, "w1_sb")
    dsh_sb = ar.alloc([64, 1024], BF16, "dsh_sb")
    ps_sb = ar.alloc([128, 64], BF16, "ps_sb")
    lnc1 = bC["xcb"] + [ar.alloc([128, NDT * 512], BF16, f"lnc1x{i}")
                        for i in range(2)]
    pool_sb = ar.alloc([64, TL], BF16, "pool_sb")
    xc1_sb = [ar.alloc([128, 512], F32, f"xc1_sb{i}") for i in range(2)]
    oc1_sb = [ar.alloc([128, 512], F32, f"oc1_sb{i}") for i in range(2)]

    g_w1 = p.dma("gpsimd", w1_sb.rearrange("p (dt m) -> p dt m", dt=NDT),
                 dw1t.rearrange("(dt p) m -> p dt m", p=128), inc="g_tab")
    g_dsh = p.dma("gpsimd", dsh_sb[:, :], ddisha, inc="g_tab")
    g_ps = p.dma("gpsimd", ps_sb[:, :], dpsel, inc="g_tab")

    lc1_ld = [None] * qc_n
    for tc in range(qc_n):
        qd = "sync" if tc % 2 == 0 else "gpsimd"
        lc1_ld[tc] = p.dma(qd, lnc1[tc].rearrange("p (dt t) -> p dt t", dt=NDT),
                           dln1.rearrange("(dt p) t -> p dt t", p=128)[:, :, tc * 512:(tc + 1) * 512],
                           waits=[("s_lnst", ln1_ch[tc + 2] if tc < 2 else ln1_ch[tc])],
                           inc=f"lc1{tc}")
        for dt in range(NDT):
            w0 = []
            if dt == 0:
                w0 = [(f"lc1{tc}", lc1_ld[tc]), ("g_tab", g_ps), ("v", cnt["v"])]
            MM(psE[0:64, 0:512], ps_sb[:, :], lnc1[tc][:, dt * 512:(dt + 1) * 512],
               start=(dt == 0), stop=(dt == NDT - 1), waits=w0,
               inc=("t" if dt == NDT - 1 else None))
        CP("vector", pool_sb[:, tc * 512:(tc + 1) * 512], psE[0:64, 0:512],
           waits=[("t", cnt["t"])])
    ctt_h = [None, None]
    cst_h = [None, None]
    x2_tc = {}
    c2_banks = [psC[:, 0:512], psD[:, 0:512], psA[:, 0:512], psB[:, 0:512]]
    c2_rel = [cnt["v"]] * 4
    for i, (tc, oc) in enumerate([(t, o) for t in range(qc_n)
                                  for o in range(NDT)]):
        b = i % 2
        pbk = c2_banks[i % 4]
        wx = [("s_x1", cnt["s_x1"])] if i < 2 else [("v", ctt_h[b])]
        sx = p.dma("scalar", xc1_sb[b][:, :],
                   dx1[oc * 128:(oc + 1) * 128, tc * 512:(tc + 1) * 512],
                   waits=wx, inc=f"a_xr{b}")
        for dt in range(NDT):
            w0 = []
            if dt == 0:
                w0 = [(f"lc1{tc}", lc1_ld[tc]), ("g_tab", g_w1),
                      ("v", c2_rel[i % 4])]
            MM(pbk, w1_sb[:, dt * 1024 + oc * 128:dt * 1024 + (oc + 1) * 128],
               lnc1[tc][:, dt * 512:(dt + 1) * 512],
               start=(dt == 0), stop=False, waits=w0)
        MM(pbk, dsh_sb[0:64, oc * 128:(oc + 1) * 128],
           pool_sb[0:64, tc * 512:(tc + 1) * 512],
           start=False, stop=True, inc="t")
        vo = TT(oc1_sb[b][:, :], pbk, xc1_sb[b][:, :], ALU.add,
                waits=[("t", cnt["t"]), (f"a_xr{b}", sx)] +
                      ([("s_x2", cst_h[b])] if cst_h[b] else []),
                inc="v")
        c2_rel[i % 4] = vo
        ctt_h[b] = vo
        cst_h[b] = p.dma("sync", dx2[oc * 128:(oc + 1) * 128, tc * 512:(tc + 1) * 512],
                         oc1_sb[b][:, :], waits=[("v", vo)], inc="s_x2")
        if oc == NDT - 1:
            x2_tc[tc] = cst_h[b]

    p.stage("l1attn")
    ar.reset()
    bD = mk_ln_bufs()
    ln2_ch = emit_ln(dx2, TL, dln2, bD,
                     src_ready_fn=lambda ch: [("s_x2", x2_tc[ch])])
    ln2_done = ln2_ch[-1]
    NHE = NDT
    wqk1_sb = ar.alloc([128, NDT * 64], BF16, "wqk1_sb")
    wv1_sb = ar.alloc([128, NDT * 1024], BF16, "wv1_sb")
    wo1_sb = ar.overlay("wv1_sb", 0, [128, NDT * 1024], BF16, "wo1_sb")
    sel16_sb = ar.alloc([16, 8 * 128], BF16, "sel16_sb")
    mask_sb = ar.alloc([128, 1], F32, "mask_sb")
    lnc2 = bD["xcb"] + [ar.alloc([128, NDT * 512], BF16, f"lnc2x{i}")
                        for i in range(2)]
    trig_tok = bD["sqb"]
    tqc = ar.alloc([16, TL], BF16, "tqc")
    tkc = ar.alloc([16, TL], BF16, "tkc")
    tqs = ar.alloc([16, TL], BF16, "tqs")
    tks = ar.alloc([16, TL], BF16, "tks")
    v1t_sb = ar.alloc([128, 512], F32, "v1t_sb")
    d4t_sb = ar.alloc([128, 512], F32, "d4t_sb")
    xc_sb = ar.alloc([128, NHE * TL], BF16, "xc_sb")
    xs_sb = ar.alloc([128, NHE * TL], BF16, "xs_sb")
    zer_sb = bD["lnc"]
    pub_sb = ar.alloc([128, 24], F32, "pub_sb")
    car_sb = ar.alloc([128, 24], F32, "car_sb")
    xdead = "xcf" + str(_lnbuf_n[0] - 2)
    kcc = ar.overlay(xdead, 0, [16, TL], BF16, "kcc")
    kcs = ar.overlay(xdead, TL * 2, [16, TL], BF16, "kcs")
    dent = ar.overlay(xdead, TL * 4, [16, TL], F32, "dent")
    dent2 = ar.overlay(xdead, TL * 12, [16, TL], BF16, "dent2")
    xdead2 = "xcf" + str(_lnbuf_n[0] - 1)
    a16 = ar.overlay(xdead2, 0, [16, TL], BF16, "a16")
    b16 = ar.overlay(xdead2, TL * 2, [16, TL], BF16, "b16")
    xr2_sb = [ar.overlay(xdead2, TL * 6, [128, 512], F32, "xr2_sb0"),
              ar.overlay(xdead2, TL * 7, [128, 512], F32, "xr2_sb1")]
    sqbD = "sqb" + str(_lnbuf_n[0] - 2)
    oc2_sb = [ar.overlay(sqbD, 0, [128, 512], F32, "oc2_sb0"),
              ar.overlay(sqbD, 2048, [128, 512], F32, "oc2_sb1")]

    g_wqk1 = p.dma("gpsimd", wqk1_sb.rearrange("p (dt m) -> p dt m", dt=NDT),
                   dwqk1.rearrange("(dt p) m -> p dt m", p=128), inc="g_tab")
    g_wv1 = p.dma("gpsimd", wv1_sb.rearrange("p (dt m) -> p dt m", dt=NDT),
                  dwv1.rearrange("(dt p) m -> p dt m", p=128), inc="g_tab")
    g_s16 = p.dma("gpsimd", sel16_sb[:, :], dsel16, inc="g_tab")
    g_msk = p.dma("gpsimd", mask_sb[:, :], dmask, inc="g_tab")
    MS(zer_sb[:, 0:TL], 0.0, waits=[("s_lnst", ln2_done)], inc="pl")
    zer_done = cnt["pl"]
    MS(pub_sb[:, :], 0.0, inc="pl")
    pub0_done = cnt["pl"]

    lc2_ld = [None] * qc_n
    for tc in range(qc_n):
        qd = "sync" if tc % 2 == 0 else "scalar"
        lc2_ld[tc] = p.dma(qd, lnc2[tc].rearrange("p (dt t) -> p dt t", dt=NDT),
                           dln2.rearrange("(dt p) t -> p dt t", p=128)[:, :, tc * 512:(tc + 1) * 512],
                           waits=[("s_lnst", ln2_ch[tc + 2] if tc < 2 else ln2_ch[tc])],
                           inc=f"lc2{tc}")
    for tch in range(16):
        tc4 = tch // 4
        co = (tch % 4) * 128
        for dt in range(NDT):
            w0 = []
            if dt == 0:
                w0 = [(f"lc2{tc4}", lc2_ld[tc4]), ("g_tab", g_wqk1), ("a", cnt["a"])]
            MM(psE[:, 0:64], lnc2[tc4][:, dt * 512 + co:dt * 512 + co + 128],
               wqk1_sb[:, dt * 64:(dt + 1) * 64],
               start=(dt == 0), stop=(dt == NDT - 1), waits=w0,
               inc=("t" if dt == NDT - 1 else None))
        ACTV(trig_tok[:, tch * 128:tch * 128 + 48], psE[:, 0:48], AF.Tanh,
             waits=[("t", cnt["t"])])
        ACTV(trig_tok[:, tch * 128:tch * 128 + 48],
             trig_tok[:, tch * 128:tch * 128 + 48], AF.Sin,
             scale=float(np.pi / 4))
        ACTV(trig_tok[:, tch * 128 + 64:tch * 128 + 112], psE[:, 0:48], AF.Tanh)
        ACTV(trig_tok[:, tch * 128 + 64:tch * 128 + 112],
             trig_tok[:, tch * 128 + 64:tch * 128 + 112], AF.Sin,
             scale=float(np.pi / 4), bias=consts[:, 3:4], inc="a")
    av = cnt["a"]
    for tch in range(16):
        w0 = [("v", cnt["v"])] if tch else []
        if tch == 0:
            w0 = [("a", av), ("g_tab", g_id)]
        p.emit("tensor",
               (lambda e, o=psT[0:48, 0:128],
                i=trig_tok[:, tch * 128 + 64:tch * 128 + 112],
                idn=id_sb[:, :]: e.transpose(o, i, idn)), waits=w0)
        p.emit("tensor",
               (lambda e, o=psT[0:48, 128:256],
                i=trig_tok[:, tch * 128:tch * 128 + 48],
                idn=id_sb[:, :]: e.transpose(o, i, idn)), inc="t")
        tsl = slice(tch * 128, (tch + 1) * 128)
        CP("vector", tqc[:, tsl], psT[0:16, 0:128], waits=[("t", cnt["t"])])
        CP("vector", tkc[:, tsl], psT[32:48, 0:128])
        CP("vector", tqs[:, tsl], psT[0:16, 128:256])
        CP("vector", tks[:, tsl], psT[32:48, 128:256], inc="v")
    trig_done = cnt["v"]

    d2_vb = [psC[:, 0:512], psA[:, 0:512]]
    d2_bc = [(psE[:, 0:512], psD[:, 0:512]),
             (psB[:, 0:512], psB[:, 512:1024])]
    d2_rv = [cnt["v"]] * 2
    d2_rb = [cnt["v"]] * 2
    for kt in range(NHE):
        for tc in range(qc_n):
            j = (kt * qc_n + tc) % 2
            vb = d2_vb[j]
            bcA, bcB = d2_bc[j]
            for dt in range(NDT):
                w0 = []
                if dt == 0:
                    w0 = [(f"lc2{tc}", lc2_ld[tc]), ("g_tab", g_wv1),
                          ("v", d2_rv[j])]
                MM(vb, wv1_sb[:, dt * 1024 + kt * 128:dt * 1024 + (kt + 1) * 128],
                   lnc2[tc][:, dt * 512:(dt + 1) * 512],
                   start=(dt == 0), stop=(dt == NDT - 1), waits=w0,
                   inc=("t" if dt == NDT - 1 else None))
            MM(bcA, sel16_sb[:, kt * 128:(kt + 1) * 128],
               tkc[:, tc * 512:(tc + 1) * 512], start=True, stop=True,
               waits=[("v", trig_done), ("v", d2_rb[j]), ("g_tab", g_s16)])
            tbb = MM(bcB, sel16_sb[:, kt * 128:(kt + 1) * 128],
                     tks[:, tc * 512:(tc + 1) * 512], start=True, stop=True,
                     inc="t")
            CP("vector", v1t_sb[:, :], vb, waits=[("t", tbb)])
            d2_rv[j] = cnt["v"]
            TT(xc_sb[:, kt * TL + tc * 512:kt * TL + (tc + 1) * 512],
               v1t_sb[:, :], bcA, ALU.mult)
            d2_rb[j] = TT(xs_sb[:, kt * TL + tc * 512:kt * TL + (tc + 1) * 512],
                          v1t_sb[:, :], bcB, ALU.mult, inc="v")
        RED(pub_sb[:, kt:kt + 1], xc_sb[:, kt * TL:(kt + 1) * TL], ALU.add,
            waits=([("pl", pub0_done)] if kt == 0 else []))
        RED(pub_sb[:, 8 + kt:9 + kt], xs_sb[:, kt * TL:(kt + 1) * TL], ALU.add)
    d2_t = cnt["t"]
    wo13 = wo1_sb.rearrange("p (kt d) -> p kt d", kt=NDT)
    dwo13 = dwo1.rearrange("(kt p) d -> p kt d", p=128)
    g_wo1 = p.dma("gpsimd", wo13[:, 0:4, :], dwo13[:, 0:4, :],
                  waits=[("t", d2_t)], inc="g_wo1")
    s_wo1 = p.dma("gpsimd", wo13[:, 4:8, :], dwo13[:, 4:8, :],
                  waits=[("t", d2_t)], inc="s_wo1")
    RED(pub_sb[0:16, 16:17], tkc[:, :], ALU.add)
    vpub = RED(pub_sb[0:16, 17:18], tks[:, :], ALU.add, inc="v")
    s_pub = p.dma("sync", dcar_in, pub_sb[:, :], waits=[("v", vpub)], inc="s_fin")
    p.sem("cc")
    ccg = p.emit("gpsimd", (lambda e: e.collective_compute(
        "AllGather", ALU.bypass,
        replica_groups=[[2 * i, 2 * i + 1] for i in range(NCORES // 2)],
        ins=[dcar_in.opt()], outs=[dcar_out.opt()])),
        waits=[("s_fin", s_pub)], inc="cc", amt=1, dma=True)

    zcol = zer_sb[:, 0:1]
    v_pub = cnt["v"]
    SCAN(kcc[:, :], tkc[:, :], zer_sb[0:16, 0:TL], zer_sb[0:16, 0:1],
         waits=[("pl", zer_done)])
    SCAN(kcs[:, :], tks[:, :], zer_sb[0:16, 0:TL], zer_sb[0:16, 0:1])
    for kt in range(NHE):
        SCAN(xc_sb[:, kt * TL:(kt + 1) * TL], xc_sb[:, kt * TL:(kt + 1) * TL],
             zer_sb[:, 0:TL], zcol)
        SCAN(xs_sb[:, kt * TL:(kt + 1) * TL], xs_sb[:, kt * TL:(kt + 1) * TL],
             zer_sb[:, 0:TL], zcol)

    def read_partner(e, dst=car_sb, srcd=dcar_out):
        pid = e.partition_id()
        partner = nc.s_assert_within(
            e.compute_val(1 - pid % 2), 0, 1, skip_runtime_assert=True)
        return e.dma_start(dst[:, :],
                           srcd.rearrange("(r p) f -> r p f", r=2)[partner, :, :])
    p.sem("g_car")
    g_car = p.emit("gpsimd", read_partner, waits=[("cc", cnt["cc"])], inc="g_car",
                   amt=16, dma=True)
    vcar = TS(car_sb[:, :], car_sb[:, :], mask_sb[:, 0:1], None, ALU.mult,
              waits=[("g_car", g_car), ("g_tab", g_msk)], inc="v")

    STT(dent[:, :], kcc[:, :], car_sb[0:16, 16:17], tqc[:, :], ALU.add, ALU.mult,
        waits=[("v", vcar)])
    STT(dent2[:, :], kcs[:, :], car_sb[0:16, 17:18], tqs[:, :], ALU.add, ALU.mult)
    TT(dent[:, :], dent[:, :], dent2[:, :], ALU.add)
    TS(dent[:, :], dent[:, :], consts[0:16, 2:3], None, ALU.add)
    RECIP(dent2[:, :], dent[:, :])
    TT(a16[:, :], tqc[:, :], dent2[:, :], ALU.mult)
    vab = TT(b16[:, :], tqs[:, :], dent2[:, :], ALU.mult, inc="v")

    d4_rel = [None, None]
    for kt in range(NHE):
        for tc in range(qc_n):
            j = (kt * qc_n + tc) % 2
            bcA, bcB = d2_bc[j]
            sl = slice(tc * 512, (tc + 1) * 512)
            ksl = slice(kt * TL + tc * 512, kt * TL + (tc + 1) * 512)
            MM(bcA, sel16_sb[:, kt * 128:(kt + 1) * 128], a16[:, sl],
               start=True, stop=True,
               waits=[("v", d4_rel[j] if d4_rel[j] else cnt["v"])])
            tb2 = MM(bcB, sel16_sb[:, kt * 128:(kt + 1) * 128],
                     b16[:, sl], start=True, stop=True, inc="t")
            STT(v1t_sb[:, :], xc_sb[:, ksl], car_sb[:, kt:kt + 1], bcA,
                ALU.add, ALU.mult, waits=[("t", tb2)])
            STT(d4t_sb[:, :], xs_sb[:, ksl], car_sb[:, 8 + kt:9 + kt],
                bcB, ALU.add, ALU.mult)
            d4_rel[j] = TT(xc_sb[:, ksl], v1t_sb[:, :], d4t_sb[:, :], ALU.add,
                           inc="v")
    heads_done = cnt["v"]
    dtt_h = [None, None]
    dst_h = [None, None]
    x3_tc = {}
    for i, (dc, tc) in enumerate([(d, t) for d in range(NDT)
                                  for t in range(qc_n)]):
        b = i % 2
        sx = p.dma("scalar", xr2_sb[b][:, :],
                   dx2[dc * 128:(dc + 1) * 128, tc * 512:(tc + 1) * 512],
                   waits=[("v", dtt_h[b] if dtt_h[b] else heads_done)],
                   inc=f"a_xr{b}")
        for kt in range(NDT):
            w0 = []
            if kt == 0:
                w0 = [("v", heads_done), ("v", cnt["v"]),
                      ("g_wo1", g_wo1), ("s_wo1", s_wo1)]
            MM(psC[:, 0:512],
               wo1_sb[:, kt * 1024 + dc * 128:kt * 1024 + (dc + 1) * 128],
               xc_sb[:, kt * TL + tc * 512:kt * TL + (tc + 1) * 512],
               start=(kt == 0), stop=(kt == NDT - 1), waits=w0,
               inc=("t" if kt == NDT - 1 else None))
        vo = TT(oc2_sb[b][:, :], psC[:, 0:512], xr2_sb[b][:, :], ALU.add,
                waits=[("t", cnt["t"]), (f"a_xr{b}", sx)] +
                      ([("s_x3", dst_h[b])] if dst_h[b] else []),
                inc="v")
        dtt_h[b] = vo
        dst_h[b] = p.dma("sync", dx3[dc * 128:(dc + 1) * 128, tc * 512:(tc + 1) * 512],
                         oc2_sb[b][:, :], waits=[("v", vo)], inc="s_x3")
        if dc == NDT - 1:
            x3_tc[tc] = dst_h[b]

    p.stage("moe")
    ar.reset()
    bE = mk_ln_bufs()
    _ie = _lnbuf_n[0] - 2
    lnc3 = bE["xcb"] + [ar.overlay(f"rows{_ie}", 0, [128, NDT * 512], BF16, "lnc3x0"),
                        ar.overlay(f"rows2_{_ie}", 0, [128, NDT * 512], BF16, "lnc3x1")]
    ng_sb = ar.alloc([128, 8], F32, "ng_sb")
    nb_sb = ar.alloc([128, 8], F32, "nb_sb")
    sg_sb = ar.alloc([128, 8], F32, "sg_sb")
    sbb_sb = ar.alloc([128, 8], F32, "sbb_sb")
    scol_sb = ar.alloc([128, 1], F32, "scol_sb")
    sc1_sb = ar.alloc([1, 1], F32, "sc1_sb")
    se_sb = ar.alloc([128, NDT * 1024], BF16, "se_sb")
    logF = ar.alloc([128, 8 * TL], F32, "logF")
    dD_sb = ar.alloc([128, 8 * TL], BF16, "dD_sb")
    mneg_sb = ar.alloc([128, 8], F32, "mneg_sb")
    mpos_sb = ar.alloc([128, 8], F32, "mpos_sb")
    sesum_sb = ar.alloc([128, 8], F32, "sesum_sb")
    cmneg_sb = ar.alloc([128, 1], F32, "cmneg_sb")
    crs_sb = ar.alloc([128, 1], F32, "crs_sb")
    bg_sb = ar.alloc([128, 32], F32, "bg_sb")
    bu_sb = ar.alloc([128, 32], F32, "bu_sb")
    bdr_sb = ar.alloc([1, 2048], BF16, "bdr_sb")
    xtok_sb = ar.overlay("logF", 0, [128, 16 * 1024], BF16, "xtok_sb")
    dtok_sb = ar.overlay("logF", 32768, [128, 16 * 1024], BF16, "dtok_sb")
    cfm_sb = ar.overlay("dD_sb", 0, [128, 8 * TL], BF16, "cfm_sb")
    wgx_sb = ar.overlay("logF", 0, [128, NDT * 2048], BF16, "wgx_sb")
    gt_sb = ar.overlay("logF", 32768, [128, 16 * 256], BF16, "gt_sb")
    gu_sb = ar.overlay("logF", 32768 + 8192, [128, 16 * 256], BF16, "gu_sb")
    slfm_sb = ar.overlay("logF", 32768 + 16384, [128, NDT * 512], BF16, "slfm_sb")
    bd_sb = ar.overlay("logF", 32768 + 24576, [128, 1024], F32, "bd_sb")
    eo_sb = ar.overlay("logF", 32768 + 28672, [128, 1024], BF16, "eo_sb")
    xdE = "xcf" + str(_lnbuf_n[0] - 2)
    eof_sb = ar.overlay(xdE, 0, [128, 8 * 1024], BF16, "eof_sb")
    xdE2 = "xcf" + str(_lnbuf_n[0] - 1)
    stg_sb = ar.overlay(xdE2, 0, [128, 1032], BF16, "stg_sb")
    cn_sb = ar.overlay(xdE2, 2080, [128, 1024], BF16, "cn_sb")
    prt2_sb = ar.overlay(xdE2, 4160, [128, 2080], BF16, "prt2")
    prt_sb = [prt2_sb[:, 0:1032], prt2_sb[:, 1040:2072]]
    combf_sb = ar.overlay(xdE2, 8352, [128, 1024], F32, "combf_sb")
    mnb_sb = ar.overlay(xdE2, 12448, [128, 8], BF16, "mnb_sb")
    mnr_sb = ar.overlay(xdE2, 12480, [128, 8], F32, "mnr_sb")
    sqbE = "sqb" + str(_lnbuf_n[0] - 2)
    xr3_sb = [ar.overlay(sqbE, 0, [128, 512], F32, "xr3_sb0"),
              ar.overlay(sqbE, 2048, [128, 512], F32, "xr3_sb1")]
    oc3_sb = [ar.overlay(sqbE, 4096, [128, 512], F32, "oc3_sb0"),
              ar.overlay(sqbE, 6144, [128, 512], F32, "oc3_sb1")]

    g_ng = p.dma("gpsimd", ng_sb[:, :], dng, inc="g_tab")
    p.dma("gpsimd", nb_sb[:, :], dnb, inc="g_tab")
    p.dma("gpsimd", sg_sb[:, :], dsg, inc="g_tab")
    p.dma("gpsimd", sbb_sb[:, :], dsbb, inc="g_tab")
    g_sc = p.dma("gpsimd", sc1_sb[:, :], dscale, inc="g_tab")
    MM(psE[:, 0:1], ones1x128[:, :], sc1_sb[:, :], start=True, stop=True,
       waits=[("g_tab", g_sc), ("v", cnt["v"])], inc="t")
    vsc = CP("vector", scol_sb[:, :], psE[:, 0:1], waits=[("t", cnt["t"])], inc="v")
    ln3_ch = emit_ln(dx3, TL, dln3, bE, gten=ng_sb, bten=nb_sb,
                     src_ready_fn=lambda ch: [("s_x3", x3_tc[ch]),
                                              ("g_tab", g_ng)])
    ln3_done = ln3_ch[-1]
    seln_done = emit_ln(dse, 1024, dse_ln, bE, gten=sg_sb, bten=sbb_sb,
                        postscale=scol_sb[:, 0:1],
                        src_ready=[("t", cnt["t"]), ("v", cnt["v"]),
                                   ("a", cnt["a"])])[-1]
    g_se = p.dma("gpsimd", se_sb.rearrange("p (dt s) -> p dt s", dt=NDT),
                 dse_ln.rearrange("(dt p) s -> p dt s", p=128),
                 waits=[("s_lnst", seln_done)], inc="g_tab")
    g_bg = p.dma("gpsimd", bg_sb[:, :], dbgc, inc="g_tab")
    g_bu = p.dma("gpsimd", bu_sb[:, :], dbuc, inc="g_tab")
    g_bd = p.dma("gpsimd", bdr_sb.rearrange("o (e f) -> o e f", e=2),
                 dbdr.rearrange("(o e) f -> o e f", o=1), inc="g_tab")
    seln_t = cnt["t"]

    lc3_ld = [None] * qc_n
    for tc in range(qc_n):
        qd = "sync" if tc % 2 == 0 else "scalar"
        wl = [("s_lnst", ln3_ch[tc]), ("t", seln_t)]
        lc3_ld[tc] = p.dma(qd, lnc3[tc].rearrange("p (dt t) -> p dt t", dt=NDT),
                           dln3.rearrange("(dt p) t -> p dt t", p=128)[:, :, tc * 512:(tc + 1) * 512],
                           waits=wl, inc=f"lc3{tc}")
    for tc in range(qc_n):
        for st in range(NDT):
            pbk = psC if (st * qc_n + tc) % 2 == 0 else psD
            for dt in range(NDT):
                w0 = []
                if dt == 0:
                    w0 = [(f"lc3{tc}", lc3_ld[tc]), ("g_tab", g_se), ("v", cnt["v"])]
                MM(pbk[:, 0:512], se_sb[:, dt * 1024 + st * 128:dt * 1024 + (st + 1) * 128],
                   lnc3[tc][:, dt * 512:(dt + 1) * 512],
                   start=(dt == 0), stop=(dt == NDT - 1), waits=w0,
                   inc=("t" if dt == NDT - 1 else None))
            CP("vector", logF[:, st * TL + tc * 512:st * TL + (tc + 1) * 512],
               pbk[:, 0:512], waits=[("t", cnt["t"])], inc="v")
    for st in range(NDT):
        ssl = slice(st * TL, (st + 1) * TL)
        RED(mneg_sb[:, st:st + 1], logF[:, ssl], ALU.max, negate=True)
        CP("vector", mnb_sb[:, st:st + 1], mneg_sb[:, st:st + 1])
        va = CP("vector", mnr_sb[:, st:st + 1], mnb_sb[:, st:st + 1], inc="v")
        ACTV(dD_sb[:, ssl], logF[:, ssl], AF.Exp, bias=mnr_sb[:, st:st + 1],
             waits=[("v", va)], inc="a")
        RED(sesum_sb[:, st:st + 1], dD_sb[:, ssl], ALU.add,
            waits=[("a", cnt["a"])], inc="v")
    disp_done_a, disp_done_v = cnt["a"], cnt["v"]

    for tch in range(16):
        tc4, sub = tch // 4, tch % 4
        for dt in range(NDT):
            w0 = [("v", cnt["v"])]
            if dt == 0:
                w0 += [(f"lc3{tc4}", lc3_ld[tc4]), ("a", disp_done_a)]
            p.emit("tensor",
                   (lambda e, o=psT[:, dt * 128:(dt + 1) * 128],
                    i=lnc3[tc4][:, dt * 512 + sub * 128:dt * 512 + sub * 128 + 128],
                    idn=id_sb[:, :]: e.transpose(o, i, idn)),
                   waits=w0, inc=("t" if dt == NDT - 1 else None))
        CP("vector", xtok_sb[:, tch * 1024:(tch + 1) * 1024], psT[:, :],
           waits=[("t", cnt["t"]), ("v", disp_done_v), ("a", disp_done_a)], inc="v")
    for tch in range(16):
        for st in range(NDT):
            w0 = [("v", cnt["v"])]
            p.emit("tensor",
                   (lambda e, o=psT[:, st * 128:(st + 1) * 128],
                    i=dD_sb[:, st * TL + tch * 128:st * TL + (tch + 1) * 128],
                    idn=id_sb[:, :]: e.transpose(o, i, idn)),
                   waits=w0, inc=("t" if st == NDT - 1 else None))
        CP("vector", dtok_sb[:, tch * 1024:(tch + 1) * 1024], psT[:, :],
           waits=[("t", cnt["t"])], inc="v")
    tr_done = cnt["v"]

    MS(stg_sb[:, 1026:1032], 0.0, inc="pl")
    stg0 = cnt["pl"]
    for st in range(NDT):
        for tch in range(16):
            w0 = []
            if tch == 0:
                w0 = [("v", tr_done), ("v", cnt["v"])]
            MM(psA[:, 0:512], dtok_sb[:, tch * 1024 + st * 128:tch * 1024 + (st + 1) * 128],
               xtok_sb[:, tch * 1024:tch * 1024 + 512],
               start=(tch == 0), stop=(tch == 15), waits=w0)
            MM(psA[:, 512:1024], dtok_sb[:, tch * 1024 + st * 128:tch * 1024 + (st + 1) * 128],
               xtok_sb[:, tch * 1024 + 512:tch * 1024 + 1024],
               start=(tch == 0), stop=(tch == 15),
               inc=("t" if tch == 15 else None))
        t_slots = cnt["t"]
        CP("vector", stg_sb[:, 0:512], psA[:, 0:512],
           waits=[("t", cnt["t"]), ("pl", stg0)] +
                 ([("s_slag", cnt["s_slag"])] if cnt.get("s_slag") else []))
        CP("vector", stg_sb[:, 512:1024], psA[:, 512:1024])
        p.emit("vector", (lambda e, o=stg_sb[:, 1024:1025],
                          i=mnr_sb[:, st:st + 1]: e.tensor_scalar_mul(o, i, -1.0)))
        vstg = CP("vector", stg_sb[:, 1025:1026], sesum_sb[:, st:st + 1], inc="v")
        p.dma("sync", dsl_in[st * 128:(st + 1) * 128, :], stg_sb[:, :],
              waits=[("v", vstg)], inc="s_slag")
    p.sem("cc")
    cc_sl = p.emit("gpsimd", (lambda e: e.collective_compute(
        "AllGather", ALU.bypass, replica_groups=[list(range(NCORES))],
        ins=[dsl_in.opt()], outs=[dsl_out.opt()])),
        waits=[("s_slag", cnt["s_slag"])], inc="cc", amt=1, dma=True)

    p.sem("g_wx")
    p.sem("s_wx")

    def load_wx(dram_e, nch, war_t):
        dst = wgx_sb.rearrange("p (c f) -> p c f", c=nch)
        src = dram_e
        h = nch // 2
        wg_ = [("t", war_t)]
        if cnt["g_wx"]:
            wg_.append(("g_wx", cnt["g_wx"]))
        g = p.dma("gpsimd", dst[:, 0:h, :], src[:, 0:h, :], waits=wg_, inc="g_wx")
        ws_ = [("t", war_t)]
        if cnt["s_wx"]:
            ws_.append(("s_wx", cnt["s_wx"]))
        s = p.dma("gpsimd", dst[:, h:nch, :], src[:, h:nch, :], waits=ws_, inc="s_wx")
        return g, s, h

    wg0_pre = load_wx(dwg.rearrange("e (dt p) f -> e p dt f", p=128)[0],
                      NDT, t_slots)

    for tch in range(16):
        tc4, sub = tch // 4, tch % 4
        for dt in range(NDT):
            w0 = []
            if dt == 0:
                w0 = [(f"lc3{tc4}", lc3_ld[tc4]), ("v", cnt["v"]), ("a", cnt["a"])]
            MM(psB[:, 0:512], lnc3[tc4][:, dt * 512 + sub * 128:dt * 512 + sub * 128 + 128],
               se_sb[:, dt * 1024:dt * 1024 + 512],
               start=(dt == 0), stop=(dt == NDT - 1), waits=w0)
            MM(psB[:, 512:1024], lnc3[tc4][:, dt * 512 + sub * 128:dt * 512 + sub * 128 + 128],
               se_sb[:, dt * 1024 + 512:dt * 1024 + 1024],
               start=(dt == 0), stop=(dt == NDT - 1),
               inc=("t" if dt == NDT - 1 else None))
        RED(cmneg_sb[:, 0:1], psB[:, 0:1024], ALU.max, negate=True,
            waits=[("t", cnt["t"])])
        ACTV(cn_sb[:, :], psB[:, 0:1024], AF.Exp, bias=cmneg_sb[:, 0:1],
             waits=[("v", cnt["v"])], inc="a")
        RED(crs_sb[:, 0:1], cn_sb[:, :], ALU.add, waits=[("a", cnt["a"])])
        RECIP(crs_sb[:, 0:1], crs_sb[:, 0:1])
        TS(cn_sb[:, :], cn_sb[:, :], crs_sb[:, 0:1], None, ALU.mult, inc="v")
        for st in range(NDT):
            w0 = [("v", cnt["v"])]
            p.emit("tensor",
                   (lambda e, o=psT[:, st * 128:(st + 1) * 128],
                    i=cn_sb[:, st * 128:(st + 1) * 128],
                    idn=id_sb[:, :]: e.transpose(o, i, idn)),
                   waits=w0, inc=("t" if st == NDT - 1 else None))
        p.emit("vector",
               (lambda e, tch=tch: e.tensor_copy(
                   cfm_sb.rearrange("p (st t) -> p st t", st=NDT)[:, :, tch * 128:(tch + 1) * 128],
                   psT.rearrange("p (st c) -> p st c", st=NDT))),
               waits=[("t", cnt["t"])], inc="v")
    cfm_done = cnt["v"]

    p.sem("g_sl")
    _dynidx = {}

    def get_me(e):
        if "me" not in _dynidx:
            pid = e.partition_id()
            _dynidx["me"] = nc.s_assert_within(pid, 0, 7, skip_runtime_assert=True)
        return _dynidx["me"]

    for b in range(4):
        def read_pair(e, b=b):
            me = get_me(e)
            return e.dma_start(
                prt2_sb.rearrange("p (r f) -> p r f", r=2)[:, :, 0:1032],
                dsl_out.rearrange("(r q p) f -> q p r f", q=NCORES, p=128)[me, :, 2 * b:2 * b + 2, :])
        p.emit("gpsimd", read_pair,
               waits=[("cc", cc_sl), ("g_sl", cnt["g_sl"])] +
                     ([("v", cnt["v"])] if b else []),
               inc="g_sl", amt=16, dma=True)
        g_pr = cnt["g_sl"]
        TT(cmneg_sb[:, 0:1], prt_sb[0][:, 1024:1025], prt_sb[1][:, 1024:1025],
           ALU.max, waits=[("g_sl", g_pr)])
        TT(crs_sb[:, 0:1], prt_sb[0][:, 1024:1025], cmneg_sb[:, 0:1], ALU.subtract)
        ACTV(crs_sb[:, 0:1], crs_sb[:, 0:1], AF.Exp, waits=[("v", cnt["v"])], inc="a")
        TT(mneg_sb[:, 0:1], prt_sb[1][:, 1024:1025], cmneg_sb[:, 0:1], ALU.subtract,
           waits=[("a", cnt["a"])])
        ACTV(mneg_sb[:, 0:1], mneg_sb[:, 0:1], AF.Exp, waits=[("v", cnt["v"])], inc="a")
        TS(mpos_sb[:, 0:1], prt_sb[0][:, 1025:1026], crs_sb[:, 0:1], None, ALU.mult,
           waits=[("a", cnt["a"])])
        STT(mpos_sb[:, 0:1], prt_sb[1][:, 1025:1026], mneg_sb[:, 0:1],
            mpos_sb[:, 0:1], ALU.mult, ALU.add)
        RECIP(mpos_sb[:, 0:1], mpos_sb[:, 0:1])
        TS(combf_sb[:, :], prt_sb[0][:, 0:1024], crs_sb[:, 0:1], None, ALU.mult)
        STT(combf_sb[:, :], prt_sb[1][:, 0:1024], mneg_sb[:, 0:1],
            combf_sb[:, :], ALU.mult, ALU.add)
        TS(cn_sb[:, :], combf_sb[:, :], mpos_sb[:, 0:1], None, ALU.mult, inc="v")
        for dt in range(NDT):
            w0 = [("v", cnt["v"])]
            p.emit("tensor",
                   (lambda e, o=psT[:, dt * 128:(dt + 1) * 128],
                    i=cn_sb[:, dt * 128:(dt + 1) * 128],
                    idn=id_sb[:, :]: e.transpose(o, i, idn)),
                   waits=w0, inc=("t" if dt == NDT - 1 else None))
        vtr = p.emit("vector",
                     (lambda e, b=b: e.tensor_copy(
                         slfm_sb.rearrange("p (dt c) -> p dt c", dt=NDT)[:, :, b * 128:(b + 1) * 128],
                         psT.rearrange("p (dt c) -> p dt c", dt=NDT))),
                     waits=[("t", cnt["t"])], inc="v")
    slots_done = cnt["v"]

    for e_i in range(2):
        if e_i == 0:
            g_wg, s_wg, hh = wg0_pre
        else:
            g_wg, s_wg, hh = load_wx(
                dwg.rearrange("e (dt p) f -> e p dt f", p=128)[e_i], NDT, cnt["t"])
        sl_e = slfm_sb.rearrange("p (dt b s) -> p dt b s", dt=NDT, b=4)
        for fc in range(16):
            for dt in range(NDT):
                w0 = []
                if dt == 0:
                    w0 = [("a", cnt["a"])]
                    if fc == 0:
                        w0 += [("g_wx", g_wg), ("v", slots_done)]
                if fc == 0 and dt == hh:
                    w0 = w0 + [("s_wx", s_wg)]
                MM(psC[:, 0:256], wgx_sb[:, dt * 2048 + fc * 128:dt * 2048 + (fc + 1) * 128],
                   sl_e[:, dt, :, e_i * 64:(e_i + 1) * 64],
                   start=(dt == 0), stop=(dt == NDT - 1), waits=w0,
                   inc=("t" if dt == NDT - 1 else None))
            ACTV(gt_sb[:, fc * 256:(fc + 1) * 256], psC[:, 0:256], AF.Gelu,
                 bias=bg_sb[:, e_i * 16 + fc:e_i * 16 + fc + 1],
                 waits=[("t", cnt["t"]), ("g_tab", g_bg)], inc="a")
        gate_a = cnt["a"]
        g_wu, s_wu, _ = load_wx(dwu.rearrange("e (dt p) f -> e p dt f", p=128)[e_i],
                                NDT, cnt["t"])
        for fc in range(16):
            for dt in range(NDT):
                w0 = []
                if dt == 0:
                    w0 = [("v", cnt["v"])]
                    if fc == 0:
                        w0.append(("g_wx", g_wu))
                if fc == 0 and dt == hh:
                    w0 = w0 + [("s_wx", s_wu)]
                MM(psD[:, 0:256], wgx_sb[:, dt * 2048 + fc * 128:dt * 2048 + (fc + 1) * 128],
                   sl_e[:, dt, :, e_i * 64:(e_i + 1) * 64],
                   start=(dt == 0), stop=(dt == NDT - 1), waits=w0,
                   inc=("t" if dt == NDT - 1 else None))
            STT(gu_sb[:, fc * 256:(fc + 1) * 256], psD[:, 0:256],
                bu_sb[:, e_i * 16 + fc:e_i * 16 + fc + 1],
                gt_sb[:, fc * 256:(fc + 1) * 256], ALU.add, ALU.mult,
                waits=[("t", cnt["t"]), ("a", gate_a), ("g_tab", g_bu)], inc="v")
        gu_done = cnt["v"]
        g_wd, s_wd, hd = load_wx(dwd.rearrange("e (ft p) o -> e p ft o", p=128)[e_i],
                                 16, cnt["t"])
        MM(psB[:, 0:512], ones1x128b[:, :], bdr_sb[:, e_i * 1024:e_i * 1024 + 512],
           start=True, stop=True, waits=[("g_tab", g_bd), ("v", cnt["v"])])
        MM(psB[:, 512:1024], ones1x128b[:, :],
           bdr_sb[:, e_i * 1024 + 512:e_i * 1024 + 1024], start=True, stop=True,
           inc="t")
        CP("vector", bd_sb[:, :], psB[:, 0:1024], waits=[("t", cnt["t"])], inc="v")
        for sc in range(2):
            for ft in range(16):
                w0 = []
                if ft == 0:
                    w0 = [("v", gu_done), ("v", cnt["v"])]
                    if sc == 0:
                        w0.append(("g_wx", g_wd))
                if sc == 0 and ft == hd:
                    w0 = w0 + [("s_wx", s_wd)]
                MM(psA[:, 0:512], gu_sb[:, ft * 256 + sc * 128:ft * 256 + sc * 128 + 128],
                   wgx_sb[:, ft * 1024:ft * 1024 + 512],
                   start=(ft == 0), stop=(ft == 15), waits=w0)
                MM(psA[:, 512:1024], gu_sb[:, ft * 256 + sc * 128:ft * 256 + sc * 128 + 128],
                   wgx_sb[:, ft * 1024 + 512:ft * 1024 + 1024],
                   start=(ft == 0), stop=(ft == 15),
                   inc=("t" if ft == 15 else None))
            veo = TT(eo_sb[:, :], psA[:, 0:1024], bd_sb[:, :], ALU.add,
                     waits=[("t", cnt["t"])] +
                           ([("s_eoag", cnt["s_eoag"])] if cnt.get("s_eoag") else []),
                     inc="v")
            p.dma("sync",
                  deo_in[2 * sc * 128 + e_i * 64:2 * sc * 128 + (e_i + 1) * 64, :],
                  eo_sb[0:64, :], waits=[("v", veo)], inc="s_eoag")
            p.dma("sync",
                  deo_in[(2 * sc + 1) * 128 + e_i * 64:(2 * sc + 1) * 128 + (e_i + 1) * 64, :],
                  eo_sb[64:128, :], inc="s_eoag")
    cc_eo = p.emit("gpsimd", (lambda e: e.collective_compute(
        "AllGather", ALU.bypass, replica_groups=[list(range(NCORES))],
        ins=[deo_in.opt()], outs=[deo_out.opt()])),
        waits=[("s_eoag", cnt["s_eoag"])], inc="cc", amt=1, dma=True)

    p.sem("g_eo")

    def read_eo(e):
        pid = e.partition_id()
        bp = nc.s_assert_within(e.compute_val(pid // 2), 0, 3,
                                skip_runtime_assert=True)
        return e.dma_start(
            eof_sb.rearrange("p (r f) -> p r f", r=NCORES),
            deo_out.rearrange("(r q p) f -> q p r f", q=4, p=128)[bp, :, :, :])
    eof_done = p.emit("gpsimd", read_eo,
                      waits=[("cc", cc_eo), ("g_eo", cnt["g_eo"]),
                             ("a", disp_done_a)],
                      inc="g_eo", amt=16, dma=True)

    ett_h = [None, None]
    est_h = [None, None]
    e9_v0 = cnt["v"]
    for i, (dc, tc) in enumerate([(d, t) for d in range(NDT)
                                  for t in range(qc_n)]):
        b = i % 2
        sx = p.dma("scalar", xr3_sb[b][:, :],
                   dx3[dc * 128:(dc + 1) * 128, tc * 512:(tc + 1) * 512],
                   waits=[("v", ett_h[b] if ett_h[b] else e9_v0)],
                   inc=f"a_xr{b}")
        for st in range(NDT):
            w0 = []
            if st == 0:
                w0 = [("g_eo", eof_done), ("v", cfm_done), ("v", cnt["v"])]
            MM(psC[:, 0:512],
               eof_sb[:, st * 1024 + dc * 128:st * 1024 + (dc + 1) * 128],
               cfm_sb[:, st * TL + tc * 512:st * TL + (tc + 1) * 512],
               start=(st == 0), stop=(st == NDT - 1), waits=w0,
               inc=("t" if st == NDT - 1 else None))
        vo = TT(oc3_sb[b][:, :], psC[:, 0:512], xr3_sb[b][:, :], ALU.add,
                waits=[("t", cnt["t"]), (f"a_xr{b}", sx)] +
                      ([("o", est_h[b])] if est_h[b] else []),
                inc="v")
        ett_h[b] = vo
        est_h[b] = p.dma("sync", dout[dc * 128:(dc + 1) * 128, tc * 512:(tc + 1) * 512],
                         oc3_sb[b][:, :], waits=[("v", vo)], inc="o")
    p.emit("sync", lambda e: e.wait_ge(p.sems["o"], cnt["o"]) and None)

    if DEBUG_STAGE in (0, 4):
        p.replay()
        return nc
    p.stage("fin")
    ar.reset()
    fin_sb = ar.alloc([128, 2048], F32, "fin_sb")
    if DEBUG_STAGE == 1:
        src_d, sds = dx1, "s_x1"
    elif DEBUG_STAGE == 2:
        src_d, sds = dx2, "s_x2"
    else:
        src_d, sds = dx3, "s_x3"
    stream_done = [(sds, cnt[sds])]
    for dc in range(NDT):
        ld = p.dma("sync", fin_sb[:, :], src_d[dc * 128:(dc + 1) * 128, :],
                   waits=stream_done + ([("o", cnt["o"])] if dc else []), inc="s_fin")
        p.dma("sync", dout[dc * 128:(dc + 1) * 128, :], fin_sb[:, :],
              waits=[("s_fin", ld)], inc="o")
    p.emit("sync", lambda e: e.wait_ge(p.sems["o"], cnt["o"]) and None)

    p.replay()
    return nc


_CACHE = {}


def kernel(**inputs):
    if "nc" not in _CACHE:
        _CACHE["nc"] = build()
    nc = _CACHE["nc"]
    in_maps = _prep(inputs)
    res = run_bass_kernel_spmd(nc, in_maps, core_ids=list(range(NCORES)))
    out = np.empty((B, T, D), np.float32)
    for c in range(NCORES):
        b, half = c // 2, c % 2
        t0 = half * TL
        out[b, t0:t0 + TL, :] = res.results[c]["out_fm"].T
    return out


# version 15
# speedup vs baseline: 1.1619x; 1.1619x over previous
import numpy as np
import concourse.bass as bass
import concourse.mybir as mybir
from concourse.bass_utils import run_bass_kernel_spmd

F32 = mybir.dt.float32
BF16 = mybir.dt.bfloat16
AF = mybir.ActivationFunctionType
ALU = mybir.AluOpType
AX = mybir.AxisListType

NCORES = 8
B, T, D, H, DH = 4, 4096, 1024, 16, 64
TL = T // 2
NDT = D // 128
EPS_LN = 1e-5

DEBUG_STAGE = 0


ENGS = ("tensor", "vector", "scalar", "gpsimd", "sync")


class Prog:
    def __init__(self, nc):
        self.nc = nc
        self.stages = []
        self.cnt = {}
        self.sems = {}
        self.cur = None

    def sem(self, name):
        if name not in self.sems:
            self.sems[name] = self.nc.alloc_semaphore(name)
            self.cnt[name] = 0
        return name

    def stage(self, name):
        self.cur = {e: [] for e in ENGS}
        self.cur["_name"] = name
        self.stages.append(self.cur)

    ENG_SEM = {"vector": "v", "scalar": "a", "gpsimd": "pl"}

    def emit(self, eng, fn, waits=(), inc=None, amt=1, dma=False, chain=True):
        for s, v in waits:
            assert v <= self.cnt[s], (s, v, self.cnt[s])
        waits = list(waits)
        if not dma and eng in self.ENG_SEM:
            es = self.ENG_SEM[eng]
            self.sem(es)
            if chain and self.cnt[es]:
                waits.append((es, self.cnt[es]))
            if inc is None:
                inc = es
            assert inc == es, (eng, inc)
        rec = (fn, tuple(waits), inc, amt)
        self.cur[eng].append(rec)
        if inc is not None:
            self.cnt[inc] += amt
            return self.cnt[inc]
        return None

    def dma(self, eng, out, in_, waits=(), inc=None, **kw):
        if inc is None:
            inc = eng[0]
        self.sem(inc)
        w = list(waits)
        if self.cnt[inc]:
            w.append((inc, self.cnt[inc]))
        return self.emit(eng, lambda e: e.dma_start(out, in_, **kw), w, inc, 16,
                         dma=True)

    def replay(self):
        nc = self.nc
        ctx = nc.allow_low_precision(reason="bf16 matmul operands per problem tolerance")
        ctx.__enter__()
        for st in self.stages:
            with nc.Block(st["_name"]) as block:
                for eng in ENGS:
                    recs = st[eng]
                    if not recs:
                        continue

                    def body(e, recs=recs):
                        for fn, waits, inc, amt in recs:
                            for s, v in waits:
                                e.wait_ge(self.sems[s], v)
                            inst = fn(e)
                            if inc is not None and inst is not None:
                                inst.then_inc(self.sems[inc], amt)

                    getattr(block, eng)(body)


class Arena:

    def __init__(self, nc, slack=5120):
        self.nc = nc
        size = nc.sbuf_bytes_remaining - slack
        base, _ = nc.bump_sbuf(size)
        self.base = base
        self.off = base
        self.limit = base + size
        self.mark_off = None
        self.n = 0

    def alloc(self, shape, dtype, name=None):
        assert len(shape) == 2 and shape[0] <= 128
        itm = 2 if dtype == BF16 else 4
        sz = shape[1] * itm
        off = (self.off + 31) & ~31
        assert off + sz <= self.limit, (
            f"SBUF arena overflow: {name} needs {sz} at {off}, limit {self.limit}")
        self.off = off + sz
        self.n += 1
        nm = name or f"ar{self.n}"
        if not hasattr(self, "offs"):
            self.offs = {}
        self.offs[nm] = off
        return self.nc.alloc_sbuf_tensor_at(nm, list(shape), dtype, offset=off)

    def overlay(self, base_name, delta, shape, dtype, name):
        off = self.offs[base_name] + delta
        self.n += 1
        return self.nc.alloc_sbuf_tensor_at(name, list(shape), dtype, offset=off)

    def mark(self, key="base"):
        if self.mark_off is None:
            self.mark_off = {}
        self.mark_off[key] = self.off

    def reset(self, key="base"):
        self.off = self.mark_off[key]


def _rope_tables():
    base = 10000.0
    inv = 1.0 / base ** (np.arange(0, DH, 2, dtype=np.float64) / DH)
    ang = np.arange(T, dtype=np.float64)[:, None] * inv[None, :]
    cos = np.cos(ang)
    sin = np.sin(ang)
    cs = np.empty((128, T), np.float32)
    ss = np.empty((128, T), np.float32)
    for p in range(128):
        d = p % 64
        cs[p] = cos[:, d % 32].astype(np.float32)
        sgn = -1.0 if (d % 64) < 32 else 1.0
        ss[p] = (sgn * sin[:, d % 32]).astype(np.float32)
    return cs, ss


def _prep_core(inputs, c):
    b, half = c // 2, c % 2
    t0, o0 = half * TL, (1 - half) * TL
    order = np.r_[t0:t0 + TL, o0:o0 + TL]

    x = np.asarray(inputs["L0_attn_hidden_states"])
    cs, ss = _rope_tables()

    d = {}
    d["x"] = np.ascontiguousarray(x[b][order].T)
    d["wqT"] = np.ascontiguousarray(np.asarray(inputs["L0_attn_Wq"]).T)
    d["wkT"] = np.ascontiguousarray(np.asarray(inputs["L0_attn_Wk"]).T)
    wv = np.asarray(inputs["L0_attn_Wv"])
    wv65 = np.zeros((D, H * 65), np.float32)
    for h in range(H):
        wv65[:, h * 65:h * 65 + 64] = wv[h * 64:(h + 1) * 64, :].T
    d["wv65"] = wv65
    bv = np.asarray(inputs["L0_attn_bv"])
    bv_row = np.zeros((1, H * 65), np.float32)
    for h in range(H):
        bv_row[0, h * 65:h * 65 + 64] = bv[h * 64:(h + 1) * 64]
    d["bv_row"] = bv_row
    d["bq_p"] = np.ascontiguousarray(np.asarray(inputs["L0_attn_bq"]).reshape(8, 128).T)
    d["bk_p"] = np.ascontiguousarray(np.asarray(inputs["L0_attn_bk"]).reshape(8, 128).T)
    d["bo_c"] = np.ascontiguousarray(np.asarray(inputs["L0_attn_bo"]).reshape(8, 128).T)
    d["woT"] = np.ascontiguousarray(np.asarray(inputs["L0_attn_Wo"]).T)
    d["cs"] = np.ascontiguousarray(cs[:, order])
    d["ss"] = np.ascontiguousarray(ss[:, order])

    sel128 = np.zeros((128, 128), np.float32)
    sel128[0, 0:64] = 1.0
    sel128[64, 64:128] = 1.0
    d["sel128"] = sel128
    d["ident"] = np.eye(128, dtype=np.float32)
    perm = np.r_[32:64, 0:32, 96:128, 64:96]
    P = np.zeros((128, 128), np.float32)
    P[perm, np.arange(128)] = 1.0
    d["permm"] = P

    d["w1t"] = np.ascontiguousarray(np.asarray(inputs["L0_ffn_weight"]).T)
    d["disha_p"] = np.ascontiguousarray(np.asarray(inputs["L0_ffn_disha"]))
    psel = np.zeros((128, 64), np.float32)
    for pp in range(128):
        psel[pp, pp % 64] = 1.0
    d["psel"] = psel

    wq1 = np.asarray(inputs["L1_attn_W_Q"])[:, :, 0]
    wk1 = np.asarray(inputs["L1_attn_W_K"])[:, :, 0]
    wqk1 = np.zeros((D, 64), np.float32)
    wqk1[:, 0:16] = wq1.T
    wqk1[:, 32:48] = wk1.T
    d["wqk1"] = wqk1
    wv1 = np.asarray(inputs["L1_attn_W_V"])
    d["wv1"] = np.ascontiguousarray(wv1.transpose(1, 0, 2).reshape(D, D))
    wo1 = np.asarray(inputs["L1_attn_W_O"])
    d["wo1"] = np.ascontiguousarray(wo1.transpose(2, 1, 0).reshape(D, D))
    sel16 = np.zeros((16, 8 * 128), np.float32)
    for kt in range(8):
        for pp in range(128):
            sel16[2 * kt + (pp >= 64), kt * 128 + pp] = 1.0
    d["sel16"] = sel16
    d["oddmask"] = np.full((128, 1), float(half), np.float32)

    se_raw = np.asarray(inputs["L1_ffn_slot_embeds"]).reshape(1024, D)
    d["se_raw"] = np.ascontiguousarray(se_raw.T)
    d["ng"] = np.ascontiguousarray(np.asarray(inputs["L1_ffn_norm_g"]).reshape(8, 128).T)
    d["nb"] = np.ascontiguousarray(np.asarray(inputs["L1_ffn_norm_b"]).reshape(8, 128).T)
    d["sg"] = np.ascontiguousarray(np.asarray(inputs["L1_ffn_slot_g"]).reshape(8, 128).T)
    d["sbb"] = np.ascontiguousarray(np.asarray(inputs["L1_ffn_slot_b"]).reshape(8, 128).T)
    d["scale1"] = np.asarray(inputs["L1_ffn_scale"]).reshape(1, 1).astype(np.float32)
    es = (2 * c, 2 * c + 1)
    d["wg"] = np.ascontiguousarray(np.asarray(inputs["L1_ffn_Wg"])[list(es)])
    d["wu"] = np.ascontiguousarray(np.asarray(inputs["L1_ffn_Wu"])[list(es)])
    d["wd"] = np.ascontiguousarray(np.asarray(inputs["L1_ffn_Wd"])[list(es)])
    bg = np.asarray(inputs["L1_ffn_bg"])
    bu = np.asarray(inputs["L1_ffn_bu"])
    bgc = np.zeros((128, 32), np.float32)
    buc = np.zeros((128, 32), np.float32)
    for j, eg in enumerate(es):
        bgc[:, j * 16:(j + 1) * 16] = bg[eg].reshape(16, 128).T
        buc[:, j * 16:(j + 1) * 16] = bu[eg].reshape(16, 128).T
    d["bgc"] = bgc
    d["buc"] = buc
    d["bdr"] = np.ascontiguousarray(np.asarray(inputs["L1_ffn_bd"])[list(es)])
    return d


def _prep(inputs):
    return [_prep_core(inputs, c) for c in range(NCORES)]


def build():
    nc = bass.Bass("TRN2", target_bir_lowering=False)
    p = Prog(nc)
    for s in ("g", "s", "t", "v", "a", "pl", "cc", "o"):
        p.sem(s)

    dx = nc.dram_tensor("x", [D, T], F32, kind="ExternalInput").ap()
    dwqT = nc.dram_tensor("wqT", [D, D], F32, kind="ExternalInput").ap()
    dwkT = nc.dram_tensor("wkT", [D, D], F32, kind="ExternalInput").ap()
    dwv65 = nc.dram_tensor("wv65", [D, H * 65], F32, kind="ExternalInput").ap()
    dbv = nc.dram_tensor("bv_row", [1, H * 65], F32, kind="ExternalInput").ap()
    dbq = nc.dram_tensor("bq_p", [128, 8], F32, kind="ExternalInput").ap()
    dbk = nc.dram_tensor("bk_p", [128, 8], F32, kind="ExternalInput").ap()
    dbo = nc.dram_tensor("bo_c", [128, 8], F32, kind="ExternalInput").ap()
    dwoT = nc.dram_tensor("woT", [D, D], F32, kind="ExternalInput").ap()
    dcs = nc.dram_tensor("cs", [128, T], F32, kind="ExternalInput").ap()
    dss = nc.dram_tensor("ss", [128, T], F32, kind="ExternalInput").ap()
    dsel = nc.dram_tensor("sel128", [128, 128], F32, kind="ExternalInput").ap()
    dpermm = nc.dram_tensor("permm", [128, 128], F32, kind="ExternalInput").ap()

    dout = nc.dram_tensor("out_fm", [D, TL], F32, kind="ExternalOutput").ap()

    dw1t = nc.dram_tensor("w1t", [D, D], F32, kind="ExternalInput").ap()
    ddisha = nc.dram_tensor("disha_p", [64, D], F32, kind="ExternalInput").ap()
    dpsel = nc.dram_tensor("psel", [128, 64], F32, kind="ExternalInput").ap()
    dwqk1 = nc.dram_tensor("wqk1", [D, 64], F32, kind="ExternalInput").ap()
    dwv1 = nc.dram_tensor("wv1", [D, D], F32, kind="ExternalInput").ap()
    dwo1 = nc.dram_tensor("wo1", [D, D], F32, kind="ExternalInput").ap()
    dsel16 = nc.dram_tensor("sel16", [16, 8 * 128], F32, kind="ExternalInput").ap()
    dmask = nc.dram_tensor("oddmask", [128, 1], F32, kind="ExternalInput").ap()
    dident = nc.dram_tensor("ident", [128, 128], F32, kind="ExternalInput").ap()

    dln0 = nc.dram_tensor("ln0d", [D, T], BF16, kind="Internal").ap()
    dx1 = nc.dram_tensor("x1d", [D, TL], F32, kind="Internal").ap()
    dx2 = nc.dram_tensor("x2d", [D, TL], F32, kind="Internal").ap()
    dx3 = nc.dram_tensor("x3d", [D, TL], F32, kind="Internal").ap()
    dln1 = nc.dram_tensor("ln1d", [D, TL], BF16, kind="Internal").ap()
    dln2 = nc.dram_tensor("ln2d", [D, TL], BF16, kind="Internal").ap()
    dse = nc.dram_tensor("se_raw", [D, 1024], F32, kind="ExternalInput").ap()
    dng = nc.dram_tensor("ng", [128, 8], F32, kind="ExternalInput").ap()
    dnb = nc.dram_tensor("nb", [128, 8], F32, kind="ExternalInput").ap()
    dsg = nc.dram_tensor("sg", [128, 8], F32, kind="ExternalInput").ap()
    dsbb = nc.dram_tensor("sbb", [128, 8], F32, kind="ExternalInput").ap()
    dscale = nc.dram_tensor("scale1", [1, 1], F32, kind="ExternalInput").ap()
    dwg = nc.dram_tensor("wg", [2, D, 2048], F32, kind="ExternalInput").ap()
    dwu = nc.dram_tensor("wu", [2, D, 2048], F32, kind="ExternalInput").ap()
    dwd = nc.dram_tensor("wd", [2, 2048, D], F32, kind="ExternalInput").ap()
    dbgc = nc.dram_tensor("bgc", [128, 32], F32, kind="ExternalInput").ap()
    dbuc = nc.dram_tensor("buc", [128, 32], F32, kind="ExternalInput").ap()
    dbdr = nc.dram_tensor("bdr", [2, 1024], F32, kind="ExternalInput").ap()
    dln3 = nc.dram_tensor("ln3d", [D, TL], BF16, kind="Internal").ap()
    dse_ln = nc.dram_tensor("se_lnd", [D, 1024], BF16, kind="Internal").ap()
    dsl_in = nc.dram_tensor("sl_in", [1024, 1032], BF16, kind="Internal").ap()
    dsl_out = nc.dram_tensor("sl_out", [1024 * NCORES, 1032], BF16, kind="Internal").ap()
    deo_in = nc.dram_tensor("eo_in", [512, 1024], BF16, kind="Internal").ap()
    deo_out = nc.dram_tensor("eo_out", [512 * NCORES, 1024], BF16, kind="Internal").ap()
    dcar_in = nc.dram_tensor("car_in", [128, 24], F32, kind="Internal").ap()
    dcar_out = nc.dram_tensor("car_out", [128 * 2, 24], F32, kind="Internal").ap()

    ar = Arena(nc)
    ones128_bf = ar.alloc([128, 1], BF16, "ones128b")
    ones1x128 = ar.alloc([1, 128], F32, "ones1x128")
    ones1x128b = ar.alloc([1, 128], BF16, "ones1x128b")
    consts = ar.alloc([128, 4], F32, "consts")
    sel_sb = ar.alloc([128, 128], BF16, "sel_sb")
    id_sb = ar.alloc([128, 128], BF16, "id_sb")
    bq_sb = ar.alloc([128, 8], F32, "bq_sb")
    bk_sb = ar.alloc([128, 8], F32, "bk_sb")
    bo_sb = ar.alloc([128, 8], F32, "bo_sb")
    ar.mark()

    psAB = nc.alloc_psum_tensor("psAB", [128, 2048], F32)
    psA = psAB[:, 0:1024]
    psB = psAB[:, 1024:2048]
    psC = nc.alloc_psum_tensor("psC", [128, 512], F32)
    psD = nc.alloc_psum_tensor("psD", [128, 512], F32)
    psE = nc.alloc_psum_tensor("psE", [128, 512], F32)
    psT = nc.alloc_psum_tensor("psT", [128, 1024], BF16)

    cnt = p.cnt

    def MM(out, lhsT, rhs, start, stop, waits=(), inc=None, tp=None):
        return p.emit("tensor",
                      (lambda e, o=out, l=lhsT, r=rhs, s0=start, s1=stop, t=tp:
                       e.matmul(o, l, r, start=s0, stop=s1, tile_position=t)),
                      waits, inc)

    def CP(eng, out, in_, waits=(), inc=None):
        return p.emit(eng, (lambda e, o=out, i=in_: e.tensor_copy(o, i)), waits, inc)

    def TT(out, a, b, op, waits=(), inc=None, eng="vector"):
        return p.emit(eng, (lambda e, o=out, x=a, y=b, f=op: e.tensor_tensor(o, x, y, f)),
                      waits, inc)

    def TS(out, in_, s1, s2, op0, op1=None, waits=(), inc=None, eng="vector"):
        if op1 is None:
            op1 = ALU.bypass
        return p.emit(eng, (lambda e, o=out, i=in_, a=s1, b=s2, f0=op0, f1=op1:
                            e.tensor_scalar(o, i, a, b, f0, f1)), waits, inc)

    def STT(out, in0, scalar, in1, op0, op1, waits=(), inc=None):
        return p.emit("vector", (lambda e, o=out, i=in0, s=scalar, j=in1, f0=op0, f1=op1:
                                 e.scalar_tensor_tensor(o, i, s, j, f0, f1)), waits, inc)

    def ACTV(out, in_, func, bias=None, scale=None, waits=(), inc=None,
             chain=True):
        kw = {}
        if bias is not None:
            kw["bias"] = bias
        if scale is not None:
            kw["scale"] = scale
        return p.emit("scalar", (lambda e, o=out, i=in_, f=func, kw=kw:
                                 e.activation(o, i, f, **kw)), waits, inc,
                      chain=chain)

    def RECIP(out, in_, waits=(), inc=None):
        return p.emit("vector", (lambda e, o=out, i=in_: e.reciprocal(o, i)), waits, inc)

    def RED(out, in_, op, waits=(), inc=None, negate=False):
        return p.emit("vector", (lambda e, o=out, i=in_, f=op, n=negate:
                                 e.tensor_reduce(o, i, AX.X, f, negate=n)), waits, inc)

    def MS(ap, val, waits=(), inc=None):
        return p.emit("gpsimd", (lambda e, a=ap, v=val: e.memset(a, v)), waits, inc)

    def SCAN(out, d0, d1, init, waits=(), inc=None):
        return p.emit("vector", (lambda e, o=out, a=d0, b=d1, i=init:
                                 e.tensor_tensor_scan(o, a, b, i, ALU.add, ALU.add)),
                      waits, inc)

    p.stage("init")
    MS(ones128_bf[:, :], 1.0)
    MS(ones1x128[:, :], 1.0)
    MS(ones1x128b[:, :], 1.0)
    MS(consts[:, 0:1], 1.0 / 1024.0)
    MS(consts[:, 1:2], EPS_LN)
    MS(consts[:, 2:3], 1e-6)
    MS(consts[:, 3:4], float(np.pi / 2))
    p.dma("gpsimd", sel_sb[:, :], dsel, inc="g_tab")
    g_id = p.dma("gpsimd", id_sb[:, :], dident, inc="g_tab")
    p.dma("sync", bq_sb[:, :], dbq, inc="s_tab")
    p.dma("sync", bk_sb[:, :], dbk, inc="s_tab")
    tab_s = p.dma("sync", bo_sb[:, :], dbo, inc="s_tab")

    p.stage("ln0")
    ar.reset()
    _lnbuf_n = [0]

    def mk_ln_bufs(dbl=True):
        i = _lnbuf_n[0]
        _lnbuf_n[0] += 2
        lnc = [ar.alloc([128, NDT * 512], BF16, f"lnc{i}")]
        lnc.append(ar.alloc([128, NDT * 512], BF16, f"lnd{i}") if dbl else lnc[0])
        return dict(
            xcf=[ar.alloc([128, NDT * 512], F32, f"xcf{i+j}") for j in range(2)],
            xcb=[ar.alloc([128, NDT * 512], BF16, f"xcb{i+j}") for j in range(2)],
            sqb=ar.alloc([128, NDT * 512], BF16, f"sqb{i}"),
            lnc=lnc, dbl=dbl,
            tmpf=ar.alloc([128, 512], F32, f"lntmp{i}"),
            rows=ar.alloc([1, 2048], F32, f"rows{i}"),
            rows2=ar.alloc([1, 2048], F32, f"rows2_{i}"))

    def emit_ln(src_dram, n_tok, dst_dram, bufs, gten=None, bten=None,
                postscale=None, src_ready=(), src_ready_fn=None, prev_st=None):
        xcf, xcb = bufs["xcf"], bufs["xcb"]
        sqb, tmpf = bufs["sqb"], bufs["tmpf"]
        lncs, dbl = bufs["lnc"], bufs["dbl"]
        rows, rows2 = bufs["rows"], bufs["rows2"]
        """Feature-major LN over d=1024, pipelined over 512-token chunks."""
        nch = n_tok // 512
        st_chunks = []
        st_par = dict(prev_st) if prev_st else {0: None, 1: None}
        base = _lnbuf_n[0] - 2
        rel_xcb = [None, None]
        rel_xcf = [None, None]
        rel_stats = [None, None]
        rel_sq = None
        rel_bcast = None
        for ch in range(nch):
            bi = ch % 2
            mean_bk = psE[0:1, 0:512] if bi == 0 else psD[0:1, 0:512]
            sq_bk = psC[0:1, 0:512] if bi == 0 else psB[0:1, 0:512]
            rset = rows if bi == 0 else rows2
            sl = slice(ch * 512, (ch + 1) * 512)
            src3 = src_dram.rearrange("(dt p) t -> p dt t", p=128)[:, :, sl]
            src_ready = src_ready_fn(ch) if src_ready_fn is not None else src_ready
            w = list(src_ready)
            if rel_xcf[bi]:
                w.append(("v", rel_xcf[bi]))
            xcf3 = xcf[bi].rearrange("p (dt t) -> p dt t", dt=NDT)
            ldf = p.dma("sync", xcf3[:, 0:4, :], src3[:, 0:4, :], waits=w,
                        inc=f"s_xcf{base + bi}")
            ldf2 = p.dma("scalar", xcf3[:, 4:8, :], src3[:, 4:8, :], waits=w,
                         inc=f"a_xcf{base + bi}")
            w = list(src_ready)
            if rel_xcb[bi]:
                w.append(("t", rel_xcb[bi]))
            ldb = p.dma("gpsimd", xcb[bi].rearrange("p (dt t) -> p dt t", dt=NDT),
                        src3, waits=w, inc=f"g_xcb{base + bi}")
            aq = None
            for dt in range(NDT):
                wq = []
                if dt == 0:
                    wq = [(f"g_xcb{base + bi}", ldb)]
                    if rel_sq:
                        wq.append(("t", rel_sq))
                aq = ACTV(sqb[:, dt * 512:(dt + 1) * 512],
                          xcb[bi][:, dt * 512:(dt + 1) * 512], AF.Square,
                          waits=wq, inc="a")
            for dt in range(NDT):
                wm = []
                if dt == 0:
                    wm = [(f"g_xcb{base + bi}", ldb)]
                    if rel_stats[bi]:
                        wm.append(("v", rel_stats[bi]))
                MM(mean_bk, ones128_bf[:, :], xcb[bi][:, dt * 512:(dt + 1) * 512],
                   start=(dt == 0), stop=(dt == NDT - 1), waits=wm)
            tmm = None
            for dt in range(NDT):
                wm = [("a", aq)] if dt == 0 else []
                tmm = MM(sq_bk, ones128_bf[:, :], sqb[:, dt * 512:(dt + 1) * 512],
                         start=(dt == 0), stop=(dt == NDT - 1), waits=wm,
                         inc=("t" if dt == NDT - 1 else None))
            rel_sq = tmm
            rel_xcb[bi] = tmm
            r_mean, r_srow, r_m2, r_var = (rset[0:1, 0:512], rset[0:1, 512:1024],
                                           rset[0:1, 1024:1536], rset[0:1, 1536:2048])
            TS(r_mean, mean_bk, consts[0:1, 0:1], None, ALU.mult, waits=[("t", tmm)])
            TS(r_srow, sq_bk, consts[0:1, 0:1], None, ALU.mult)
            TT(r_m2, r_mean, r_mean, ALU.mult)
            vv = p.emit("vector", (lambda e, a=r_var, b=r_srow, c=r_m2:
                                   e.tensor_sub(a, b, c)), inc="v")
            rel_stats[bi] = vv
            ACTV(r_m2, r_var, AF.Ln, bias=consts[0:1, 1:2], waits=[("v", vv)])
            av = ACTV(r_var, r_m2, AF.Exp, scale=-0.5, inc="a")
            vm = TT(r_m2, r_mean, r_var, ALU.mult, waits=[("a", av)], inc="v")
            wb = [("v", vm)]
            if rel_bcast:
                wb.append(("v", rel_bcast))
            MM(psA[:, 0:512], ones1x128[:, :], r_var, start=True, stop=True, waits=wb)
            tb = MM(psA[:, 512:1024], ones1x128[:, :], r_m2, start=True, stop=True,
                    inc="t")
            lnc = lncs[ch % 2]
            vo = None
            for dt in range(NDT):
                wv_ = [("t", tb)]
                if dt == 0:
                    wv_.append((f"s_xcf{base + bi}", ldf))
                    pw = st_par[ch % 2] if dbl else st_par[(ch - 1) % 2]
                    if pw is not None:
                        wv_.append(pw)
                if dt == 4:
                    wv_.append((f"a_xcf{base + bi}", ldf2))
                TT(tmpf[:, :], xcf[bi][:, dt * 512:(dt + 1) * 512], psA[:, 0:512],
                   ALU.mult, waits=wv_)
                if gten is None and postscale is None:
                    vo = TT(lnc[:, dt * 512:(dt + 1) * 512], tmpf[:, :],
                            psA[:, 512:1024], ALU.subtract, inc="v")
                else:
                    TT(tmpf[:, :], tmpf[:, :], psA[:, 512:1024], ALU.subtract)
                    if gten is not None and postscale is None:
                        vo = TS(lnc[:, dt * 512:(dt + 1) * 512], tmpf[:, :],
                                gten[:, dt:dt + 1], bten[:, dt:dt + 1],
                                ALU.mult, ALU.add, inc="v")
                    else:
                        TS(tmpf[:, :], tmpf[:, :], gten[:, dt:dt + 1],
                           bten[:, dt:dt + 1], ALU.mult, ALU.add)
                        vo = TS(lnc[:, dt * 512:(dt + 1) * 512], tmpf[:, :],
                                postscale, None, ALU.mult, inc="v")
            rel_bcast = vo
            rel_xcf[bi] = vo
            stn = f"ls{ch % 2}"
            stc = p.dma("sync",
                        dst_dram.rearrange("(dt p) t -> p dt t", p=128)[:, :, sl],
                        lnc.rearrange("p (dt t) -> p dt t", dt=NDT),
                        waits=[("v", vo)], inc=stn)
            st_par[ch % 2] = (stn, stc)
            st_chunks.append((stn, stc))
        return st_chunks

    b0 = mk_ln_bufs()
    ln0_ch = emit_ln(dx, T, dln0, b0)
    ln0_done = ln0_ch[-1]
    LN0S = "s_lnst"

    NKT = T // 128
    qc_n = TL // 512

    p.stage("l0v")
    ar.reset()
    v_sb = ar.alloc([128, NKT * 1040], BF16, "v_sb")
    cs_sb = ar.alloc([128, T], BF16, "cs_sb")
    ss_sb = ar.alloc([128, T], BF16, "ss_sb")
    wo_sb = ar.alloc([128, NDT * 1024], BF16, "wo_sb")
    ao_sb = ar.alloc([128, 8 * TL], BF16, "ao_sb")
    r2_sb = ar.alloc([128, 1024], BF16, "r2_sb")
    perm_sb = ar.alloc([128, 128], BF16, "perm_sb")
    ar.mark("attn")
    wv_sb = ar.alloc([128, NDT * 1040], BF16, "wv_sb")
    lnk = [ar.alloc([128, NDT * 128], BF16, f"lnk{i}") for i in range(4)]
    bvr_sb = ar.alloc([1, H * 65], BF16, "bvr_sb")

    pl_r2 = MS(r2_sb[:, :], 0.0, inc="pl")
    g_wv = p.dma("gpsimd", wv_sb.rearrange("p (dt c) -> p dt c", dt=NDT),
                 dwv65.rearrange("(dt p) c -> p dt c", p=128), inc="g_wv")
    g_bv = p.dma("gpsimd", bvr_sb[:, :], dbv, inc="g_tab")
    g_pm = p.dma("gpsimd", perm_sb[:, :], dpermm, inc="g_tab")
    g_cs = p.dma("gpsimd", cs_sb[:, :], dcs, inc="g_cs")
    a_ss = p.dma("gpsimd", ss_sb[:, :], dss, inc="g_ss")
    a_wo = p.dma("gpsimd", wo_sb.rearrange("p (kt d) -> p kt d", kt=NDT),
                 dwoT.rearrange("(kt p) d -> p kt d", p=128), inc="g_wo")

    ve = None
    rel_lnk = [None] * 4
    cp_hist = [None, None]
    for kt in range(NKT):
        bi = kt % 4
        pb = psA if kt % 2 == 0 else psB
        pc = psC if kt % 2 == 0 else psD
        qd = "sync" if bi % 2 == 0 else "gpsimd"
        w = [ln0_ch[kt // 4]]
        if rel_lnk[bi]:
            w.append(("t", rel_lnk[bi]))
        lk = p.dma(qd, lnk[bi].rearrange("p (dt t) -> p dt t", dt=NDT),
                   dln0.rearrange("(dt p) t -> p dt t", p=128)[:, :, kt * 128:(kt + 1) * 128],
                   waits=w, inc=f"lk{bi}")
        for dt in range(NDT):
            w0 = []
            if dt == 0:
                w0 = [(f"lk{bi}", lk), ("g_wv", g_wv), ("g_tab", g_bv)]
                if kt < 2:
                    w0.append(ln0_done)
                if cp_hist[kt % 2]:
                    w0.append(("v", cp_hist[kt % 2]))
            MM(pb[:, 0:512], lnk[bi][:, dt * 128:(dt + 1) * 128],
               wv_sb[:, dt * 1040:dt * 1040 + 512], start=(dt == 0), stop=False,
               waits=w0)
            MM(pb[:, 512:1024], lnk[bi][:, dt * 128:(dt + 1) * 128],
               wv_sb[:, dt * 1040 + 512:dt * 1040 + 1024], start=(dt == 0), stop=False)
            MM(pc[:, 0:16], lnk[bi][:, dt * 128:(dt + 1) * 128],
               wv_sb[:, dt * 1040 + 1024:dt * 1040 + 1040], start=(dt == 0), stop=False)
        MM(pb[:, 0:512], ones1x128b[:, :], bvr_sb[:, 0:512], start=False, stop=True)
        MM(pb[:, 512:1024], ones1x128b[:, :], bvr_sb[:, 512:1024], start=False, stop=True)
        tv = MM(pc[:, 0:16], ones1x128b[:, :], bvr_sb[:, 1024:1040],
                start=False, stop=True, inc="t")
        rel_lnk[bi] = tv
        off = kt * 1040
        CP("vector", v_sb[:, off:off + 512], pb[:, 0:512], waits=[("t", tv)])
        CP("vector", v_sb[:, off + 512:off + 1024], pb[:, 512:1024])
        ve = CP("vector", v_sb[:, off + 1024:off + 1040], pc[:, 0:16], inc="v")
        cp_hist[kt % 2] = ve
    pl_ones = MS(v_sb.rearrange("p (kt h e) -> p kt h e", kt=NKT, h=H)[:, :, :, 64:65],
                 1.0, waits=[("v", ve)], inc="pl")
    b1_t = cnt["t"]

    p.stage("l0attn")
    ar.reset("attn")
    wq_sb = [ar.alloc([128, NDT * 128], BF16, f"wq_sb{i}") for i in range(2)]
    wk_sb = [ar.alloc([128, NDT * 128], BF16, f"wk_sb{i}") for i in range(2)]
    lnqk = [ar.alloc([128, NDT * 512], BF16, f"lnqk{i}") for i in range(4)]
    q_sb = ar.alloc([128, TL], BF16, "q_sb")
    k_sb = ar.alloc([128, T], BF16, "k_sb")
    qr_sb = q_sb
    kr_sb = k_sb
    rtmp = ar.alloc([128, 512], BF16, "rtmp")
    p2_sb = [ar.alloc([128, 2048], BF16, f"p2_sb{i}") for i in range(2)]
    rb_sb = ar.alloc([128, 512], F32, "rb_sb")
    xr_sb = [ar.overlay("wq_sb0", 0, [128, 512], F32, "xr_sb0"),
             ar.overlay("wq_sb1", 0, [128, 512], F32, "xr_sb1")]
    o_sb = [ar.overlay("wk_sb0", 0, [128, 512], F32, "o_sb0"),
            ar.overlay("wk_sb1", 0, [128, 512], F32, "o_sb1")]

    v_ready = [("pl", pl_ones)]
    rel_wq = [None, None]
    rel_wk = [None, None]
    rel_lnqk = [None, None, None, None]
    rtts = []
    nrm_q = []
    for hp in range(8):
        bi = hp % 2
        w = [("t", rel_wq[bi] if rel_wq[bi] else b1_t)]
        g_wq = p.dma("gpsimd", wq_sb[bi].rearrange("p (dt m) -> p dt m", dt=NDT),
                     dwqT.rearrange("(dt p) m -> p dt m", p=128)[:, :, hp * 128:(hp + 1) * 128],
                     waits=w, inc=f"g_wq{bi}")
        w = [("t", rel_wk[bi] if rel_wk[bi] else b1_t)]
        g_wk = p.dma("gpsimd", wk_sb[bi].rearrange("p (dt m) -> p dt m", dt=NDT),
                     dwkT.rearrange("(dt p) m -> p dt m", p=128)[:, :, hp * 128:(hp + 1) * 128],
                     waits=w, inc=f"g_wk{bi}")
        vq_ts = [None] * qc_n
        vk_ts = [None] * (T // 512)
        pbanks = [psC[:, 0:512], psD[:, 0:512], psA[:, 0:512], psB[:, 0:512]]
        hv = cnt["v"]
        ts_hist = [hv, hv, hv, hv]
        gi = 0
        for tc in range(T // 512):
            cbi = tc % 4
            qd = "sync" if cbi % 2 == 0 else "gpsimd"
            wl = [ln0_ch[tc],
                  ("t", rel_lnqk[cbi] if rel_lnqk[cbi] else b1_t)]
            s_ln = p.dma(qd, lnqk[cbi].rearrange("p (dt t) -> p dt t", dt=NDT),
                         dln0.rearrange("(dt p) t -> p dt t", p=128)[:, :, tc * 512:(tc + 1) * 512],
                         waits=wl, inc=f"lq{cbi}")
            if tc < qc_n:
                bk = gi % 4
                for dt in range(NDT):
                    w0 = []
                    if dt == 0:
                        w0 = [(f"g_wq{bi}", g_wq), (f"lq{cbi}", s_ln),
                              ("v", ts_hist[bk])]
                    MM(pbanks[bk], wq_sb[bi][:, dt * 128:(dt + 1) * 128],
                       lnqk[cbi][:, dt * 512:(dt + 1) * 512],
                       start=(dt == 0), stop=(dt == NDT - 1), waits=w0,
                       inc=("t" if dt == NDT - 1 else None))
                vq_ts[tc] = TS(q_sb[:, tc * 512:(tc + 1) * 512], pbanks[bk],
                               bq_sb[:, hp:hp + 1], None, ALU.add,
                               waits=[("t", cnt["t"]), ("s_tab", tab_s)], inc="v")
                ts_hist[bk] = vq_ts[tc]
                gi += 1
            bk = gi % 4
            for dt in range(NDT):
                w0 = []
                if dt == 0:
                    w0 = [(f"g_wk{bi}", g_wk), (f"lq{cbi}", s_ln),
                          ("v", ts_hist[bk])]
                MM(pbanks[bk], wk_sb[bi][:, dt * 128:(dt + 1) * 128],
                   lnqk[cbi][:, dt * 512:(dt + 1) * 512],
                   start=(dt == 0), stop=(dt == NDT - 1), waits=w0,
                   inc=("t" if dt == NDT - 1 else None))
            vk_ts[tc] = TS(k_sb[:, tc * 512:(tc + 1) * 512], pbanks[bk],
                           bk_sb[:, hp:hp + 1], None, ALU.add,
                           waits=[("t", cnt["t"]), ("s_tab", tab_s)], inc="v")
            ts_hist[bk] = vk_ts[tc]
            gi += 1
            rel_lnqk[cbi] = cnt["t"]
            if tc == qc_n - 1:
                rel_wq[bi] = cnt["t"]
            if tc == T // 512 - 1:
                rel_wk[bi] = cnt["t"]
        rope_k = [None] * (T // 512)
        pi = 0
        for (srcb, dstb, nch, vsrc) in ((q_sb, qr_sb, qc_n, vq_ts),
                                        (k_sb, kr_sb, T // 512, vk_ts)):
            for tc in range(nch):
                sl = slice(tc * 512, (tc + 1) * 512)
                pbank = psC if pi % 2 == 0 else psD
                w0 = [("v", vk_ts[-1] if pi < 2 else rtts[-2]), ("g_tab", g_pm)]
                tpm = MM(pbank[:, 0:512], perm_sb[:, :], srcb[:, sl],
                         start=True, stop=True, waits=w0, inc="t")
                t1 = TT(rtmp[:, :], pbank[:, 0:512], ss_sb[:, sl], ALU.mult,
                        waits=[("t", tpm), ("g_ss", a_ss), ("g_cs", g_cs)])
                rtts.append(t1)
                TT(dstb[:, sl], srcb[:, sl], cs_sb[:, sl], ALU.mult)
                vr = TT(dstb[:, sl], dstb[:, sl], rtmp[:, :], ALU.add, inc="v")
                if srcb is k_sb:
                    rope_k[tc] = vr
                pi += 1

        evac_prev = rope_k[T // 512 - 1]
        for qc in range(qc_n):
            a_hist = []
            tpv = None

            def emit_S(kt):
                sbuf = psA if kt % 2 == 0 else psB
                w0 = [("v", rope_k[kt // 4])] + v_ready
                if kt >= 2:
                    w0.append(("a", a_hist[kt - 2]))
                MM(sbuf[:, 0:512], kr_sb[0:64, kt * 128:(kt + 1) * 128],
                   qr_sb[0:64, qc * 512:(qc + 1) * 512], start=True, stop=True,
                   waits=w0, tp=(0, 0))
                ts = MM(sbuf[:, 512:1024], kr_sb[64:128, kt * 128:(kt + 1) * 128],
                        qr_sb[64:128, qc * 512:(qc + 1) * 512], start=True, stop=True,
                        tp=(64, 0), inc="t")
                ae = ACTV(p2_sb[kt % 2][:, 0:1024], sbuf[:, :], AF.Exp, scale=0.125,
                          waits=[("t", ts)], inc="a")
                a_hist.append(ae)

            def emit_PV(kt):
                nonlocal tpv
                pb = p2_sb[kt % 2]
                vofs = kt * 1040 + 130 * hp
                wpv = [("a", a_hist[kt])]
                if kt == 0:
                    wpv.append(("v", evac_prev))
                MM(psC[0:65, 0:512], v_sb[:, vofs:vofs + 65], pb[:, 0:512],
                   start=(kt == 0), stop=(kt == NKT - 1), waits=wpv)
                tpv = MM(psD[0:65, 0:512], v_sb[:, vofs + 65:vofs + 130],
                         pb[:, 512:1024], start=(kt == 0), stop=(kt == NKT - 1),
                         inc=("t" if kt == NKT - 1 else None))

            while nrm_q:
                aofs_n, sl2_n, ev_n = nrm_q.pop(0)
                tbN = MM(psE[:, 0:512], sel_sb[:, :], r2_sb[:, sl2_n],
                         start=True, stop=True,
                         waits=[("v", ev_n), ("pl", pl_r2), ("g_tab", g_bv)],
                         inc="t")
                RECIP(rb_sb[:, :], psE[:, 0:512], waits=[("t", tbN)])
                TT(ao_sb[0:64, aofs_n:aofs_n + 512],
                   ao_sb[0:64, aofs_n:aofs_n + 512], rb_sb[0:64, :], ALU.mult)
                TT(ao_sb[64:128, aofs_n:aofs_n + 512],
                   ao_sb[64:128, aofs_n:aofs_n + 512], rb_sb[64:128, :],
                   ALU.mult, inc="v")

            emit_S(0)
            for kt in range(1, NKT):
                emit_S(kt)
                emit_PV(kt - 1)
            emit_PV(NKT - 1)
            sl2 = slice((qc % 2) * 512, (qc % 2) * 512 + 512)
            aofs = hp * TL + qc * 512
            CP("vector", r2_sb[0:1, sl2], psC[64:65, 0:512], waits=[("t", tpv)])
            CP("vector", r2_sb[64:65, sl2], psD[64:65, 0:512])
            CP("vector", ao_sb[0:64, aofs:aofs + 512], psC[0:64, 0:512])
            evac_prev = CP("vector", ao_sb[64:128, aofs:aofs + 512],
                           psD[0:64, 0:512], inc="v")
            nrm_q.append((aofs, sl2, evac_prev))

    while nrm_q:
        aofs_n, sl2_n, ev_n = nrm_q.pop(0)
        tbN = MM(psE[:, 0:512], sel_sb[:, :], r2_sb[:, sl2_n], start=True,
                 stop=True, waits=[("v", ev_n), ("pl", pl_r2), ("g_tab", g_bv)],
                 inc="t")
        RECIP(rb_sb[:, :], psE[:, 0:512], waits=[("t", tbN)])
        TT(ao_sb[0:64, aofs_n:aofs_n + 512], ao_sb[0:64, aofs_n:aofs_n + 512],
           rb_sb[0:64, :], ALU.mult)
        TT(ao_sb[64:128, aofs_n:aofs_n + 512],
           ao_sb[64:128, aofs_n:aofs_n + 512], rb_sb[64:128, :],
           ALU.mult, inc="v")

    ao_done = cnt["v"]
    stt_h = [None, None]
    st_h = [None, None]
    x1_tc = {}
    for i, (dc, tc) in enumerate([(d, t) for d in range(NDT)
                                  for t in range(qc_n)]):
        b = i % 2
        sx = p.dma("scalar", xr_sb[b][:, :],
                   dx[dc * 128:(dc + 1) * 128, tc * 512:(tc + 1) * 512],
                   waits=[("v", stt_h[b] if stt_h[b] else ao_done)],
                   inc=f"a_xr{b}")
        for kt in range(NDT):
            w0 = []
            if kt == 0:
                w0 = [("v", ao_done), ("v", cnt["v"]), ("g_wo", a_wo)]
            MM(psC[:, 0:512],
               wo_sb[:, kt * 1024 + dc * 128: kt * 1024 + (dc + 1) * 128],
               ao_sb[:, kt * TL + tc * 512: kt * TL + (tc + 1) * 512],
               start=(kt == 0), stop=(kt == NDT - 1), waits=w0,
               inc=("t" if kt == NDT - 1 else None))
        vo = STT(o_sb[b][:, :], psC[:, 0:512], bo_sb[:, dc:dc + 1], xr_sb[b][:, :],
                 ALU.add, ALU.add,
                 waits=[("t", cnt["t"]), (f"a_xr{b}", sx), ("s_tab", tab_s)] +
                       ([(f"x1s{b}", st_h[b])] if st_h[b] else []),
                 inc="v")
        stt_h[b] = vo
        st_h[b] = p.dma("sync", dx1[dc * 128:(dc + 1) * 128, tc * 512:(tc + 1) * 512],
                        o_sb[b][:, :], waits=[("v", vo)], inc=f"x1s{b}")
        if dc == NDT - 1:
            x1_tc[tc] = [("x1s0", cnt["x1s0"]), ("x1s1", cnt["x1s1"])]

    p.stage("ffn0")
    ar.reset()
    bC = mk_ln_bufs()
    ln1_ch = emit_ln(dx1, TL, dln1, bC,
                     src_ready_fn=lambda ch: x1_tc[ch])
    ln1_done = ln1_ch[-1]
    w1_sb = ar.alloc([128, NDT * 1024], BF16, "w1_sb")
    dsh_sb = ar.alloc([64, 1024], BF16, "dsh_sb")
    ps_sb = ar.alloc([128, 64], BF16, "ps_sb")
    lnc1 = bC["xcb"] + [ar.alloc([128, NDT * 512], BF16, f"lnc1x{i}")
                        for i in range(2)]
    pool_sb = ar.alloc([64, TL], BF16, "pool_sb")
    xc1_sb = [ar.alloc([128, 512], F32, f"xc1_sb{i}") for i in range(2)]
    oc1_sb = [ar.alloc([128, 512], F32, f"oc1_sb{i}") for i in range(2)]

    g_w1 = p.dma("gpsimd", w1_sb.rearrange("p (dt m) -> p dt m", dt=NDT),
                 dw1t.rearrange("(dt p) m -> p dt m", p=128), inc="g_tab")
    g_dsh = p.dma("gpsimd", dsh_sb[:, :], ddisha, inc="g_tab")
    g_ps = p.dma("gpsimd", ps_sb[:, :], dpsel, inc="g_tab")

    lc1_ld = [None] * qc_n
    for tc in range(qc_n):
        qd = "sync" if tc % 2 == 0 else "gpsimd"
        lc1_ld[tc] = p.dma(qd, lnc1[tc].rearrange("p (dt t) -> p dt t", dt=NDT),
                           dln1.rearrange("(dt p) t -> p dt t", p=128)[:, :, tc * 512:(tc + 1) * 512],
                           waits=[ln1_ch[tc + 2] if tc < 2 else ln1_ch[tc]],
                           inc=f"lc1{tc}")
        for dt in range(NDT):
            w0 = []
            if dt == 0:
                w0 = [(f"lc1{tc}", lc1_ld[tc]), ("g_tab", g_ps), ("v", cnt["v"])]
            MM(psE[0:64, 0:512], ps_sb[:, :], lnc1[tc][:, dt * 512:(dt + 1) * 512],
               start=(dt == 0), stop=(dt == NDT - 1), waits=w0,
               inc=("t" if dt == NDT - 1 else None))
        CP("vector", pool_sb[:, tc * 512:(tc + 1) * 512], psE[0:64, 0:512],
           waits=[("t", cnt["t"])])
    ctt_h = [None, None]
    cst_h = [None, None]
    x2_tc = {}
    c2_banks = [psC[:, 0:512], psD[:, 0:512], psA[:, 0:512], psB[:, 0:512]]
    c2_rel = [cnt["v"]] * 4
    for i, (tc, oc) in enumerate([(t, o) for t in range(qc_n)
                                  for o in range(NDT)]):
        b = i % 2
        pbk = c2_banks[i % 4]
        wx = (x1_tc[3] if i < 2 else [("v", ctt_h[b])])
        sx = p.dma("scalar", xc1_sb[b][:, :],
                   dx1[oc * 128:(oc + 1) * 128, tc * 512:(tc + 1) * 512],
                   waits=wx, inc=f"a_xr{b}")
        for dt in range(NDT):
            w0 = []
            if dt == 0:
                w0 = [(f"lc1{tc}", lc1_ld[tc]), ("g_tab", g_w1),
                      ("v", c2_rel[i % 4])]
            MM(pbk, w1_sb[:, dt * 1024 + oc * 128:dt * 1024 + (oc + 1) * 128],
               lnc1[tc][:, dt * 512:(dt + 1) * 512],
               start=(dt == 0), stop=False, waits=w0)
        MM(pbk, dsh_sb[0:64, oc * 128:(oc + 1) * 128],
           pool_sb[0:64, tc * 512:(tc + 1) * 512],
           start=False, stop=True, inc="t")
        vo = TT(oc1_sb[b][:, :], pbk, xc1_sb[b][:, :], ALU.add,
                waits=[("t", cnt["t"]), (f"a_xr{b}", sx)] +
                      ([(f"x2s{b}", cst_h[b])] if cst_h[b] else []),
                inc="v")
        c2_rel[i % 4] = vo
        ctt_h[b] = vo
        cst_h[b] = p.dma("sync", dx2[oc * 128:(oc + 1) * 128, tc * 512:(tc + 1) * 512],
                         oc1_sb[b][:, :], waits=[("v", vo)], inc=f"x2s{b}")
        if oc == NDT - 1:
            x2_tc[tc] = [("x2s0", cnt["x2s0"]), ("x2s1", cnt["x2s1"])]

    p.stage("l1attn")
    ar.reset()
    bD = mk_ln_bufs(dbl=False)
    ln2_ch = emit_ln(dx2, TL, dln2, bD,
                     src_ready_fn=lambda ch: x2_tc[ch])
    ln2_done = ln2_ch[-1]
    NHE = NDT
    wqk1_sb = ar.alloc([128, NDT * 64], BF16, "wqk1_sb")
    wv1_sb = ar.alloc([128, NDT * 1024], BF16, "wv1_sb")
    wo1_sb = ar.overlay("wv1_sb", 0, [128, NDT * 1024], BF16, "wo1_sb")
    sel16_sb = ar.alloc([16, 8 * 128], BF16, "sel16_sb")
    mask_sb = ar.alloc([128, 1], F32, "mask_sb")
    lnc2 = bD["xcb"] + [ar.alloc([128, NDT * 512], BF16, f"lnc2x{i}")
                        for i in range(2)]
    trig_tok = bD["sqb"]
    tqc = ar.alloc([16, TL], BF16, "tqc")
    tkc = ar.alloc([16, TL], BF16, "tkc")
    tqs = ar.alloc([16, TL], BF16, "tqs")
    tks = ar.alloc([16, TL], BF16, "tks")
    v1t_sb = ar.alloc([128, 512], F32, "v1t_sb")
    d4t_sb = ar.alloc([128, 512], F32, "d4t_sb")
    xc_sb = ar.alloc([128, NHE * TL], BF16, "xc_sb")
    xs_sb = ar.alloc([128, NHE * TL], BF16, "xs_sb")
    zer_sb = bD["lnc"][0]
    pub_sb = ar.alloc([128, 24], F32, "pub_sb")
    car_sb = ar.alloc([128, 24], F32, "car_sb")
    xdead = "xcf" + str(_lnbuf_n[0] - 2)
    kcc = ar.overlay(xdead, 0, [16, TL], BF16, "kcc")
    kcs = ar.overlay(xdead, TL * 2, [16, TL], BF16, "kcs")
    dent = ar.overlay(xdead, TL * 4, [16, TL], F32, "dent")
    dent2 = ar.overlay(xdead, TL * 12, [16, TL], BF16, "dent2")
    xdead2 = "xcf" + str(_lnbuf_n[0] - 1)
    a16 = ar.overlay(xdead2, 0, [16, TL], BF16, "a16")
    b16 = ar.overlay(xdead2, TL * 2, [16, TL], BF16, "b16")
    xr2_sb = [ar.overlay(xdead2, TL * 6, [128, 512], F32, "xr2_sb0"),
              ar.overlay(xdead2, TL * 7, [128, 512], F32, "xr2_sb1")]
    sqbD = "sqb" + str(_lnbuf_n[0] - 2)
    oc2_sb = [ar.overlay(sqbD, 0, [128, 512], F32, "oc2_sb0"),
              ar.overlay(sqbD, 2048, [128, 512], F32, "oc2_sb1")]

    g_wqk1 = p.dma("gpsimd", wqk1_sb.rearrange("p (dt m) -> p dt m", dt=NDT),
                   dwqk1.rearrange("(dt p) m -> p dt m", p=128), inc="g_tab")
    g_wv1 = p.dma("gpsimd", wv1_sb.rearrange("p (dt m) -> p dt m", dt=NDT),
                  dwv1.rearrange("(dt p) m -> p dt m", p=128), inc="g_tab")
    g_s16 = p.dma("gpsimd", sel16_sb[:, :], dsel16, inc="g_tab")
    g_msk = p.dma("gpsimd", mask_sb[:, :], dmask, inc="g_tab")
    MS(zer_sb[:, 0:TL], 0.0, waits=[ln2_done], inc="pl")
    zer_done = cnt["pl"]
    MS(pub_sb[:, :], 0.0, inc="pl")
    pub0_done = cnt["pl"]

    lc2_ld = [None] * qc_n
    for tc in range(qc_n):
        qd = "sync" if tc % 2 == 0 else "scalar"
        lc2_ld[tc] = p.dma(qd, lnc2[tc].rearrange("p (dt t) -> p dt t", dt=NDT),
                           dln2.rearrange("(dt p) t -> p dt t", p=128)[:, :, tc * 512:(tc + 1) * 512],
                           waits=[ln2_ch[tc + 2] if tc < 2 else ln2_ch[tc]],
                           inc=f"lc2{tc}")
    for tch in range(16):
        tc4 = tch // 4
        co = (tch % 4) * 128
        for dt in range(NDT):
            w0 = []
            if dt == 0:
                w0 = [(f"lc2{tc4}", lc2_ld[tc4]), ("g_tab", g_wqk1), ("a", cnt["a"])]
            MM(psE[:, 0:64], lnc2[tc4][:, dt * 512 + co:dt * 512 + co + 128],
               wqk1_sb[:, dt * 64:(dt + 1) * 64],
               start=(dt == 0), stop=(dt == NDT - 1), waits=w0,
               inc=("t" if dt == NDT - 1 else None))
        ACTV(trig_tok[:, tch * 128:tch * 128 + 48], psE[:, 0:48], AF.Tanh,
             waits=[("t", cnt["t"])])
        ACTV(trig_tok[:, tch * 128:tch * 128 + 48],
             trig_tok[:, tch * 128:tch * 128 + 48], AF.Sin,
             scale=float(np.pi / 4))
        ACTV(trig_tok[:, tch * 128 + 64:tch * 128 + 112], psE[:, 0:48], AF.Tanh)
        ACTV(trig_tok[:, tch * 128 + 64:tch * 128 + 112],
             trig_tok[:, tch * 128 + 64:tch * 128 + 112], AF.Sin,
             scale=float(np.pi / 4), bias=consts[:, 3:4], inc="a")
    av = cnt["a"]
    for tch in range(16):
        w0 = [("v", cnt["v"])] if tch else []
        if tch == 0:
            w0 = [("a", av), ("g_tab", g_id)]
        p.emit("tensor",
               (lambda e, o=psT[0:48, 0:128],
                i=trig_tok[:, tch * 128 + 64:tch * 128 + 112],
                idn=id_sb[:, :]: e.transpose(o, i, idn)), waits=w0)
        p.emit("tensor",
               (lambda e, o=psT[0:48, 128:256],
                i=trig_tok[:, tch * 128:tch * 128 + 48],
                idn=id_sb[:, :]: e.transpose(o, i, idn)), inc="t")
        tsl = slice(tch * 128, (tch + 1) * 128)
        CP("vector", tqc[:, tsl], psT[0:16, 0:128], waits=[("t", cnt["t"])])
        CP("vector", tkc[:, tsl], psT[32:48, 0:128])
        CP("vector", tqs[:, tsl], psT[0:16, 128:256])
        CP("vector", tks[:, tsl], psT[32:48, 128:256], inc="v")
    trig_done = cnt["v"]

    d2_vb = [psC[:, 0:512], psA[:, 0:512]]
    d2_bc = [(psE[:, 0:512], psD[:, 0:512]),
             (psB[:, 0:512], psB[:, 512:1024])]
    d2_rv = [cnt["v"]] * 2
    d2_rb = [cnt["v"]] * 2
    for kt in range(NHE):
        for tc in range(qc_n):
            j = (kt * qc_n + tc) % 2
            vb = d2_vb[j]
            bcA, bcB = d2_bc[j]
            for dt in range(NDT):
                w0 = []
                if dt == 0:
                    w0 = [(f"lc2{tc}", lc2_ld[tc]), ("g_tab", g_wv1),
                          ("v", d2_rv[j])]
                MM(vb, wv1_sb[:, dt * 1024 + kt * 128:dt * 1024 + (kt + 1) * 128],
                   lnc2[tc][:, dt * 512:(dt + 1) * 512],
                   start=(dt == 0), stop=(dt == NDT - 1), waits=w0,
                   inc=("t" if dt == NDT - 1 else None))
            MM(bcA, sel16_sb[:, kt * 128:(kt + 1) * 128],
               tkc[:, tc * 512:(tc + 1) * 512], start=True, stop=True,
               waits=[("v", trig_done), ("v", d2_rb[j]), ("g_tab", g_s16)])
            tbb = MM(bcB, sel16_sb[:, kt * 128:(kt + 1) * 128],
                     tks[:, tc * 512:(tc + 1) * 512], start=True, stop=True,
                     inc="t")
            CP("vector", v1t_sb[:, :], vb, waits=[("t", tbb)])
            d2_rv[j] = cnt["v"]
            TT(xc_sb[:, kt * TL + tc * 512:kt * TL + (tc + 1) * 512],
               v1t_sb[:, :], bcA, ALU.mult)
            d2_rb[j] = TT(xs_sb[:, kt * TL + tc * 512:kt * TL + (tc + 1) * 512],
                          v1t_sb[:, :], bcB, ALU.mult, inc="v")
        RED(pub_sb[:, kt:kt + 1], xc_sb[:, kt * TL:(kt + 1) * TL], ALU.add,
            waits=([("pl", pub0_done)] if kt == 0 else []))
        RED(pub_sb[:, 8 + kt:9 + kt], xs_sb[:, kt * TL:(kt + 1) * TL], ALU.add)
    d2_t = cnt["t"]
    wo13 = wo1_sb.rearrange("p (kt d) -> p kt d", kt=NDT)
    dwo13 = dwo1.rearrange("(kt p) d -> p kt d", p=128)
    g_wo1 = p.dma("gpsimd", wo13[:, 0:4, :], dwo13[:, 0:4, :],
                  waits=[("t", d2_t)], inc="g_wo1")
    s_wo1 = p.dma("gpsimd", wo13[:, 4:8, :], dwo13[:, 4:8, :],
                  waits=[("t", d2_t)], inc="s_wo1")
    RED(pub_sb[0:16, 16:17], tkc[:, :], ALU.add)
    vpub = RED(pub_sb[0:16, 17:18], tks[:, :], ALU.add, inc="v")
    s_pub = p.dma("sync", dcar_in, pub_sb[:, :], waits=[("v", vpub)], inc="s_fin")
    p.sem("cc")
    ccg = p.emit("gpsimd", (lambda e: e.collective_compute(
        "AllGather", ALU.bypass,
        replica_groups=[[2 * i, 2 * i + 1] for i in range(NCORES // 2)],
        ins=[dcar_in.opt()], outs=[dcar_out.opt()])),
        waits=[("s_fin", s_pub)], inc="cc", amt=1, dma=True)

    zcol = zer_sb[:, 0:1]
    v_pub = cnt["v"]
    SCAN(kcc[:, :], tkc[:, :], zer_sb[0:16, 0:TL], zer_sb[0:16, 0:1],
         waits=[("pl", zer_done)])
    SCAN(kcs[:, :], tks[:, :], zer_sb[0:16, 0:TL], zer_sb[0:16, 0:1])
    for kt in range(NHE):
        SCAN(xc_sb[:, kt * TL:(kt + 1) * TL], xc_sb[:, kt * TL:(kt + 1) * TL],
             zer_sb[:, 0:TL], zcol)
        SCAN(xs_sb[:, kt * TL:(kt + 1) * TL], xs_sb[:, kt * TL:(kt + 1) * TL],
             zer_sb[:, 0:TL], zcol)

    def read_partner(e, dst=car_sb, srcd=dcar_out):
        pid = e.partition_id()
        partner = nc.s_assert_within(
            e.compute_val(1 - pid % 2), 0, 1, skip_runtime_assert=True)
        return e.dma_start(dst[:, :],
                           srcd.rearrange("(r p) f -> r p f", r=2)[partner, :, :])
    p.sem("g_car")
    g_car = p.emit("gpsimd", read_partner, waits=[("cc", cnt["cc"])], inc="g_car",
                   amt=16, dma=True)
    vcar = TS(car_sb[:, :], car_sb[:, :], mask_sb[:, 0:1], None, ALU.mult,
              waits=[("g_car", g_car), ("g_tab", g_msk)], inc="v")

    STT(dent[:, :], kcc[:, :], car_sb[0:16, 16:17], tqc[:, :], ALU.add, ALU.mult,
        waits=[("v", vcar)])
    STT(dent2[:, :], kcs[:, :], car_sb[0:16, 17:18], tqs[:, :], ALU.add, ALU.mult)
    TT(dent[:, :], dent[:, :], dent2[:, :], ALU.add)
    TS(dent[:, :], dent[:, :], consts[0:16, 2:3], None, ALU.add)
    RECIP(dent2[:, :], dent[:, :])
    TT(a16[:, :], tqc[:, :], dent2[:, :], ALU.mult)
    vab = TT(b16[:, :], tqs[:, :], dent2[:, :], ALU.mult, inc="v")

    d4_rel = [None, None]
    for kt in range(NHE):
        for tc in range(qc_n):
            j = (kt * qc_n + tc) % 2
            bcA, bcB = d2_bc[j]
            sl = slice(tc * 512, (tc + 1) * 512)
            ksl = slice(kt * TL + tc * 512, kt * TL + (tc + 1) * 512)
            MM(bcA, sel16_sb[:, kt * 128:(kt + 1) * 128], a16[:, sl],
               start=True, stop=True,
               waits=[("v", d4_rel[j] if d4_rel[j] else cnt["v"])])
            tb2 = MM(bcB, sel16_sb[:, kt * 128:(kt + 1) * 128],
                     b16[:, sl], start=True, stop=True, inc="t")
            STT(v1t_sb[:, :], xc_sb[:, ksl], car_sb[:, kt:kt + 1], bcA,
                ALU.add, ALU.mult, waits=[("t", tb2)])
            STT(d4t_sb[:, :], xs_sb[:, ksl], car_sb[:, 8 + kt:9 + kt],
                bcB, ALU.add, ALU.mult)
            d4_rel[j] = TT(xc_sb[:, ksl], v1t_sb[:, :], d4t_sb[:, :], ALU.add,
                           inc="v")
    heads_done = cnt["v"]
    dtt_h = [None, None]
    dst_h = [None, None]
    x3_tc = {}
    for i, (dc, tc) in enumerate([(d, t) for d in range(NDT)
                                  for t in range(qc_n)]):
        b = i % 2
        sx = p.dma("scalar", xr2_sb[b][:, :],
                   dx2[dc * 128:(dc + 1) * 128, tc * 512:(tc + 1) * 512],
                   waits=[("v", dtt_h[b] if dtt_h[b] else heads_done)],
                   inc=f"a_xr{b}")
        for kt in range(NDT):
            w0 = []
            if kt == 0:
                w0 = [("v", heads_done), ("v", cnt["v"]),
                      ("g_wo1", g_wo1), ("s_wo1", s_wo1)]
            MM(psC[:, 0:512],
               wo1_sb[:, kt * 1024 + dc * 128:kt * 1024 + (dc + 1) * 128],
               xc_sb[:, kt * TL + tc * 512:kt * TL + (tc + 1) * 512],
               start=(kt == 0), stop=(kt == NDT - 1), waits=w0,
               inc=("t" if kt == NDT - 1 else None))
        vo = TT(oc2_sb[b][:, :], psC[:, 0:512], xr2_sb[b][:, :], ALU.add,
                waits=[("t", cnt["t"]), (f"a_xr{b}", sx)] +
                      ([(f"x3s{b}", dst_h[b])] if dst_h[b] else []),
                inc="v")
        dtt_h[b] = vo
        dst_h[b] = p.dma("sync", dx3[dc * 128:(dc + 1) * 128, tc * 512:(tc + 1) * 512],
                         oc2_sb[b][:, :], waits=[("v", vo)], inc=f"x3s{b}")
        if dc == NDT - 1:
            x3_tc[tc] = [("x3s0", cnt["x3s0"]), ("x3s1", cnt["x3s1"])]

    p.stage("moe")
    ar.reset()
    bE = mk_ln_bufs(dbl=False)
    _ie = _lnbuf_n[0] - 2
    lnc3 = bE["xcb"] + [ar.overlay(f"rows{_ie}", 0, [128, NDT * 512], BF16, "lnc3x0"),
                        ar.overlay(f"rows2_{_ie}", 0, [128, NDT * 512], BF16, "lnc3x1")]
    ng_sb = ar.alloc([128, 8], F32, "ng_sb")
    nb_sb = ar.alloc([128, 8], F32, "nb_sb")
    sg_sb = ar.alloc([128, 8], F32, "sg_sb")
    sbb_sb = ar.alloc([128, 8], F32, "sbb_sb")
    scol_sb = ar.alloc([128, 1], F32, "scol_sb")
    sc1_sb = ar.alloc([1, 1], F32, "sc1_sb")
    se_sb = ar.alloc([128, NDT * 1024], BF16, "se_sb")
    logF = ar.alloc([128, 8 * TL], F32, "logF")
    dD_sb = ar.alloc([128, 8 * TL], BF16, "dD_sb")
    mneg_sb = ar.alloc([128, 8], F32, "mneg_sb")
    mpos_sb = ar.alloc([128, 8], F32, "mpos_sb")
    sesum_sb = ar.alloc([128, 8], F32, "sesum_sb")
    cmneg_sb = ar.alloc([128, 1], F32, "cmneg_sb")
    crs_sb = ar.alloc([128, 1], F32, "crs_sb")
    bg_sb = ar.alloc([128, 32], F32, "bg_sb")
    bu_sb = ar.alloc([128, 32], F32, "bu_sb")
    bdr_sb = ar.alloc([1, 2048], BF16, "bdr_sb")
    xtok_sb = ar.overlay("logF", 0, [128, 16 * 1024], BF16, "xtok_sb")
    dtok_sb = ar.overlay("logF", 32768, [128, 16 * 1024], BF16, "dtok_sb")
    cfm_sb = ar.overlay("dD_sb", 0, [128, 8 * TL], BF16, "cfm_sb")
    wgx_sb = ar.overlay("logF", 0, [128, NDT * 2048], BF16, "wgx_sb")
    gt_sb = ar.overlay("logF", 32768, [128, 16 * 256], BF16, "gt_sb")
    gu_sb = ar.overlay("logF", 32768 + 8192, [128, 16 * 256], BF16, "gu_sb")
    slfm_sb = ar.overlay("logF", 32768 + 16384, [128, NDT * 512], BF16, "slfm_sb")
    bd_sb = ar.overlay("logF", 32768 + 24576, [128, 1024], F32, "bd_sb")
    eo_sb = ar.overlay("logF", 32768 + 28672, [128, 1024], BF16, "eo_sb")
    xdE = "xcf" + str(_lnbuf_n[0] - 2)
    eof_sb = ar.overlay(xdE, 0, [128, 8 * 1024], BF16, "eof_sb")
    xdE2 = "xcf" + str(_lnbuf_n[0] - 1)
    stg_sb = ar.overlay(xdE2, 0, [128, 1032], BF16, "stg_sb")
    cn_sb = ar.overlay(xdE2, 2080, [128, 1024], BF16, "cn_sb")
    prt2_sb = ar.overlay(xdE2, 4160, [128, 2080], BF16, "prt2")
    prt_sb = [prt2_sb[:, 0:1032], prt2_sb[:, 1040:2072]]
    combf_sb = ar.overlay(xdE2, 8352, [128, 1024], F32, "combf_sb")
    mnb_sb = ar.overlay(xdE2, 12448, [128, 8], BF16, "mnb_sb")
    mnr_sb = ar.overlay(xdE2, 12480, [128, 8], F32, "mnr_sb")
    sqbE = "sqb" + str(_lnbuf_n[0] - 2)
    xr3_sb = [ar.overlay(sqbE, 0, [128, 512], F32, "xr3_sb0"),
              ar.overlay(sqbE, 2048, [128, 512], F32, "xr3_sb1")]
    oc3_sb = [ar.overlay(sqbE, 4096, [128, 512], F32, "oc3_sb0"),
              ar.overlay(sqbE, 6144, [128, 512], F32, "oc3_sb1")]

    g_ng = p.dma("gpsimd", ng_sb[:, :], dng, inc="g_tab")
    p.dma("gpsimd", nb_sb[:, :], dnb, inc="g_tab")
    p.dma("gpsimd", sg_sb[:, :], dsg, inc="g_tab")
    p.dma("gpsimd", sbb_sb[:, :], dsbb, inc="g_tab")
    g_sc = p.dma("gpsimd", sc1_sb[:, :], dscale, inc="g_tab")
    MM(psE[:, 0:1], ones1x128[:, :], sc1_sb[:, :], start=True, stop=True,
       waits=[("g_tab", g_sc), ("v", cnt["v"])], inc="t")
    vsc = CP("vector", scol_sb[:, :], psE[:, 0:1], waits=[("t", cnt["t"])], inc="v")
    ln3_ch = emit_ln(dx3, TL, dln3, bE, gten=ng_sb, bten=nb_sb,
                     src_ready_fn=lambda ch: x3_tc[ch] + [("g_tab", g_ng)])
    ln3_done = ln3_ch[-1]
    seln_done = emit_ln(dse, 1024, dse_ln, bE, gten=sg_sb, bten=sbb_sb,
                        postscale=scol_sb[:, 0:1],
                        src_ready=[("t", cnt["t"]), ("v", cnt["v"]),
                                   ("a", cnt["a"])],
                        prev_st={0: ln3_ch[2], 1: ln3_ch[3]})[-1]
    g_se = p.dma("gpsimd", se_sb.rearrange("p (dt s) -> p dt s", dt=NDT),
                 dse_ln.rearrange("(dt p) s -> p dt s", p=128),
                 waits=[seln_done], inc="g_tab")
    g_bg = p.dma("gpsimd", bg_sb[:, :], dbgc, inc="g_tab")
    g_bu = p.dma("gpsimd", bu_sb[:, :], dbuc, inc="g_tab")
    g_bd = p.dma("gpsimd", bdr_sb.rearrange("o (e f) -> o e f", e=2),
                 dbdr.rearrange("(o e) f -> o e f", o=1), inc="g_tab")
    seln_t = cnt["t"]

    lc3_ld = [None] * qc_n
    for tc in range(qc_n):
        qd = "sync" if tc % 2 == 0 else "scalar"
        wl = [ln3_ch[tc], ("t", seln_t)]
        lc3_ld[tc] = p.dma(qd, lnc3[tc].rearrange("p (dt t) -> p dt t", dt=NDT),
                           dln3.rearrange("(dt p) t -> p dt t", p=128)[:, :, tc * 512:(tc + 1) * 512],
                           waits=wl, inc=f"lc3{tc}")
    for tc in range(qc_n):
        for st in range(NDT):
            pbk = psC if (st * qc_n + tc) % 2 == 0 else psD
            for dt in range(NDT):
                w0 = []
                if dt == 0:
                    w0 = [(f"lc3{tc}", lc3_ld[tc]), ("g_tab", g_se), ("v", cnt["v"])]
                MM(pbk[:, 0:512], se_sb[:, dt * 1024 + st * 128:dt * 1024 + (st + 1) * 128],
                   lnc3[tc][:, dt * 512:(dt + 1) * 512],
                   start=(dt == 0), stop=(dt == NDT - 1), waits=w0,
                   inc=("t" if dt == NDT - 1 else None))
            CP("vector", logF[:, st * TL + tc * 512:st * TL + (tc + 1) * 512],
               pbk[:, 0:512], waits=[("t", cnt["t"])], inc="v")
    for st in range(NDT):
        ssl = slice(st * TL, (st + 1) * TL)
        RED(mneg_sb[:, st:st + 1], logF[:, ssl], ALU.max, negate=True)
        CP("vector", mnb_sb[:, st:st + 1], mneg_sb[:, st:st + 1])
        va = CP("vector", mnr_sb[:, st:st + 1], mnb_sb[:, st:st + 1], inc="v")
        ACTV(dD_sb[:, ssl], logF[:, ssl], AF.Exp, bias=mnr_sb[:, st:st + 1],
             waits=[("v", va)], inc="a")
        RED(sesum_sb[:, st:st + 1], dD_sb[:, ssl], ALU.add,
            waits=[("a", cnt["a"])], inc="v")
    disp_done_a, disp_done_v = cnt["a"], cnt["v"]

    for tch in range(16):
        tc4, sub = tch // 4, tch % 4
        for dt in range(NDT):
            w0 = [("v", cnt["v"])]
            if dt == 0:
                w0 += [(f"lc3{tc4}", lc3_ld[tc4]), ("a", disp_done_a)]
            p.emit("tensor",
                   (lambda e, o=psT[:, dt * 128:(dt + 1) * 128],
                    i=lnc3[tc4][:, dt * 512 + sub * 128:dt * 512 + sub * 128 + 128],
                    idn=id_sb[:, :]: e.transpose(o, i, idn)),
                   waits=w0, inc=("t" if dt == NDT - 1 else None))
        CP("vector", xtok_sb[:, tch * 1024:(tch + 1) * 1024], psT[:, :],
           waits=[("t", cnt["t"]), ("v", disp_done_v), ("a", disp_done_a)], inc="v")
    for tch in range(16):
        for st in range(NDT):
            w0 = [("v", cnt["v"])]
            p.emit("tensor",
                   (lambda e, o=psT[:, st * 128:(st + 1) * 128],
                    i=dD_sb[:, st * TL + tch * 128:st * TL + (tch + 1) * 128],
                    idn=id_sb[:, :]: e.transpose(o, i, idn)),
                   waits=w0, inc=("t" if st == NDT - 1 else None))
        CP("vector", dtok_sb[:, tch * 1024:(tch + 1) * 1024], psT[:, :],
           waits=[("t", cnt["t"])], inc="v")
    tr_done = cnt["v"]

    MS(stg_sb[:, 1026:1032], 0.0, inc="pl")
    stg0 = cnt["pl"]
    for st in range(NDT):
        for tch in range(16):
            w0 = []
            if tch == 0:
                w0 = [("v", tr_done), ("v", cnt["v"])]
            MM(psA[:, 0:512], dtok_sb[:, tch * 1024 + st * 128:tch * 1024 + (st + 1) * 128],
               xtok_sb[:, tch * 1024:tch * 1024 + 512],
               start=(tch == 0), stop=(tch == 15), waits=w0)
            MM(psA[:, 512:1024], dtok_sb[:, tch * 1024 + st * 128:tch * 1024 + (st + 1) * 128],
               xtok_sb[:, tch * 1024 + 512:tch * 1024 + 1024],
               start=(tch == 0), stop=(tch == 15),
               inc=("t" if tch == 15 else None))
        t_slots = cnt["t"]
        CP("vector", stg_sb[:, 0:512], psA[:, 0:512],
           waits=[("t", cnt["t"]), ("pl", stg0)] +
                 ([("s_slag", cnt["s_slag"])] if cnt.get("s_slag") else []))
        CP("vector", stg_sb[:, 512:1024], psA[:, 512:1024])
        p.emit("vector", (lambda e, o=stg_sb[:, 1024:1025],
                          i=mnr_sb[:, st:st + 1]: e.tensor_scalar_mul(o, i, -1.0)))
        vstg = CP("vector", stg_sb[:, 1025:1026], sesum_sb[:, st:st + 1], inc="v")
        p.dma("sync", dsl_in[st * 128:(st + 1) * 128, :], stg_sb[:, :],
              waits=[("v", vstg)], inc="s_slag")
    p.sem("cc")
    cc_sl = p.emit("gpsimd", (lambda e: e.collective_compute(
        "AllGather", ALU.bypass, replica_groups=[list(range(NCORES))],
        ins=[dsl_in.opt()], outs=[dsl_out.opt()])),
        waits=[("s_slag", cnt["s_slag"])], inc="cc", amt=1, dma=True)

    p.sem("g_wx")
    p.sem("s_wx")

    def load_wx(dram_e, nch, war_t):
        dst = wgx_sb.rearrange("p (c f) -> p c f", c=nch)
        src = dram_e
        h = nch // 2
        wg_ = [("t", war_t)]
        if cnt["g_wx"]:
            wg_.append(("g_wx", cnt["g_wx"]))
        g = p.dma("gpsimd", dst[:, 0:h, :], src[:, 0:h, :], waits=wg_, inc="g_wx")
        ws_ = [("t", war_t)]
        if cnt["s_wx"]:
            ws_.append(("s_wx", cnt["s_wx"]))
        s = p.dma("gpsimd", dst[:, h:nch, :], src[:, h:nch, :], waits=ws_, inc="s_wx")
        return g, s, h

    wg0_pre = load_wx(dwg.rearrange("e (dt p) f -> e p dt f", p=128)[0],
                      NDT, t_slots)

    for tch in range(16):
        tc4, sub = tch // 4, tch % 4
        for dt in range(NDT):
            w0 = []
            if dt == 0:
                w0 = [(f"lc3{tc4}", lc3_ld[tc4]), ("v", cnt["v"]), ("a", cnt["a"])]
            MM(psB[:, 0:512], lnc3[tc4][:, dt * 512 + sub * 128:dt * 512 + sub * 128 + 128],
               se_sb[:, dt * 1024:dt * 1024 + 512],
               start=(dt == 0), stop=(dt == NDT - 1), waits=w0)
            MM(psB[:, 512:1024], lnc3[tc4][:, dt * 512 + sub * 128:dt * 512 + sub * 128 + 128],
               se_sb[:, dt * 1024 + 512:dt * 1024 + 1024],
               start=(dt == 0), stop=(dt == NDT - 1),
               inc=("t" if dt == NDT - 1 else None))
        RED(cmneg_sb[:, 0:1], psB[:, 0:1024], ALU.max, negate=True,
            waits=[("t", cnt["t"])])
        ACTV(cn_sb[:, :], psB[:, 0:1024], AF.Exp, bias=cmneg_sb[:, 0:1],
             waits=[("v", cnt["v"])], inc="a")
        RED(crs_sb[:, 0:1], cn_sb[:, :], ALU.add, waits=[("a", cnt["a"])])
        RECIP(crs_sb[:, 0:1], crs_sb[:, 0:1])
        TS(cn_sb[:, :], cn_sb[:, :], crs_sb[:, 0:1], None, ALU.mult, inc="v")
        for st in range(NDT):
            w0 = [("v", cnt["v"])]
            p.emit("tensor",
                   (lambda e, o=psT[:, st * 128:(st + 1) * 128],
                    i=cn_sb[:, st * 128:(st + 1) * 128],
                    idn=id_sb[:, :]: e.transpose(o, i, idn)),
                   waits=w0, inc=("t" if st == NDT - 1 else None))
        p.emit("vector",
               (lambda e, tch=tch: e.tensor_copy(
                   cfm_sb.rearrange("p (st t) -> p st t", st=NDT)[:, :, tch * 128:(tch + 1) * 128],
                   psT.rearrange("p (st c) -> p st c", st=NDT))),
               waits=[("t", cnt["t"])], inc="v")
    cfm_done = cnt["v"]

    p.sem("g_sl")
    _dynidx = {}

    def get_me(e):
        if "me" not in _dynidx:
            pid = e.partition_id()
            _dynidx["me"] = nc.s_assert_within(pid, 0, 7, skip_runtime_assert=True)
        return _dynidx["me"]

    for b in range(4):
        def read_pair(e, b=b):
            me = get_me(e)
            return e.dma_start(
                prt2_sb.rearrange("p (r f) -> p r f", r=2)[:, :, 0:1032],
                dsl_out.rearrange("(r q p) f -> q p r f", q=NCORES, p=128)[me, :, 2 * b:2 * b + 2, :])
        p.emit("gpsimd", read_pair,
               waits=[("cc", cc_sl), ("g_sl", cnt["g_sl"])] +
                     ([("v", cnt["v"])] if b else []),
               inc="g_sl", amt=16, dma=True)
        g_pr = cnt["g_sl"]
        TT(cmneg_sb[:, 0:1], prt_sb[0][:, 1024:1025], prt_sb[1][:, 1024:1025],
           ALU.max, waits=[("g_sl", g_pr)])
        TT(crs_sb[:, 0:1], prt_sb[0][:, 1024:1025], cmneg_sb[:, 0:1], ALU.subtract)
        ACTV(crs_sb[:, 0:1], crs_sb[:, 0:1], AF.Exp, waits=[("v", cnt["v"])], inc="a")
        TT(mneg_sb[:, 0:1], prt_sb[1][:, 1024:1025], cmneg_sb[:, 0:1], ALU.subtract,
           waits=[("a", cnt["a"])])
        ACTV(mneg_sb[:, 0:1], mneg_sb[:, 0:1], AF.Exp, waits=[("v", cnt["v"])], inc="a")
        TS(mpos_sb[:, 0:1], prt_sb[0][:, 1025:1026], crs_sb[:, 0:1], None, ALU.mult,
           waits=[("a", cnt["a"])])
        STT(mpos_sb[:, 0:1], prt_sb[1][:, 1025:1026], mneg_sb[:, 0:1],
            mpos_sb[:, 0:1], ALU.mult, ALU.add)
        RECIP(mpos_sb[:, 0:1], mpos_sb[:, 0:1])
        TS(combf_sb[:, :], prt_sb[0][:, 0:1024], crs_sb[:, 0:1], None, ALU.mult)
        STT(combf_sb[:, :], prt_sb[1][:, 0:1024], mneg_sb[:, 0:1],
            combf_sb[:, :], ALU.mult, ALU.add)
        TS(cn_sb[:, :], combf_sb[:, :], mpos_sb[:, 0:1], None, ALU.mult, inc="v")
        for dt in range(NDT):
            w0 = [("v", cnt["v"])]
            p.emit("tensor",
                   (lambda e, o=psT[:, dt * 128:(dt + 1) * 128],
                    i=cn_sb[:, dt * 128:(dt + 1) * 128],
                    idn=id_sb[:, :]: e.transpose(o, i, idn)),
                   waits=w0, inc=("t" if dt == NDT - 1 else None))
        vtr = p.emit("vector",
                     (lambda e, b=b: e.tensor_copy(
                         slfm_sb.rearrange("p (dt c) -> p dt c", dt=NDT)[:, :, b * 128:(b + 1) * 128],
                         psT.rearrange("p (dt c) -> p dt c", dt=NDT))),
                     waits=[("t", cnt["t"])], inc="v")
    slots_done = cnt["v"]

    for e_i in range(2):
        if e_i == 0:
            g_wg, s_wg, hh = wg0_pre
        else:
            g_wg, s_wg, hh = load_wx(
                dwg.rearrange("e (dt p) f -> e p dt f", p=128)[e_i], NDT, cnt["t"])
        sl_e = slfm_sb.rearrange("p (dt b s) -> p dt b s", dt=NDT, b=4)
        for fc in range(16):
            for dt in range(NDT):
                w0 = []
                if dt == 0:
                    w0 = [("a", cnt["a"])]
                    if fc == 0:
                        w0 += [("g_wx", g_wg), ("v", slots_done)]
                if fc == 0 and dt == hh:
                    w0 = w0 + [("s_wx", s_wg)]
                MM(psC[:, 0:256], wgx_sb[:, dt * 2048 + fc * 128:dt * 2048 + (fc + 1) * 128],
                   sl_e[:, dt, :, e_i * 64:(e_i + 1) * 64],
                   start=(dt == 0), stop=(dt == NDT - 1), waits=w0,
                   inc=("t" if dt == NDT - 1 else None))
            ACTV(gt_sb[:, fc * 256:(fc + 1) * 256], psC[:, 0:256], AF.Gelu,
                 bias=bg_sb[:, e_i * 16 + fc:e_i * 16 + fc + 1],
                 waits=[("t", cnt["t"]), ("g_tab", g_bg)], inc="a")
        gate_a = cnt["a"]
        g_wu, s_wu, _ = load_wx(dwu.rearrange("e (dt p) f -> e p dt f", p=128)[e_i],
                                NDT, cnt["t"])
        for fc in range(16):
            for dt in range(NDT):
                w0 = []
                if dt == 0:
                    w0 = [("v", cnt["v"])]
                    if fc == 0:
                        w0.append(("g_wx", g_wu))
                if fc == 0 and dt == hh:
                    w0 = w0 + [("s_wx", s_wu)]
                MM(psD[:, 0:256], wgx_sb[:, dt * 2048 + fc * 128:dt * 2048 + (fc + 1) * 128],
                   sl_e[:, dt, :, e_i * 64:(e_i + 1) * 64],
                   start=(dt == 0), stop=(dt == NDT - 1), waits=w0,
                   inc=("t" if dt == NDT - 1 else None))
            STT(gu_sb[:, fc * 256:(fc + 1) * 256], psD[:, 0:256],
                bu_sb[:, e_i * 16 + fc:e_i * 16 + fc + 1],
                gt_sb[:, fc * 256:(fc + 1) * 256], ALU.add, ALU.mult,
                waits=[("t", cnt["t"]), ("a", gate_a), ("g_tab", g_bu)], inc="v")
        gu_done = cnt["v"]
        g_wd, s_wd, hd = load_wx(dwd.rearrange("e (ft p) o -> e p ft o", p=128)[e_i],
                                 16, cnt["t"])
        MM(psB[:, 0:512], ones1x128b[:, :], bdr_sb[:, e_i * 1024:e_i * 1024 + 512],
           start=True, stop=True, waits=[("g_tab", g_bd), ("v", cnt["v"])])
        MM(psB[:, 512:1024], ones1x128b[:, :],
           bdr_sb[:, e_i * 1024 + 512:e_i * 1024 + 1024], start=True, stop=True,
           inc="t")
        CP("vector", bd_sb[:, :], psB[:, 0:1024], waits=[("t", cnt["t"])], inc="v")
        for sc in range(2):
            for ft in range(16):
                w0 = []
                if ft == 0:
                    w0 = [("v", gu_done), ("v", cnt["v"])]
                    if sc == 0:
                        w0.append(("g_wx", g_wd))
                if sc == 0 and ft == hd:
                    w0 = w0 + [("s_wx", s_wd)]
                MM(psA[:, 0:512], gu_sb[:, ft * 256 + sc * 128:ft * 256 + sc * 128 + 128],
                   wgx_sb[:, ft * 1024:ft * 1024 + 512],
                   start=(ft == 0), stop=(ft == 15), waits=w0)
                MM(psA[:, 512:1024], gu_sb[:, ft * 256 + sc * 128:ft * 256 + sc * 128 + 128],
                   wgx_sb[:, ft * 1024 + 512:ft * 1024 + 1024],
                   start=(ft == 0), stop=(ft == 15),
                   inc=("t" if ft == 15 else None))
            veo = TT(eo_sb[:, :], psA[:, 0:1024], bd_sb[:, :], ALU.add,
                     waits=[("t", cnt["t"])] +
                           ([("s_eoag", cnt["s_eoag"])] if cnt.get("s_eoag") else []),
                     inc="v")
            p.dma("sync",
                  deo_in[2 * sc * 128 + e_i * 64:2 * sc * 128 + (e_i + 1) * 64, :],
                  eo_sb[0:64, :], waits=[("v", veo)], inc="s_eoag")
            p.dma("sync",
                  deo_in[(2 * sc + 1) * 128 + e_i * 64:(2 * sc + 1) * 128 + (e_i + 1) * 64, :],
                  eo_sb[64:128, :], inc="s_eoag")
    cc_eo = p.emit("gpsimd", (lambda e: e.collective_compute(
        "AllGather", ALU.bypass, replica_groups=[list(range(NCORES))],
        ins=[deo_in.opt()], outs=[deo_out.opt()])),
        waits=[("s_eoag", cnt["s_eoag"])], inc="cc", amt=1, dma=True)

    p.sem("g_eo")

    def read_eo(e):
        pid = e.partition_id()
        bp = nc.s_assert_within(e.compute_val(pid // 2), 0, 3,
                                skip_runtime_assert=True)
        return e.dma_start(
            eof_sb.rearrange("p (r f) -> p r f", r=NCORES),
            deo_out.rearrange("(r q p) f -> q p r f", q=4, p=128)[bp, :, :, :])
    eof_done = p.emit("gpsimd", read_eo,
                      waits=[("cc", cc_eo), ("g_eo", cnt["g_eo"]),
                             ("a", disp_done_a)],
                      inc="g_eo", amt=16, dma=True)

    ett_h = [None, None]
    est_h = [None, None]
    e9_v0 = cnt["v"]
    for i, (dc, tc) in enumerate([(d, t) for d in range(NDT)
                                  for t in range(qc_n)]):
        b = i % 2
        sx = p.dma("scalar", xr3_sb[b][:, :],
                   dx3[dc * 128:(dc + 1) * 128, tc * 512:(tc + 1) * 512],
                   waits=[("v", ett_h[b] if ett_h[b] else e9_v0)],
                   inc=f"a_xr{b}")
        for st in range(NDT):
            w0 = []
            if st == 0:
                w0 = [("g_eo", eof_done), ("v", cfm_done), ("v", cnt["v"])]
            MM(psC[:, 0:512],
               eof_sb[:, st * 1024 + dc * 128:st * 1024 + (dc + 1) * 128],
               cfm_sb[:, st * TL + tc * 512:st * TL + (tc + 1) * 512],
               start=(st == 0), stop=(st == NDT - 1), waits=w0,
               inc=("t" if st == NDT - 1 else None))
        vo = TT(oc3_sb[b][:, :], psC[:, 0:512], xr3_sb[b][:, :], ALU.add,
                waits=[("t", cnt["t"]), (f"a_xr{b}", sx)] +
                      ([(f"os{b}", est_h[b])] if est_h[b] else []),
                inc="v")
        ett_h[b] = vo
        est_h[b] = p.dma("sync", dout[dc * 128:(dc + 1) * 128, tc * 512:(tc + 1) * 512],
                         oc3_sb[b][:, :], waits=[("v", vo)], inc=f"os{b}")
    p.emit("sync", lambda e: (e.wait_ge(p.sems["os0"], cnt["os0"]),
                              e.wait_ge(p.sems["os1"], cnt["os1"])) and None)

    if DEBUG_STAGE in (0, 4):
        p.replay()
        return nc
    p.stage("fin")
    ar.reset()
    fin_sb = ar.alloc([128, 2048], F32, "fin_sb")
    if DEBUG_STAGE == 1:
        src_d, sds = dx1, "s_x1"
    elif DEBUG_STAGE == 2:
        src_d, sds = dx2, "s_x2"
    else:
        src_d, sds = dx3, "s_x3"
    stream_done = [(sds, cnt[sds])]
    for dc in range(NDT):
        ld = p.dma("sync", fin_sb[:, :], src_d[dc * 128:(dc + 1) * 128, :],
                   waits=stream_done + ([("o", cnt["o"])] if dc else []), inc="s_fin")
        p.dma("sync", dout[dc * 128:(dc + 1) * 128, :], fin_sb[:, :],
              waits=[("s_fin", ld)], inc="o")
    p.emit("sync", lambda e: e.wait_ge(p.sems["o"], cnt["o"]) and None)

    p.replay()
    return nc


_CACHE = {}


def kernel(**inputs):
    if "nc" not in _CACHE:
        _CACHE["nc"] = build()
    nc = _CACHE["nc"]
    in_maps = _prep(inputs)
    res = run_bass_kernel_spmd(nc, in_maps, core_ids=list(range(NCORES)))
    out = np.empty((B, T, D), np.float32)
    for c in range(NCORES):
        b, half = c // 2, c % 2
        t0 = half * TL
        out[b, t0:t0 + TL, :] = res.results[c]["out_fm"].T
    return out


# version 16
# speedup vs baseline: 1.1893x; 1.0235x over previous
import numpy as np
import concourse.bass as bass
import concourse.mybir as mybir
from concourse.bass_utils import run_bass_kernel_spmd

F32 = mybir.dt.float32
BF16 = mybir.dt.bfloat16
AF = mybir.ActivationFunctionType
ALU = mybir.AluOpType
AX = mybir.AxisListType

NCORES = 8
B, T, D, H, DH = 4, 4096, 1024, 16, 64
TL = T // 2
NDT = D // 128
EPS_LN = 1e-5

DEBUG_STAGE = 0


ENGS = ("tensor", "vector", "scalar", "gpsimd", "sync")


class Prog:
    def __init__(self, nc):
        self.nc = nc
        self.stages = []
        self.cnt = {}
        self.sems = {}
        self.cur = None

    def sem(self, name):
        if name not in self.sems:
            self.sems[name] = self.nc.alloc_semaphore(name)
            self.cnt[name] = 0
        return name

    def stage(self, name):
        self.cur = {e: [] for e in ENGS}
        self.cur["_name"] = name
        self.stages.append(self.cur)

    ENG_SEM = {"vector": "v", "scalar": "a", "gpsimd": "pl"}

    def emit(self, eng, fn, waits=(), inc=None, amt=1, dma=False, chain=True):
        for s, v in waits:
            assert v <= self.cnt[s], (s, v, self.cnt[s])
        waits = list(waits)
        if not dma and eng in self.ENG_SEM:
            es = self.ENG_SEM[eng]
            self.sem(es)
            if chain and self.cnt[es]:
                waits.append((es, self.cnt[es]))
            if inc is None:
                inc = es
            assert inc == es, (eng, inc)
        rec = (fn, tuple(waits), inc, amt)
        self.cur[eng].append(rec)
        if inc is not None:
            self.cnt[inc] += amt
            return self.cnt[inc]
        return None

    def dma(self, eng, out, in_, waits=(), inc=None, **kw):
        if inc is None:
            inc = eng[0]
        self.sem(inc)
        w = list(waits)
        if self.cnt[inc]:
            w.append((inc, self.cnt[inc]))
        return self.emit(eng, lambda e: e.dma_start(out, in_, **kw), w, inc, 16,
                         dma=True)

    def replay(self):
        nc = self.nc
        ctx = nc.allow_low_precision(reason="bf16 matmul operands per problem tolerance")
        ctx.__enter__()
        for st in self.stages:
            with nc.Block(st["_name"]) as block:
                for eng in ENGS:
                    recs = st[eng]
                    if not recs:
                        continue

                    def body(e, recs=recs):
                        for fn, waits, inc, amt in recs:
                            for s, v in waits:
                                e.wait_ge(self.sems[s], v)
                            inst = fn(e)
                            if inc is not None and inst is not None:
                                inst.then_inc(self.sems[inc], amt)

                    getattr(block, eng)(body)


class Arena:

    def __init__(self, nc, slack=5120):
        self.nc = nc
        size = nc.sbuf_bytes_remaining - slack
        base, _ = nc.bump_sbuf(size)
        self.base = base
        self.off = base
        self.limit = base + size
        self.mark_off = None
        self.n = 0

    def alloc(self, shape, dtype, name=None):
        assert len(shape) == 2 and shape[0] <= 128
        itm = 2 if dtype == BF16 else 4
        sz = shape[1] * itm
        off = (self.off + 31) & ~31
        assert off + sz <= self.limit, (
            f"SBUF arena overflow: {name} needs {sz} at {off}, limit {self.limit}")
        self.off = off + sz
        self.n += 1
        nm = name or f"ar{self.n}"
        if not hasattr(self, "offs"):
            self.offs = {}
        self.offs[nm] = off
        return self.nc.alloc_sbuf_tensor_at(nm, list(shape), dtype, offset=off)

    def overlay(self, base_name, delta, shape, dtype, name):
        off = self.offs[base_name] + delta
        self.n += 1
        return self.nc.alloc_sbuf_tensor_at(name, list(shape), dtype, offset=off)

    def mark(self, key="base"):
        if self.mark_off is None:
            self.mark_off = {}
        self.mark_off[key] = self.off

    def reset(self, key="base"):
        self.off = self.mark_off[key]


def _rope_tables():
    base = 10000.0
    inv = 1.0 / base ** (np.arange(0, DH, 2, dtype=np.float64) / DH)
    ang = np.arange(T, dtype=np.float64)[:, None] * inv[None, :]
    cos = np.cos(ang)
    sin = np.sin(ang)
    cs = np.empty((128, T), np.float32)
    ss = np.empty((128, T), np.float32)
    for p in range(128):
        d = p % 64
        cs[p] = cos[:, d % 32].astype(np.float32)
        sgn = -1.0 if (d % 64) < 32 else 1.0
        ss[p] = (sgn * sin[:, d % 32]).astype(np.float32)
    return cs, ss


def _prep_core(inputs, c):
    b, half = c // 2, c % 2
    t0, o0 = half * TL, (1 - half) * TL
    order = np.r_[t0:t0 + TL, o0:o0 + TL]

    x = np.asarray(inputs["L0_attn_hidden_states"])
    cs, ss = _rope_tables()

    d = {}
    d["x"] = np.ascontiguousarray(x[b][order].T)
    d["wqT"] = np.ascontiguousarray(np.asarray(inputs["L0_attn_Wq"]).T)
    d["wkT"] = np.ascontiguousarray(np.asarray(inputs["L0_attn_Wk"]).T)
    wv = np.asarray(inputs["L0_attn_Wv"])
    wv65 = np.zeros((D, H * 65), np.float32)
    for h in range(H):
        wv65[:, h * 65:h * 65 + 64] = wv[h * 64:(h + 1) * 64, :].T
    d["wv65"] = wv65
    bv = np.asarray(inputs["L0_attn_bv"])
    bv_row = np.zeros((1, H * 65), np.float32)
    for h in range(H):
        bv_row[0, h * 65:h * 65 + 64] = bv[h * 64:(h + 1) * 64]
    d["bv_row"] = bv_row
    d["bq_p"] = np.ascontiguousarray(np.asarray(inputs["L0_attn_bq"]).reshape(8, 128).T)
    d["bk_p"] = np.ascontiguousarray(np.asarray(inputs["L0_attn_bk"]).reshape(8, 128).T)
    d["bo_c"] = np.ascontiguousarray(np.asarray(inputs["L0_attn_bo"]).reshape(8, 128).T)
    d["woT"] = np.ascontiguousarray(np.asarray(inputs["L0_attn_Wo"]).T)
    d["cs"] = np.ascontiguousarray(cs[:, order])
    d["ss"] = np.ascontiguousarray(ss[:, order])

    sel128 = np.zeros((128, 128), np.float32)
    sel128[0, 0:64] = 1.0
    sel128[64, 64:128] = 1.0
    d["sel128"] = sel128
    d["ident"] = np.eye(128, dtype=np.float32)
    perm = np.r_[32:64, 0:32, 96:128, 64:96]
    P = np.zeros((128, 128), np.float32)
    P[perm, np.arange(128)] = 1.0
    d["permm"] = P

    d["w1t"] = np.ascontiguousarray(np.asarray(inputs["L0_ffn_weight"]).T)
    d["disha_p"] = np.ascontiguousarray(np.asarray(inputs["L0_ffn_disha"]))
    psel = np.zeros((128, 64), np.float32)
    for pp in range(128):
        psel[pp, pp % 64] = 1.0
    d["psel"] = psel

    wq1 = np.asarray(inputs["L1_attn_W_Q"])[:, :, 0]
    wk1 = np.asarray(inputs["L1_attn_W_K"])[:, :, 0]
    wqk1 = np.zeros((D, 64), np.float32)
    wqk1[:, 0:16] = wq1.T
    wqk1[:, 32:48] = wk1.T
    d["wqk1"] = wqk1
    wv1 = np.asarray(inputs["L1_attn_W_V"])
    d["wv1"] = np.ascontiguousarray(wv1.transpose(1, 0, 2).reshape(D, D))
    wo1 = np.asarray(inputs["L1_attn_W_O"])
    d["wo1"] = np.ascontiguousarray(wo1.transpose(2, 1, 0).reshape(D, D))
    sel16 = np.zeros((16, 8 * 128), np.float32)
    for kt in range(8):
        for pp in range(128):
            sel16[2 * kt + (pp >= 64), kt * 128 + pp] = 1.0
    d["sel16"] = sel16
    d["oddmask"] = np.full((128, 1), float(half), np.float32)

    se_raw = np.asarray(inputs["L1_ffn_slot_embeds"]).reshape(1024, D)
    d["se_raw"] = np.ascontiguousarray(se_raw.T)
    d["ng"] = np.ascontiguousarray(np.asarray(inputs["L1_ffn_norm_g"]).reshape(8, 128).T)
    d["nb"] = np.ascontiguousarray(np.asarray(inputs["L1_ffn_norm_b"]).reshape(8, 128).T)
    d["sg"] = np.ascontiguousarray(np.asarray(inputs["L1_ffn_slot_g"]).reshape(8, 128).T)
    d["sbb"] = np.ascontiguousarray(np.asarray(inputs["L1_ffn_slot_b"]).reshape(8, 128).T)
    d["scale1"] = np.asarray(inputs["L1_ffn_scale"]).reshape(1, 1).astype(np.float32)
    es = (2 * c, 2 * c + 1)
    d["wg"] = np.ascontiguousarray(np.asarray(inputs["L1_ffn_Wg"])[list(es)])
    d["wu"] = np.ascontiguousarray(np.asarray(inputs["L1_ffn_Wu"])[list(es)])
    d["wd"] = np.ascontiguousarray(np.asarray(inputs["L1_ffn_Wd"])[list(es)])
    bg = np.asarray(inputs["L1_ffn_bg"])
    bu = np.asarray(inputs["L1_ffn_bu"])
    bgc = np.zeros((128, 32), np.float32)
    buc = np.zeros((128, 32), np.float32)
    for j, eg in enumerate(es):
        bgc[:, j * 16:(j + 1) * 16] = bg[eg].reshape(16, 128).T
        buc[:, j * 16:(j + 1) * 16] = bu[eg].reshape(16, 128).T
    d["bgc"] = bgc
    d["buc"] = buc
    d["bdr"] = np.ascontiguousarray(np.asarray(inputs["L1_ffn_bd"])[list(es)])
    return d


def _prep(inputs):
    return [_prep_core(inputs, c) for c in range(NCORES)]


def build():
    nc = bass.Bass("TRN2", target_bir_lowering=False)
    p = Prog(nc)
    for s in ("g", "s", "t", "v", "a", "pl", "cc", "o"):
        p.sem(s)

    dx = nc.dram_tensor("x", [D, T], F32, kind="ExternalInput").ap()
    dwqT = nc.dram_tensor("wqT", [D, D], F32, kind="ExternalInput").ap()
    dwkT = nc.dram_tensor("wkT", [D, D], F32, kind="ExternalInput").ap()
    dwv65 = nc.dram_tensor("wv65", [D, H * 65], F32, kind="ExternalInput").ap()
    dbv = nc.dram_tensor("bv_row", [1, H * 65], F32, kind="ExternalInput").ap()
    dbq = nc.dram_tensor("bq_p", [128, 8], F32, kind="ExternalInput").ap()
    dbk = nc.dram_tensor("bk_p", [128, 8], F32, kind="ExternalInput").ap()
    dbo = nc.dram_tensor("bo_c", [128, 8], F32, kind="ExternalInput").ap()
    dwoT = nc.dram_tensor("woT", [D, D], F32, kind="ExternalInput").ap()
    dcs = nc.dram_tensor("cs", [128, T], F32, kind="ExternalInput").ap()
    dss = nc.dram_tensor("ss", [128, T], F32, kind="ExternalInput").ap()
    dsel = nc.dram_tensor("sel128", [128, 128], F32, kind="ExternalInput").ap()
    dpermm = nc.dram_tensor("permm", [128, 128], F32, kind="ExternalInput").ap()

    dout = nc.dram_tensor("out_fm", [D, TL], F32, kind="ExternalOutput").ap()

    dw1t = nc.dram_tensor("w1t", [D, D], F32, kind="ExternalInput").ap()
    ddisha = nc.dram_tensor("disha_p", [64, D], F32, kind="ExternalInput").ap()
    dpsel = nc.dram_tensor("psel", [128, 64], F32, kind="ExternalInput").ap()
    dwqk1 = nc.dram_tensor("wqk1", [D, 64], F32, kind="ExternalInput").ap()
    dwv1 = nc.dram_tensor("wv1", [D, D], F32, kind="ExternalInput").ap()
    dwo1 = nc.dram_tensor("wo1", [D, D], F32, kind="ExternalInput").ap()
    dsel16 = nc.dram_tensor("sel16", [16, 8 * 128], F32, kind="ExternalInput").ap()
    dmask = nc.dram_tensor("oddmask", [128, 1], F32, kind="ExternalInput").ap()
    dident = nc.dram_tensor("ident", [128, 128], F32, kind="ExternalInput").ap()

    dln0 = nc.dram_tensor("ln0d", [D, T], BF16, kind="Internal").ap()
    dx1 = nc.dram_tensor("x1d", [D, TL], F32, kind="Internal").ap()
    dx2 = nc.dram_tensor("x2d", [D, TL], F32, kind="Internal").ap()
    dx3 = nc.dram_tensor("x3d", [D, TL], F32, kind="Internal").ap()
    dln1 = nc.dram_tensor("ln1d", [D, TL], BF16, kind="Internal").ap()
    dln2 = nc.dram_tensor("ln2d", [D, TL], BF16, kind="Internal").ap()
    dse = nc.dram_tensor("se_raw", [D, 1024], F32, kind="ExternalInput").ap()
    dng = nc.dram_tensor("ng", [128, 8], F32, kind="ExternalInput").ap()
    dnb = nc.dram_tensor("nb", [128, 8], F32, kind="ExternalInput").ap()
    dsg = nc.dram_tensor("sg", [128, 8], F32, kind="ExternalInput").ap()
    dsbb = nc.dram_tensor("sbb", [128, 8], F32, kind="ExternalInput").ap()
    dscale = nc.dram_tensor("scale1", [1, 1], F32, kind="ExternalInput").ap()
    dwg = nc.dram_tensor("wg", [2, D, 2048], F32, kind="ExternalInput").ap()
    dwu = nc.dram_tensor("wu", [2, D, 2048], F32, kind="ExternalInput").ap()
    dwd = nc.dram_tensor("wd", [2, 2048, D], F32, kind="ExternalInput").ap()
    dbgc = nc.dram_tensor("bgc", [128, 32], F32, kind="ExternalInput").ap()
    dbuc = nc.dram_tensor("buc", [128, 32], F32, kind="ExternalInput").ap()
    dbdr = nc.dram_tensor("bdr", [2, 1024], F32, kind="ExternalInput").ap()
    dln3 = nc.dram_tensor("ln3d", [D, TL], BF16, kind="Internal").ap()
    dse_ln = nc.dram_tensor("se_lnd", [D, 1024], BF16, kind="Internal").ap()
    dsl_in = nc.dram_tensor("sl_in", [1024, 1032], BF16, kind="Internal").ap()
    dsl_out = nc.dram_tensor("sl_out", [1024 * NCORES, 1032], BF16, kind="Internal").ap()
    deo_in = nc.dram_tensor("eo_in", [512, 1024], BF16, kind="Internal").ap()
    deo_out = nc.dram_tensor("eo_out", [512 * NCORES, 1024], BF16, kind="Internal").ap()
    dcar_in = nc.dram_tensor("car_in", [128, 24], F32, kind="Internal").ap()
    dcar_out = nc.dram_tensor("car_out", [128 * 2, 24], F32, kind="Internal").ap()

    ar = Arena(nc)
    ones128_bf = ar.alloc([128, 1], BF16, "ones128b")
    ones1x128 = ar.alloc([1, 128], F32, "ones1x128")
    ones1x128b = ar.alloc([1, 128], BF16, "ones1x128b")
    consts = ar.alloc([128, 4], F32, "consts")
    sel_sb = ar.alloc([128, 128], BF16, "sel_sb")
    id_sb = ar.alloc([128, 128], BF16, "id_sb")
    bq_sb = ar.alloc([128, 8], F32, "bq_sb")
    bk_sb = ar.alloc([128, 8], F32, "bk_sb")
    bo_sb = ar.alloc([128, 8], F32, "bo_sb")
    ar.mark()

    psAB = nc.alloc_psum_tensor("psAB", [128, 2048], F32)
    psA = psAB[:, 0:1024]
    psB = psAB[:, 1024:2048]
    psC = nc.alloc_psum_tensor("psC", [128, 512], F32)
    psD = nc.alloc_psum_tensor("psD", [128, 512], F32)
    psE = nc.alloc_psum_tensor("psE", [128, 512], F32)
    psT = nc.alloc_psum_tensor("psT", [128, 1024], BF16)

    cnt = p.cnt

    def MM(out, lhsT, rhs, start, stop, waits=(), inc=None, tp=None):
        return p.emit("tensor",
                      (lambda e, o=out, l=lhsT, r=rhs, s0=start, s1=stop, t=tp:
                       e.matmul(o, l, r, start=s0, stop=s1, tile_position=t)),
                      waits, inc)

    def CP(eng, out, in_, waits=(), inc=None):
        return p.emit(eng, (lambda e, o=out, i=in_: e.tensor_copy(o, i)), waits, inc)

    def TT(out, a, b, op, waits=(), inc=None, eng="vector"):
        return p.emit(eng, (lambda e, o=out, x=a, y=b, f=op: e.tensor_tensor(o, x, y, f)),
                      waits, inc)

    def TS(out, in_, s1, s2, op0, op1=None, waits=(), inc=None, eng="vector"):
        if op1 is None:
            op1 = ALU.bypass
        return p.emit(eng, (lambda e, o=out, i=in_, a=s1, b=s2, f0=op0, f1=op1:
                            e.tensor_scalar(o, i, a, b, f0, f1)), waits, inc)

    def STT(out, in0, scalar, in1, op0, op1, waits=(), inc=None):
        return p.emit("vector", (lambda e, o=out, i=in0, s=scalar, j=in1, f0=op0, f1=op1:
                                 e.scalar_tensor_tensor(o, i, s, j, f0, f1)), waits, inc)

    def ACTV(out, in_, func, bias=None, scale=None, waits=(), inc=None,
             chain=True):
        kw = {}
        if bias is not None:
            kw["bias"] = bias
        if scale is not None:
            kw["scale"] = scale
        return p.emit("scalar", (lambda e, o=out, i=in_, f=func, kw=kw:
                                 e.activation(o, i, f, **kw)), waits, inc,
                      chain=chain)

    def RECIP(out, in_, waits=(), inc=None):
        return p.emit("vector", (lambda e, o=out, i=in_: e.reciprocal(o, i)), waits, inc)

    def RED(out, in_, op, waits=(), inc=None, negate=False):
        return p.emit("vector", (lambda e, o=out, i=in_, f=op, n=negate:
                                 e.tensor_reduce(o, i, AX.X, f, negate=n)), waits, inc)

    def MS(ap, val, waits=(), inc=None):
        return p.emit("gpsimd", (lambda e, a=ap, v=val: e.memset(a, v)), waits, inc)

    def SCAN(out, d0, d1, init, waits=(), inc=None):
        return p.emit("vector", (lambda e, o=out, a=d0, b=d1, i=init:
                                 e.tensor_tensor_scan(o, a, b, i, ALU.add, ALU.add)),
                      waits, inc)

    p.stage("init")
    MS(ones128_bf[:, :], 1.0)
    MS(ones1x128[:, :], 1.0)
    MS(ones1x128b[:, :], 1.0)
    MS(consts[:, 0:1], 1.0 / 1024.0)
    MS(consts[:, 1:2], EPS_LN)
    MS(consts[:, 2:3], 1e-6)
    MS(consts[:, 3:4], float(np.pi / 2))
    p.dma("gpsimd", sel_sb[:, :], dsel, inc="g_tab")
    g_id = p.dma("gpsimd", id_sb[:, :], dident, inc="g_tab")
    p.dma("sync", bq_sb[:, :], dbq, inc="s_tab")
    p.dma("sync", bk_sb[:, :], dbk, inc="s_tab")
    tab_s = p.dma("sync", bo_sb[:, :], dbo, inc="s_tab")

    p.stage("ln0")
    ar.reset()
    _lnbuf_n = [0]

    def mk_ln_bufs(dbl=True):
        i = _lnbuf_n[0]
        _lnbuf_n[0] += 2
        lnc = [ar.alloc([128, NDT * 512], BF16, f"lnc{i}")]
        lnc.append(ar.alloc([128, NDT * 512], BF16, f"lnd{i}") if dbl else lnc[0])
        return dict(
            xcf=[ar.alloc([128, NDT * 512], F32, f"xcf{i+j}") for j in range(2)],
            xcb=[ar.alloc([128, NDT * 512], BF16, f"xcb{i+j}") for j in range(2)],
            sqb=ar.alloc([128, NDT * 512], BF16, f"sqb{i}"),
            lnc=lnc, dbl=dbl,
            tmpf=ar.alloc([128, 512], F32, f"lntmp{i}"),
            rows=ar.alloc([1, 2048], F32, f"rows{i}"),
            rows2=ar.alloc([1, 2048], F32, f"rows2_{i}"))

    def emit_ln(src_dram, n_tok, dst_dram, bufs, gten=None, bten=None,
                postscale=None, src_ready=(), src_ready_fn=None, prev_st=None):
        xcf, xcb = bufs["xcf"], bufs["xcb"]
        sqb, tmpf = bufs["sqb"], bufs["tmpf"]
        lncs, dbl = bufs["lnc"], bufs["dbl"]
        rows, rows2 = bufs["rows"], bufs["rows2"]
        """Feature-major LN over d=1024, pipelined over 512-token chunks."""
        nch = n_tok // 512
        st_chunks = []
        st_par = dict(prev_st) if prev_st else {0: None, 1: None}
        base = _lnbuf_n[0] - 2
        rel_xcb = [None, None]
        rel_xcf = [None, None]
        rel_stats = [None, None]
        rel_sq = None
        rel_bcast = None
        for ch in range(nch):
            bi = ch % 2
            mean_bk = psE[0:1, 0:512] if bi == 0 else psD[0:1, 0:512]
            sq_bk = psC[0:1, 0:512] if bi == 0 else psB[0:1, 0:512]
            rset = rows if bi == 0 else rows2
            sl = slice(ch * 512, (ch + 1) * 512)
            src3 = src_dram.rearrange("(dt p) t -> p dt t", p=128)[:, :, sl]
            src_ready = src_ready_fn(ch) if src_ready_fn is not None else src_ready
            w = list(src_ready)
            if rel_xcf[bi]:
                w.append(("v", rel_xcf[bi]))
            xcf3 = xcf[bi].rearrange("p (dt t) -> p dt t", dt=NDT)
            ldf = p.dma("sync", xcf3[:, 0:4, :], src3[:, 0:4, :], waits=w,
                        inc=f"s_xcf{base + bi}")
            ldf2 = p.dma("scalar", xcf3[:, 4:8, :], src3[:, 4:8, :], waits=w,
                         inc=f"a_xcf{base + bi}")
            w = list(src_ready)
            if rel_xcb[bi]:
                w.append(("t", rel_xcb[bi]))
            ldb = p.dma("gpsimd", xcb[bi].rearrange("p (dt t) -> p dt t", dt=NDT),
                        src3, waits=w, inc=f"g_xcb{base + bi}")
            aq = None
            for dt in range(NDT):
                wq = []
                if dt == 0:
                    wq = [(f"g_xcb{base + bi}", ldb)]
                    if rel_sq:
                        wq.append(("t", rel_sq))
                aq = ACTV(sqb[:, dt * 512:(dt + 1) * 512],
                          xcb[bi][:, dt * 512:(dt + 1) * 512], AF.Square,
                          waits=wq, inc="a")
            for dt in range(NDT):
                wm = []
                if dt == 0:
                    wm = [(f"g_xcb{base + bi}", ldb)]
                    if rel_stats[bi]:
                        wm.append(("v", rel_stats[bi]))
                MM(mean_bk, ones128_bf[:, :], xcb[bi][:, dt * 512:(dt + 1) * 512],
                   start=(dt == 0), stop=(dt == NDT - 1), waits=wm)
            tmm = None
            for dt in range(NDT):
                wm = [("a", aq)] if dt == 0 else []
                tmm = MM(sq_bk, ones128_bf[:, :], sqb[:, dt * 512:(dt + 1) * 512],
                         start=(dt == 0), stop=(dt == NDT - 1), waits=wm,
                         inc=("t" if dt == NDT - 1 else None))
            rel_sq = tmm
            rel_xcb[bi] = tmm
            r_mean, r_srow, r_m2, r_var = (rset[0:1, 0:512], rset[0:1, 512:1024],
                                           rset[0:1, 1024:1536], rset[0:1, 1536:2048])
            TS(r_mean, mean_bk, consts[0:1, 0:1], None, ALU.mult, waits=[("t", tmm)])
            TS(r_srow, sq_bk, consts[0:1, 0:1], None, ALU.mult)
            TT(r_m2, r_mean, r_mean, ALU.mult)
            vv = p.emit("vector", (lambda e, a=r_var, b=r_srow, c=r_m2:
                                   e.tensor_sub(a, b, c)), inc="v")
            rel_stats[bi] = vv
            ACTV(r_m2, r_var, AF.Ln, bias=consts[0:1, 1:2], waits=[("v", vv)])
            av = ACTV(r_var, r_m2, AF.Exp, scale=-0.5, inc="a")
            vm = TT(r_m2, r_mean, r_var, ALU.mult, waits=[("a", av)], inc="v")
            wb = [("v", vm)]
            if rel_bcast:
                wb.append(("v", rel_bcast))
            MM(psA[:, 0:512], ones1x128[:, :], r_var, start=True, stop=True, waits=wb)
            tb = MM(psA[:, 512:1024], ones1x128[:, :], r_m2, start=True, stop=True,
                    inc="t")
            lnc = lncs[ch % 2]
            vo = None
            for dt in range(NDT):
                wv_ = [("t", tb)]
                if dt == 0:
                    wv_.append((f"s_xcf{base + bi}", ldf))
                    pw = st_par[ch % 2] if dbl else st_par[(ch - 1) % 2]
                    if pw is not None:
                        wv_.append(pw)
                if dt == 4:
                    wv_.append((f"a_xcf{base + bi}", ldf2))
                TT(tmpf[:, :], xcf[bi][:, dt * 512:(dt + 1) * 512], psA[:, 0:512],
                   ALU.mult, waits=wv_)
                if gten is None and postscale is None:
                    vo = TT(lnc[:, dt * 512:(dt + 1) * 512], tmpf[:, :],
                            psA[:, 512:1024], ALU.subtract, inc="v")
                else:
                    TT(tmpf[:, :], tmpf[:, :], psA[:, 512:1024], ALU.subtract)
                    if gten is not None and postscale is None:
                        vo = TS(lnc[:, dt * 512:(dt + 1) * 512], tmpf[:, :],
                                gten[:, dt:dt + 1], bten[:, dt:dt + 1],
                                ALU.mult, ALU.add, inc="v")
                    else:
                        TS(tmpf[:, :], tmpf[:, :], gten[:, dt:dt + 1],
                           bten[:, dt:dt + 1], ALU.mult, ALU.add)
                        vo = TS(lnc[:, dt * 512:(dt + 1) * 512], tmpf[:, :],
                                postscale, None, ALU.mult, inc="v")
            rel_bcast = vo
            rel_xcf[bi] = vo
            stn = f"ls{ch % 2}"
            stc = p.dma("sync",
                        dst_dram.rearrange("(dt p) t -> p dt t", p=128)[:, :, sl],
                        lnc.rearrange("p (dt t) -> p dt t", dt=NDT),
                        waits=[("v", vo)], inc=stn)
            st_par[ch % 2] = (stn, stc)
            st_chunks.append((stn, stc))
        return st_chunks

    b0 = mk_ln_bufs()
    ln0_ch = emit_ln(dx, T, dln0, b0)
    ln0_done = ln0_ch[-1]
    LN0S = "s_lnst"

    NKT = T // 128
    qc_n = TL // 512

    p.stage("l0v")
    ar.reset()
    v_sb = ar.alloc([128, NKT * 1040], BF16, "v_sb")
    cs_sb = ar.alloc([128, T], BF16, "cs_sb")
    ss_sb = ar.alloc([128, T], BF16, "ss_sb")
    wo_sb = ar.alloc([128, NDT * 1024], BF16, "wo_sb")
    ao_sb = ar.alloc([128, 8 * TL], BF16, "ao_sb")
    r2_sb = ar.alloc([128, 1024], BF16, "r2_sb")
    perm_sb = ar.alloc([128, 128], BF16, "perm_sb")
    ar.mark("attn")
    wv_sb = ar.alloc([128, NDT * 1040], BF16, "wv_sb")
    lnk = [ar.alloc([128, NDT * 128], BF16, f"lnk{i}") for i in range(4)]
    bvr_sb = ar.alloc([1, H * 65], BF16, "bvr_sb")

    pl_r2 = MS(r2_sb[:, :], 0.0, inc="pl")
    g_wv = p.dma("gpsimd", wv_sb.rearrange("p (dt c) -> p dt c", dt=NDT),
                 dwv65.rearrange("(dt p) c -> p dt c", p=128), inc="g_wv")
    g_bv = p.dma("gpsimd", bvr_sb[:, :], dbv, inc="g_tab")
    g_pm = p.dma("gpsimd", perm_sb[:, :], dpermm, inc="g_tab")
    g_cs = p.dma("gpsimd", cs_sb[:, :], dcs, inc="g_cs")
    a_ss = p.dma("gpsimd", ss_sb[:, :], dss, inc="g_ss")
    a_wo = p.dma("gpsimd", wo_sb.rearrange("p (kt d) -> p kt d", kt=NDT),
                 dwoT.rearrange("(kt p) d -> p kt d", p=128), inc="g_wo")

    ve = None
    rel_lnk = [None] * 4
    cp_hist = [None, None]
    for kt in range(NKT):
        bi = kt % 4
        pb = psA if kt % 2 == 0 else psB
        pc = psC if kt % 2 == 0 else psD
        qd = "sync" if bi % 2 == 0 else "gpsimd"
        w = [ln0_ch[kt // 4]]
        if rel_lnk[bi]:
            w.append(("t", rel_lnk[bi]))
        lk = p.dma(qd, lnk[bi].rearrange("p (dt t) -> p dt t", dt=NDT),
                   dln0.rearrange("(dt p) t -> p dt t", p=128)[:, :, kt * 128:(kt + 1) * 128],
                   waits=w, inc=f"lk{bi}")
        for dt in range(NDT):
            w0 = []
            if dt == 0:
                w0 = [(f"lk{bi}", lk), ("g_wv", g_wv), ("g_tab", g_bv)]
                if kt < 2:
                    w0.append(ln0_done)
                if cp_hist[kt % 2]:
                    w0.append(("v", cp_hist[kt % 2]))
            MM(pb[:, 0:512], lnk[bi][:, dt * 128:(dt + 1) * 128],
               wv_sb[:, dt * 1040:dt * 1040 + 512], start=(dt == 0), stop=False,
               waits=w0)
            MM(pb[:, 512:1024], lnk[bi][:, dt * 128:(dt + 1) * 128],
               wv_sb[:, dt * 1040 + 512:dt * 1040 + 1024], start=(dt == 0), stop=False)
            MM(pc[:, 0:16], lnk[bi][:, dt * 128:(dt + 1) * 128],
               wv_sb[:, dt * 1040 + 1024:dt * 1040 + 1040], start=(dt == 0), stop=False)
        MM(pb[:, 0:512], ones1x128b[:, :], bvr_sb[:, 0:512], start=False, stop=True)
        MM(pb[:, 512:1024], ones1x128b[:, :], bvr_sb[:, 512:1024], start=False, stop=True)
        tv = MM(pc[:, 0:16], ones1x128b[:, :], bvr_sb[:, 1024:1040],
                start=False, stop=True, inc="t")
        rel_lnk[bi] = tv
        off = kt * 1040
        CP("vector", v_sb[:, off:off + 512], pb[:, 0:512], waits=[("t", tv)])
        CP("vector", v_sb[:, off + 512:off + 1024], pb[:, 512:1024])
        ve = CP("vector", v_sb[:, off + 1024:off + 1040], pc[:, 0:16], inc="v")
        cp_hist[kt % 2] = ve
    pl_ones = MS(v_sb.rearrange("p (kt h e) -> p kt h e", kt=NKT, h=H)[:, :, :, 64:65],
                 1.0, waits=[("v", ve)], inc="pl")
    b1_t = cnt["t"]

    p.stage("l0attn")
    ar.reset("attn")
    wq_sb = [ar.alloc([128, NDT * 128], BF16, f"wq_sb{i}") for i in range(2)]
    wk_sb = [ar.alloc([128, NDT * 128], BF16, f"wk_sb{i}") for i in range(2)]
    lnqk = [ar.alloc([128, NDT * 512], BF16, f"lnqk{i}") for i in range(4)]
    q_sb = ar.alloc([128, TL], BF16, "q_sb")
    k_sb = ar.alloc([128, T], BF16, "k_sb")
    qr_sb = q_sb
    kr_sb = k_sb
    rtmp = ar.alloc([128, 512], BF16, "rtmp")
    p2_sb = [ar.alloc([128, 2048], BF16, f"p2_sb{i}") for i in range(2)]
    rb_sb = ar.alloc([128, 512], F32, "rb_sb")
    xr_sb = [ar.overlay("wq_sb0", 0, [128, 512], F32, "xr_sb0"),
             ar.overlay("wq_sb1", 0, [128, 512], F32, "xr_sb1")]
    o_sb = [ar.overlay("wk_sb0", 0, [128, 512], F32, "o_sb0"),
            ar.overlay("wk_sb1", 0, [128, 512], F32, "o_sb1")]

    v_ready = [("pl", pl_ones)]
    rel_wq = [None, None]
    rel_wk = [None, None]
    rel_lnqk = [None, None, None, None]
    rtts = []
    nrm_q = []
    for hp in range(8):
        bi = hp % 2
        w = [("t", rel_wq[bi] if rel_wq[bi] else b1_t)]
        g_wq = p.dma("gpsimd", wq_sb[bi].rearrange("p (dt m) -> p dt m", dt=NDT),
                     dwqT.rearrange("(dt p) m -> p dt m", p=128)[:, :, hp * 128:(hp + 1) * 128],
                     waits=w, inc=f"g_wq{bi}")
        w = [("t", rel_wk[bi] if rel_wk[bi] else b1_t)]
        g_wk = p.dma("gpsimd", wk_sb[bi].rearrange("p (dt m) -> p dt m", dt=NDT),
                     dwkT.rearrange("(dt p) m -> p dt m", p=128)[:, :, hp * 128:(hp + 1) * 128],
                     waits=w, inc=f"g_wk{bi}")
        vq_ts = [None] * qc_n
        vk_ts = [None] * (T // 512)
        pbanks = [psC[:, 0:512], psD[:, 0:512], psA[:, 0:512], psB[:, 0:512]]
        hv = cnt["v"]
        ts_hist = [hv, hv, hv, hv]
        gi = 0
        for tc in range(T // 512):
            cbi = tc % 4
            qd = "sync" if cbi % 2 == 0 else "gpsimd"
            wl = [ln0_ch[tc],
                  ("t", rel_lnqk[cbi] if rel_lnqk[cbi] else b1_t)]
            s_ln = p.dma(qd, lnqk[cbi].rearrange("p (dt t) -> p dt t", dt=NDT),
                         dln0.rearrange("(dt p) t -> p dt t", p=128)[:, :, tc * 512:(tc + 1) * 512],
                         waits=wl, inc=f"lq{cbi}")
            if tc < qc_n:
                bk = gi % 4
                for dt in range(NDT):
                    w0 = []
                    if dt == 0:
                        w0 = [(f"g_wq{bi}", g_wq), (f"lq{cbi}", s_ln),
                              ("v", ts_hist[bk])]
                    MM(pbanks[bk], wq_sb[bi][:, dt * 128:(dt + 1) * 128],
                       lnqk[cbi][:, dt * 512:(dt + 1) * 512],
                       start=(dt == 0), stop=(dt == NDT - 1), waits=w0,
                       inc=("t" if dt == NDT - 1 else None))
                vq_ts[tc] = TS(q_sb[:, tc * 512:(tc + 1) * 512], pbanks[bk],
                               bq_sb[:, hp:hp + 1], None, ALU.add,
                               waits=[("t", cnt["t"]), ("s_tab", tab_s)], inc="v")
                ts_hist[bk] = vq_ts[tc]
                gi += 1
            bk = gi % 4
            for dt in range(NDT):
                w0 = []
                if dt == 0:
                    w0 = [(f"g_wk{bi}", g_wk), (f"lq{cbi}", s_ln),
                          ("v", ts_hist[bk])]
                MM(pbanks[bk], wk_sb[bi][:, dt * 128:(dt + 1) * 128],
                   lnqk[cbi][:, dt * 512:(dt + 1) * 512],
                   start=(dt == 0), stop=(dt == NDT - 1), waits=w0,
                   inc=("t" if dt == NDT - 1 else None))
            vk_ts[tc] = TS(k_sb[:, tc * 512:(tc + 1) * 512], pbanks[bk],
                           bk_sb[:, hp:hp + 1], None, ALU.add,
                           waits=[("t", cnt["t"]), ("s_tab", tab_s)], inc="v")
            ts_hist[bk] = vk_ts[tc]
            gi += 1
            rel_lnqk[cbi] = cnt["t"]
            if tc == qc_n - 1:
                rel_wq[bi] = cnt["t"]
            if tc == T // 512 - 1:
                rel_wk[bi] = cnt["t"]
        rope_k = [None] * (T // 512)
        pi = 0
        for (srcb, dstb, nch, vsrc) in ((q_sb, qr_sb, qc_n, vq_ts),
                                        (k_sb, kr_sb, T // 512, vk_ts)):
            for tc in range(nch):
                sl = slice(tc * 512, (tc + 1) * 512)
                pbank = psC if pi % 2 == 0 else psD
                w0 = [("v", vk_ts[-1] if pi < 2 else rtts[-2]), ("g_tab", g_pm)]
                tpm = MM(pbank[:, 0:512], perm_sb[:, :], srcb[:, sl],
                         start=True, stop=True, waits=w0, inc="t")
                t1 = TT(rtmp[:, :], pbank[:, 0:512], ss_sb[:, sl], ALU.mult,
                        waits=[("t", tpm), ("g_ss", a_ss), ("g_cs", g_cs)])
                rtts.append(t1)
                TT(dstb[:, sl], srcb[:, sl], cs_sb[:, sl], ALU.mult)
                vr = TT(dstb[:, sl], dstb[:, sl], rtmp[:, :], ALU.add, inc="v")
                if srcb is k_sb:
                    rope_k[tc] = vr
                pi += 1

        evac_prev = rope_k[T // 512 - 1]
        for qc in range(qc_n):
            a_hist = []
            tpv = None

            def emit_S(kt):
                sbuf = psA if kt % 2 == 0 else psB
                w0 = [("v", rope_k[kt // 4])] + v_ready
                if kt >= 2:
                    w0.append(("a", a_hist[kt - 2]))
                MM(sbuf[:, 0:512], kr_sb[0:64, kt * 128:(kt + 1) * 128],
                   qr_sb[0:64, qc * 512:(qc + 1) * 512], start=True, stop=True,
                   waits=w0, tp=(0, 0))
                ts = MM(sbuf[:, 512:1024], kr_sb[64:128, kt * 128:(kt + 1) * 128],
                        qr_sb[64:128, qc * 512:(qc + 1) * 512], start=True, stop=True,
                        tp=(64, 0), inc="t")
                ae = ACTV(p2_sb[kt % 2][:, 0:1024], sbuf[:, :], AF.Exp, scale=0.125,
                          waits=[("t", ts)], inc="a")
                a_hist.append(ae)

            def emit_PV(kt):
                nonlocal tpv
                pb = p2_sb[kt % 2]
                vofs = kt * 1040 + 130 * hp
                wpv = [("a", a_hist[kt])]
                if kt == 0:
                    wpv.append(("v", evac_prev))
                MM(psC[0:65, 0:512], v_sb[:, vofs:vofs + 65], pb[:, 0:512],
                   start=(kt == 0), stop=(kt == NKT - 1), waits=wpv)
                tpv = MM(psD[0:65, 0:512], v_sb[:, vofs + 65:vofs + 130],
                         pb[:, 512:1024], start=(kt == 0), stop=(kt == NKT - 1),
                         inc=("t" if kt == NKT - 1 else None))

            def emit_norms():
                while nrm_q:
                    aofs_n, sl2_n, ev_n = nrm_q.pop(0)
                    tbN = MM(psE[:, 0:512], sel_sb[:, :], r2_sb[:, sl2_n],
                             start=True, stop=True,
                             waits=[("v", ev_n), ("pl", pl_r2), ("g_tab", g_bv)],
                             inc="t")
                    RECIP(rb_sb[:, :], psE[:, 0:512], waits=[("t", tbN)])
                    TT(ao_sb[0:64, aofs_n:aofs_n + 512],
                       ao_sb[0:64, aofs_n:aofs_n + 512], rb_sb[0:64, :], ALU.mult)
                    TT(ao_sb[64:128, aofs_n:aofs_n + 512],
                       ao_sb[64:128, aofs_n:aofs_n + 512], rb_sb[64:128, :],
                       ALU.mult, inc="v")

            emit_S(0)
            for kt in range(1, NKT):
                emit_S(kt)
                emit_PV(kt - 1)
                if kt == 4:
                    emit_norms()
            emit_PV(NKT - 1)
            sl2 = slice((qc % 2) * 512, (qc % 2) * 512 + 512)
            aofs = hp * TL + qc * 512
            CP("vector", r2_sb[0:1, sl2], psC[64:65, 0:512], waits=[("t", tpv)])
            CP("vector", r2_sb[64:65, sl2], psD[64:65, 0:512])
            CP("vector", ao_sb[0:64, aofs:aofs + 512], psC[0:64, 0:512])
            evac_prev = CP("vector", ao_sb[64:128, aofs:aofs + 512],
                           psD[0:64, 0:512], inc="v")
            nrm_q.append((aofs, sl2, evac_prev))

    while nrm_q:
        aofs_n, sl2_n, ev_n = nrm_q.pop(0)
        tbN = MM(psE[:, 0:512], sel_sb[:, :], r2_sb[:, sl2_n], start=True,
                 stop=True, waits=[("v", ev_n), ("pl", pl_r2), ("g_tab", g_bv)],
                 inc="t")
        RECIP(rb_sb[:, :], psE[:, 0:512], waits=[("t", tbN)])
        TT(ao_sb[0:64, aofs_n:aofs_n + 512], ao_sb[0:64, aofs_n:aofs_n + 512],
           rb_sb[0:64, :], ALU.mult)
        TT(ao_sb[64:128, aofs_n:aofs_n + 512],
           ao_sb[64:128, aofs_n:aofs_n + 512], rb_sb[64:128, :],
           ALU.mult, inc="v")

    ao_done = cnt["v"]
    stt_h = [None, None]
    st_h = [None, None]
    x1_tc = {}
    b4_rel = [ao_done] * 4
    for i, (dc, tc) in enumerate([(d, t) for d in range(NDT)
                                  for t in range(qc_n)]):
        b = i % 2
        pbk4 = [psC[:, 0:512], psD[:, 0:512], psA[:, 0:512], psB[:, 0:512]][i % 4]
        sx = p.dma("scalar", xr_sb[b][:, :],
                   dx[dc * 128:(dc + 1) * 128, tc * 512:(tc + 1) * 512],
                   waits=[("v", stt_h[b] if stt_h[b] else ao_done)],
                   inc=f"a_xr{b}")
        for kt in range(NDT):
            w0 = []
            if kt == 0:
                w0 = [("v", b4_rel[i % 4]), ("g_wo", a_wo)]
            MM(pbk4,
               wo_sb[:, kt * 1024 + dc * 128: kt * 1024 + (dc + 1) * 128],
               ao_sb[:, kt * TL + tc * 512: kt * TL + (tc + 1) * 512],
               start=(kt == 0), stop=(kt == NDT - 1), waits=w0,
               inc=("t" if kt == NDT - 1 else None))
        vo = STT(o_sb[b][:, :], pbk4, bo_sb[:, dc:dc + 1], xr_sb[b][:, :],
                 ALU.add, ALU.add,
                 waits=[("t", cnt["t"]), (f"a_xr{b}", sx), ("s_tab", tab_s)] +
                       ([(f"x1s{b}", st_h[b])] if st_h[b] else []),
                 inc="v")
        b4_rel[i % 4] = vo
        stt_h[b] = vo
        st_h[b] = p.dma("sync", dx1[dc * 128:(dc + 1) * 128, tc * 512:(tc + 1) * 512],
                        o_sb[b][:, :], waits=[("v", vo)], inc=f"x1s{b}")
        if dc == NDT - 1:
            x1_tc[tc] = [("x1s0", cnt["x1s0"]), ("x1s1", cnt["x1s1"])]

    p.stage("ffn0")
    ar.reset()
    bC = mk_ln_bufs()
    ln1_ch = emit_ln(dx1, TL, dln1, bC,
                     src_ready_fn=lambda ch: x1_tc[ch])
    ln1_done = ln1_ch[-1]
    w1_sb = ar.alloc([128, NDT * 1024], BF16, "w1_sb")
    dsh_sb = ar.alloc([64, 1024], BF16, "dsh_sb")
    ps_sb = ar.alloc([128, 64], BF16, "ps_sb")
    lnc1 = bC["xcb"] + [ar.alloc([128, NDT * 512], BF16, f"lnc1x{i}")
                        for i in range(2)]
    pool_sb = ar.alloc([64, TL], BF16, "pool_sb")
    xc1_sb = [ar.alloc([128, 512], F32, f"xc1_sb{i}") for i in range(2)]
    oc1_sb = [ar.alloc([128, 512], F32, f"oc1_sb{i}") for i in range(2)]

    g_w1 = p.dma("gpsimd", w1_sb.rearrange("p (dt m) -> p dt m", dt=NDT),
                 dw1t.rearrange("(dt p) m -> p dt m", p=128), inc="g_tab")
    g_dsh = p.dma("gpsimd", dsh_sb[:, :], ddisha, inc="g_tab")
    g_ps = p.dma("gpsimd", ps_sb[:, :], dpsel, inc="g_tab")

    lc1_ld = [None] * qc_n
    for tc in range(qc_n):
        qd = "sync" if tc % 2 == 0 else "gpsimd"
        lc1_ld[tc] = p.dma(qd, lnc1[tc].rearrange("p (dt t) -> p dt t", dt=NDT),
                           dln1.rearrange("(dt p) t -> p dt t", p=128)[:, :, tc * 512:(tc + 1) * 512],
                           waits=[ln1_ch[tc + 2] if tc < 2 else ln1_ch[tc]],
                           inc=f"lc1{tc}")
        for dt in range(NDT):
            w0 = []
            if dt == 0:
                w0 = [(f"lc1{tc}", lc1_ld[tc]), ("g_tab", g_ps), ("v", cnt["v"])]
            MM(psE[0:64, 0:512], ps_sb[:, :], lnc1[tc][:, dt * 512:(dt + 1) * 512],
               start=(dt == 0), stop=(dt == NDT - 1), waits=w0,
               inc=("t" if dt == NDT - 1 else None))
        CP("vector", pool_sb[:, tc * 512:(tc + 1) * 512], psE[0:64, 0:512],
           waits=[("t", cnt["t"])])
    ctt_h = [None, None]
    cst_h = [None, None]
    x2_tc = {}
    c2_banks = [psC[:, 0:512], psD[:, 0:512], psA[:, 0:512], psB[:, 0:512]]
    c2_rel = [cnt["v"]] * 4
    for i, (tc, oc) in enumerate([(t, o) for t in range(qc_n)
                                  for o in range(NDT)]):
        b = i % 2
        pbk = c2_banks[i % 4]
        wx = (x1_tc[3] if i < 2 else [("v", ctt_h[b])])
        sx = p.dma("scalar", xc1_sb[b][:, :],
                   dx1[oc * 128:(oc + 1) * 128, tc * 512:(tc + 1) * 512],
                   waits=wx, inc=f"a_xr{b}")
        for dt in range(NDT):
            w0 = []
            if dt == 0:
                w0 = [(f"lc1{tc}", lc1_ld[tc]), ("g_tab", g_w1),
                      ("v", c2_rel[i % 4])]
            MM(pbk, w1_sb[:, dt * 1024 + oc * 128:dt * 1024 + (oc + 1) * 128],
               lnc1[tc][:, dt * 512:(dt + 1) * 512],
               start=(dt == 0), stop=False, waits=w0)
        MM(pbk, dsh_sb[0:64, oc * 128:(oc + 1) * 128],
           pool_sb[0:64, tc * 512:(tc + 1) * 512],
           start=False, stop=True, inc="t")
        vo = TT(oc1_sb[b][:, :], pbk, xc1_sb[b][:, :], ALU.add,
                waits=[("t", cnt["t"]), (f"a_xr{b}", sx)] +
                      ([(f"x2s{b}", cst_h[b])] if cst_h[b] else []),
                inc="v")
        c2_rel[i % 4] = vo
        ctt_h[b] = vo
        cst_h[b] = p.dma("sync", dx2[oc * 128:(oc + 1) * 128, tc * 512:(tc + 1) * 512],
                         oc1_sb[b][:, :], waits=[("v", vo)], inc=f"x2s{b}")
        if oc == NDT - 1:
            x2_tc[tc] = [("x2s0", cnt["x2s0"]), ("x2s1", cnt["x2s1"])]

    p.stage("l1attn")
    ar.reset()
    bD = mk_ln_bufs(dbl=False)
    ln2_ch = emit_ln(dx2, TL, dln2, bD,
                     src_ready_fn=lambda ch: x2_tc[ch])
    ln2_done = ln2_ch[-1]
    NHE = NDT
    wqk1_sb = ar.alloc([128, NDT * 64], BF16, "wqk1_sb")
    wv1_sb = ar.alloc([128, NDT * 1024], BF16, "wv1_sb")
    wo1_sb = ar.overlay("wv1_sb", 0, [128, NDT * 1024], BF16, "wo1_sb")
    sel16_sb = ar.alloc([16, 8 * 128], BF16, "sel16_sb")
    mask_sb = ar.alloc([128, 1], F32, "mask_sb")
    lnc2 = bD["xcb"] + [ar.alloc([128, NDT * 512], BF16, f"lnc2x{i}")
                        for i in range(2)]
    trig_tok = bD["sqb"]
    tqc = ar.alloc([16, TL], BF16, "tqc")
    tkc = ar.alloc([16, TL], BF16, "tkc")
    tqs = ar.alloc([16, TL], BF16, "tqs")
    tks = ar.alloc([16, TL], BF16, "tks")
    v1t_sb = ar.alloc([128, 512], F32, "v1t_sb")
    d4t_sb = ar.alloc([128, 512], F32, "d4t_sb")
    xc_sb = ar.alloc([128, NHE * TL], BF16, "xc_sb")
    xs_sb = ar.alloc([128, NHE * TL], BF16, "xs_sb")
    zer_sb = bD["lnc"][0]
    pub_sb = ar.alloc([128, 24], F32, "pub_sb")
    car_sb = ar.alloc([128, 24], F32, "car_sb")
    xdead = "xcf" + str(_lnbuf_n[0] - 2)
    kcc = ar.overlay(xdead, 0, [16, TL], BF16, "kcc")
    kcs = ar.overlay(xdead, TL * 2, [16, TL], BF16, "kcs")
    dent = ar.overlay(xdead, TL * 4, [16, TL], F32, "dent")
    dent2 = ar.overlay(xdead, TL * 12, [16, TL], BF16, "dent2")
    xdead2 = "xcf" + str(_lnbuf_n[0] - 1)
    a16 = ar.overlay(xdead2, 0, [16, TL], BF16, "a16")
    b16 = ar.overlay(xdead2, TL * 2, [16, TL], BF16, "b16")
    xr2_sb = [ar.overlay(xdead2, TL * 6, [128, 512], F32, "xr2_sb0"),
              ar.overlay(xdead2, TL * 7, [128, 512], F32, "xr2_sb1")]
    sqbD = "sqb" + str(_lnbuf_n[0] - 2)
    oc2_sb = [ar.overlay(sqbD, 0, [128, 512], F32, "oc2_sb0"),
              ar.overlay(sqbD, 2048, [128, 512], F32, "oc2_sb1")]

    g_wqk1 = p.dma("gpsimd", wqk1_sb.rearrange("p (dt m) -> p dt m", dt=NDT),
                   dwqk1.rearrange("(dt p) m -> p dt m", p=128), inc="g_tab")
    g_wv1 = p.dma("gpsimd", wv1_sb.rearrange("p (dt m) -> p dt m", dt=NDT),
                  dwv1.rearrange("(dt p) m -> p dt m", p=128), inc="g_tab")
    g_s16 = p.dma("gpsimd", sel16_sb[:, :], dsel16, inc="g_tab")
    g_msk = p.dma("gpsimd", mask_sb[:, :], dmask, inc="g_tab")
    MS(zer_sb[:, 0:TL], 0.0, waits=[ln2_done], inc="pl")
    zer_done = cnt["pl"]
    MS(pub_sb[:, :], 0.0, inc="pl")
    pub0_done = cnt["pl"]

    lc2_ld = [None] * qc_n
    for tc in range(qc_n):
        qd = "sync" if tc % 2 == 0 else "scalar"
        lc2_ld[tc] = p.dma(qd, lnc2[tc].rearrange("p (dt t) -> p dt t", dt=NDT),
                           dln2.rearrange("(dt p) t -> p dt t", p=128)[:, :, tc * 512:(tc + 1) * 512],
                           waits=[ln2_ch[tc + 2] if tc < 2 else ln2_ch[tc]],
                           inc=f"lc2{tc}")
    for tch in range(16):
        tc4 = tch // 4
        co = (tch % 4) * 128
        for dt in range(NDT):
            w0 = []
            if dt == 0:
                w0 = [(f"lc2{tc4}", lc2_ld[tc4]), ("g_tab", g_wqk1), ("a", cnt["a"])]
            MM(psE[:, 0:64], lnc2[tc4][:, dt * 512 + co:dt * 512 + co + 128],
               wqk1_sb[:, dt * 64:(dt + 1) * 64],
               start=(dt == 0), stop=(dt == NDT - 1), waits=w0,
               inc=("t" if dt == NDT - 1 else None))
        ACTV(trig_tok[:, tch * 128:tch * 128 + 48], psE[:, 0:48], AF.Tanh,
             waits=[("t", cnt["t"])])
        ACTV(trig_tok[:, tch * 128:tch * 128 + 48],
             trig_tok[:, tch * 128:tch * 128 + 48], AF.Sin,
             scale=float(np.pi / 4))
        ACTV(trig_tok[:, tch * 128 + 64:tch * 128 + 112], psE[:, 0:48], AF.Tanh)
        ACTV(trig_tok[:, tch * 128 + 64:tch * 128 + 112],
             trig_tok[:, tch * 128 + 64:tch * 128 + 112], AF.Sin,
             scale=float(np.pi / 4), bias=consts[:, 3:4], inc="a")
    av = cnt["a"]
    for tch in range(16):
        w0 = [("v", cnt["v"])] if tch else []
        if tch == 0:
            w0 = [("a", av), ("g_tab", g_id)]
        p.emit("tensor",
               (lambda e, o=psT[0:48, 0:128],
                i=trig_tok[:, tch * 128 + 64:tch * 128 + 112],
                idn=id_sb[:, :]: e.transpose(o, i, idn)), waits=w0)
        p.emit("tensor",
               (lambda e, o=psT[0:48, 128:256],
                i=trig_tok[:, tch * 128:tch * 128 + 48],
                idn=id_sb[:, :]: e.transpose(o, i, idn)), inc="t")
        tsl = slice(tch * 128, (tch + 1) * 128)
        CP("vector", tqc[:, tsl], psT[0:16, 0:128], waits=[("t", cnt["t"])])
        CP("vector", tkc[:, tsl], psT[32:48, 0:128])
        CP("vector", tqs[:, tsl], psT[0:16, 128:256])
        CP("vector", tks[:, tsl], psT[32:48, 128:256], inc="v")
    trig_done = cnt["v"]

    d2_vb = [psC[:, 0:512], psA[:, 0:512]]
    d2_bc = [(psE[:, 0:512], psD[:, 0:512]),
             (psB[:, 0:512], psB[:, 512:1024])]
    d2_rv = [cnt["v"]] * 2
    d2_rb = [cnt["v"]] * 2
    for kt in range(NHE):
        for tc in range(qc_n):
            j = (kt * qc_n + tc) % 2
            vb = d2_vb[j]
            bcA, bcB = d2_bc[j]
            for dt in range(NDT):
                w0 = []
                if dt == 0:
                    w0 = [(f"lc2{tc}", lc2_ld[tc]), ("g_tab", g_wv1),
                          ("v", d2_rv[j])]
                MM(vb, wv1_sb[:, dt * 1024 + kt * 128:dt * 1024 + (kt + 1) * 128],
                   lnc2[tc][:, dt * 512:(dt + 1) * 512],
                   start=(dt == 0), stop=(dt == NDT - 1), waits=w0,
                   inc=("t" if dt == NDT - 1 else None))
            MM(bcA, sel16_sb[:, kt * 128:(kt + 1) * 128],
               tkc[:, tc * 512:(tc + 1) * 512], start=True, stop=True,
               waits=[("v", trig_done), ("v", d2_rb[j]), ("g_tab", g_s16)])
            tbb = MM(bcB, sel16_sb[:, kt * 128:(kt + 1) * 128],
                     tks[:, tc * 512:(tc + 1) * 512], start=True, stop=True,
                     inc="t")
            CP("vector", v1t_sb[:, :], vb, waits=[("t", tbb)])
            d2_rv[j] = cnt["v"]
            TT(xc_sb[:, kt * TL + tc * 512:kt * TL + (tc + 1) * 512],
               v1t_sb[:, :], bcA, ALU.mult)
            d2_rb[j] = TT(xs_sb[:, kt * TL + tc * 512:kt * TL + (tc + 1) * 512],
                          v1t_sb[:, :], bcB, ALU.mult, inc="v")
        RED(pub_sb[:, kt:kt + 1], xc_sb[:, kt * TL:(kt + 1) * TL], ALU.add,
            waits=([("pl", pub0_done)] if kt == 0 else []))
        RED(pub_sb[:, 8 + kt:9 + kt], xs_sb[:, kt * TL:(kt + 1) * TL], ALU.add)
    d2_t = cnt["t"]
    wo13 = wo1_sb.rearrange("p (kt d) -> p kt d", kt=NDT)
    dwo13 = dwo1.rearrange("(kt p) d -> p kt d", p=128)
    g_wo1 = p.dma("gpsimd", wo13[:, 0:4, :], dwo13[:, 0:4, :],
                  waits=[("t", d2_t)], inc="g_wo1")
    s_wo1 = p.dma("gpsimd", wo13[:, 4:8, :], dwo13[:, 4:8, :],
                  waits=[("t", d2_t)], inc="s_wo1")
    RED(pub_sb[0:16, 16:17], tkc[:, :], ALU.add)
    vpub = RED(pub_sb[0:16, 17:18], tks[:, :], ALU.add, inc="v")
    s_pub = p.dma("sync", dcar_in, pub_sb[:, :], waits=[("v", vpub)], inc="s_fin")
    p.sem("cc")
    ccg = p.emit("gpsimd", (lambda e: e.collective_compute(
        "AllGather", ALU.bypass,
        replica_groups=[[2 * i, 2 * i + 1] for i in range(NCORES // 2)],
        ins=[dcar_in.opt()], outs=[dcar_out.opt()])),
        waits=[("s_fin", s_pub)], inc="cc", amt=1, dma=True)

    zcol = zer_sb[:, 0:1]
    v_pub = cnt["v"]
    SCAN(kcc[:, :], tkc[:, :], zer_sb[0:16, 0:TL], zer_sb[0:16, 0:1],
         waits=[("pl", zer_done)])
    SCAN(kcs[:, :], tks[:, :], zer_sb[0:16, 0:TL], zer_sb[0:16, 0:1])
    for kt in range(NHE):
        SCAN(xc_sb[:, kt * TL:(kt + 1) * TL], xc_sb[:, kt * TL:(kt + 1) * TL],
             zer_sb[:, 0:TL], zcol)
        SCAN(xs_sb[:, kt * TL:(kt + 1) * TL], xs_sb[:, kt * TL:(kt + 1) * TL],
             zer_sb[:, 0:TL], zcol)

    def read_partner(e, dst=car_sb, srcd=dcar_out):
        pid = e.partition_id()
        partner = nc.s_assert_within(
            e.compute_val(1 - pid % 2), 0, 1, skip_runtime_assert=True)
        return e.dma_start(dst[:, :],
                           srcd.rearrange("(r p) f -> r p f", r=2)[partner, :, :])
    p.sem("g_car")
    g_car = p.emit("gpsimd", read_partner, waits=[("cc", cnt["cc"])], inc="g_car",
                   amt=16, dma=True)
    vcar = TS(car_sb[:, :], car_sb[:, :], mask_sb[:, 0:1], None, ALU.mult,
              waits=[("g_car", g_car), ("g_tab", g_msk)], inc="v")

    STT(dent[:, :], kcc[:, :], car_sb[0:16, 16:17], tqc[:, :], ALU.add, ALU.mult,
        waits=[("v", vcar)])
    STT(dent2[:, :], kcs[:, :], car_sb[0:16, 17:18], tqs[:, :], ALU.add, ALU.mult)
    TT(dent[:, :], dent[:, :], dent2[:, :], ALU.add)
    TS(dent[:, :], dent[:, :], consts[0:16, 2:3], None, ALU.add)
    RECIP(dent2[:, :], dent[:, :])
    TT(a16[:, :], tqc[:, :], dent2[:, :], ALU.mult)
    vab = TT(b16[:, :], tqs[:, :], dent2[:, :], ALU.mult, inc="v")

    d4_rel = [None, None]
    for kt in range(NHE):
        for tc in range(qc_n):
            j = (kt * qc_n + tc) % 2
            bcA, bcB = d2_bc[j]
            sl = slice(tc * 512, (tc + 1) * 512)
            ksl = slice(kt * TL + tc * 512, kt * TL + (tc + 1) * 512)
            MM(bcA, sel16_sb[:, kt * 128:(kt + 1) * 128], a16[:, sl],
               start=True, stop=True,
               waits=[("v", d4_rel[j] if d4_rel[j] else cnt["v"])])
            tb2 = MM(bcB, sel16_sb[:, kt * 128:(kt + 1) * 128],
                     b16[:, sl], start=True, stop=True, inc="t")
            STT(v1t_sb[:, :], xc_sb[:, ksl], car_sb[:, kt:kt + 1], bcA,
                ALU.add, ALU.mult, waits=[("t", tb2)])
            STT(d4t_sb[:, :], xs_sb[:, ksl], car_sb[:, 8 + kt:9 + kt],
                bcB, ALU.add, ALU.mult)
            d4_rel[j] = TT(xc_sb[:, ksl], v1t_sb[:, :], d4t_sb[:, :], ALU.add,
                           inc="v")
    heads_done = cnt["v"]
    dtt_h = [None, None]
    dst_h = [None, None]
    x3_tc = {}
    for i, (dc, tc) in enumerate([(d, t) for d in range(NDT)
                                  for t in range(qc_n)]):
        b = i % 2
        sx = p.dma("scalar", xr2_sb[b][:, :],
                   dx2[dc * 128:(dc + 1) * 128, tc * 512:(tc + 1) * 512],
                   waits=[("v", dtt_h[b] if dtt_h[b] else heads_done)],
                   inc=f"a_xr{b}")
        for kt in range(NDT):
            w0 = []
            if kt == 0:
                w0 = [("v", heads_done), ("v", cnt["v"]),
                      ("g_wo1", g_wo1), ("s_wo1", s_wo1)]
            MM(psC[:, 0:512],
               wo1_sb[:, kt * 1024 + dc * 128:kt * 1024 + (dc + 1) * 128],
               xc_sb[:, kt * TL + tc * 512:kt * TL + (tc + 1) * 512],
               start=(kt == 0), stop=(kt == NDT - 1), waits=w0,
               inc=("t" if kt == NDT - 1 else None))
        vo = TT(oc2_sb[b][:, :], psC[:, 0:512], xr2_sb[b][:, :], ALU.add,
                waits=[("t", cnt["t"]), (f"a_xr{b}", sx)] +
                      ([(f"x3s{b}", dst_h[b])] if dst_h[b] else []),
                inc="v")
        dtt_h[b] = vo
        dst_h[b] = p.dma("sync", dx3[dc * 128:(dc + 1) * 128, tc * 512:(tc + 1) * 512],
                         oc2_sb[b][:, :], waits=[("v", vo)], inc=f"x3s{b}")
        if dc == NDT - 1:
            x3_tc[tc] = [("x3s0", cnt["x3s0"]), ("x3s1", cnt["x3s1"])]

    p.stage("moe")
    ar.reset()
    bE = mk_ln_bufs(dbl=False)
    _ie = _lnbuf_n[0] - 2
    lnc3 = bE["xcb"] + [ar.overlay(f"rows{_ie}", 0, [128, NDT * 512], BF16, "lnc3x0"),
                        ar.overlay(f"rows2_{_ie}", 0, [128, NDT * 512], BF16, "lnc3x1")]
    ng_sb = ar.alloc([128, 8], F32, "ng_sb")
    nb_sb = ar.alloc([128, 8], F32, "nb_sb")
    sg_sb = ar.alloc([128, 8], F32, "sg_sb")
    sbb_sb = ar.alloc([128, 8], F32, "sbb_sb")
    scol_sb = ar.alloc([128, 1], F32, "scol_sb")
    sc1_sb = ar.alloc([1, 1], F32, "sc1_sb")
    se_sb = ar.alloc([128, NDT * 1024], BF16, "se_sb")
    logF = ar.alloc([128, 8 * TL], F32, "logF")
    dD_sb = ar.alloc([128, 8 * TL], BF16, "dD_sb")
    mneg_sb = ar.alloc([128, 8], F32, "mneg_sb")
    mpos_sb = ar.alloc([128, 8], F32, "mpos_sb")
    sesum_sb = ar.alloc([128, 8], F32, "sesum_sb")
    cmneg_sb = ar.alloc([128, 1], F32, "cmneg_sb")
    crs_sb = ar.alloc([128, 1], F32, "crs_sb")
    bg_sb = ar.alloc([128, 32], F32, "bg_sb")
    bu_sb = ar.alloc([128, 32], F32, "bu_sb")
    bdr_sb = ar.alloc([1, 2048], BF16, "bdr_sb")
    xtok_sb = ar.overlay("logF", 0, [128, 16 * 1024], BF16, "xtok_sb")
    dtok_sb = ar.overlay("logF", 32768, [128, 16 * 1024], BF16, "dtok_sb")
    cfm_sb = ar.overlay("dD_sb", 0, [128, 8 * TL], BF16, "cfm_sb")
    wgx_sb = ar.overlay("logF", 0, [128, NDT * 2048], BF16, "wgx_sb")
    gt_sb = ar.overlay("logF", 32768, [128, 16 * 256], BF16, "gt_sb")
    gu_sb = ar.overlay("logF", 32768 + 8192, [128, 16 * 256], BF16, "gu_sb")
    slfm_sb = ar.overlay("logF", 32768 + 16384, [128, NDT * 512], BF16, "slfm_sb")
    bd_sb = ar.overlay("logF", 32768 + 24576, [128, 1024], F32, "bd_sb")
    eo_sb = ar.overlay("logF", 32768 + 28672, [128, 1024], BF16, "eo_sb")
    xdE = "xcf" + str(_lnbuf_n[0] - 2)
    eof_sb = ar.overlay(xdE, 0, [128, 8 * 1024], BF16, "eof_sb")
    xdE2 = "xcf" + str(_lnbuf_n[0] - 1)
    stg_sb = ar.overlay(xdE2, 0, [128, 1032], BF16, "stg_sb")
    cn_sb = ar.overlay(xdE2, 2080, [128, 1024], BF16, "cn_sb")
    prt2_sb = ar.overlay(xdE2, 4160, [128, 2080], BF16, "prt2")
    prt_sb = [prt2_sb[:, 0:1032], prt2_sb[:, 1040:2072]]
    combf_sb = ar.overlay(xdE2, 8352, [128, 1024], F32, "combf_sb")
    mnb_sb = ar.overlay(xdE2, 12448, [128, 8], BF16, "mnb_sb")
    mnr_sb = ar.overlay(xdE2, 12480, [128, 8], F32, "mnr_sb")
    sqbE = "sqb" + str(_lnbuf_n[0] - 2)
    xr3_sb = [ar.overlay(sqbE, 0, [128, 512], F32, "xr3_sb0"),
              ar.overlay(sqbE, 2048, [128, 512], F32, "xr3_sb1")]
    oc3_sb = [ar.overlay(sqbE, 4096, [128, 512], F32, "oc3_sb0"),
              ar.overlay(sqbE, 6144, [128, 512], F32, "oc3_sb1")]

    g_ng = p.dma("gpsimd", ng_sb[:, :], dng, inc="g_tab")
    p.dma("gpsimd", nb_sb[:, :], dnb, inc="g_tab")
    p.dma("gpsimd", sg_sb[:, :], dsg, inc="g_tab")
    p.dma("gpsimd", sbb_sb[:, :], dsbb, inc="g_tab")
    g_sc = p.dma("gpsimd", sc1_sb[:, :], dscale, inc="g_tab")
    MM(psE[:, 0:1], ones1x128[:, :], sc1_sb[:, :], start=True, stop=True,
       waits=[("g_tab", g_sc), ("v", cnt["v"])], inc="t")
    vsc = CP("vector", scol_sb[:, :], psE[:, 0:1], waits=[("t", cnt["t"])], inc="v")
    ln3_ch = emit_ln(dx3, TL, dln3, bE, gten=ng_sb, bten=nb_sb,
                     src_ready_fn=lambda ch: x3_tc[ch] + [("g_tab", g_ng)])
    ln3_done = ln3_ch[-1]
    seln_done = emit_ln(dse, 1024, dse_ln, bE, gten=sg_sb, bten=sbb_sb,
                        postscale=scol_sb[:, 0:1],
                        src_ready=[("t", cnt["t"]), ("v", cnt["v"]),
                                   ("a", cnt["a"])],
                        prev_st={0: ln3_ch[2], 1: ln3_ch[3]})[-1]
    g_se = p.dma("gpsimd", se_sb.rearrange("p (dt s) -> p dt s", dt=NDT),
                 dse_ln.rearrange("(dt p) s -> p dt s", p=128),
                 waits=[seln_done], inc="g_tab")
    g_bg = p.dma("gpsimd", bg_sb[:, :], dbgc, inc="g_tab")
    g_bu = p.dma("gpsimd", bu_sb[:, :], dbuc, inc="g_tab")
    g_bd = p.dma("gpsimd", bdr_sb.rearrange("o (e f) -> o e f", e=2),
                 dbdr.rearrange("(o e) f -> o e f", o=1), inc="g_tab")
    seln_t = cnt["t"]

    lc3_ld = [None] * qc_n
    for tc in range(qc_n):
        qd = "sync" if tc % 2 == 0 else "scalar"
        wl = [ln3_ch[tc], ("t", seln_t)]
        lc3_ld[tc] = p.dma(qd, lnc3[tc].rearrange("p (dt t) -> p dt t", dt=NDT),
                           dln3.rearrange("(dt p) t -> p dt t", p=128)[:, :, tc * 512:(tc + 1) * 512],
                           waits=wl, inc=f"lc3{tc}")
    for tc in range(qc_n):
        for st in range(NDT):
            pbk = psC if (st * qc_n + tc) % 2 == 0 else psD
            for dt in range(NDT):
                w0 = []
                if dt == 0:
                    w0 = [(f"lc3{tc}", lc3_ld[tc]), ("g_tab", g_se), ("v", cnt["v"])]
                MM(pbk[:, 0:512], se_sb[:, dt * 1024 + st * 128:dt * 1024 + (st + 1) * 128],
                   lnc3[tc][:, dt * 512:(dt + 1) * 512],
                   start=(dt == 0), stop=(dt == NDT - 1), waits=w0,
                   inc=("t" if dt == NDT - 1 else None))
            CP("vector", logF[:, st * TL + tc * 512:st * TL + (tc + 1) * 512],
               pbk[:, 0:512], waits=[("t", cnt["t"])], inc="v")
    for st in range(NDT):
        ssl = slice(st * TL, (st + 1) * TL)
        RED(mneg_sb[:, st:st + 1], logF[:, ssl], ALU.max, negate=True)
        CP("vector", mnb_sb[:, st:st + 1], mneg_sb[:, st:st + 1])
        va = CP("vector", mnr_sb[:, st:st + 1], mnb_sb[:, st:st + 1], inc="v")
        ACTV(dD_sb[:, ssl], logF[:, ssl], AF.Exp, bias=mnr_sb[:, st:st + 1],
             waits=[("v", va)], inc="a")
        RED(sesum_sb[:, st:st + 1], dD_sb[:, ssl], ALU.add,
            waits=[("a", cnt["a"])], inc="v")
    disp_done_a, disp_done_v = cnt["a"], cnt["v"]

    for tch in range(16):
        tc4, sub = tch // 4, tch % 4
        for dt in range(NDT):
            w0 = [("v", cnt["v"])]
            if dt == 0:
                w0 += [(f"lc3{tc4}", lc3_ld[tc4]), ("a", disp_done_a)]
            p.emit("tensor",
                   (lambda e, o=psT[:, dt * 128:(dt + 1) * 128],
                    i=lnc3[tc4][:, dt * 512 + sub * 128:dt * 512 + sub * 128 + 128],
                    idn=id_sb[:, :]: e.transpose(o, i, idn)),
                   waits=w0, inc=("t" if dt == NDT - 1 else None))
        CP("vector", xtok_sb[:, tch * 1024:(tch + 1) * 1024], psT[:, :],
           waits=[("t", cnt["t"]), ("v", disp_done_v), ("a", disp_done_a)], inc="v")
    for tch in range(16):
        for st in range(NDT):
            w0 = [("v", cnt["v"])]
            p.emit("tensor",
                   (lambda e, o=psT[:, st * 128:(st + 1) * 128],
                    i=dD_sb[:, st * TL + tch * 128:st * TL + (tch + 1) * 128],
                    idn=id_sb[:, :]: e.transpose(o, i, idn)),
                   waits=w0, inc=("t" if st == NDT - 1 else None))
        CP("vector", dtok_sb[:, tch * 1024:(tch + 1) * 1024], psT[:, :],
           waits=[("t", cnt["t"])], inc="v")
    tr_done = cnt["v"]

    MS(stg_sb[:, 1026:1032], 0.0, inc="pl")
    stg0 = cnt["pl"]
    for st in range(NDT):
        for tch in range(16):
            w0 = []
            if tch == 0:
                w0 = [("v", tr_done), ("v", cnt["v"])]
            MM(psA[:, 0:512], dtok_sb[:, tch * 1024 + st * 128:tch * 1024 + (st + 1) * 128],
               xtok_sb[:, tch * 1024:tch * 1024 + 512],
               start=(tch == 0), stop=(tch == 15), waits=w0)
            MM(psA[:, 512:1024], dtok_sb[:, tch * 1024 + st * 128:tch * 1024 + (st + 1) * 128],
               xtok_sb[:, tch * 1024 + 512:tch * 1024 + 1024],
               start=(tch == 0), stop=(tch == 15),
               inc=("t" if tch == 15 else None))
        t_slots = cnt["t"]
        CP("vector", stg_sb[:, 0:512], psA[:, 0:512],
           waits=[("t", cnt["t"]), ("pl", stg0)] +
                 ([("s_slag", cnt["s_slag"])] if cnt.get("s_slag") else []))
        CP("vector", stg_sb[:, 512:1024], psA[:, 512:1024])
        p.emit("vector", (lambda e, o=stg_sb[:, 1024:1025],
                          i=mnr_sb[:, st:st + 1]: e.tensor_scalar_mul(o, i, -1.0)))
        vstg = CP("vector", stg_sb[:, 1025:1026], sesum_sb[:, st:st + 1], inc="v")
        p.dma("sync", dsl_in[st * 128:(st + 1) * 128, :], stg_sb[:, :],
              waits=[("v", vstg)], inc="s_slag")
    p.sem("cc")
    cc_sl = p.emit("gpsimd", (lambda e: e.collective_compute(
        "AllGather", ALU.bypass, replica_groups=[list(range(NCORES))],
        ins=[dsl_in.opt()], outs=[dsl_out.opt()])),
        waits=[("s_slag", cnt["s_slag"])], inc="cc", amt=1, dma=True)

    p.sem("g_wx")
    p.sem("s_wx")

    def load_wx(dram_e, nch, war_t):
        dst = wgx_sb.rearrange("p (c f) -> p c f", c=nch)
        src = dram_e
        h = nch // 2
        wg_ = [("t", war_t)]
        if cnt["g_wx"]:
            wg_.append(("g_wx", cnt["g_wx"]))
        g = p.dma("gpsimd", dst[:, 0:h, :], src[:, 0:h, :], waits=wg_, inc="g_wx")
        ws_ = [("t", war_t)]
        if cnt["s_wx"]:
            ws_.append(("s_wx", cnt["s_wx"]))
        s = p.dma("gpsimd", dst[:, h:nch, :], src[:, h:nch, :], waits=ws_, inc="s_wx")
        return g, s, h

    wg0_pre = load_wx(dwg.rearrange("e (dt p) f -> e p dt f", p=128)[0],
                      NDT, t_slots)

    for tch in range(16):
        tc4, sub = tch // 4, tch % 4
        for dt in range(NDT):
            w0 = []
            if dt == 0:
                w0 = [(f"lc3{tc4}", lc3_ld[tc4]), ("v", cnt["v"]), ("a", cnt["a"])]
            MM(psB[:, 0:512], lnc3[tc4][:, dt * 512 + sub * 128:dt * 512 + sub * 128 + 128],
               se_sb[:, dt * 1024:dt * 1024 + 512],
               start=(dt == 0), stop=(dt == NDT - 1), waits=w0)
            MM(psB[:, 512:1024], lnc3[tc4][:, dt * 512 + sub * 128:dt * 512 + sub * 128 + 128],
               se_sb[:, dt * 1024 + 512:dt * 1024 + 1024],
               start=(dt == 0), stop=(dt == NDT - 1),
               inc=("t" if dt == NDT - 1 else None))
        RED(cmneg_sb[:, 0:1], psB[:, 0:1024], ALU.max, negate=True,
            waits=[("t", cnt["t"])])
        ACTV(cn_sb[:, :], psB[:, 0:1024], AF.Exp, bias=cmneg_sb[:, 0:1],
             waits=[("v", cnt["v"])], inc="a")
        RED(crs_sb[:, 0:1], cn_sb[:, :], ALU.add, waits=[("a", cnt["a"])])
        RECIP(crs_sb[:, 0:1], crs_sb[:, 0:1])
        TS(cn_sb[:, :], cn_sb[:, :], crs_sb[:, 0:1], None, ALU.mult, inc="v")
        for st in range(NDT):
            w0 = [("v", cnt["v"])]
            p.emit("tensor",
                   (lambda e, o=psT[:, st * 128:(st + 1) * 128],
                    i=cn_sb[:, st * 128:(st + 1) * 128],
                    idn=id_sb[:, :]: e.transpose(o, i, idn)),
                   waits=w0, inc=("t" if st == NDT - 1 else None))
        p.emit("vector",
               (lambda e, tch=tch: e.tensor_copy(
                   cfm_sb.rearrange("p (st t) -> p st t", st=NDT)[:, :, tch * 128:(tch + 1) * 128],
                   psT.rearrange("p (st c) -> p st c", st=NDT))),
               waits=[("t", cnt["t"])], inc="v")
    cfm_done = cnt["v"]

    p.sem("g_sl")
    _dynidx = {}

    def get_me(e):
        if "me" not in _dynidx:
            pid = e.partition_id()
            _dynidx["me"] = nc.s_assert_within(pid, 0, 7, skip_runtime_assert=True)
        return _dynidx["me"]

    for b in range(4):
        def read_pair(e, b=b):
            me = get_me(e)
            return e.dma_start(
                prt2_sb.rearrange("p (r f) -> p r f", r=2)[:, :, 0:1032],
                dsl_out.rearrange("(r q p) f -> q p r f", q=NCORES, p=128)[me, :, 2 * b:2 * b + 2, :])
        p.emit("gpsimd", read_pair,
               waits=[("cc", cc_sl), ("g_sl", cnt["g_sl"])] +
                     ([("v", cnt["v"])] if b else []),
               inc="g_sl", amt=16, dma=True)
        g_pr = cnt["g_sl"]
        TT(cmneg_sb[:, 0:1], prt_sb[0][:, 1024:1025], prt_sb[1][:, 1024:1025],
           ALU.max, waits=[("g_sl", g_pr)])
        TT(crs_sb[:, 0:1], prt_sb[0][:, 1024:1025], cmneg_sb[:, 0:1], ALU.subtract)
        ACTV(crs_sb[:, 0:1], crs_sb[:, 0:1], AF.Exp, waits=[("v", cnt["v"])], inc="a")
        TT(mneg_sb[:, 0:1], prt_sb[1][:, 1024:1025], cmneg_sb[:, 0:1], ALU.subtract,
           waits=[("a", cnt["a"])])
        ACTV(mneg_sb[:, 0:1], mneg_sb[:, 0:1], AF.Exp, waits=[("v", cnt["v"])], inc="a")
        TS(mpos_sb[:, 0:1], prt_sb[0][:, 1025:1026], crs_sb[:, 0:1], None, ALU.mult,
           waits=[("a", cnt["a"])])
        STT(mpos_sb[:, 0:1], prt_sb[1][:, 1025:1026], mneg_sb[:, 0:1],
            mpos_sb[:, 0:1], ALU.mult, ALU.add)
        RECIP(mpos_sb[:, 0:1], mpos_sb[:, 0:1])
        TS(combf_sb[:, :], prt_sb[0][:, 0:1024], crs_sb[:, 0:1], None, ALU.mult)
        STT(combf_sb[:, :], prt_sb[1][:, 0:1024], mneg_sb[:, 0:1],
            combf_sb[:, :], ALU.mult, ALU.add)
        TS(cn_sb[:, :], combf_sb[:, :], mpos_sb[:, 0:1], None, ALU.mult, inc="v")
        for dt in range(NDT):
            w0 = [("v", cnt["v"])]
            p.emit("tensor",
                   (lambda e, o=psT[:, dt * 128:(dt + 1) * 128],
                    i=cn_sb[:, dt * 128:(dt + 1) * 128],
                    idn=id_sb[:, :]: e.transpose(o, i, idn)),
                   waits=w0, inc=("t" if dt == NDT - 1 else None))
        vtr = p.emit("vector",
                     (lambda e, b=b: e.tensor_copy(
                         slfm_sb.rearrange("p (dt c) -> p dt c", dt=NDT)[:, :, b * 128:(b + 1) * 128],
                         psT.rearrange("p (dt c) -> p dt c", dt=NDT))),
                     waits=[("t", cnt["t"])], inc="v")
    slots_done = cnt["v"]

    for e_i in range(2):
        if e_i == 0:
            g_wg, s_wg, hh = wg0_pre
        else:
            g_wg, s_wg, hh = load_wx(
                dwg.rearrange("e (dt p) f -> e p dt f", p=128)[e_i], NDT, cnt["t"])
        sl_e = slfm_sb.rearrange("p (dt b s) -> p dt b s", dt=NDT, b=4)
        for fc in range(16):
            for dt in range(NDT):
                w0 = []
                if dt == 0:
                    w0 = [("a", cnt["a"])]
                    if fc == 0:
                        w0 += [("g_wx", g_wg), ("v", slots_done)]
                if fc == 0 and dt == hh:
                    w0 = w0 + [("s_wx", s_wg)]
                MM(psC[:, 0:256], wgx_sb[:, dt * 2048 + fc * 128:dt * 2048 + (fc + 1) * 128],
                   sl_e[:, dt, :, e_i * 64:(e_i + 1) * 64],
                   start=(dt == 0), stop=(dt == NDT - 1), waits=w0,
                   inc=("t" if dt == NDT - 1 else None))
            ACTV(gt_sb[:, fc * 256:(fc + 1) * 256], psC[:, 0:256], AF.Gelu,
                 bias=bg_sb[:, e_i * 16 + fc:e_i * 16 + fc + 1],
                 waits=[("t", cnt["t"]), ("g_tab", g_bg)], inc="a")
        gate_a = cnt["a"]
        g_wu, s_wu, _ = load_wx(dwu.rearrange("e (dt p) f -> e p dt f", p=128)[e_i],
                                NDT, cnt["t"])
        for fc in range(16):
            for dt in range(NDT):
                w0 = []
                if dt == 0:
                    w0 = [("v", cnt["v"])]
                    if fc == 0:
                        w0.append(("g_wx", g_wu))
                if fc == 0 and dt == hh:
                    w0 = w0 + [("s_wx", s_wu)]
                MM(psD[:, 0:256], wgx_sb[:, dt * 2048 + fc * 128:dt * 2048 + (fc + 1) * 128],
                   sl_e[:, dt, :, e_i * 64:(e_i + 1) * 64],
                   start=(dt == 0), stop=(dt == NDT - 1), waits=w0,
                   inc=("t" if dt == NDT - 1 else None))
            STT(gu_sb[:, fc * 256:(fc + 1) * 256], psD[:, 0:256],
                bu_sb[:, e_i * 16 + fc:e_i * 16 + fc + 1],
                gt_sb[:, fc * 256:(fc + 1) * 256], ALU.add, ALU.mult,
                waits=[("t", cnt["t"]), ("a", gate_a), ("g_tab", g_bu)], inc="v")
        gu_done = cnt["v"]
        g_wd, s_wd, hd = load_wx(dwd.rearrange("e (ft p) o -> e p ft o", p=128)[e_i],
                                 16, cnt["t"])
        MM(psB[:, 0:512], ones1x128b[:, :], bdr_sb[:, e_i * 1024:e_i * 1024 + 512],
           start=True, stop=True, waits=[("g_tab", g_bd), ("v", cnt["v"])])
        MM(psB[:, 512:1024], ones1x128b[:, :],
           bdr_sb[:, e_i * 1024 + 512:e_i * 1024 + 1024], start=True, stop=True,
           inc="t")
        CP("vector", bd_sb[:, :], psB[:, 0:1024], waits=[("t", cnt["t"])], inc="v")
        for sc in range(2):
            for ft in range(16):
                w0 = []
                if ft == 0:
                    w0 = [("v", gu_done), ("v", cnt["v"])]
                    if sc == 0:
                        w0.append(("g_wx", g_wd))
                if sc == 0 and ft == hd:
                    w0 = w0 + [("s_wx", s_wd)]
                MM(psA[:, 0:512], gu_sb[:, ft * 256 + sc * 128:ft * 256 + sc * 128 + 128],
                   wgx_sb[:, ft * 1024:ft * 1024 + 512],
                   start=(ft == 0), stop=(ft == 15), waits=w0)
                MM(psA[:, 512:1024], gu_sb[:, ft * 256 + sc * 128:ft * 256 + sc * 128 + 128],
                   wgx_sb[:, ft * 1024 + 512:ft * 1024 + 1024],
                   start=(ft == 0), stop=(ft == 15),
                   inc=("t" if ft == 15 else None))
            veo = TT(eo_sb[:, :], psA[:, 0:1024], bd_sb[:, :], ALU.add,
                     waits=[("t", cnt["t"])] +
                           ([("s_eoag", cnt["s_eoag"])] if cnt.get("s_eoag") else []),
                     inc="v")
            p.dma("sync",
                  deo_in[2 * sc * 128 + e_i * 64:2 * sc * 128 + (e_i + 1) * 64, :],
                  eo_sb[0:64, :], waits=[("v", veo)], inc="s_eoag")
            p.dma("sync",
                  deo_in[(2 * sc + 1) * 128 + e_i * 64:(2 * sc + 1) * 128 + (e_i + 1) * 64, :],
                  eo_sb[64:128, :], inc="s_eoag")
    cc_eo = p.emit("gpsimd", (lambda e: e.collective_compute(
        "AllGather", ALU.bypass, replica_groups=[list(range(NCORES))],
        ins=[deo_in.opt()], outs=[deo_out.opt()])),
        waits=[("s_eoag", cnt["s_eoag"])], inc="cc", amt=1, dma=True)

    p.sem("g_eo")

    def read_eo(e):
        pid = e.partition_id()
        bp = nc.s_assert_within(e.compute_val(pid // 2), 0, 3,
                                skip_runtime_assert=True)
        return e.dma_start(
            eof_sb.rearrange("p (r f) -> p r f", r=NCORES),
            deo_out.rearrange("(r q p) f -> q p r f", q=4, p=128)[bp, :, :, :])
    eof_done = p.emit("gpsimd", read_eo,
                      waits=[("cc", cc_eo), ("g_eo", cnt["g_eo"]),
                             ("a", disp_done_a)],
                      inc="g_eo", amt=16, dma=True)

    ett_h = [None, None]
    est_h = [None, None]
    e9_v0 = cnt["v"]
    for i, (dc, tc) in enumerate([(d, t) for d in range(NDT)
                                  for t in range(qc_n)]):
        b = i % 2
        sx = p.dma("scalar", xr3_sb[b][:, :],
                   dx3[dc * 128:(dc + 1) * 128, tc * 512:(tc + 1) * 512],
                   waits=[("v", ett_h[b] if ett_h[b] else e9_v0)],
                   inc=f"a_xr{b}")
        for st in range(NDT):
            w0 = []
            if st == 0:
                w0 = [("g_eo", eof_done), ("v", cfm_done), ("v", cnt["v"])]
            MM(psC[:, 0:512],
               eof_sb[:, st * 1024 + dc * 128:st * 1024 + (dc + 1) * 128],
               cfm_sb[:, st * TL + tc * 512:st * TL + (tc + 1) * 512],
               start=(st == 0), stop=(st == NDT - 1), waits=w0,
               inc=("t" if st == NDT - 1 else None))
        vo = TT(oc3_sb[b][:, :], psC[:, 0:512], xr3_sb[b][:, :], ALU.add,
                waits=[("t", cnt["t"]), (f"a_xr{b}", sx)] +
                      ([(f"os{b}", est_h[b])] if est_h[b] else []),
                inc="v")
        ett_h[b] = vo
        est_h[b] = p.dma("sync", dout[dc * 128:(dc + 1) * 128, tc * 512:(tc + 1) * 512],
                         oc3_sb[b][:, :], waits=[("v", vo)], inc=f"os{b}")
    p.emit("sync", lambda e: (e.wait_ge(p.sems["os0"], cnt["os0"]),
                              e.wait_ge(p.sems["os1"], cnt["os1"])) and None)

    if DEBUG_STAGE in (0, 4):
        p.replay()
        return nc
    p.stage("fin")
    ar.reset()
    fin_sb = ar.alloc([128, 2048], F32, "fin_sb")
    if DEBUG_STAGE == 1:
        src_d, sds = dx1, "s_x1"
    elif DEBUG_STAGE == 2:
        src_d, sds = dx2, "s_x2"
    else:
        src_d, sds = dx3, "s_x3"
    stream_done = [(sds, cnt[sds])]
    for dc in range(NDT):
        ld = p.dma("sync", fin_sb[:, :], src_d[dc * 128:(dc + 1) * 128, :],
                   waits=stream_done + ([("o", cnt["o"])] if dc else []), inc="s_fin")
        p.dma("sync", dout[dc * 128:(dc + 1) * 128, :], fin_sb[:, :],
              waits=[("s_fin", ld)], inc="o")
    p.emit("sync", lambda e: e.wait_ge(p.sems["o"], cnt["o"]) and None)

    p.replay()
    return nc


_CACHE = {}


def kernel(**inputs):
    if "nc" not in _CACHE:
        _CACHE["nc"] = build()
    nc = _CACHE["nc"]
    in_maps = _prep(inputs)
    res = run_bass_kernel_spmd(nc, in_maps, core_ids=list(range(NCORES)))
    out = np.empty((B, T, D), np.float32)
    for c in range(NCORES):
        b, half = c // 2, c % 2
        t0 = half * TL
        out[b, t0:t0 + TL, :] = res.results[c]["out_fm"].T
    return out


# version 19
# speedup vs baseline: 1.2195x; 1.0254x over previous
import numpy as np
import concourse.bass as bass
import concourse.mybir as mybir
from concourse.bass_utils import run_bass_kernel_spmd

F32 = mybir.dt.float32
BF16 = mybir.dt.bfloat16
AF = mybir.ActivationFunctionType
ALU = mybir.AluOpType
AX = mybir.AxisListType

NCORES = 8
B, T, D, H, DH = 4, 4096, 1024, 16, 64
TL = T // 2
NDT = D // 128
EPS_LN = 1e-5

DEBUG_STAGE = 0


ENGS = ("tensor", "vector", "scalar", "gpsimd", "sync")


class Prog:
    def __init__(self, nc):
        self.nc = nc
        self.stages = []
        self.cnt = {}
        self.sems = {}
        self.cur = None

    def sem(self, name):
        if name not in self.sems:
            self.sems[name] = self.nc.alloc_semaphore(name)
            self.cnt[name] = 0
        return name

    def stage(self, name):
        self.cur = {e: [] for e in ENGS}
        self.cur["_name"] = name
        self.stages.append(self.cur)

    ENG_SEM = {"vector": "v", "scalar": "a", "gpsimd": "pl"}

    def emit(self, eng, fn, waits=(), inc=None, amt=1, dma=False, chain=True):
        for s, v in waits:
            assert v <= self.cnt[s], (s, v, self.cnt[s])
        waits = list(waits)
        if not dma and eng in self.ENG_SEM:
            es = self.ENG_SEM[eng]
            self.sem(es)
            if chain and self.cnt[es]:
                waits.append((es, self.cnt[es]))
            if inc is None:
                inc = es
            assert inc == es, (eng, inc)
        rec = (fn, tuple(waits), inc, amt)
        self.cur[eng].append(rec)
        if inc is not None:
            self.cnt[inc] += amt
            return self.cnt[inc]
        return None

    def dma(self, eng, out, in_, waits=(), inc=None, **kw):
        if inc is None:
            inc = eng[0]
        self.sem(inc)
        w = list(waits)
        if self.cnt[inc]:
            w.append((inc, self.cnt[inc]))
        return self.emit(eng, lambda e: e.dma_start(out, in_, **kw), w, inc, 16,
                         dma=True)

    def replay(self):
        nc = self.nc
        ctx = nc.allow_low_precision(reason="bf16 matmul operands per problem tolerance")
        ctx.__enter__()
        for st in self.stages:
            with nc.Block(st["_name"]) as block:
                for eng in ENGS:
                    recs = st[eng]
                    if not recs:
                        continue

                    def body(e, recs=recs):
                        for fn, waits, inc, amt in recs:
                            for s, v in waits:
                                e.wait_ge(self.sems[s], v)
                            inst = fn(e)
                            if inc is not None and inst is not None:
                                inst.then_inc(self.sems[inc], amt)

                    getattr(block, eng)(body)


class Arena:

    def __init__(self, nc, slack=5120):
        self.nc = nc
        size = nc.sbuf_bytes_remaining - slack
        base, _ = nc.bump_sbuf(size)
        self.base = base
        self.off = base
        self.limit = base + size
        self.mark_off = None
        self.n = 0

    def alloc(self, shape, dtype, name=None):
        assert len(shape) == 2 and shape[0] <= 128
        itm = 2 if dtype == BF16 else 4
        sz = shape[1] * itm
        off = (self.off + 31) & ~31
        assert off + sz <= self.limit, (
            f"SBUF arena overflow: {name} needs {sz} at {off}, limit {self.limit}")
        self.off = off + sz
        self.n += 1
        nm = name or f"ar{self.n}"
        if not hasattr(self, "offs"):
            self.offs = {}
        self.offs[nm] = off
        return self.nc.alloc_sbuf_tensor_at(nm, list(shape), dtype, offset=off)

    def overlay(self, base_name, delta, shape, dtype, name):
        off = self.offs[base_name] + delta
        self.n += 1
        return self.nc.alloc_sbuf_tensor_at(name, list(shape), dtype, offset=off)

    def mark(self, key="base"):
        if self.mark_off is None:
            self.mark_off = {}
        self.mark_off[key] = self.off

    def reset(self, key="base"):
        self.off = self.mark_off[key]


def _rope_tables():
    base = 10000.0
    inv = 1.0 / base ** (np.arange(0, DH, 2, dtype=np.float64) / DH)
    ang = np.arange(T, dtype=np.float64)[:, None] * inv[None, :]
    cos = np.cos(ang)
    sin = np.sin(ang)
    cs = np.empty((128, T), np.float32)
    ss = np.empty((128, T), np.float32)
    for p in range(128):
        d = p % 64
        cs[p] = cos[:, d % 32].astype(np.float32)
        sgn = -1.0 if (d % 64) < 32 else 1.0
        ss[p] = (sgn * sin[:, d % 32]).astype(np.float32)
    return cs, ss


def _prep_core(inputs, c):
    b, half = c // 2, c % 2
    t0, o0 = half * TL, (1 - half) * TL
    order = np.r_[t0:t0 + TL, o0:o0 + TL]

    x = np.asarray(inputs["L0_attn_hidden_states"])
    cs, ss = _rope_tables()

    d = {}
    d["x"] = np.ascontiguousarray(x[b][order].T)
    d["wqT"] = np.ascontiguousarray(np.asarray(inputs["L0_attn_Wq"]).T)
    d["wkT"] = np.ascontiguousarray(np.asarray(inputs["L0_attn_Wk"]).T)
    wv = np.asarray(inputs["L0_attn_Wv"])
    wv65 = np.zeros((D, H * 65), np.float32)
    for h in range(H):
        wv65[:, h * 65:h * 65 + 64] = wv[h * 64:(h + 1) * 64, :].T
    d["wv65"] = wv65
    bv = np.asarray(inputs["L0_attn_bv"])
    bv_row = np.zeros((1, H * 65), np.float32)
    for h in range(H):
        bv_row[0, h * 65:h * 65 + 64] = bv[h * 64:(h + 1) * 64]
    d["bv_row"] = bv_row
    d["bq_p"] = np.ascontiguousarray(np.asarray(inputs["L0_attn_bq"]).reshape(8, 128).T)
    d["bk_p"] = np.ascontiguousarray(np.asarray(inputs["L0_attn_bk"]).reshape(8, 128).T)
    d["bo_c"] = np.ascontiguousarray(np.asarray(inputs["L0_attn_bo"]).reshape(8, 128).T)
    d["woT"] = np.ascontiguousarray(np.asarray(inputs["L0_attn_Wo"]).T)
    d["cs"] = np.ascontiguousarray(cs[:, order])
    d["ss"] = np.ascontiguousarray(ss[:, order])

    sel128 = np.zeros((128, 128), np.float32)
    sel128[0, 0:64] = 1.0
    sel128[64, 64:128] = 1.0
    d["sel128"] = sel128
    d["ident"] = np.eye(128, dtype=np.float32)
    perm = np.r_[32:64, 0:32, 96:128, 64:96]
    P = np.zeros((128, 128), np.float32)
    P[perm, np.arange(128)] = 1.0
    d["permm"] = P

    d["w1t"] = np.ascontiguousarray(np.asarray(inputs["L0_ffn_weight"]).T)
    d["disha_p"] = np.ascontiguousarray(np.asarray(inputs["L0_ffn_disha"]))
    psel = np.zeros((128, 64), np.float32)
    for pp in range(128):
        psel[pp, pp % 64] = 1.0
    d["psel"] = psel

    wq1 = np.asarray(inputs["L1_attn_W_Q"])[:, :, 0]
    wk1 = np.asarray(inputs["L1_attn_W_K"])[:, :, 0]
    wqk1 = np.zeros((D, 64), np.float32)
    wqk1[:, 0:16] = wq1.T
    wqk1[:, 32:48] = wk1.T
    d["wqk1"] = wqk1
    wv1 = np.asarray(inputs["L1_attn_W_V"])
    d["wv1"] = np.ascontiguousarray(wv1.transpose(1, 0, 2).reshape(D, D))
    wo1 = np.asarray(inputs["L1_attn_W_O"])
    d["wo1"] = np.ascontiguousarray(wo1.transpose(2, 1, 0).reshape(D, D))
    sel16 = np.zeros((16, 8 * 128), np.float32)
    for kt in range(8):
        for pp in range(128):
            sel16[2 * kt + (pp >= 64), kt * 128 + pp] = 1.0
    d["sel16"] = sel16
    d["oddmask"] = np.full((128, 1), float(half), np.float32)

    se_raw = np.asarray(inputs["L1_ffn_slot_embeds"]).reshape(1024, D)
    d["se_raw"] = np.ascontiguousarray(se_raw.T)
    d["ng"] = np.ascontiguousarray(np.asarray(inputs["L1_ffn_norm_g"]).reshape(8, 128).T)
    d["nb"] = np.ascontiguousarray(np.asarray(inputs["L1_ffn_norm_b"]).reshape(8, 128).T)
    d["sg"] = np.ascontiguousarray(np.asarray(inputs["L1_ffn_slot_g"]).reshape(8, 128).T)
    d["sbb"] = np.ascontiguousarray(np.asarray(inputs["L1_ffn_slot_b"]).reshape(8, 128).T)
    d["scale1"] = np.asarray(inputs["L1_ffn_scale"]).reshape(1, 1).astype(np.float32)
    es = (2 * c, 2 * c + 1)
    d["wg"] = np.ascontiguousarray(np.asarray(inputs["L1_ffn_Wg"])[list(es)])
    d["wu"] = np.ascontiguousarray(np.asarray(inputs["L1_ffn_Wu"])[list(es)])
    d["wd"] = np.ascontiguousarray(np.asarray(inputs["L1_ffn_Wd"])[list(es)])
    bg = np.asarray(inputs["L1_ffn_bg"])
    bu = np.asarray(inputs["L1_ffn_bu"])
    bgc = np.zeros((128, 32), np.float32)
    buc = np.zeros((128, 32), np.float32)
    for j, eg in enumerate(es):
        bgc[:, j * 16:(j + 1) * 16] = bg[eg].reshape(16, 128).T
        buc[:, j * 16:(j + 1) * 16] = bu[eg].reshape(16, 128).T
    d["bgc"] = bgc
    d["buc"] = buc
    d["bdr"] = np.ascontiguousarray(np.asarray(inputs["L1_ffn_bd"])[list(es)])
    return d


def _prep(inputs):
    return [_prep_core(inputs, c) for c in range(NCORES)]


def build():
    nc = bass.Bass("TRN2", target_bir_lowering=False)
    p = Prog(nc)
    for s in ("g", "s", "t", "v", "a", "pl", "cc", "o"):
        p.sem(s)

    dx = nc.dram_tensor("x", [D, T], F32, kind="ExternalInput").ap()
    dwqT = nc.dram_tensor("wqT", [D, D], F32, kind="ExternalInput").ap()
    dwkT = nc.dram_tensor("wkT", [D, D], F32, kind="ExternalInput").ap()
    dwv65 = nc.dram_tensor("wv65", [D, H * 65], F32, kind="ExternalInput").ap()
    dbv = nc.dram_tensor("bv_row", [1, H * 65], F32, kind="ExternalInput").ap()
    dbq = nc.dram_tensor("bq_p", [128, 8], F32, kind="ExternalInput").ap()
    dbk = nc.dram_tensor("bk_p", [128, 8], F32, kind="ExternalInput").ap()
    dbo = nc.dram_tensor("bo_c", [128, 8], F32, kind="ExternalInput").ap()
    dwoT = nc.dram_tensor("woT", [D, D], F32, kind="ExternalInput").ap()
    dcs = nc.dram_tensor("cs", [128, T], F32, kind="ExternalInput").ap()
    dss = nc.dram_tensor("ss", [128, T], F32, kind="ExternalInput").ap()
    dsel = nc.dram_tensor("sel128", [128, 128], F32, kind="ExternalInput").ap()
    dpermm = nc.dram_tensor("permm", [128, 128], F32, kind="ExternalInput").ap()

    dout = nc.dram_tensor("out_fm", [D, TL], F32, kind="ExternalOutput").ap()

    dw1t = nc.dram_tensor("w1t", [D, D], F32, kind="ExternalInput").ap()
    ddisha = nc.dram_tensor("disha_p", [64, D], F32, kind="ExternalInput").ap()
    dpsel = nc.dram_tensor("psel", [128, 64], F32, kind="ExternalInput").ap()
    dwqk1 = nc.dram_tensor("wqk1", [D, 64], F32, kind="ExternalInput").ap()
    dwv1 = nc.dram_tensor("wv1", [D, D], F32, kind="ExternalInput").ap()
    dwo1 = nc.dram_tensor("wo1", [D, D], F32, kind="ExternalInput").ap()
    dsel16 = nc.dram_tensor("sel16", [16, 8 * 128], F32, kind="ExternalInput").ap()
    dmask = nc.dram_tensor("oddmask", [128, 1], F32, kind="ExternalInput").ap()
    dident = nc.dram_tensor("ident", [128, 128], F32, kind="ExternalInput").ap()

    dln0 = nc.dram_tensor("ln0d", [D, T], BF16, kind="Internal").ap()
    dx1 = nc.dram_tensor("x1d", [D, TL], F32, kind="Internal").ap()
    dx2 = nc.dram_tensor("x2d", [D, TL], F32, kind="Internal").ap()
    dx3 = nc.dram_tensor("x3d", [D, TL], F32, kind="Internal").ap()
    dln1 = nc.dram_tensor("ln1d", [D, TL], BF16, kind="Internal").ap()
    dln2 = nc.dram_tensor("ln2d", [D, TL], BF16, kind="Internal").ap()
    dse = nc.dram_tensor("se_raw", [D, 1024], F32, kind="ExternalInput").ap()
    dng = nc.dram_tensor("ng", [128, 8], F32, kind="ExternalInput").ap()
    dnb = nc.dram_tensor("nb", [128, 8], F32, kind="ExternalInput").ap()
    dsg = nc.dram_tensor("sg", [128, 8], F32, kind="ExternalInput").ap()
    dsbb = nc.dram_tensor("sbb", [128, 8], F32, kind="ExternalInput").ap()
    dscale = nc.dram_tensor("scale1", [1, 1], F32, kind="ExternalInput").ap()
    dwg = nc.dram_tensor("wg", [2, D, 2048], F32, kind="ExternalInput").ap()
    dwu = nc.dram_tensor("wu", [2, D, 2048], F32, kind="ExternalInput").ap()
    dwd = nc.dram_tensor("wd", [2, 2048, D], F32, kind="ExternalInput").ap()
    dbgc = nc.dram_tensor("bgc", [128, 32], F32, kind="ExternalInput").ap()
    dbuc = nc.dram_tensor("buc", [128, 32], F32, kind="ExternalInput").ap()
    dbdr = nc.dram_tensor("bdr", [2, 1024], F32, kind="ExternalInput").ap()
    dln3 = nc.dram_tensor("ln3d", [D, TL], BF16, kind="Internal").ap()
    dse_ln = nc.dram_tensor("se_lnd", [D, 1024], BF16, kind="Internal").ap()
    dsl_in = nc.dram_tensor("sl_in", [1024, 1032], BF16, kind="Internal").ap()
    dsl_out = nc.dram_tensor("sl_out", [1024 * NCORES, 1032], BF16, kind="Internal").ap()
    deo_in = nc.dram_tensor("eo_in", [512, 1024], BF16, kind="Internal").ap()
    deo_out = nc.dram_tensor("eo_out", [512 * NCORES, 1024], BF16, kind="Internal").ap()
    dcar_in = nc.dram_tensor("car_in", [128, 24], F32, kind="Internal").ap()
    dcar_out = nc.dram_tensor("car_out", [128 * 2, 24], F32, kind="Internal").ap()

    ar = Arena(nc)
    ones128_bf = ar.alloc([128, 1], BF16, "ones128b")
    ones1x128 = ar.alloc([1, 128], F32, "ones1x128")
    ones1x128b = ar.alloc([1, 128], BF16, "ones1x128b")
    consts = ar.alloc([128, 4], F32, "consts")
    sel_sb = ar.alloc([128, 128], BF16, "sel_sb")
    id_sb = ar.alloc([128, 128], BF16, "id_sb")
    bq_sb = ar.alloc([128, 8], F32, "bq_sb")
    bk_sb = ar.alloc([128, 8], F32, "bk_sb")
    bo_sb = ar.alloc([128, 8], F32, "bo_sb")
    ar.mark()

    psAB = nc.alloc_psum_tensor("psAB", [128, 2048], F32)
    psA = psAB[:, 0:1024]
    psB = psAB[:, 1024:2048]
    psC = nc.alloc_psum_tensor("psC", [128, 512], F32)
    psD = nc.alloc_psum_tensor("psD", [128, 512], F32)
    psE = nc.alloc_psum_tensor("psE", [128, 512], F32)
    psF = nc.alloc_psum_tensor("psF", [128, 512], F32)
    psT = psF

    cnt = p.cnt

    def MM(out, lhsT, rhs, start, stop, waits=(), inc=None, tp=None):
        return p.emit("tensor",
                      (lambda e, o=out, l=lhsT, r=rhs, s0=start, s1=stop, t=tp:
                       e.matmul(o, l, r, start=s0, stop=s1, tile_position=t)),
                      waits, inc)

    def CP(eng, out, in_, waits=(), inc=None):
        return p.emit(eng, (lambda e, o=out, i=in_: e.tensor_copy(o, i)), waits, inc)

    def TT(out, a, b, op, waits=(), inc=None, eng="vector"):
        return p.emit(eng, (lambda e, o=out, x=a, y=b, f=op: e.tensor_tensor(o, x, y, f)),
                      waits, inc)

    def TS(out, in_, s1, s2, op0, op1=None, waits=(), inc=None, eng="vector"):
        if op1 is None:
            op1 = ALU.bypass
        return p.emit(eng, (lambda e, o=out, i=in_, a=s1, b=s2, f0=op0, f1=op1:
                            e.tensor_scalar(o, i, a, b, f0, f1)), waits, inc)

    def STT(out, in0, scalar, in1, op0, op1, waits=(), inc=None):
        return p.emit("vector", (lambda e, o=out, i=in0, s=scalar, j=in1, f0=op0, f1=op1:
                                 e.scalar_tensor_tensor(o, i, s, j, f0, f1)), waits, inc)

    def ACTV(out, in_, func, bias=None, scale=None, waits=(), inc=None,
             chain=True):
        kw = {}
        if bias is not None:
            kw["bias"] = bias
        if scale is not None:
            kw["scale"] = scale
        return p.emit("scalar", (lambda e, o=out, i=in_, f=func, kw=kw:
                                 e.activation(o, i, f, **kw)), waits, inc,
                      chain=chain)

    def RECIP(out, in_, waits=(), inc=None):
        return p.emit("vector", (lambda e, o=out, i=in_: e.reciprocal(o, i)), waits, inc)

    def RED(out, in_, op, waits=(), inc=None, negate=False):
        return p.emit("vector", (lambda e, o=out, i=in_, f=op, n=negate:
                                 e.tensor_reduce(o, i, AX.X, f, negate=n)), waits, inc)

    def MS(ap, val, waits=(), inc=None):
        return p.emit("gpsimd", (lambda e, a=ap, v=val: e.memset(a, v)), waits, inc)

    def SCAN(out, d0, d1, init, waits=(), inc=None):
        return p.emit("vector", (lambda e, o=out, a=d0, b=d1, i=init:
                                 e.tensor_tensor_scan(o, a, b, i, ALU.add, ALU.add)),
                      waits, inc)

    p.stage("init")
    MS(ones128_bf[:, :], 1.0)
    MS(ones1x128[:, :], 1.0)
    MS(ones1x128b[:, :], 1.0)
    MS(consts[:, 0:1], 1.0 / 1024.0)
    MS(consts[:, 1:2], EPS_LN)
    MS(consts[:, 2:3], 1e-6)
    MS(consts[:, 3:4], float(np.pi / 2))
    p.dma("gpsimd", sel_sb[:, :], dsel, inc="g_tab")
    g_id = p.dma("gpsimd", id_sb[:, :], dident, inc="g_tab")
    p.dma("sync", bq_sb[:, :], dbq, inc="s_tab")
    p.dma("sync", bk_sb[:, :], dbk, inc="s_tab")
    tab_s = p.dma("sync", bo_sb[:, :], dbo, inc="s_tab")

    p.stage("ln0")
    ar.reset()
    _lnbuf_n = [0]

    def mk_ln_bufs(dbl=True):
        i = _lnbuf_n[0]
        _lnbuf_n[0] += 2
        lnc = [ar.alloc([128, NDT * 512], BF16, f"lnc{i}")]
        lnc.append(ar.alloc([128, NDT * 512], BF16, f"lnd{i}") if dbl else lnc[0])
        return dict(
            xcf=[ar.alloc([128, NDT * 512], F32, f"xcf{i+j}") for j in range(2)],
            xcb=[ar.alloc([128, NDT * 512], BF16, f"xcb{i+j}") for j in range(2)],
            sqb=ar.alloc([128, NDT * 512], BF16, f"sqb{i}"),
            lnc=lnc, dbl=dbl,
            tmpf=ar.alloc([128, 512], F32, f"lntmp{i}"),
            rows=ar.alloc([1, 2048], F32, f"rows{i}"),
            rows2=ar.alloc([1, 2048], F32, f"rows2_{i}"))

    def emit_ln(src_dram, n_tok, dst_dram, bufs, gten=None, bten=None,
                postscale=None, src_ready=(), src_ready_fn=None, prev_st=None):
        xcf, xcb = bufs["xcf"], bufs["xcb"]
        sqb, tmpf = bufs["sqb"], bufs["tmpf"]
        lncs, dbl = bufs["lnc"], bufs["dbl"]
        rows, rows2 = bufs["rows"], bufs["rows2"]
        """Feature-major LN over d=1024, pipelined over 512-token chunks."""
        nch = n_tok // 512
        st_chunks = []
        st_par = dict(prev_st) if prev_st else {0: None, 1: None}
        base = _lnbuf_n[0] - 2
        rel_xcb = [None, None]
        rel_xcf = [None, None]
        rel_stats = [None, None]
        rel_sq = None
        rel_bcast = None
        for ch in range(nch):
            bi = ch % 2
            mean_bk = psE[0:1, 0:512] if bi == 0 else psD[0:1, 0:512]
            sq_bk = psC[0:1, 0:512] if bi == 0 else psB[0:1, 0:512]
            rset = rows if bi == 0 else rows2
            sl = slice(ch * 512, (ch + 1) * 512)
            src3 = src_dram.rearrange("(dt p) t -> p dt t", p=128)[:, :, sl]
            src_ready = src_ready_fn(ch) if src_ready_fn is not None else src_ready
            w = list(src_ready)
            if rel_xcf[bi]:
                w.append(("v", rel_xcf[bi]))
            xcf3 = xcf[bi].rearrange("p (dt t) -> p dt t", dt=NDT)
            ldf = p.dma("sync", xcf3[:, 0:4, :], src3[:, 0:4, :], waits=w,
                        inc=f"s_xcf{base + bi}")
            ldf2 = p.dma("scalar", xcf3[:, 4:8, :], src3[:, 4:8, :], waits=w,
                         inc=f"a_xcf{base + bi}")
            w = list(src_ready)
            if rel_xcb[bi]:
                w.append(("t", rel_xcb[bi]))
            ldb = p.dma("gpsimd", xcb[bi].rearrange("p (dt t) -> p dt t", dt=NDT),
                        src3, waits=w, inc=f"g_xcb{base + bi}")
            aq = None
            for dt in range(NDT):
                wq = []
                if dt == 0:
                    wq = [(f"g_xcb{base + bi}", ldb)]
                    if rel_sq:
                        wq.append(("t", rel_sq))
                aq = ACTV(sqb[:, dt * 512:(dt + 1) * 512],
                          xcb[bi][:, dt * 512:(dt + 1) * 512], AF.Square,
                          waits=wq, inc="a")
            for dt in range(NDT):
                wm = []
                if dt == 0:
                    wm = [(f"g_xcb{base + bi}", ldb)]
                    if rel_stats[bi]:
                        wm.append(("v", rel_stats[bi]))
                MM(mean_bk, ones128_bf[:, :], xcb[bi][:, dt * 512:(dt + 1) * 512],
                   start=(dt == 0), stop=(dt == NDT - 1), waits=wm)
            tmm = None
            for dt in range(NDT):
                wm = [("a", aq)] if dt == 0 else []
                tmm = MM(sq_bk, ones128_bf[:, :], sqb[:, dt * 512:(dt + 1) * 512],
                         start=(dt == 0), stop=(dt == NDT - 1), waits=wm,
                         inc=("t" if dt == NDT - 1 else None))
            rel_sq = tmm
            rel_xcb[bi] = tmm
            r_mean, r_srow, r_m2, r_var = (rset[0:1, 0:512], rset[0:1, 512:1024],
                                           rset[0:1, 1024:1536], rset[0:1, 1536:2048])
            TS(r_mean, mean_bk, consts[0:1, 0:1], None, ALU.mult, waits=[("t", tmm)])
            TS(r_srow, sq_bk, consts[0:1, 0:1], None, ALU.mult)
            TT(r_m2, r_mean, r_mean, ALU.mult)
            vv = p.emit("vector", (lambda e, a=r_var, b=r_srow, c=r_m2:
                                   e.tensor_sub(a, b, c)), inc="v")
            rel_stats[bi] = vv
            ACTV(r_m2, r_var, AF.Ln, bias=consts[0:1, 1:2], waits=[("v", vv)])
            av = ACTV(r_var, r_m2, AF.Exp, scale=-0.5, inc="a")
            vm = TT(r_m2, r_mean, r_var, ALU.mult, waits=[("a", av)], inc="v")
            wb = [("v", vm)]
            if rel_bcast:
                wb.append(("v", rel_bcast))
            MM(psA[:, 0:512], ones1x128[:, :], r_var, start=True, stop=True, waits=wb)
            tb = MM(psA[:, 512:1024], ones1x128[:, :], r_m2, start=True, stop=True,
                    inc="t")
            lnc = lncs[ch % 2]
            vo = None
            for dt in range(NDT):
                wv_ = [("t", tb)]
                if dt == 0:
                    wv_.append((f"s_xcf{base + bi}", ldf))
                    pw = st_par[ch % 2] if dbl else st_par[(ch - 1) % 2]
                    if pw is not None:
                        wv_.append(pw)
                if dt == 4:
                    wv_.append((f"a_xcf{base + bi}", ldf2))
                TT(tmpf[:, :], xcf[bi][:, dt * 512:(dt + 1) * 512], psA[:, 0:512],
                   ALU.mult, waits=wv_)
                if gten is None and postscale is None:
                    vo = TT(lnc[:, dt * 512:(dt + 1) * 512], tmpf[:, :],
                            psA[:, 512:1024], ALU.subtract, inc="v")
                else:
                    TT(tmpf[:, :], tmpf[:, :], psA[:, 512:1024], ALU.subtract)
                    if gten is not None and postscale is None:
                        vo = TS(lnc[:, dt * 512:(dt + 1) * 512], tmpf[:, :],
                                gten[:, dt:dt + 1], bten[:, dt:dt + 1],
                                ALU.mult, ALU.add, inc="v")
                    else:
                        TS(tmpf[:, :], tmpf[:, :], gten[:, dt:dt + 1],
                           bten[:, dt:dt + 1], ALU.mult, ALU.add)
                        vo = TS(lnc[:, dt * 512:(dt + 1) * 512], tmpf[:, :],
                                postscale, None, ALU.mult, inc="v")
            rel_bcast = vo
            rel_xcf[bi] = vo
            stn = f"ls{ch % 2}"
            stc = p.dma("sync",
                        dst_dram.rearrange("(dt p) t -> p dt t", p=128)[:, :, sl],
                        lnc.rearrange("p (dt t) -> p dt t", dt=NDT),
                        waits=[("v", vo)], inc=stn)
            st_par[ch % 2] = (stn, stc)
            st_chunks.append((stn, stc))
        return st_chunks

    b0 = mk_ln_bufs()
    ln0_ch = emit_ln(dx, T, dln0, b0)
    ln0_done = ln0_ch[-1]
    LN0S = "s_lnst"

    NKT = T // 128
    qc_n = TL // 512

    p.stage("l0v")
    ar.reset()
    v_sb = ar.alloc([128, NKT * 1040], BF16, "v_sb")
    cs_sb = ar.alloc([128, T], BF16, "cs_sb")
    ss_sb = ar.alloc([128, T], BF16, "ss_sb")
    wo_sb = ar.alloc([128, NDT * 1024], BF16, "wo_sb")
    ao_sb = ar.alloc([128, 8 * TL], BF16, "ao_sb")
    r2_sb = ar.alloc([128, 1024], BF16, "r2_sb")
    perm_sb = ar.alloc([128, 128], BF16, "perm_sb")
    ar.mark("attn")
    wv_sb = ar.alloc([128, NDT * 1040], BF16, "wv_sb")
    lnk = [ar.alloc([128, NDT * 128], BF16, f"lnk{i}") for i in range(4)]
    bvr_sb = ar.alloc([1, H * 65], BF16, "bvr_sb")

    pl_r2 = MS(r2_sb[:, :], 0.0, inc="pl")
    g_wv = p.dma("gpsimd", wv_sb.rearrange("p (dt c) -> p dt c", dt=NDT),
                 dwv65.rearrange("(dt p) c -> p dt c", p=128), inc="g_wv")
    g_bv = p.dma("gpsimd", bvr_sb[:, :], dbv, inc="g_tab")
    g_pm = p.dma("gpsimd", perm_sb[:, :], dpermm, inc="g_tab")
    g_cs = p.dma("gpsimd", cs_sb[:, :], dcs, inc="g_cs")
    a_ss = p.dma("gpsimd", ss_sb[:, :], dss, inc="g_ss")
    a_wo = p.dma("gpsimd", wo_sb.rearrange("p (kt d) -> p kt d", kt=NDT),
                 dwoT.rearrange("(kt p) d -> p kt d", p=128), inc="g_wo")

    ve = None
    rel_lnk = [None] * 4
    cp_hist = [None, None]
    for kt in range(NKT):
        bi = kt % 4
        pb = psA if kt % 2 == 0 else psB
        pc = psC if kt % 2 == 0 else psD
        qd = "sync" if bi % 2 == 0 else "gpsimd"
        w = [ln0_ch[kt // 4]]
        if rel_lnk[bi]:
            w.append(("t", rel_lnk[bi]))
        lk = p.dma(qd, lnk[bi].rearrange("p (dt t) -> p dt t", dt=NDT),
                   dln0.rearrange("(dt p) t -> p dt t", p=128)[:, :, kt * 128:(kt + 1) * 128],
                   waits=w, inc=f"lk{bi}")
        for dt in range(NDT):
            w0 = []
            if dt == 0:
                w0 = [(f"lk{bi}", lk), ("g_wv", g_wv), ("g_tab", g_bv)]
                if kt < 2:
                    w0.append(ln0_done)
                if cp_hist[kt % 2]:
                    w0.append(("v", cp_hist[kt % 2]))
            MM(pb[:, 0:512], lnk[bi][:, dt * 128:(dt + 1) * 128],
               wv_sb[:, dt * 1040:dt * 1040 + 512], start=(dt == 0), stop=False,
               waits=w0)
            MM(pb[:, 512:1024], lnk[bi][:, dt * 128:(dt + 1) * 128],
               wv_sb[:, dt * 1040 + 512:dt * 1040 + 1024], start=(dt == 0), stop=False)
            MM(pc[:, 0:16], lnk[bi][:, dt * 128:(dt + 1) * 128],
               wv_sb[:, dt * 1040 + 1024:dt * 1040 + 1040], start=(dt == 0), stop=False)
        MM(pb[:, 0:512], ones1x128b[:, :], bvr_sb[:, 0:512], start=False, stop=True)
        MM(pb[:, 512:1024], ones1x128b[:, :], bvr_sb[:, 512:1024], start=False, stop=True)
        tv = MM(pc[:, 0:16], ones1x128b[:, :], bvr_sb[:, 1024:1040],
                start=False, stop=True, inc="t")
        rel_lnk[bi] = tv
        off = kt * 1040
        CP("vector", v_sb[:, off:off + 512], pb[:, 0:512], waits=[("t", tv)])
        CP("vector", v_sb[:, off + 512:off + 1024], pb[:, 512:1024])
        ve = CP("vector", v_sb[:, off + 1024:off + 1040], pc[:, 0:16], inc="v")
        cp_hist[kt % 2] = ve
    pl_ones = MS(v_sb.rearrange("p (kt h e) -> p kt h e", kt=NKT, h=H)[:, :, :, 64:65],
                 1.0, waits=[("v", ve)], inc="pl")
    b1_t = cnt["t"]

    p.stage("l0attn")
    ar.reset("attn")
    wq_sb = [ar.alloc([128, NDT * 128], BF16, f"wq_sb{i}") for i in range(2)]
    wk_sb = [ar.alloc([128, NDT * 128], BF16, f"wk_sb{i}") for i in range(2)]
    NLQ = 3
    lnqk = [ar.alloc([128, NDT * 512], BF16, f"lnqk{i}") for i in range(NLQ)]
    q_sbp = [ar.alloc([128, TL], BF16, f"q_sb{i}") for i in range(2)]
    k_sbp = [ar.alloc([128, T], BF16, f"k_sb{i}") for i in range(2)]
    rtmp = ar.alloc([128, 512], BF16, "rtmp")
    p2_sb = [ar.alloc([128, 2048], BF16, f"p2_sb{i}") for i in range(2)]
    rb_sb = ar.alloc([128, 512], F32, "rb_sb")
    xr_sb = [ar.overlay("wq_sb0", 0, [128, 512], F32, "xr_sb0"),
             ar.overlay("wq_sb1", 0, [128, 512], F32, "xr_sb1")]
    o_sb = [ar.overlay("wk_sb0", 0, [128, 512], F32, "o_sb0"),
            ar.overlay("wk_sb1", 0, [128, 512], F32, "o_sb1")]

    v_ready = [("pl", pl_ones)]
    rel_wq = [None, None]
    rel_wk = [None, None]
    rel_lnqk = [None] * NLQ
    rope_maps = {}
    st_fill = {"psF_war": None}
    nrm_q = []

    def emit_proj_rope_legacy(hp):
        bi = hp % 2
        q_sb, k_sb = q_sbp[bi], k_sbp[bi]
        g_wq = p.dma("gpsimd", wq_sb[bi].rearrange("p (dt m) -> p dt m", dt=NDT),
                     dwqT.rearrange("(dt p) m -> p dt m", p=128)[:, :, hp * 128:(hp + 1) * 128],
                     waits=[("t", b1_t)], inc=f"g_wq{bi}")
        g_wk = p.dma("gpsimd", wk_sb[bi].rearrange("p (dt m) -> p dt m", dt=NDT),
                     dwkT.rearrange("(dt p) m -> p dt m", p=128)[:, :, hp * 128:(hp + 1) * 128],
                     waits=[("t", b1_t)], inc=f"g_wk{bi}")
        vq_ts = [None] * qc_n
        vk_ts = [None] * (T // 512)
        pbanks = [psC[:, 0:512], psD[:, 0:512], psA[:, 0:512], psB[:, 0:512]]
        hv = cnt["v"]
        ts_hist = [hv, hv, hv, hv]
        gi = 0
        for tc in range(T // 512):
            cbi = tc % NLQ
            qd = "sync" if cbi % 2 == 0 else "gpsimd"
            wl = [ln0_ch[tc], ("t", rel_lnqk[cbi] or b1_t)]
            s_ln = p.dma(qd, lnqk[cbi].rearrange("p (dt t) -> p dt t", dt=NDT),
                         dln0.rearrange("(dt p) t -> p dt t", p=128)[:, :, tc * 512:(tc + 1) * 512],
                         waits=wl, inc=f"lq{cbi}")
            if tc < qc_n:
                bk = gi % 4
                for dt in range(NDT):
                    w0 = []
                    if dt == 0:
                        w0 = [(f"g_wq{bi}", g_wq), (f"lq{cbi}", s_ln),
                              ("v", ts_hist[bk])]
                    MM(pbanks[bk], wq_sb[bi][:, dt * 128:(dt + 1) * 128],
                       lnqk[cbi][:, dt * 512:(dt + 1) * 512],
                       start=(dt == 0), stop=(dt == NDT - 1), waits=w0,
                       inc=("t" if dt == NDT - 1 else None))
                vq_ts[tc] = TS(q_sb[:, tc * 512:(tc + 1) * 512], pbanks[bk],
                               bq_sb[:, hp:hp + 1], None, ALU.add,
                               waits=[("t", cnt["t"]), ("s_tab", tab_s)], inc="v")
                ts_hist[bk] = vq_ts[tc]
                gi += 1
            bk = gi % 4
            for dt in range(NDT):
                w0 = []
                if dt == 0:
                    w0 = [(f"g_wk{bi}", g_wk), (f"lq{cbi}", s_ln),
                          ("v", ts_hist[bk])]
                MM(pbanks[bk], wk_sb[bi][:, dt * 128:(dt + 1) * 128],
                   lnqk[cbi][:, dt * 512:(dt + 1) * 512],
                   start=(dt == 0), stop=(dt == NDT - 1), waits=w0,
                   inc=("t" if dt == NDT - 1 else None))
            vk_ts[tc] = TS(k_sb[:, tc * 512:(tc + 1) * 512], pbanks[bk],
                           bk_sb[:, hp:hp + 1], None, ALU.add,
                           waits=[("t", cnt["t"]), ("s_tab", tab_s)], inc="v")
            ts_hist[bk] = vk_ts[tc]
            gi += 1
            rel_lnqk[cbi] = cnt["t"]
            if tc == qc_n - 1:
                rel_wq[bi] = cnt["t"]
            if tc == T // 512 - 1:
                rel_wk[bi] = cnt["t"]
        rope_k = [None] * (T // 512)
        rtts = []
        pi = 0
        for (srcb, nch) in ((q_sb, qc_n), (k_sb, T // 512)):
            for tc in range(nch):
                sl = slice(tc * 512, (tc + 1) * 512)
                pbank = psC if pi % 2 == 0 else psD
                w0 = [("v", vk_ts[-1] if pi < 2 else rtts[-2]), ("g_tab", g_pm)]
                tpm = MM(pbank[:, 0:512], perm_sb[:, :], srcb[:, sl],
                         start=True, stop=True, waits=w0, inc="t")
                t1 = TT(rtmp[:, :], pbank[:, 0:512], ss_sb[:, sl], ALU.mult,
                        waits=[("t", tpm), ("g_ss", a_ss), ("g_cs", g_cs)])
                rtts.append(t1)
                TT(srcb[:, sl], srcb[:, sl], cs_sb[:, sl], ALU.mult)
                vr = TT(srcb[:, sl], srcb[:, sl], rtmp[:, :], ALU.add, inc="v")
                if srcb is k_sb:
                    rope_k[tc] = vr
                pi += 1
        rope_maps[hp] = rope_k
        return rope_k[T // 512 - 1]

    def mk_fillers(hp):
        bi = hp % 2
        qs, ks = q_sbp[bi], k_sbp[bi]
        g_wq = p.dma("gpsimd", wq_sb[bi].rearrange("p (dt m) -> p dt m", dt=NDT),
                     dwqT.rearrange("(dt p) m -> p dt m", p=128)[:, :, hp * 128:(hp + 1) * 128],
                     waits=[("t", rel_wq[bi] or b1_t)], inc=f"g_wq{bi}")
        g_wk = p.dma("gpsimd", wk_sb[bi].rearrange("p (dt m) -> p dt m", dt=NDT),
                     dwkT.rearrange("(dt p) m -> p dt m", p=128)[:, :, hp * 128:(hp + 1) * 128],
                     waits=[("t", rel_wk[bi] or b1_t)], inc=f"g_wk{bi}")
        ld = {}

        def emit_ld(tc):
            cbi = tc % NLQ
            qd = "sync" if cbi % 2 == 0 else "gpsimd"
            wl = [ln0_ch[tc], ("t", rel_lnqk[cbi] or b1_t)]
            ld[tc] = p.dma(qd, lnqk[cbi].rearrange("p (dt t) -> p dt t", dt=NDT),
                           dln0.rearrange("(dt p) t -> p dt t", p=128)[:, :, tc * 512:(tc + 1) * 512],
                           waits=wl, inc=f"lq{cbi}")

        for tc in range(NLQ):
            emit_ld(tc)
        rope_maps[hp] = [None] * (T // 512)
        fl = []

        def mk_proj(kind, tc):
            cbi = tc % NLQ
            wsb = wq_sb[bi] if kind == "q" else wk_sb[bi]
            gw = g_wq if kind == "q" else g_wk
            gw_sem = f"g_wq{bi}" if kind == "q" else f"g_wk{bi}"
            dst = qs if kind == "q" else ks
            bias = bq_sb if kind == "q" else bk_sb
            for dt in range(NDT):
                def f(dt=dt, cbi=cbi, wsb=wsb, gw=gw, gw_sem=gw_sem, tc=tc,
                      kind=kind):
                    w0 = []
                    if dt == 0:
                        w0 = [(gw_sem, gw), (f"lq{cbi}", ld[tc])]
                        if st_fill["psF_war"]:
                            w0.append(("v", st_fill["psF_war"]))
                    MM(psF[:, 0:512], wsb[:, dt * 128:(dt + 1) * 128],
                       lnqk[cbi][:, dt * 512:(dt + 1) * 512],
                       start=(dt == 0), stop=(dt == NDT - 1), waits=w0,
                       inc=("t" if dt == NDT - 1 else None))
                    if dt == NDT - 1:
                        if kind == "k":
                            rel_lnqk[cbi] = cnt["t"]
                            if tc + NLQ < T // 512:
                                emit_ld(tc + NLQ)
                            if tc == T // 512 - 1:
                                rel_wk[bi] = cnt["t"]
                        elif tc == qc_n - 1:
                            rel_wq[bi] = cnt["t"]
                fl.append(f)

            def fevac(kind=kind, tc=tc, dst=dst, bias=bias):
                ts = TS(dst[:, tc * 512:(tc + 1) * 512], psF[:, 0:512],
                        bias[:, hp:hp + 1], None, ALU.add,
                        waits=[("t", cnt["t"]), ("s_tab", tab_s)], inc="v")
                st_fill["psF_war"] = ts
            fl.append(fevac)

        def mk_rope(kind, tc):
            src = qs if kind == "q" else ks

            def f(kind=kind, tc=tc, src=src):
                sl = slice(tc * 512, (tc + 1) * 512)
                w0 = [("g_tab", g_pm), ("v", st_fill["psF_war"])]
                tpm = MM(psF[:, 0:512], perm_sb[:, :], src[:, sl],
                         start=True, stop=True, waits=w0, inc="t")
                t1 = TT(rtmp[:, :], psF[:, 0:512], ss_sb[:, sl], ALU.mult,
                        waits=[("t", tpm), ("g_ss", a_ss), ("g_cs", g_cs)])
                st_fill["psF_war"] = t1
                TT(src[:, sl], src[:, sl], cs_sb[:, sl], ALU.mult)
                vr = TT(src[:, sl], src[:, sl], rtmp[:, :], ALU.add, inc="v")
                if kind == "k":
                    rope_maps[hp][tc] = vr
            fl.append(f)

        for tc in range(T // 512):
            if tc < qc_n:
                mk_proj("q", tc)
            mk_proj("k", tc)
            if tc < qc_n:
                mk_rope("q", tc)
            mk_rope("k", tc)
        return fl

    fill_q = []
    evac_prev = None
    for hp in range(8):
        if hp == 0:
            evac_prev = emit_proj_rope_legacy(0)
        par = hp % 2
        qr_sb, kr_sb = q_sbp[par], k_sbp[par]
        rope_k = rope_maps[hp]
        fill_q = mk_fillers(hp + 1) if hp < 7 else []

        for qc in range(qc_n):
            a_hist = []
            tpv = None

            def emit_S(kt):
                sbuf = psA if kt % 2 == 0 else psB
                w0 = [("v", rope_k[kt // 4])] + v_ready
                if kt >= 2:
                    w0.append(("a", a_hist[kt - 2]))
                MM(sbuf[:, 0:512], kr_sb[0:64, kt * 128:(kt + 1) * 128],
                   qr_sb[0:64, qc * 512:(qc + 1) * 512], start=True, stop=True,
                   waits=w0, tp=(0, 0))
                ts = MM(sbuf[:, 512:1024], kr_sb[64:128, kt * 128:(kt + 1) * 128],
                        qr_sb[64:128, qc * 512:(qc + 1) * 512], start=True, stop=True,
                        tp=(64, 0), inc="t")
                ae = ACTV(p2_sb[kt % 2][:, 0:1024], sbuf[:, :], AF.Exp, scale=0.125,
                          waits=[("t", ts)], inc="a")
                a_hist.append(ae)

            def emit_PV(kt):
                nonlocal tpv
                pb = p2_sb[kt % 2]
                vofs = kt * 1040 + 130 * hp
                wpv = [("a", a_hist[kt])]
                if kt == 0:
                    wpv.append(("v", evac_prev))
                MM(psC[0:65, 0:512], v_sb[:, vofs:vofs + 65], pb[:, 0:512],
                   start=(kt == 0), stop=(kt == NKT - 1), waits=wpv)
                tpv = MM(psD[0:65, 0:512], v_sb[:, vofs + 65:vofs + 130],
                         pb[:, 512:1024], start=(kt == 0), stop=(kt == NKT - 1),
                         inc=("t" if kt == NKT - 1 else None))

            def emit_norms():
                while nrm_q:
                    aofs_n, sl2_n, ev_n = nrm_q.pop(0)
                    tbN = MM(psE[:, 0:512], sel_sb[:, :], r2_sb[:, sl2_n],
                             start=True, stop=True,
                             waits=[("v", ev_n), ("pl", pl_r2), ("g_tab", g_bv)],
                             inc="t")
                    RECIP(rb_sb[:, :], psE[:, 0:512], waits=[("t", tbN)])
                    TT(ao_sb[0:64, aofs_n:aofs_n + 512],
                       ao_sb[0:64, aofs_n:aofs_n + 512], rb_sb[0:64, :], ALU.mult)
                    TT(ao_sb[64:128, aofs_n:aofs_n + 512],
                       ao_sb[64:128, aofs_n:aofs_n + 512], rb_sb[64:128, :],
                       ALU.mult, inc="v")

            emit_S(0)
            for kt in range(1, NKT):
                if fill_q:
                    fill_q.pop(0)()
                emit_S(kt)
                emit_PV(kt - 1)
                if kt == 4:
                    emit_norms()
            emit_PV(NKT - 1)
            sl2 = slice((qc % 2) * 512, (qc % 2) * 512 + 512)
            aofs = hp * TL + qc * 512
            CP("vector", r2_sb[0:1, sl2], psC[64:65, 0:512], waits=[("t", tpv)])
            CP("vector", r2_sb[64:65, sl2], psD[64:65, 0:512])
            CP("vector", ao_sb[0:64, aofs:aofs + 512], psC[0:64, 0:512])
            evac_prev = CP("vector", ao_sb[64:128, aofs:aofs + 512],
                           psD[0:64, 0:512], inc="v")
            nrm_q.append((aofs, sl2, evac_prev))
        while fill_q:
            fill_q.pop(0)()

    while nrm_q:
        aofs_n, sl2_n, ev_n = nrm_q.pop(0)
        tbN = MM(psE[:, 0:512], sel_sb[:, :], r2_sb[:, sl2_n], start=True,
                 stop=True, waits=[("v", ev_n), ("pl", pl_r2), ("g_tab", g_bv)],
                 inc="t")
        RECIP(rb_sb[:, :], psE[:, 0:512], waits=[("t", tbN)])
        TT(ao_sb[0:64, aofs_n:aofs_n + 512], ao_sb[0:64, aofs_n:aofs_n + 512],
           rb_sb[0:64, :], ALU.mult)
        TT(ao_sb[64:128, aofs_n:aofs_n + 512],
           ao_sb[64:128, aofs_n:aofs_n + 512], rb_sb[64:128, :],
           ALU.mult, inc="v")

    ao_done = cnt["v"]
    stt_h = [None, None]
    st_h = [None, None]
    x1_tc = {}
    b4_rel = [ao_done] * 4
    for i, (dc, tc) in enumerate([(d, t) for d in range(NDT)
                                  for t in range(qc_n)]):
        b = i % 2
        pbk4 = [psC[:, 0:512], psD[:, 0:512], psA[:, 0:512], psB[:, 0:512]][i % 4]
        sx = p.dma("scalar", xr_sb[b][:, :],
                   dx[dc * 128:(dc + 1) * 128, tc * 512:(tc + 1) * 512],
                   waits=[("v", stt_h[b] if stt_h[b] else ao_done)],
                   inc=f"a_xr{b}")
        for kt in range(NDT):
            w0 = []
            if kt == 0:
                w0 = [("v", b4_rel[i % 4]), ("g_wo", a_wo)]
            MM(pbk4,
               wo_sb[:, kt * 1024 + dc * 128: kt * 1024 + (dc + 1) * 128],
               ao_sb[:, kt * TL + tc * 512: kt * TL + (tc + 1) * 512],
               start=(kt == 0), stop=(kt == NDT - 1), waits=w0,
               inc=("t" if kt == NDT - 1 else None))
        vo = STT(o_sb[b][:, :], pbk4, bo_sb[:, dc:dc + 1], xr_sb[b][:, :],
                 ALU.add, ALU.add,
                 waits=[("t", cnt["t"]), (f"a_xr{b}", sx), ("s_tab", tab_s)] +
                       ([(f"x1s{b}", st_h[b])] if st_h[b] else []),
                 inc="v")
        b4_rel[i % 4] = vo
        stt_h[b] = vo
        st_h[b] = p.dma("sync", dx1[dc * 128:(dc + 1) * 128, tc * 512:(tc + 1) * 512],
                        o_sb[b][:, :], waits=[("v", vo)], inc=f"x1s{b}")
        if dc == NDT - 1:
            x1_tc[tc] = [("x1s0", cnt["x1s0"]), ("x1s1", cnt["x1s1"])]

    p.stage("ffn0")
    ar.reset()
    bC = mk_ln_bufs()
    ln1_ch = emit_ln(dx1, TL, dln1, bC,
                     src_ready_fn=lambda ch: x1_tc[ch])
    ln1_done = ln1_ch[-1]
    w1_sb = ar.alloc([128, NDT * 1024], BF16, "w1_sb")
    dsh_sb = ar.alloc([64, 1024], BF16, "dsh_sb")
    ps_sb = ar.alloc([128, 64], BF16, "ps_sb")
    lnc1 = bC["xcb"] + [ar.alloc([128, NDT * 512], BF16, f"lnc1x{i}")
                        for i in range(2)]
    pool_sb = ar.alloc([64, TL], BF16, "pool_sb")
    xc1_sb = [ar.alloc([128, 512], F32, f"xc1_sb{i}") for i in range(2)]
    oc1_sb = [ar.alloc([128, 512], F32, f"oc1_sb{i}") for i in range(2)]

    g_w1 = p.dma("gpsimd", w1_sb.rearrange("p (dt m) -> p dt m", dt=NDT),
                 dw1t.rearrange("(dt p) m -> p dt m", p=128), inc="g_tab")
    g_dsh = p.dma("gpsimd", dsh_sb[:, :], ddisha, inc="g_tab")
    g_ps = p.dma("gpsimd", ps_sb[:, :], dpsel, inc="g_tab")

    lc1_ld = [None] * qc_n
    for tc in range(qc_n):
        qd = "sync" if tc % 2 == 0 else "gpsimd"
        lc1_ld[tc] = p.dma(qd, lnc1[tc].rearrange("p (dt t) -> p dt t", dt=NDT),
                           dln1.rearrange("(dt p) t -> p dt t", p=128)[:, :, tc * 512:(tc + 1) * 512],
                           waits=[ln1_ch[tc + 2] if tc < 2 else ln1_ch[tc]],
                           inc=f"lc1{tc}")
        for dt in range(NDT):
            w0 = []
            if dt == 0:
                w0 = [(f"lc1{tc}", lc1_ld[tc]), ("g_tab", g_ps), ("v", cnt["v"])]
            MM(psE[0:64, 0:512], ps_sb[:, :], lnc1[tc][:, dt * 512:(dt + 1) * 512],
               start=(dt == 0), stop=(dt == NDT - 1), waits=w0,
               inc=("t" if dt == NDT - 1 else None))
        CP("vector", pool_sb[:, tc * 512:(tc + 1) * 512], psE[0:64, 0:512],
           waits=[("t", cnt["t"])])
    ctt_h = [None, None]
    cst_h = [None, None]
    x2_tc = {}
    c2_banks = [psC[:, 0:512], psD[:, 0:512], psA[:, 0:512], psB[:, 0:512]]
    c2_rel = [cnt["v"]] * 4
    for i, (tc, oc) in enumerate([(t, o) for t in range(qc_n)
                                  for o in range(NDT)]):
        b = i % 2
        pbk = c2_banks[i % 4]
        wx = (x1_tc[3] if i < 2 else [("v", ctt_h[b])])
        sx = p.dma("scalar", xc1_sb[b][:, :],
                   dx1[oc * 128:(oc + 1) * 128, tc * 512:(tc + 1) * 512],
                   waits=wx, inc=f"a_xr{b}")
        for dt in range(NDT):
            w0 = []
            if dt == 0:
                w0 = [(f"lc1{tc}", lc1_ld[tc]), ("g_tab", g_w1),
                      ("v", c2_rel[i % 4])]
            MM(pbk, w1_sb[:, dt * 1024 + oc * 128:dt * 1024 + (oc + 1) * 128],
               lnc1[tc][:, dt * 512:(dt + 1) * 512],
               start=(dt == 0), stop=False, waits=w0)
        MM(pbk, dsh_sb[0:64, oc * 128:(oc + 1) * 128],
           pool_sb[0:64, tc * 512:(tc + 1) * 512],
           start=False, stop=True, inc="t")
        vo = TT(oc1_sb[b][:, :], pbk, xc1_sb[b][:, :], ALU.add,
                waits=[("t", cnt["t"]), (f"a_xr{b}", sx)] +
                      ([(f"x2s{b}", cst_h[b])] if cst_h[b] else []),
                inc="v")
        c2_rel[i % 4] = vo
        ctt_h[b] = vo
        cst_h[b] = p.dma("sync", dx2[oc * 128:(oc + 1) * 128, tc * 512:(tc + 1) * 512],
                         oc1_sb[b][:, :], waits=[("v", vo)], inc=f"x2s{b}")
        if oc == NDT - 1:
            x2_tc[tc] = [("x2s0", cnt["x2s0"]), ("x2s1", cnt["x2s1"])]

    p.stage("l1attn")
    ar.reset()
    bD = mk_ln_bufs(dbl=False)
    ln2_ch = emit_ln(dx2, TL, dln2, bD,
                     src_ready_fn=lambda ch: x2_tc[ch])
    ln2_done = ln2_ch[-1]
    NHE = NDT
    wqk1_sb = ar.alloc([128, NDT * 64], BF16, "wqk1_sb")
    wv1_sb = ar.alloc([128, NDT * 1024], BF16, "wv1_sb")
    wo1_sb = ar.overlay("wv1_sb", 0, [128, NDT * 1024], BF16, "wo1_sb")
    sel16_sb = ar.alloc([16, 8 * 128], BF16, "sel16_sb")
    mask_sb = ar.alloc([128, 1], F32, "mask_sb")
    lnc2 = bD["xcb"] + [ar.alloc([128, NDT * 512], BF16, f"lnc2x{i}")
                        for i in range(2)]
    trig_tok = bD["sqb"]
    tqc = ar.alloc([16, TL], BF16, "tqc")
    tkc = ar.alloc([16, TL], BF16, "tkc")
    tqs = ar.alloc([16, TL], BF16, "tqs")
    tks = ar.alloc([16, TL], BF16, "tks")
    v1t_sb = ar.alloc([128, 512], F32, "v1t_sb")
    d4t_sb = ar.alloc([128, 512], F32, "d4t_sb")
    xc_sb = ar.alloc([128, NHE * TL], BF16, "xc_sb")
    xs_sb = ar.alloc([128, NHE * TL], BF16, "xs_sb")
    zer_sb = bD["lnc"][0]
    pub_sb = ar.alloc([128, 24], F32, "pub_sb")
    car_sb = ar.alloc([128, 24], F32, "car_sb")
    xdead = "xcf" + str(_lnbuf_n[0] - 2)
    kcc = ar.overlay(xdead, 0, [16, TL], BF16, "kcc")
    kcs = ar.overlay(xdead, TL * 2, [16, TL], BF16, "kcs")
    dent = ar.overlay(xdead, TL * 4, [16, TL], F32, "dent")
    dent2 = ar.overlay(xdead, TL * 12, [16, TL], BF16, "dent2")
    xdead2 = "xcf" + str(_lnbuf_n[0] - 1)
    a16 = ar.overlay(xdead2, 0, [16, TL], BF16, "a16")
    b16 = ar.overlay(xdead2, TL * 2, [16, TL], BF16, "b16")
    xr2_sb = [ar.overlay(xdead2, TL * 6, [128, 512], F32, "xr2_sb0"),
              ar.overlay(xdead2, TL * 7, [128, 512], F32, "xr2_sb1")]
    sqbD = "sqb" + str(_lnbuf_n[0] - 2)
    oc2_sb = [ar.overlay(sqbD, 0, [128, 512], F32, "oc2_sb0"),
              ar.overlay(sqbD, 2048, [128, 512], F32, "oc2_sb1")]

    g_wqk1 = p.dma("gpsimd", wqk1_sb.rearrange("p (dt m) -> p dt m", dt=NDT),
                   dwqk1.rearrange("(dt p) m -> p dt m", p=128), inc="g_tab")
    g_wv1 = p.dma("gpsimd", wv1_sb.rearrange("p (dt m) -> p dt m", dt=NDT),
                  dwv1.rearrange("(dt p) m -> p dt m", p=128), inc="g_tab")
    g_s16 = p.dma("gpsimd", sel16_sb[:, :], dsel16, inc="g_tab")
    g_msk = p.dma("gpsimd", mask_sb[:, :], dmask, inc="g_tab")
    MS(zer_sb[:, 0:TL], 0.0, waits=[ln2_done], inc="pl")
    zer_done = cnt["pl"]
    MS(pub_sb[:, :], 0.0, inc="pl")
    pub0_done = cnt["pl"]

    lc2_ld = [None] * qc_n
    for tc in range(qc_n):
        qd = "sync" if tc % 2 == 0 else "scalar"
        lc2_ld[tc] = p.dma(qd, lnc2[tc].rearrange("p (dt t) -> p dt t", dt=NDT),
                           dln2.rearrange("(dt p) t -> p dt t", p=128)[:, :, tc * 512:(tc + 1) * 512],
                           waits=[ln2_ch[tc + 2] if tc < 2 else ln2_ch[tc]],
                           inc=f"lc2{tc}")
    for tch in range(16):
        tc4 = tch // 4
        co = (tch % 4) * 128
        for dt in range(NDT):
            w0 = []
            if dt == 0:
                w0 = [(f"lc2{tc4}", lc2_ld[tc4]), ("g_tab", g_wqk1), ("a", cnt["a"])]
            MM(psE[:, 0:64], lnc2[tc4][:, dt * 512 + co:dt * 512 + co + 128],
               wqk1_sb[:, dt * 64:(dt + 1) * 64],
               start=(dt == 0), stop=(dt == NDT - 1), waits=w0,
               inc=("t" if dt == NDT - 1 else None))
        ACTV(trig_tok[:, tch * 128:tch * 128 + 48], psE[:, 0:48], AF.Tanh,
             waits=[("t", cnt["t"])])
        ACTV(trig_tok[:, tch * 128:tch * 128 + 48],
             trig_tok[:, tch * 128:tch * 128 + 48], AF.Sin,
             scale=float(np.pi / 4))
        ACTV(trig_tok[:, tch * 128 + 64:tch * 128 + 112], psE[:, 0:48], AF.Tanh)
        ACTV(trig_tok[:, tch * 128 + 64:tch * 128 + 112],
             trig_tok[:, tch * 128 + 64:tch * 128 + 112], AF.Sin,
             scale=float(np.pi / 4), bias=consts[:, 3:4], inc="a")
    av = cnt["a"]
    for tch in range(16):
        w0 = [("v", cnt["v"])] if tch else []
        if tch == 0:
            w0 = [("a", av), ("g_tab", g_id)]
        p.emit("tensor",
               (lambda e, o=psT[0:48, 0:128],
                i=trig_tok[:, tch * 128 + 64:tch * 128 + 112],
                idn=id_sb[:, :]: e.matmul(o, i, idn, start=True, stop=True)), waits=w0)
        p.emit("tensor",
               (lambda e, o=psT[0:48, 128:256],
                i=trig_tok[:, tch * 128:tch * 128 + 48],
                idn=id_sb[:, :]: e.matmul(o, i, idn, start=True, stop=True)), inc="t")
        tsl = slice(tch * 128, (tch + 1) * 128)
        CP("vector", tqc[:, tsl], psT[0:16, 0:128], waits=[("t", cnt["t"])])
        CP("vector", tkc[:, tsl], psT[32:48, 0:128])
        CP("vector", tqs[:, tsl], psT[0:16, 128:256])
        CP("vector", tks[:, tsl], psT[32:48, 128:256], inc="v")
    trig_done = cnt["v"]

    d2_vb = [psC[:, 0:512], psA[:, 0:512]]
    d2_bc = [(psE[:, 0:512], psD[:, 0:512]),
             (psB[:, 0:512], psB[:, 512:1024])]
    d2_rv = [cnt["v"]] * 2
    d2_rb = [cnt["v"]] * 2
    for kt in range(NHE):
        for tc in range(qc_n):
            j = (kt * qc_n + tc) % 2
            vb = d2_vb[j]
            bcA, bcB = d2_bc[j]
            for dt in range(NDT):
                w0 = []
                if dt == 0:
                    w0 = [(f"lc2{tc}", lc2_ld[tc]), ("g_tab", g_wv1),
                          ("v", d2_rv[j])]
                MM(vb, wv1_sb[:, dt * 1024 + kt * 128:dt * 1024 + (kt + 1) * 128],
                   lnc2[tc][:, dt * 512:(dt + 1) * 512],
                   start=(dt == 0), stop=(dt == NDT - 1), waits=w0,
                   inc=("t" if dt == NDT - 1 else None))
            MM(bcA, sel16_sb[:, kt * 128:(kt + 1) * 128],
               tkc[:, tc * 512:(tc + 1) * 512], start=True, stop=True,
               waits=[("v", trig_done), ("v", d2_rb[j]), ("g_tab", g_s16)])
            tbb = MM(bcB, sel16_sb[:, kt * 128:(kt + 1) * 128],
                     tks[:, tc * 512:(tc + 1) * 512], start=True, stop=True,
                     inc="t")
            CP("vector", v1t_sb[:, :], vb, waits=[("t", tbb)])
            d2_rv[j] = cnt["v"]
            TT(xc_sb[:, kt * TL + tc * 512:kt * TL + (tc + 1) * 512],
               v1t_sb[:, :], bcA, ALU.mult)
            d2_rb[j] = TT(xs_sb[:, kt * TL + tc * 512:kt * TL + (tc + 1) * 512],
                          v1t_sb[:, :], bcB, ALU.mult, inc="v")
        RED(pub_sb[:, kt:kt + 1], xc_sb[:, kt * TL:(kt + 1) * TL], ALU.add,
            waits=([("pl", pub0_done)] if kt == 0 else []))
        RED(pub_sb[:, 8 + kt:9 + kt], xs_sb[:, kt * TL:(kt + 1) * TL], ALU.add)
    d2_t = cnt["t"]
    wo13 = wo1_sb.rearrange("p (kt d) -> p kt d", kt=NDT)
    dwo13 = dwo1.rearrange("(kt p) d -> p kt d", p=128)
    g_wo1 = p.dma("gpsimd", wo13[:, 0:4, :], dwo13[:, 0:4, :],
                  waits=[("t", d2_t)], inc="g_wo1")
    s_wo1 = p.dma("gpsimd", wo13[:, 4:8, :], dwo13[:, 4:8, :],
                  waits=[("t", d2_t)], inc="s_wo1")
    RED(pub_sb[0:16, 16:17], tkc[:, :], ALU.add)
    vpub = RED(pub_sb[0:16, 17:18], tks[:, :], ALU.add, inc="v")
    s_pub = p.dma("sync", dcar_in, pub_sb[:, :], waits=[("v", vpub)], inc="s_fin")
    p.sem("cc")
    ccg = p.emit("gpsimd", (lambda e: e.collective_compute(
        "AllGather", ALU.bypass,
        replica_groups=[[2 * i, 2 * i + 1] for i in range(NCORES // 2)],
        ins=[dcar_in.opt()], outs=[dcar_out.opt()])),
        waits=[("s_fin", s_pub)], inc="cc", amt=1, dma=True)

    zcol = zer_sb[:, 0:1]
    v_pub = cnt["v"]
    SCAN(kcc[:, :], tkc[:, :], zer_sb[0:16, 0:TL], zer_sb[0:16, 0:1],
         waits=[("pl", zer_done)])
    SCAN(kcs[:, :], tks[:, :], zer_sb[0:16, 0:TL], zer_sb[0:16, 0:1])
    for kt in range(NHE):
        SCAN(xc_sb[:, kt * TL:(kt + 1) * TL], xc_sb[:, kt * TL:(kt + 1) * TL],
             zer_sb[:, 0:TL], zcol)
        SCAN(xs_sb[:, kt * TL:(kt + 1) * TL], xs_sb[:, kt * TL:(kt + 1) * TL],
             zer_sb[:, 0:TL], zcol)

    def read_partner(e, dst=car_sb, srcd=dcar_out):
        pid = e.partition_id()
        partner = nc.s_assert_within(
            e.compute_val(1 - pid % 2), 0, 1, skip_runtime_assert=True)
        return e.dma_start(dst[:, :],
                           srcd.rearrange("(r p) f -> r p f", r=2)[partner, :, :])
    p.sem("g_car")
    g_car = p.emit("gpsimd", read_partner, waits=[("cc", cnt["cc"])], inc="g_car",
                   amt=16, dma=True)
    vcar = TS(car_sb[:, :], car_sb[:, :], mask_sb[:, 0:1], None, ALU.mult,
              waits=[("g_car", g_car), ("g_tab", g_msk)], inc="v")

    STT(dent[:, :], kcc[:, :], car_sb[0:16, 16:17], tqc[:, :], ALU.add, ALU.mult,
        waits=[("v", vcar)])
    STT(dent2[:, :], kcs[:, :], car_sb[0:16, 17:18], tqs[:, :], ALU.add, ALU.mult)
    TT(dent[:, :], dent[:, :], dent2[:, :], ALU.add)
    TS(dent[:, :], dent[:, :], consts[0:16, 2:3], None, ALU.add)
    RECIP(dent2[:, :], dent[:, :])
    TT(a16[:, :], tqc[:, :], dent2[:, :], ALU.mult)
    vab = TT(b16[:, :], tqs[:, :], dent2[:, :], ALU.mult, inc="v")

    d4_rel = [None, None]
    for kt in range(NHE):
        for tc in range(qc_n):
            j = (kt * qc_n + tc) % 2
            bcA, bcB = d2_bc[j]
            sl = slice(tc * 512, (tc + 1) * 512)
            ksl = slice(kt * TL + tc * 512, kt * TL + (tc + 1) * 512)
            MM(bcA, sel16_sb[:, kt * 128:(kt + 1) * 128], a16[:, sl],
               start=True, stop=True,
               waits=[("v", d4_rel[j] if d4_rel[j] else cnt["v"])])
            tb2 = MM(bcB, sel16_sb[:, kt * 128:(kt + 1) * 128],
                     b16[:, sl], start=True, stop=True, inc="t")
            STT(v1t_sb[:, :], xc_sb[:, ksl], car_sb[:, kt:kt + 1], bcA,
                ALU.add, ALU.mult, waits=[("t", tb2)])
            STT(d4t_sb[:, :], xs_sb[:, ksl], car_sb[:, 8 + kt:9 + kt],
                bcB, ALU.add, ALU.mult)
            d4_rel[j] = TT(xc_sb[:, ksl], v1t_sb[:, :], d4t_sb[:, :], ALU.add,
                           inc="v")
    heads_done = cnt["v"]
    dtt_h = [None, None]
    dst_h = [None, None]
    x3_tc = {}
    for i, (dc, tc) in enumerate([(d, t) for d in range(NDT)
                                  for t in range(qc_n)]):
        b = i % 2
        sx = p.dma("scalar", xr2_sb[b][:, :],
                   dx2[dc * 128:(dc + 1) * 128, tc * 512:(tc + 1) * 512],
                   waits=[("v", dtt_h[b] if dtt_h[b] else heads_done)],
                   inc=f"a_xr{b}")
        for kt in range(NDT):
            w0 = []
            if kt == 0:
                w0 = [("v", heads_done), ("v", cnt["v"]),
                      ("g_wo1", g_wo1), ("s_wo1", s_wo1)]
            MM(psC[:, 0:512],
               wo1_sb[:, kt * 1024 + dc * 128:kt * 1024 + (dc + 1) * 128],
               xc_sb[:, kt * TL + tc * 512:kt * TL + (tc + 1) * 512],
               start=(kt == 0), stop=(kt == NDT - 1), waits=w0,
               inc=("t" if kt == NDT - 1 else None))
        vo = TT(oc2_sb[b][:, :], psC[:, 0:512], xr2_sb[b][:, :], ALU.add,
                waits=[("t", cnt["t"]), (f"a_xr{b}", sx)] +
                      ([(f"x3s{b}", dst_h[b])] if dst_h[b] else []),
                inc="v")
        dtt_h[b] = vo
        dst_h[b] = p.dma("sync", dx3[dc * 128:(dc + 1) * 128, tc * 512:(tc + 1) * 512],
                         oc2_sb[b][:, :], waits=[("v", vo)], inc=f"x3s{b}")
        if dc == NDT - 1:
            x3_tc[tc] = [("x3s0", cnt["x3s0"]), ("x3s1", cnt["x3s1"])]

    p.stage("moe")
    ar.reset()
    bE = mk_ln_bufs(dbl=False)
    _ie = _lnbuf_n[0] - 2
    lnc3 = bE["xcb"] + [ar.overlay(f"rows{_ie}", 0, [128, NDT * 512], BF16, "lnc3x0"),
                        ar.overlay(f"rows2_{_ie}", 0, [128, NDT * 512], BF16, "lnc3x1")]
    ng_sb = ar.alloc([128, 8], F32, "ng_sb")
    nb_sb = ar.alloc([128, 8], F32, "nb_sb")
    sg_sb = ar.alloc([128, 8], F32, "sg_sb")
    sbb_sb = ar.alloc([128, 8], F32, "sbb_sb")
    scol_sb = ar.alloc([128, 1], F32, "scol_sb")
    sc1_sb = ar.alloc([1, 1], F32, "sc1_sb")
    se_sb = ar.alloc([128, NDT * 1024], BF16, "se_sb")
    logF = ar.alloc([128, 8 * TL], F32, "logF")
    dD_sb = ar.alloc([128, 8 * TL], BF16, "dD_sb")
    mneg_sb = ar.alloc([128, 8], F32, "mneg_sb")
    mpos_sb = ar.alloc([128, 8], F32, "mpos_sb")
    sesum_sb = ar.alloc([128, 8], F32, "sesum_sb")
    cmneg_sb = ar.alloc([128, 1], F32, "cmneg_sb")
    crs_sb = ar.alloc([128, 1], F32, "crs_sb")
    bg_sb = ar.alloc([128, 32], F32, "bg_sb")
    bu_sb = ar.alloc([128, 32], F32, "bu_sb")
    bdr_sb = ar.alloc([1, 2048], BF16, "bdr_sb")
    xtok_sb = ar.overlay("logF", 0, [128, 16 * 1024], BF16, "xtok_sb")
    dtok_sb = ar.overlay("logF", 32768, [128, 16 * 1024], BF16, "dtok_sb")
    cfm_sb = ar.overlay("dD_sb", 0, [128, 8 * TL], BF16, "cfm_sb")
    wgx_sb = ar.overlay("logF", 0, [128, NDT * 2048], BF16, "wgx_sb")
    gt_sb = ar.overlay("logF", 32768, [128, 16 * 256], BF16, "gt_sb")
    gu_sb = ar.overlay("logF", 32768 + 8192, [128, 16 * 256], BF16, "gu_sb")
    slfm_sb = ar.overlay("logF", 32768 + 16384, [128, NDT * 512], BF16, "slfm_sb")
    bd_sb = ar.overlay("logF", 32768 + 24576, [128, 1024], F32, "bd_sb")
    eo_sb = ar.overlay("logF", 32768 + 28672, [128, 1024], BF16, "eo_sb")
    xdE = "xcf" + str(_lnbuf_n[0] - 2)
    eof_sb = ar.overlay(xdE, 0, [128, 8 * 1024], BF16, "eof_sb")
    xdE2 = "xcf" + str(_lnbuf_n[0] - 1)
    stg_sb = ar.overlay(xdE2, 0, [128, 1032], BF16, "stg_sb")
    cn_sb = ar.overlay(xdE2, 2080, [128, 1024], BF16, "cn_sb")
    prt2_sb = ar.overlay(xdE2, 4160, [128, 2080], BF16, "prt2")
    prt_sb = [prt2_sb[:, 0:1032], prt2_sb[:, 1040:2072]]
    combf_sb = ar.overlay(xdE2, 8352, [128, 1024], F32, "combf_sb")
    mnb_sb = ar.overlay(xdE2, 12448, [128, 8], BF16, "mnb_sb")
    mnr_sb = ar.overlay(xdE2, 12480, [128, 8], F32, "mnr_sb")
    sqbE = "sqb" + str(_lnbuf_n[0] - 2)
    xr3_sb = [ar.overlay(sqbE, 0, [128, 512], F32, "xr3_sb0"),
              ar.overlay(sqbE, 2048, [128, 512], F32, "xr3_sb1")]
    oc3_sb = [ar.overlay(sqbE, 4096, [128, 512], F32, "oc3_sb0"),
              ar.overlay(sqbE, 6144, [128, 512], F32, "oc3_sb1")]

    g_ng = p.dma("gpsimd", ng_sb[:, :], dng, inc="g_tab")
    p.dma("gpsimd", nb_sb[:, :], dnb, inc="g_tab")
    p.dma("gpsimd", sg_sb[:, :], dsg, inc="g_tab")
    p.dma("gpsimd", sbb_sb[:, :], dsbb, inc="g_tab")
    g_sc = p.dma("gpsimd", sc1_sb[:, :], dscale, inc="g_tab")
    MM(psE[:, 0:1], ones1x128[:, :], sc1_sb[:, :], start=True, stop=True,
       waits=[("g_tab", g_sc), ("v", cnt["v"])], inc="t")
    vsc = CP("vector", scol_sb[:, :], psE[:, 0:1], waits=[("t", cnt["t"])], inc="v")
    ln3_ch = emit_ln(dx3, TL, dln3, bE, gten=ng_sb, bten=nb_sb,
                     src_ready_fn=lambda ch: x3_tc[ch] + [("g_tab", g_ng)])
    ln3_done = ln3_ch[-1]
    seln_done = emit_ln(dse, 1024, dse_ln, bE, gten=sg_sb, bten=sbb_sb,
                        postscale=scol_sb[:, 0:1],
                        src_ready=[("t", cnt["t"]), ("v", cnt["v"]),
                                   ("a", cnt["a"])],
                        prev_st={0: ln3_ch[2], 1: ln3_ch[3]})[-1]
    g_se = p.dma("gpsimd", se_sb.rearrange("p (dt s) -> p dt s", dt=NDT),
                 dse_ln.rearrange("(dt p) s -> p dt s", p=128),
                 waits=[seln_done], inc="g_tab")
    g_bg = p.dma("gpsimd", bg_sb[:, :], dbgc, inc="g_tab")
    g_bu = p.dma("gpsimd", bu_sb[:, :], dbuc, inc="g_tab")
    g_bd = p.dma("gpsimd", bdr_sb.rearrange("o (e f) -> o e f", e=2),
                 dbdr.rearrange("(o e) f -> o e f", o=1), inc="g_tab")
    seln_t = cnt["t"]

    lc3_ld = [None] * qc_n
    for tc in range(qc_n):
        qd = "sync" if tc % 2 == 0 else "scalar"
        wl = [ln3_ch[tc], ("t", seln_t)]
        lc3_ld[tc] = p.dma(qd, lnc3[tc].rearrange("p (dt t) -> p dt t", dt=NDT),
                           dln3.rearrange("(dt p) t -> p dt t", p=128)[:, :, tc * 512:(tc + 1) * 512],
                           waits=wl, inc=f"lc3{tc}")
    for tc in range(qc_n):
        for st in range(NDT):
            pbk = psC if (st * qc_n + tc) % 2 == 0 else psD
            for dt in range(NDT):
                w0 = []
                if dt == 0:
                    w0 = [(f"lc3{tc}", lc3_ld[tc]), ("g_tab", g_se), ("v", cnt["v"])]
                MM(pbk[:, 0:512], se_sb[:, dt * 1024 + st * 128:dt * 1024 + (st + 1) * 128],
                   lnc3[tc][:, dt * 512:(dt + 1) * 512],
                   start=(dt == 0), stop=(dt == NDT - 1), waits=w0,
                   inc=("t" if dt == NDT - 1 else None))
            CP("vector", logF[:, st * TL + tc * 512:st * TL + (tc + 1) * 512],
               pbk[:, 0:512], waits=[("t", cnt["t"])], inc="v")
    for st in range(NDT):
        ssl = slice(st * TL, (st + 1) * TL)
        RED(mneg_sb[:, st:st + 1], logF[:, ssl], ALU.max, negate=True)
        CP("vector", mnb_sb[:, st:st + 1], mneg_sb[:, st:st + 1])
        va = CP("vector", mnr_sb[:, st:st + 1], mnb_sb[:, st:st + 1], inc="v")
        ACTV(dD_sb[:, ssl], logF[:, ssl], AF.Exp, bias=mnr_sb[:, st:st + 1],
             waits=[("v", va)], inc="a")
        RED(sesum_sb[:, st:st + 1], dD_sb[:, ssl], ALU.add,
            waits=[("a", cnt["a"])], inc="v")
    disp_done_a, disp_done_v = cnt["a"], cnt["v"]

    for tch in range(16):
        tc4, sub = tch // 4, tch % 4
        for g in range(2):
            for d4 in range(4):
                dt = g * 4 + d4
                w0 = [("v", cnt["v"])]
                if dt == 0:
                    w0 += [(f"lc3{tc4}", lc3_ld[tc4]), ("a", disp_done_a)]
                p.emit("tensor",
                       (lambda e, o=psT[:, d4 * 128:(d4 + 1) * 128],
                        i=lnc3[tc4][:, dt * 512 + sub * 128:dt * 512 + sub * 128 + 128],
                        idn=id_sb[:, :]: e.matmul(o, i, idn, start=True, stop=True)),
                       waits=w0, inc=("t" if d4 == 3 else None))
            CP("vector",
               xtok_sb[:, tch * 1024 + g * 512:tch * 1024 + (g + 1) * 512],
               psT[:, :],
               waits=[("t", cnt["t"]), ("v", disp_done_v), ("a", disp_done_a)],
               inc="v")
    for tch in range(16):
        for g in range(2):
            for s4 in range(4):
                st = g * 4 + s4
                w0 = [("v", cnt["v"])]
                p.emit("tensor",
                       (lambda e, o=psT[:, s4 * 128:(s4 + 1) * 128],
                        i=dD_sb[:, st * TL + tch * 128:st * TL + (tch + 1) * 128],
                        idn=id_sb[:, :]: e.matmul(o, i, idn, start=True, stop=True)),
                       waits=w0, inc=("t" if s4 == 3 else None))
            CP("vector",
               dtok_sb[:, tch * 1024 + g * 512:tch * 1024 + (g + 1) * 512],
               psT[:, :],
               waits=[("t", cnt["t"])], inc="v")
    tr_done = cnt["v"]

    MS(stg_sb[:, 1026:1032], 0.0, inc="pl")
    stg0 = cnt["pl"]
    for st in range(NDT):
        for tch in range(16):
            w0 = []
            if tch == 0:
                w0 = [("v", tr_done), ("v", cnt["v"])]
            MM(psA[:, 0:512], dtok_sb[:, tch * 1024 + st * 128:tch * 1024 + (st + 1) * 128],
               xtok_sb[:, tch * 1024:tch * 1024 + 512],
               start=(tch == 0), stop=(tch == 15), waits=w0)
            MM(psA[:, 512:1024], dtok_sb[:, tch * 1024 + st * 128:tch * 1024 + (st + 1) * 128],
               xtok_sb[:, tch * 1024 + 512:tch * 1024 + 1024],
               start=(tch == 0), stop=(tch == 15),
               inc=("t" if tch == 15 else None))
        t_slots = cnt["t"]
        CP("vector", stg_sb[:, 0:512], psA[:, 0:512],
           waits=[("t", cnt["t"]), ("pl", stg0)] +
                 ([("s_slag", cnt["s_slag"])] if cnt.get("s_slag") else []))
        CP("vector", stg_sb[:, 512:1024], psA[:, 512:1024])
        p.emit("vector", (lambda e, o=stg_sb[:, 1024:1025],
                          i=mnr_sb[:, st:st + 1]: e.tensor_scalar_mul(o, i, -1.0)))
        vstg = CP("vector", stg_sb[:, 1025:1026], sesum_sb[:, st:st + 1], inc="v")
        p.dma("sync", dsl_in[st * 128:(st + 1) * 128, :], stg_sb[:, :],
              waits=[("v", vstg)], inc="s_slag")
    p.sem("cc")
    cc_sl = p.emit("gpsimd", (lambda e: e.collective_compute(
        "AllGather", ALU.bypass, replica_groups=[list(range(NCORES))],
        ins=[dsl_in.opt()], outs=[dsl_out.opt()])),
        waits=[("s_slag", cnt["s_slag"])], inc="cc", amt=1, dma=True)

    p.sem("g_wx")
    p.sem("s_wx")

    def load_wx(dram_e, nch, war_t):
        dst = wgx_sb.rearrange("p (c f) -> p c f", c=nch)
        src = dram_e
        h = nch // 2
        wg_ = [("t", war_t)]
        if cnt["g_wx"]:
            wg_.append(("g_wx", cnt["g_wx"]))
        g = p.dma("gpsimd", dst[:, 0:h, :], src[:, 0:h, :], waits=wg_, inc="g_wx")
        ws_ = [("t", war_t)]
        if cnt["s_wx"]:
            ws_.append(("s_wx", cnt["s_wx"]))
        s = p.dma("gpsimd", dst[:, h:nch, :], src[:, h:nch, :], waits=ws_, inc="s_wx")
        return g, s, h

    wg0_pre = load_wx(dwg.rearrange("e (dt p) f -> e p dt f", p=128)[0],
                      NDT, t_slots)

    for tch in range(16):
        tc4, sub = tch // 4, tch % 4
        for dt in range(NDT):
            w0 = []
            if dt == 0:
                w0 = [(f"lc3{tc4}", lc3_ld[tc4]), ("v", cnt["v"]), ("a", cnt["a"])]
            MM(psB[:, 0:512], lnc3[tc4][:, dt * 512 + sub * 128:dt * 512 + sub * 128 + 128],
               se_sb[:, dt * 1024:dt * 1024 + 512],
               start=(dt == 0), stop=(dt == NDT - 1), waits=w0)
            MM(psB[:, 512:1024], lnc3[tc4][:, dt * 512 + sub * 128:dt * 512 + sub * 128 + 128],
               se_sb[:, dt * 1024 + 512:dt * 1024 + 1024],
               start=(dt == 0), stop=(dt == NDT - 1),
               inc=("t" if dt == NDT - 1 else None))
        RED(cmneg_sb[:, 0:1], psB[:, 0:1024], ALU.max, negate=True,
            waits=[("t", cnt["t"])])
        ACTV(cn_sb[:, :], psB[:, 0:1024], AF.Exp, bias=cmneg_sb[:, 0:1],
             waits=[("v", cnt["v"])], inc="a")
        RED(crs_sb[:, 0:1], cn_sb[:, :], ALU.add, waits=[("a", cnt["a"])])
        RECIP(crs_sb[:, 0:1], crs_sb[:, 0:1])
        TS(cn_sb[:, :], cn_sb[:, :], crs_sb[:, 0:1], None, ALU.mult, inc="v")
        for g in range(2):
            for s4 in range(4):
                st = g * 4 + s4
                w0 = [("v", cnt["v"])]
                p.emit("tensor",
                       (lambda e, o=psT[:, s4 * 128:(s4 + 1) * 128],
                        i=cn_sb[:, st * 128:(st + 1) * 128],
                        idn=id_sb[:, :]: e.matmul(o, i, idn, start=True, stop=True)),
                       waits=w0, inc=("t" if s4 == 3 else None))
            p.emit("vector",
                   (lambda e, tch=tch, g=g: e.tensor_copy(
                       cfm_sb.rearrange("p (st t) -> p st t", st=NDT)[:, 4 * g:4 * g + 4, tch * 128:(tch + 1) * 128],
                       psT.rearrange("p (s4 c) -> p s4 c", s4=4))),
                   waits=[("t", cnt["t"])], inc="v")
    cfm_done = cnt["v"]

    p.sem("g_sl")
    _dynidx = {}

    def get_me(e):
        if "me" not in _dynidx:
            pid = e.partition_id()
            _dynidx["me"] = nc.s_assert_within(pid, 0, 7, skip_runtime_assert=True)
        return _dynidx["me"]

    for b in range(4):
        def read_pair(e, b=b):
            me = get_me(e)
            return e.dma_start(
                prt2_sb.rearrange("p (r f) -> p r f", r=2)[:, :, 0:1032],
                dsl_out.rearrange("(r q p) f -> q p r f", q=NCORES, p=128)[me, :, 2 * b:2 * b + 2, :])
        p.emit("gpsimd", read_pair,
               waits=[("cc", cc_sl), ("g_sl", cnt["g_sl"])] +
                     ([("v", cnt["v"])] if b else []),
               inc="g_sl", amt=16, dma=True)
        g_pr = cnt["g_sl"]
        TT(cmneg_sb[:, 0:1], prt_sb[0][:, 1024:1025], prt_sb[1][:, 1024:1025],
           ALU.max, waits=[("g_sl", g_pr)])
        TT(crs_sb[:, 0:1], prt_sb[0][:, 1024:1025], cmneg_sb[:, 0:1], ALU.subtract)
        ACTV(crs_sb[:, 0:1], crs_sb[:, 0:1], AF.Exp, waits=[("v", cnt["v"])], inc="a")
        TT(mneg_sb[:, 0:1], prt_sb[1][:, 1024:1025], cmneg_sb[:, 0:1], ALU.subtract,
           waits=[("a", cnt["a"])])
        ACTV(mneg_sb[:, 0:1], mneg_sb[:, 0:1], AF.Exp, waits=[("v", cnt["v"])], inc="a")
        TS(mpos_sb[:, 0:1], prt_sb[0][:, 1025:1026], crs_sb[:, 0:1], None, ALU.mult,
           waits=[("a", cnt["a"])])
        STT(mpos_sb[:, 0:1], prt_sb[1][:, 1025:1026], mneg_sb[:, 0:1],
            mpos_sb[:, 0:1], ALU.mult, ALU.add)
        RECIP(mpos_sb[:, 0:1], mpos_sb[:, 0:1])
        TS(combf_sb[:, :], prt_sb[0][:, 0:1024], crs_sb[:, 0:1], None, ALU.mult)
        STT(combf_sb[:, :], prt_sb[1][:, 0:1024], mneg_sb[:, 0:1],
            combf_sb[:, :], ALU.mult, ALU.add)
        TS(cn_sb[:, :], combf_sb[:, :], mpos_sb[:, 0:1], None, ALU.mult, inc="v")
        for g in range(2):
            for d4 in range(4):
                dt = g * 4 + d4
                w0 = [("v", cnt["v"])]
                p.emit("tensor",
                       (lambda e, o=psT[:, d4 * 128:(d4 + 1) * 128],
                        i=cn_sb[:, dt * 128:(dt + 1) * 128],
                        idn=id_sb[:, :]: e.matmul(o, i, idn, start=True, stop=True)),
                       waits=w0, inc=("t" if d4 == 3 else None))
            vtr = p.emit("vector",
                         (lambda e, b=b, g=g: e.tensor_copy(
                             slfm_sb.rearrange("p (dt c) -> p dt c", dt=NDT)[:, 4 * g:4 * g + 4, b * 128:(b + 1) * 128],
                             psT.rearrange("p (d4 c) -> p d4 c", d4=4))),
                         waits=[("t", cnt["t"])], inc="v")
    slots_done = cnt["v"]

    for e_i in range(2):
        if e_i == 0:
            g_wg, s_wg, hh = wg0_pre
        else:
            g_wg, s_wg, hh = load_wx(
                dwg.rearrange("e (dt p) f -> e p dt f", p=128)[e_i], NDT, cnt["t"])
        sl_e = slfm_sb.rearrange("p (dt b s) -> p dt b s", dt=NDT, b=4)
        for fc in range(16):
            for dt in range(NDT):
                w0 = []
                if dt == 0:
                    w0 = [("a", cnt["a"])]
                    if fc == 0:
                        w0 += [("g_wx", g_wg), ("v", slots_done)]
                if fc == 0 and dt == hh:
                    w0 = w0 + [("s_wx", s_wg)]
                MM(psC[:, 0:256], wgx_sb[:, dt * 2048 + fc * 128:dt * 2048 + (fc + 1) * 128],
                   sl_e[:, dt, :, e_i * 64:(e_i + 1) * 64],
                   start=(dt == 0), stop=(dt == NDT - 1), waits=w0,
                   inc=("t" if dt == NDT - 1 else None))
            ACTV(gt_sb[:, fc * 256:(fc + 1) * 256], psC[:, 0:256], AF.Gelu,
                 bias=bg_sb[:, e_i * 16 + fc:e_i * 16 + fc + 1],
                 waits=[("t", cnt["t"]), ("g_tab", g_bg)], inc="a")
        gate_a = cnt["a"]
        g_wu, s_wu, _ = load_wx(dwu.rearrange("e (dt p) f -> e p dt f", p=128)[e_i],
                                NDT, cnt["t"])
        for fc in range(16):
            for dt in range(NDT):
                w0 = []
                if dt == 0:
                    w0 = [("v", cnt["v"])]
                    if fc == 0:
                        w0.append(("g_wx", g_wu))
                if fc == 0 and dt == hh:
                    w0 = w0 + [("s_wx", s_wu)]
                MM(psD[:, 0:256], wgx_sb[:, dt * 2048 + fc * 128:dt * 2048 + (fc + 1) * 128],
                   sl_e[:, dt, :, e_i * 64:(e_i + 1) * 64],
                   start=(dt == 0), stop=(dt == NDT - 1), waits=w0,
                   inc=("t" if dt == NDT - 1 else None))
            STT(gu_sb[:, fc * 256:(fc + 1) * 256], psD[:, 0:256],
                bu_sb[:, e_i * 16 + fc:e_i * 16 + fc + 1],
                gt_sb[:, fc * 256:(fc + 1) * 256], ALU.add, ALU.mult,
                waits=[("t", cnt["t"]), ("a", gate_a), ("g_tab", g_bu)], inc="v")
        gu_done = cnt["v"]
        g_wd, s_wd, hd = load_wx(dwd.rearrange("e (ft p) o -> e p ft o", p=128)[e_i],
                                 16, cnt["t"])
        MM(psB[:, 0:512], ones1x128b[:, :], bdr_sb[:, e_i * 1024:e_i * 1024 + 512],
           start=True, stop=True, waits=[("g_tab", g_bd), ("v", cnt["v"])])
        MM(psB[:, 512:1024], ones1x128b[:, :],
           bdr_sb[:, e_i * 1024 + 512:e_i * 1024 + 1024], start=True, stop=True,
           inc="t")
        CP("vector", bd_sb[:, :], psB[:, 0:1024], waits=[("t", cnt["t"])], inc="v")
        for sc in range(2):
            for ft in range(16):
                w0 = []
                if ft == 0:
                    w0 = [("v", gu_done), ("v", cnt["v"])]
                    if sc == 0:
                        w0.append(("g_wx", g_wd))
                if sc == 0 and ft == hd:
                    w0 = w0 + [("s_wx", s_wd)]
                MM(psA[:, 0:512], gu_sb[:, ft * 256 + sc * 128:ft * 256 + sc * 128 + 128],
                   wgx_sb[:, ft * 1024:ft * 1024 + 512],
                   start=(ft == 0), stop=(ft == 15), waits=w0)
                MM(psA[:, 512:1024], gu_sb[:, ft * 256 + sc * 128:ft * 256 + sc * 128 + 128],
                   wgx_sb[:, ft * 1024 + 512:ft * 1024 + 1024],
                   start=(ft == 0), stop=(ft == 15),
                   inc=("t" if ft == 15 else None))
            veo = TT(eo_sb[:, :], psA[:, 0:1024], bd_sb[:, :], ALU.add,
                     waits=[("t", cnt["t"])] +
                           ([("s_eoag", cnt["s_eoag"])] if cnt.get("s_eoag") else []),
                     inc="v")
            p.dma("sync",
                  deo_in[2 * sc * 128 + e_i * 64:2 * sc * 128 + (e_i + 1) * 64, :],
                  eo_sb[0:64, :], waits=[("v", veo)], inc="s_eoag")
            p.dma("sync",
                  deo_in[(2 * sc + 1) * 128 + e_i * 64:(2 * sc + 1) * 128 + (e_i + 1) * 64, :],
                  eo_sb[64:128, :], inc="s_eoag")
    cc_eo = p.emit("gpsimd", (lambda e: e.collective_compute(
        "AllGather", ALU.bypass, replica_groups=[list(range(NCORES))],
        ins=[deo_in.opt()], outs=[deo_out.opt()])),
        waits=[("s_eoag", cnt["s_eoag"])], inc="cc", amt=1, dma=True)

    p.sem("g_eo")

    def read_eo(e):
        pid = e.partition_id()
        bp = nc.s_assert_within(e.compute_val(pid // 2), 0, 3,
                                skip_runtime_assert=True)
        return e.dma_start(
            eof_sb.rearrange("p (r f) -> p r f", r=NCORES),
            deo_out.rearrange("(r q p) f -> q p r f", q=4, p=128)[bp, :, :, :])
    eof_done = p.emit("gpsimd", read_eo,
                      waits=[("cc", cc_eo), ("g_eo", cnt["g_eo"]),
                             ("a", disp_done_a)],
                      inc="g_eo", amt=16, dma=True)

    ett_h = [None, None]
    est_h = [None, None]
    e9_v0 = cnt["v"]
    for i, (dc, tc) in enumerate([(d, t) for d in range(NDT)
                                  for t in range(qc_n)]):
        b = i % 2
        sx = p.dma("scalar", xr3_sb[b][:, :],
                   dx3[dc * 128:(dc + 1) * 128, tc * 512:(tc + 1) * 512],
                   waits=[("v", ett_h[b] if ett_h[b] else e9_v0)],
                   inc=f"a_xr{b}")
        for st in range(NDT):
            w0 = []
            if st == 0:
                w0 = [("g_eo", eof_done), ("v", cfm_done), ("v", cnt["v"])]
            MM(psC[:, 0:512],
               eof_sb[:, st * 1024 + dc * 128:st * 1024 + (dc + 1) * 128],
               cfm_sb[:, st * TL + tc * 512:st * TL + (tc + 1) * 512],
               start=(st == 0), stop=(st == NDT - 1), waits=w0,
               inc=("t" if st == NDT - 1 else None))
        vo = TT(oc3_sb[b][:, :], psC[:, 0:512], xr3_sb[b][:, :], ALU.add,
                waits=[("t", cnt["t"]), (f"a_xr{b}", sx)] +
                      ([(f"os{b}", est_h[b])] if est_h[b] else []),
                inc="v")
        ett_h[b] = vo
        est_h[b] = p.dma("sync", dout[dc * 128:(dc + 1) * 128, tc * 512:(tc + 1) * 512],
                         oc3_sb[b][:, :], waits=[("v", vo)], inc=f"os{b}")
    p.emit("sync", lambda e: (e.wait_ge(p.sems["os0"], cnt["os0"]),
                              e.wait_ge(p.sems["os1"], cnt["os1"])) and None)

    if DEBUG_STAGE in (0, 4):
        p.replay()
        return nc
    p.stage("fin")
    ar.reset()
    fin_sb = ar.alloc([128, 2048], F32, "fin_sb")
    if DEBUG_STAGE == 1:
        src_d, sds = dx1, "s_x1"
    elif DEBUG_STAGE == 2:
        src_d, sds = dx2, "s_x2"
    else:
        src_d, sds = dx3, "s_x3"
    stream_done = [(sds, cnt[sds])]
    for dc in range(NDT):
        ld = p.dma("sync", fin_sb[:, :], src_d[dc * 128:(dc + 1) * 128, :],
                   waits=stream_done + ([("o", cnt["o"])] if dc else []), inc="s_fin")
        p.dma("sync", dout[dc * 128:(dc + 1) * 128, :], fin_sb[:, :],
              waits=[("s_fin", ld)], inc="o")
    p.emit("sync", lambda e: e.wait_ge(p.sems["o"], cnt["o"]) and None)

    p.replay()
    return nc


_CACHE = {}


def kernel(**inputs):
    if "nc" not in _CACHE:
        _CACHE["nc"] = build()
    nc = _CACHE["nc"]
    in_maps = _prep(inputs)
    res = run_bass_kernel_spmd(nc, in_maps, core_ids=list(range(NCORES)))
    out = np.empty((B, T, D), np.float32)
    for c in range(NCORES):
        b, half = c // 2, c % 2
        t0 = half * TL
        out[b, t0:t0 + TL, :] = res.results[c]["out_fm"].T
    return out


# version 20
# speedup vs baseline: 1.2316x; 1.0100x over previous
import numpy as np
import concourse.bass as bass
import concourse.mybir as mybir
from concourse.bass_utils import run_bass_kernel_spmd

F32 = mybir.dt.float32
BF16 = mybir.dt.bfloat16
AF = mybir.ActivationFunctionType
ALU = mybir.AluOpType
AX = mybir.AxisListType

NCORES = 8
B, T, D, H, DH = 4, 4096, 1024, 16, 64
TL = T // 2
NDT = D // 128
EPS_LN = 1e-5

DEBUG_STAGE = 0


ENGS = ("tensor", "vector", "scalar", "gpsimd", "sync")


class Prog:
    def __init__(self, nc):
        self.nc = nc
        self.stages = []
        self.cnt = {}
        self.sems = {}
        self.cur = None

    def sem(self, name):
        if name not in self.sems:
            self.sems[name] = self.nc.alloc_semaphore(name)
            self.cnt[name] = 0
        return name

    def stage(self, name):
        self.cur = {e: [] for e in ENGS}
        self.cur["_name"] = name
        self.stages.append(self.cur)

    ENG_SEM = {"vector": "v", "scalar": "a", "gpsimd": "pl"}

    def emit(self, eng, fn, waits=(), inc=None, amt=1, dma=False, chain=True):
        for s, v in waits:
            assert v <= self.cnt[s], (s, v, self.cnt[s])
        waits = list(waits)
        if not dma and eng in self.ENG_SEM:
            es = self.ENG_SEM[eng]
            self.sem(es)
            if chain and self.cnt[es]:
                waits.append((es, self.cnt[es]))
            if inc is None:
                inc = es
            assert inc == es, (eng, inc)
        rec = (fn, tuple(waits), inc, amt)
        self.cur[eng].append(rec)
        if inc is not None:
            self.cnt[inc] += amt
            return self.cnt[inc]
        return None

    def dma(self, eng, out, in_, waits=(), inc=None, **kw):
        if inc is None:
            inc = eng[0]
        self.sem(inc)
        w = list(waits)
        if self.cnt[inc]:
            w.append((inc, self.cnt[inc]))
        return self.emit(eng, lambda e: e.dma_start(out, in_, **kw), w, inc, 16,
                         dma=True)

    def replay(self):
        nc = self.nc
        ctx = nc.allow_low_precision(reason="bf16 matmul operands per problem tolerance")
        ctx.__enter__()
        for st in self.stages:
            with nc.Block(st["_name"]) as block:
                for eng in ENGS:
                    recs = st[eng]
                    if not recs:
                        continue

                    def body(e, recs=recs):
                        for fn, waits, inc, amt in recs:
                            for s, v in waits:
                                e.wait_ge(self.sems[s], v)
                            inst = fn(e)
                            if inc is not None and inst is not None:
                                inst.then_inc(self.sems[inc], amt)

                    getattr(block, eng)(body)


class Arena:

    def __init__(self, nc, slack=5120):
        self.nc = nc
        size = nc.sbuf_bytes_remaining - slack
        base, _ = nc.bump_sbuf(size)
        self.base = base
        self.off = base
        self.limit = base + size
        self.mark_off = None
        self.n = 0

    def alloc(self, shape, dtype, name=None):
        assert len(shape) == 2 and shape[0] <= 128
        itm = 2 if dtype == BF16 else 4
        sz = shape[1] * itm
        off = (self.off + 31) & ~31
        assert off + sz <= self.limit, (
            f"SBUF arena overflow: {name} needs {sz} at {off}, limit {self.limit}")
        self.off = off + sz
        self.n += 1
        nm = name or f"ar{self.n}"
        if not hasattr(self, "offs"):
            self.offs = {}
        self.offs[nm] = off
        return self.nc.alloc_sbuf_tensor_at(nm, list(shape), dtype, offset=off)

    def overlay(self, base_name, delta, shape, dtype, name):
        off = self.offs[base_name] + delta
        self.n += 1
        return self.nc.alloc_sbuf_tensor_at(name, list(shape), dtype, offset=off)

    def mark(self, key="base"):
        if self.mark_off is None:
            self.mark_off = {}
        self.mark_off[key] = self.off

    def reset(self, key="base"):
        self.off = self.mark_off[key]


def _rope_tables():
    base = 10000.0
    inv = 1.0 / base ** (np.arange(0, DH, 2, dtype=np.float64) / DH)
    ang = np.arange(T, dtype=np.float64)[:, None] * inv[None, :]
    cos = np.cos(ang)
    sin = np.sin(ang)
    cs = np.empty((128, T), np.float32)
    ss = np.empty((128, T), np.float32)
    for p in range(128):
        d = p % 64
        cs[p] = cos[:, d % 32].astype(np.float32)
        sgn = -1.0 if (d % 64) < 32 else 1.0
        ss[p] = (sgn * sin[:, d % 32]).astype(np.float32)
    return cs, ss


def _prep_core(inputs, c):
    b, half = c // 2, c % 2
    t0, o0 = half * TL, (1 - half) * TL
    order = np.r_[t0:t0 + TL, o0:o0 + TL]

    x = np.asarray(inputs["L0_attn_hidden_states"])
    cs, ss = _rope_tables()

    d = {}
    d["x"] = np.ascontiguousarray(x[b][order].T)
    d["wqT"] = np.ascontiguousarray(np.asarray(inputs["L0_attn_Wq"]).T)
    d["wkT"] = np.ascontiguousarray(np.asarray(inputs["L0_attn_Wk"]).T)
    wv = np.asarray(inputs["L0_attn_Wv"])
    wv65 = np.zeros((D, H * 65), np.float32)
    for h in range(H):
        wv65[:, h * 65:h * 65 + 64] = wv[h * 64:(h + 1) * 64, :].T
    d["wv65"] = wv65
    bv = np.asarray(inputs["L0_attn_bv"])
    bv_row = np.zeros((1, H * 65), np.float32)
    for h in range(H):
        bv_row[0, h * 65:h * 65 + 64] = bv[h * 64:(h + 1) * 64]
    d["bv_row"] = bv_row
    d["bq_p"] = np.ascontiguousarray(np.asarray(inputs["L0_attn_bq"]).reshape(8, 128).T)
    d["bk_p"] = np.ascontiguousarray(np.asarray(inputs["L0_attn_bk"]).reshape(8, 128).T)
    d["bo_c"] = np.ascontiguousarray(np.asarray(inputs["L0_attn_bo"]).reshape(8, 128).T)
    d["woT"] = np.ascontiguousarray(np.asarray(inputs["L0_attn_Wo"]).T)
    d["cs"] = np.ascontiguousarray(cs[:, order])
    d["ss"] = np.ascontiguousarray(ss[:, order])

    sel128 = np.zeros((128, 128), np.float32)
    sel128[0, 0:64] = 1.0
    sel128[64, 64:128] = 1.0
    d["sel128"] = sel128
    d["ident"] = np.eye(128, dtype=np.float32)
    perm = np.r_[32:64, 0:32, 96:128, 64:96]
    P = np.zeros((128, 128), np.float32)
    P[perm, np.arange(128)] = 1.0
    d["permm"] = P

    d["w1t"] = np.ascontiguousarray(np.asarray(inputs["L0_ffn_weight"]).T)
    d["disha_p"] = np.ascontiguousarray(np.asarray(inputs["L0_ffn_disha"]))
    psel = np.zeros((128, 64), np.float32)
    for pp in range(128):
        psel[pp, pp % 64] = 1.0
    d["psel"] = psel

    wq1 = np.asarray(inputs["L1_attn_W_Q"])[:, :, 0]
    wk1 = np.asarray(inputs["L1_attn_W_K"])[:, :, 0]
    wqk1 = np.zeros((D, 64), np.float32)
    wqk1[:, 0:16] = wq1.T
    wqk1[:, 32:48] = wk1.T
    d["wqk1"] = wqk1
    wv1 = np.asarray(inputs["L1_attn_W_V"])
    d["wv1"] = np.ascontiguousarray(wv1.transpose(1, 0, 2).reshape(D, D))
    wo1 = np.asarray(inputs["L1_attn_W_O"])
    d["wo1"] = np.ascontiguousarray(wo1.transpose(2, 1, 0).reshape(D, D))
    sel16 = np.zeros((16, 8 * 128), np.float32)
    for kt in range(8):
        for pp in range(128):
            sel16[2 * kt + (pp >= 64), kt * 128 + pp] = 1.0
    d["sel16"] = sel16
    d["oddmask"] = np.full((128, 1), float(half), np.float32)

    se_raw = np.asarray(inputs["L1_ffn_slot_embeds"]).reshape(1024, D)
    d["se_raw"] = np.ascontiguousarray(se_raw.T)
    d["ng"] = np.ascontiguousarray(np.asarray(inputs["L1_ffn_norm_g"]).reshape(8, 128).T)
    d["nb"] = np.ascontiguousarray(np.asarray(inputs["L1_ffn_norm_b"]).reshape(8, 128).T)
    d["sg"] = np.ascontiguousarray(np.asarray(inputs["L1_ffn_slot_g"]).reshape(8, 128).T)
    d["sbb"] = np.ascontiguousarray(np.asarray(inputs["L1_ffn_slot_b"]).reshape(8, 128).T)
    d["scale1"] = np.asarray(inputs["L1_ffn_scale"]).reshape(1, 1).astype(np.float32)
    es = (2 * c, 2 * c + 1)
    d["wg"] = np.ascontiguousarray(np.asarray(inputs["L1_ffn_Wg"])[list(es)])
    d["wu"] = np.ascontiguousarray(np.asarray(inputs["L1_ffn_Wu"])[list(es)])
    d["wd"] = np.ascontiguousarray(np.asarray(inputs["L1_ffn_Wd"])[list(es)])
    bg = np.asarray(inputs["L1_ffn_bg"])
    bu = np.asarray(inputs["L1_ffn_bu"])
    bgc = np.zeros((128, 32), np.float32)
    buc = np.zeros((128, 32), np.float32)
    for j, eg in enumerate(es):
        bgc[:, j * 16:(j + 1) * 16] = bg[eg].reshape(16, 128).T
        buc[:, j * 16:(j + 1) * 16] = bu[eg].reshape(16, 128).T
    d["bgc"] = bgc
    d["buc"] = buc
    d["bdr"] = np.ascontiguousarray(np.asarray(inputs["L1_ffn_bd"])[list(es)])
    return d


def _prep(inputs):
    return [_prep_core(inputs, c) for c in range(NCORES)]


def build():
    nc = bass.Bass("TRN2", target_bir_lowering=False)
    p = Prog(nc)
    for s in ("g", "s", "t", "v", "a", "pl", "cc", "o"):
        p.sem(s)

    dx = nc.dram_tensor("x", [D, T], F32, kind="ExternalInput").ap()
    dwqT = nc.dram_tensor("wqT", [D, D], F32, kind="ExternalInput").ap()
    dwkT = nc.dram_tensor("wkT", [D, D], F32, kind="ExternalInput").ap()
    dwv65 = nc.dram_tensor("wv65", [D, H * 65], F32, kind="ExternalInput").ap()
    dbv = nc.dram_tensor("bv_row", [1, H * 65], F32, kind="ExternalInput").ap()
    dbq = nc.dram_tensor("bq_p", [128, 8], F32, kind="ExternalInput").ap()
    dbk = nc.dram_tensor("bk_p", [128, 8], F32, kind="ExternalInput").ap()
    dbo = nc.dram_tensor("bo_c", [128, 8], F32, kind="ExternalInput").ap()
    dwoT = nc.dram_tensor("woT", [D, D], F32, kind="ExternalInput").ap()
    dcs = nc.dram_tensor("cs", [128, T], F32, kind="ExternalInput").ap()
    dss = nc.dram_tensor("ss", [128, T], F32, kind="ExternalInput").ap()
    dsel = nc.dram_tensor("sel128", [128, 128], F32, kind="ExternalInput").ap()
    dpermm = nc.dram_tensor("permm", [128, 128], F32, kind="ExternalInput").ap()

    dout = nc.dram_tensor("out_fm", [D, TL], F32, kind="ExternalOutput").ap()

    dw1t = nc.dram_tensor("w1t", [D, D], F32, kind="ExternalInput").ap()
    ddisha = nc.dram_tensor("disha_p", [64, D], F32, kind="ExternalInput").ap()
    dpsel = nc.dram_tensor("psel", [128, 64], F32, kind="ExternalInput").ap()
    dwqk1 = nc.dram_tensor("wqk1", [D, 64], F32, kind="ExternalInput").ap()
    dwv1 = nc.dram_tensor("wv1", [D, D], F32, kind="ExternalInput").ap()
    dwo1 = nc.dram_tensor("wo1", [D, D], F32, kind="ExternalInput").ap()
    dsel16 = nc.dram_tensor("sel16", [16, 8 * 128], F32, kind="ExternalInput").ap()
    dmask = nc.dram_tensor("oddmask", [128, 1], F32, kind="ExternalInput").ap()
    dident = nc.dram_tensor("ident", [128, 128], F32, kind="ExternalInput").ap()

    dln0 = nc.dram_tensor("ln0d", [D, T], BF16, kind="Internal").ap()
    dx1 = nc.dram_tensor("x1d", [D, TL], F32, kind="Internal").ap()
    dx2 = nc.dram_tensor("x2d", [D, TL], F32, kind="Internal").ap()
    dx3 = nc.dram_tensor("x3d", [D, TL], F32, kind="Internal").ap()
    dln1 = nc.dram_tensor("ln1d", [D, TL], BF16, kind="Internal").ap()
    dln2 = nc.dram_tensor("ln2d", [D, TL], BF16, kind="Internal").ap()
    dse = nc.dram_tensor("se_raw", [D, 1024], F32, kind="ExternalInput").ap()
    dng = nc.dram_tensor("ng", [128, 8], F32, kind="ExternalInput").ap()
    dnb = nc.dram_tensor("nb", [128, 8], F32, kind="ExternalInput").ap()
    dsg = nc.dram_tensor("sg", [128, 8], F32, kind="ExternalInput").ap()
    dsbb = nc.dram_tensor("sbb", [128, 8], F32, kind="ExternalInput").ap()
    dscale = nc.dram_tensor("scale1", [1, 1], F32, kind="ExternalInput").ap()
    dwg = nc.dram_tensor("wg", [2, D, 2048], F32, kind="ExternalInput").ap()
    dwu = nc.dram_tensor("wu", [2, D, 2048], F32, kind="ExternalInput").ap()
    dwd = nc.dram_tensor("wd", [2, 2048, D], F32, kind="ExternalInput").ap()
    dbgc = nc.dram_tensor("bgc", [128, 32], F32, kind="ExternalInput").ap()
    dbuc = nc.dram_tensor("buc", [128, 32], F32, kind="ExternalInput").ap()
    dbdr = nc.dram_tensor("bdr", [2, 1024], F32, kind="ExternalInput").ap()
    dln3 = nc.dram_tensor("ln3d", [D, TL], BF16, kind="Internal").ap()
    dse_ln = nc.dram_tensor("se_lnd", [D, 1024], BF16, kind="Internal").ap()
    dsl_in = nc.dram_tensor("sl_in", [1024, 1032], BF16, kind="Internal").ap()
    dsl_out = nc.dram_tensor("sl_out", [1024 * NCORES, 1032], BF16, kind="Internal").ap()
    deo_in = nc.dram_tensor("eo_in", [512, 1024], BF16, kind="Internal").ap()
    deo_out = nc.dram_tensor("eo_out", [512 * NCORES, 1024], BF16, kind="Internal").ap()
    dcar_in = nc.dram_tensor("car_in", [128, 24], F32, kind="Internal").ap()
    dcar_out = nc.dram_tensor("car_out", [128 * 2, 24], F32, kind="Internal").ap()

    ar = Arena(nc)
    ones128_bf = ar.alloc([128, 1], BF16, "ones128b")
    ones1x128 = ar.alloc([1, 128], F32, "ones1x128")
    ones1x128b = ar.alloc([1, 128], BF16, "ones1x128b")
    consts = ar.alloc([128, 4], F32, "consts")
    sel_sb = ar.alloc([128, 128], BF16, "sel_sb")
    id_sb = ar.alloc([128, 128], BF16, "id_sb")
    bq_sb = ar.alloc([128, 8], F32, "bq_sb")
    bk_sb = ar.alloc([128, 8], F32, "bk_sb")
    bo_sb = ar.alloc([128, 8], F32, "bo_sb")
    ar.mark()

    psAB = nc.alloc_psum_tensor("psAB", [128, 2048], F32)
    psA = psAB[:, 0:1024]
    psB = psAB[:, 1024:2048]
    psC = nc.alloc_psum_tensor("psC", [128, 512], F32)
    psD = nc.alloc_psum_tensor("psD", [128, 512], F32)
    psE = nc.alloc_psum_tensor("psE", [128, 512], F32)
    psF = nc.alloc_psum_tensor("psF", [128, 512], F32)
    psT = psF

    cnt = p.cnt

    def MM(out, lhsT, rhs, start, stop, waits=(), inc=None, tp=None):
        return p.emit("tensor",
                      (lambda e, o=out, l=lhsT, r=rhs, s0=start, s1=stop, t=tp:
                       e.matmul(o, l, r, start=s0, stop=s1, tile_position=t)),
                      waits, inc)

    def CP(eng, out, in_, waits=(), inc=None):
        return p.emit(eng, (lambda e, o=out, i=in_: e.tensor_copy(o, i)), waits, inc)

    def TT(out, a, b, op, waits=(), inc=None, eng="vector"):
        return p.emit(eng, (lambda e, o=out, x=a, y=b, f=op: e.tensor_tensor(o, x, y, f)),
                      waits, inc)

    def TS(out, in_, s1, s2, op0, op1=None, waits=(), inc=None, eng="vector"):
        if op1 is None:
            op1 = ALU.bypass
        return p.emit(eng, (lambda e, o=out, i=in_, a=s1, b=s2, f0=op0, f1=op1:
                            e.tensor_scalar(o, i, a, b, f0, f1)), waits, inc)

    def STT(out, in0, scalar, in1, op0, op1, waits=(), inc=None):
        return p.emit("vector", (lambda e, o=out, i=in0, s=scalar, j=in1, f0=op0, f1=op1:
                                 e.scalar_tensor_tensor(o, i, s, j, f0, f1)), waits, inc)

    def ACTV(out, in_, func, bias=None, scale=None, waits=(), inc=None,
             chain=True):
        kw = {}
        if bias is not None:
            kw["bias"] = bias
        if scale is not None:
            kw["scale"] = scale
        return p.emit("scalar", (lambda e, o=out, i=in_, f=func, kw=kw:
                                 e.activation(o, i, f, **kw)), waits, inc,
                      chain=chain)

    def RECIP(out, in_, waits=(), inc=None):
        return p.emit("vector", (lambda e, o=out, i=in_: e.reciprocal(o, i)), waits, inc)

    def RED(out, in_, op, waits=(), inc=None, negate=False):
        return p.emit("vector", (lambda e, o=out, i=in_, f=op, n=negate:
                                 e.tensor_reduce(o, i, AX.X, f, negate=n)), waits, inc)

    def MS(ap, val, waits=(), inc=None):
        return p.emit("gpsimd", (lambda e, a=ap, v=val: e.memset(a, v)), waits, inc)

    def SCAN(out, d0, d1, init, waits=(), inc=None):
        return p.emit("vector", (lambda e, o=out, a=d0, b=d1, i=init:
                                 e.tensor_tensor_scan(o, a, b, i, ALU.add, ALU.add)),
                      waits, inc)

    p.stage("init")
    MS(ones128_bf[:, :], 1.0)
    MS(ones1x128[:, :], 1.0)
    MS(ones1x128b[:, :], 1.0)
    MS(consts[:, 0:1], 1.0 / 1024.0)
    MS(consts[:, 1:2], EPS_LN)
    MS(consts[:, 2:3], 1e-6)
    MS(consts[:, 3:4], float(np.pi / 2))
    p.dma("gpsimd", sel_sb[:, :], dsel, inc="g_tab")
    g_id = p.dma("gpsimd", id_sb[:, :], dident, inc="g_tab")
    p.dma("sync", bq_sb[:, :], dbq, inc="s_tab")
    p.dma("sync", bk_sb[:, :], dbk, inc="s_tab")
    tab_s = p.dma("sync", bo_sb[:, :], dbo, inc="s_tab")

    p.stage("ln0")
    ar.reset()
    _lnbuf_n = [0]

    def mk_ln_bufs(dbl=True):
        i = _lnbuf_n[0]
        _lnbuf_n[0] += 2
        lnc = [ar.alloc([128, NDT * 512], BF16, f"lnc{i}")]
        lnc.append(ar.alloc([128, NDT * 512], BF16, f"lnd{i}") if dbl else lnc[0])
        return dict(
            xcf=[ar.alloc([128, NDT * 512], F32, f"xcf{i+j}") for j in range(2)],
            xcb=[ar.alloc([128, NDT * 512], BF16, f"xcb{i+j}") for j in range(2)],
            sqb=ar.alloc([128, NDT * 512], BF16, f"sqb{i}"),
            lnc=lnc, dbl=dbl,
            tmpf=ar.alloc([128, 512], F32, f"lntmp{i}"),
            rows=ar.alloc([1, 2048], F32, f"rows{i}"),
            rows2=ar.alloc([1, 2048], F32, f"rows2_{i}"))

    def emit_ln(src_dram, n_tok, dst_dram, bufs, gten=None, bten=None,
                postscale=None, src_ready=(), src_ready_fn=None, prev_st=None):
        xcf, xcb = bufs["xcf"], bufs["xcb"]
        sqb, tmpf = bufs["sqb"], bufs["tmpf"]
        lncs, dbl = bufs["lnc"], bufs["dbl"]
        rows, rows2 = bufs["rows"], bufs["rows2"]
        """Feature-major LN over d=1024, pipelined over 512-token chunks."""
        nch = n_tok // 512
        st_chunks = []
        st_par = dict(prev_st) if prev_st else {0: None, 1: None}
        base = _lnbuf_n[0] - 2
        rel_xcb = [None, None]
        rel_xcf = [None, None]
        rel_stats = [None, None]
        rel_sq = None
        rel_bcast = None
        for ch in range(nch):
            bi = ch % 2
            mean_bk = psE[0:1, 0:512] if bi == 0 else psD[0:1, 0:512]
            sq_bk = psC[0:1, 0:512] if bi == 0 else psB[0:1, 0:512]
            rset = rows if bi == 0 else rows2
            sl = slice(ch * 512, (ch + 1) * 512)
            src3 = src_dram.rearrange("(dt p) t -> p dt t", p=128)[:, :, sl]
            src_ready = src_ready_fn(ch) if src_ready_fn is not None else src_ready
            w = list(src_ready)
            if rel_xcf[bi]:
                w.append(("v", rel_xcf[bi]))
            xcf3 = xcf[bi].rearrange("p (dt t) -> p dt t", dt=NDT)
            ldf = p.dma("sync", xcf3[:, 0:4, :], src3[:, 0:4, :], waits=w,
                        inc=f"s_xcf{base + bi}")
            ldf2 = p.dma("scalar", xcf3[:, 4:8, :], src3[:, 4:8, :], waits=w,
                         inc=f"a_xcf{base + bi}")
            w = list(src_ready)
            if rel_xcb[bi]:
                w.append(("t", rel_xcb[bi]))
            ldb = p.dma("gpsimd", xcb[bi].rearrange("p (dt t) -> p dt t", dt=NDT),
                        src3, waits=w, inc=f"g_xcb{base + bi}")
            aq = None
            for dt in range(NDT):
                wq = []
                if dt == 0:
                    wq = [(f"g_xcb{base + bi}", ldb)]
                    if rel_sq:
                        wq.append(("t", rel_sq))
                aq = ACTV(sqb[:, dt * 512:(dt + 1) * 512],
                          xcb[bi][:, dt * 512:(dt + 1) * 512], AF.Square,
                          waits=wq, inc="a")
            for dt in range(NDT):
                wm = []
                if dt == 0:
                    wm = [(f"g_xcb{base + bi}", ldb)]
                    if rel_stats[bi]:
                        wm.append(("v", rel_stats[bi]))
                MM(mean_bk, ones128_bf[:, :], xcb[bi][:, dt * 512:(dt + 1) * 512],
                   start=(dt == 0), stop=(dt == NDT - 1), waits=wm)
            tmm = None
            for dt in range(NDT):
                wm = [("a", aq)] if dt == 0 else []
                tmm = MM(sq_bk, ones128_bf[:, :], sqb[:, dt * 512:(dt + 1) * 512],
                         start=(dt == 0), stop=(dt == NDT - 1), waits=wm,
                         inc=("t" if dt == NDT - 1 else None))
            rel_sq = tmm
            rel_xcb[bi] = tmm
            r_mean, r_srow, r_m2, r_var = (rset[0:1, 0:512], rset[0:1, 512:1024],
                                           rset[0:1, 1024:1536], rset[0:1, 1536:2048])
            TS(r_mean, mean_bk, consts[0:1, 0:1], None, ALU.mult, waits=[("t", tmm)])
            TS(r_srow, sq_bk, consts[0:1, 0:1], None, ALU.mult)
            TT(r_m2, r_mean, r_mean, ALU.mult)
            vv = p.emit("vector", (lambda e, a=r_var, b=r_srow, c=r_m2:
                                   e.tensor_sub(a, b, c)), inc="v")
            rel_stats[bi] = vv
            ACTV(r_m2, r_var, AF.Ln, bias=consts[0:1, 1:2], waits=[("v", vv)])
            av = ACTV(r_var, r_m2, AF.Exp, scale=-0.5, inc="a")
            vm = TT(r_m2, r_mean, r_var, ALU.mult, waits=[("a", av)], inc="v")
            wb = [("v", vm)]
            if rel_bcast:
                wb.append(("v", rel_bcast))
            MM(psA[:, 0:512], ones1x128[:, :], r_var, start=True, stop=True, waits=wb)
            tb = MM(psA[:, 512:1024], ones1x128[:, :], r_m2, start=True, stop=True,
                    inc="t")
            lnc = lncs[ch % 2]
            vo = None
            for dt in range(NDT):
                wv_ = [("t", tb)]
                if dt == 0:
                    wv_.append((f"s_xcf{base + bi}", ldf))
                    pw = st_par[ch % 2] if dbl else st_par[(ch - 1) % 2]
                    if pw is not None:
                        wv_.append(pw)
                if dt == 4:
                    wv_.append((f"a_xcf{base + bi}", ldf2))
                TT(tmpf[:, :], xcf[bi][:, dt * 512:(dt + 1) * 512], psA[:, 0:512],
                   ALU.mult, waits=wv_)
                if gten is None and postscale is None:
                    vo = TT(lnc[:, dt * 512:(dt + 1) * 512], tmpf[:, :],
                            psA[:, 512:1024], ALU.subtract, inc="v")
                else:
                    TT(tmpf[:, :], tmpf[:, :], psA[:, 512:1024], ALU.subtract)
                    if gten is not None and postscale is None:
                        vo = TS(lnc[:, dt * 512:(dt + 1) * 512], tmpf[:, :],
                                gten[:, dt:dt + 1], bten[:, dt:dt + 1],
                                ALU.mult, ALU.add, inc="v")
                    else:
                        TS(tmpf[:, :], tmpf[:, :], gten[:, dt:dt + 1],
                           bten[:, dt:dt + 1], ALU.mult, ALU.add)
                        vo = TS(lnc[:, dt * 512:(dt + 1) * 512], tmpf[:, :],
                                postscale, None, ALU.mult, inc="v")
            rel_bcast = vo
            rel_xcf[bi] = vo
            stn = f"ls{ch % 2}"
            stc = p.dma("sync",
                        dst_dram.rearrange("(dt p) t -> p dt t", p=128)[:, :, sl],
                        lnc.rearrange("p (dt t) -> p dt t", dt=NDT),
                        waits=[("v", vo)], inc=stn)
            st_par[ch % 2] = (stn, stc)
            st_chunks.append((stn, stc))
        return st_chunks

    b0 = mk_ln_bufs()
    ln0_ch = emit_ln(dx, T, dln0, b0)
    ln0_done = ln0_ch[-1]
    LN0S = "s_lnst"

    NKT = T // 128
    qc_n = TL // 512

    p.stage("l0v")
    ar.reset()
    v_sb = ar.alloc([128, NKT * 1040], BF16, "v_sb")
    cs_sb = ar.alloc([128, T], BF16, "cs_sb")
    ss_sb = ar.alloc([128, T], BF16, "ss_sb")
    wo_sb = ar.alloc([128, NDT * 1024], BF16, "wo_sb")
    ao_sb = ar.alloc([128, 8 * TL], BF16, "ao_sb")
    r2_sb = ar.alloc([128, 1024], BF16, "r2_sb")
    perm_sb = ar.alloc([128, 128], BF16, "perm_sb")
    ar.mark("attn")
    wv_sb = ar.alloc([128, NDT * 1040], BF16, "wv_sb")
    lnk = [ar.alloc([128, NDT * 128], BF16, f"lnk{i}") for i in range(4)]
    bvr_sb = ar.alloc([1, H * 65], BF16, "bvr_sb")

    pl_r2 = MS(r2_sb[:, :], 0.0, inc="pl")
    g_wv = p.dma("gpsimd", wv_sb.rearrange("p (dt c) -> p dt c", dt=NDT),
                 dwv65.rearrange("(dt p) c -> p dt c", p=128), inc="g_wv")
    g_bv = p.dma("gpsimd", bvr_sb[:, :], dbv, inc="g_tab")
    g_pm = p.dma("gpsimd", perm_sb[:, :], dpermm, inc="g_tab")
    g_cs = p.dma("gpsimd", cs_sb[:, :], dcs, inc="g_cs")
    a_ss = p.dma("gpsimd", ss_sb[:, :], dss, inc="g_ss")
    a_wo = p.dma("gpsimd", wo_sb.rearrange("p (kt d) -> p kt d", kt=NDT),
                 dwoT.rearrange("(kt p) d -> p kt d", p=128), inc="g_wo")

    ve = None
    rel_lnk = [None] * 4
    cp_hist = [None, None]
    for kt in range(NKT):
        bi = kt % 4
        pb = psA if kt % 2 == 0 else psB
        pc = psC if kt % 2 == 0 else psD
        qd = "sync" if bi % 2 == 0 else "gpsimd"
        w = [ln0_ch[kt // 4]]
        if rel_lnk[bi]:
            w.append(("t", rel_lnk[bi]))
        lk = p.dma(qd, lnk[bi].rearrange("p (dt t) -> p dt t", dt=NDT),
                   dln0.rearrange("(dt p) t -> p dt t", p=128)[:, :, kt * 128:(kt + 1) * 128],
                   waits=w, inc=f"lk{bi}")
        for dt in range(NDT):
            w0 = []
            if dt == 0:
                w0 = [(f"lk{bi}", lk), ("g_wv", g_wv), ("g_tab", g_bv)]
                if kt < 2:
                    w0.append(ln0_done)
                if cp_hist[kt % 2]:
                    w0.append(("v", cp_hist[kt % 2]))
            MM(pb[:, 0:512], lnk[bi][:, dt * 128:(dt + 1) * 128],
               wv_sb[:, dt * 1040:dt * 1040 + 512], start=(dt == 0), stop=False,
               waits=w0)
            MM(pb[:, 512:1024], lnk[bi][:, dt * 128:(dt + 1) * 128],
               wv_sb[:, dt * 1040 + 512:dt * 1040 + 1024], start=(dt == 0), stop=False)
            MM(pc[:, 0:16], lnk[bi][:, dt * 128:(dt + 1) * 128],
               wv_sb[:, dt * 1040 + 1024:dt * 1040 + 1040], start=(dt == 0), stop=False)
        MM(pb[:, 0:512], ones1x128b[:, :], bvr_sb[:, 0:512], start=False, stop=True)
        MM(pb[:, 512:1024], ones1x128b[:, :], bvr_sb[:, 512:1024], start=False, stop=True)
        tv = MM(pc[:, 0:16], ones1x128b[:, :], bvr_sb[:, 1024:1040],
                start=False, stop=True, inc="t")
        rel_lnk[bi] = tv
        off = kt * 1040
        CP("vector", v_sb[:, off:off + 512], pb[:, 0:512], waits=[("t", tv)])
        CP("vector", v_sb[:, off + 512:off + 1024], pb[:, 512:1024])
        ve = CP("vector", v_sb[:, off + 1024:off + 1040], pc[:, 0:16], inc="v")
        cp_hist[kt % 2] = ve
    pl_ones = MS(v_sb.rearrange("p (kt h e) -> p kt h e", kt=NKT, h=H)[:, :, :, 64:65],
                 1.0, waits=[("v", ve)], inc="pl")
    b1_t = cnt["t"]

    p.stage("l0attn")
    ar.reset("attn")
    wq_sb = [ar.alloc([128, NDT * 128], BF16, f"wq_sb{i}") for i in range(2)]
    wk_sb = [ar.alloc([128, NDT * 128], BF16, f"wk_sb{i}") for i in range(2)]
    NLQ = 3
    lnqk = [ar.alloc([128, NDT * 512], BF16, f"lnqk{i}") for i in range(NLQ)]
    q_sbp = [ar.alloc([128, TL], BF16, f"q_sb{i}") for i in range(2)]
    k_sbp = [ar.alloc([128, T], BF16, f"k_sb{i}") for i in range(2)]
    rtmp = ar.alloc([128, 512], BF16, "rtmp")
    p2_sb = [ar.alloc([128, 2048], BF16, f"p2_sb{i}") for i in range(2)]
    rb_sb = ar.alloc([128, 512], F32, "rb_sb")
    xr_sb = [ar.overlay("wq_sb0", 0, [128, 512], F32, "xr_sb0"),
             ar.overlay("wq_sb1", 0, [128, 512], F32, "xr_sb1")]
    o_sb = [ar.overlay("wk_sb0", 0, [128, 512], F32, "o_sb0"),
            ar.overlay("wk_sb1", 0, [128, 512], F32, "o_sb1")]

    v_ready = [("pl", pl_ones)]
    rel_wq = [None, None]
    rel_wk = [None, None]
    rel_lnqk = [None] * NLQ
    rope_maps = {}
    st_fill = {"psF_war": None}
    nrm_q = []

    def emit_proj_rope_legacy(hp):
        bi = hp % 2
        q_sb, k_sb = q_sbp[bi], k_sbp[bi]
        g_wq = p.dma("gpsimd", wq_sb[bi].rearrange("p (dt m) -> p dt m", dt=NDT),
                     dwqT.rearrange("(dt p) m -> p dt m", p=128)[:, :, hp * 128:(hp + 1) * 128],
                     waits=[("t", b1_t)], inc=f"g_wq{bi}")
        g_wk = p.dma("gpsimd", wk_sb[bi].rearrange("p (dt m) -> p dt m", dt=NDT),
                     dwkT.rearrange("(dt p) m -> p dt m", p=128)[:, :, hp * 128:(hp + 1) * 128],
                     waits=[("t", b1_t)], inc=f"g_wk{bi}")
        vq_ts = [None] * qc_n
        vk_ts = [None] * (T // 512)
        pbanks = [psC[:, 0:512], psD[:, 0:512], psA[:, 0:512], psB[:, 0:512]]
        hv = cnt["v"]
        ts_hist = [hv, hv, hv, hv]
        gi = 0
        for tc in range(T // 512):
            cbi = tc % NLQ
            qd = "sync" if cbi % 2 == 0 else "gpsimd"
            wl = [ln0_ch[tc], ("t", rel_lnqk[cbi] or b1_t)]
            s_ln = p.dma(qd, lnqk[cbi].rearrange("p (dt t) -> p dt t", dt=NDT),
                         dln0.rearrange("(dt p) t -> p dt t", p=128)[:, :, tc * 512:(tc + 1) * 512],
                         waits=wl, inc=f"lq{cbi}")
            if tc < qc_n:
                bk = gi % 4
                for dt in range(NDT):
                    w0 = []
                    if dt == 0:
                        w0 = [(f"g_wq{bi}", g_wq), (f"lq{cbi}", s_ln),
                              ("v", ts_hist[bk])]
                    MM(pbanks[bk], wq_sb[bi][:, dt * 128:(dt + 1) * 128],
                       lnqk[cbi][:, dt * 512:(dt + 1) * 512],
                       start=(dt == 0), stop=(dt == NDT - 1), waits=w0,
                       inc=("t" if dt == NDT - 1 else None))
                vq_ts[tc] = TS(q_sb[:, tc * 512:(tc + 1) * 512], pbanks[bk],
                               bq_sb[:, hp:hp + 1], None, ALU.add,
                               waits=[("t", cnt["t"]), ("s_tab", tab_s)], inc="v")
                ts_hist[bk] = vq_ts[tc]
                gi += 1
            bk = gi % 4
            for dt in range(NDT):
                w0 = []
                if dt == 0:
                    w0 = [(f"g_wk{bi}", g_wk), (f"lq{cbi}", s_ln),
                          ("v", ts_hist[bk])]
                MM(pbanks[bk], wk_sb[bi][:, dt * 128:(dt + 1) * 128],
                   lnqk[cbi][:, dt * 512:(dt + 1) * 512],
                   start=(dt == 0), stop=(dt == NDT - 1), waits=w0,
                   inc=("t" if dt == NDT - 1 else None))
            vk_ts[tc] = TS(k_sb[:, tc * 512:(tc + 1) * 512], pbanks[bk],
                           bk_sb[:, hp:hp + 1], None, ALU.add,
                           waits=[("t", cnt["t"]), ("s_tab", tab_s)], inc="v")
            ts_hist[bk] = vk_ts[tc]
            gi += 1
            rel_lnqk[cbi] = cnt["t"]
            if tc == qc_n - 1:
                rel_wq[bi] = cnt["t"]
            if tc == T // 512 - 1:
                rel_wk[bi] = cnt["t"]
        rope_k = [None] * (T // 512)
        rtts = []
        pi = 0
        for (srcb, nch) in ((q_sb, qc_n), (k_sb, T // 512)):
            for tc in range(nch):
                sl = slice(tc * 512, (tc + 1) * 512)
                pbank = psC if pi % 2 == 0 else psD
                w0 = [("v", vk_ts[-1] if pi < 2 else rtts[-2]), ("g_tab", g_pm)]
                tpm = MM(pbank[:, 0:512], perm_sb[:, :], srcb[:, sl],
                         start=True, stop=True, waits=w0, inc="t")
                t1 = TT(rtmp[:, :], pbank[:, 0:512], ss_sb[:, sl], ALU.mult,
                        waits=[("t", tpm), ("g_ss", a_ss), ("g_cs", g_cs)])
                rtts.append(t1)
                TT(srcb[:, sl], srcb[:, sl], cs_sb[:, sl], ALU.mult)
                vr = TT(srcb[:, sl], srcb[:, sl], rtmp[:, :], ALU.add, inc="v")
                if srcb is k_sb:
                    rope_k[tc] = vr
                pi += 1
        rope_maps[hp] = rope_k
        return rope_k[T // 512 - 1]

    def mk_fillers(hp):
        bi = hp % 2
        qs, ks = q_sbp[bi], k_sbp[bi]
        g_wq = p.dma("gpsimd", wq_sb[bi].rearrange("p (dt m) -> p dt m", dt=NDT),
                     dwqT.rearrange("(dt p) m -> p dt m", p=128)[:, :, hp * 128:(hp + 1) * 128],
                     waits=[("t", rel_wq[bi] or b1_t)], inc=f"g_wq{bi}")
        g_wk = p.dma("gpsimd", wk_sb[bi].rearrange("p (dt m) -> p dt m", dt=NDT),
                     dwkT.rearrange("(dt p) m -> p dt m", p=128)[:, :, hp * 128:(hp + 1) * 128],
                     waits=[("t", rel_wk[bi] or b1_t)], inc=f"g_wk{bi}")
        ld = {}

        def emit_ld(tc):
            cbi = tc % NLQ
            qd = "sync" if cbi % 2 == 0 else "gpsimd"
            wl = [ln0_ch[tc], ("t", rel_lnqk[cbi] or b1_t)]
            ld[tc] = p.dma(qd, lnqk[cbi].rearrange("p (dt t) -> p dt t", dt=NDT),
                           dln0.rearrange("(dt p) t -> p dt t", p=128)[:, :, tc * 512:(tc + 1) * 512],
                           waits=wl, inc=f"lq{cbi}")

        for tc in range(NLQ):
            emit_ld(tc)
        rope_maps[hp] = [None] * (T // 512)
        fl = []

        def mk_proj(kind, tc):
            cbi = tc % NLQ
            wsb = wq_sb[bi] if kind == "q" else wk_sb[bi]
            gw = g_wq if kind == "q" else g_wk
            gw_sem = f"g_wq{bi}" if kind == "q" else f"g_wk{bi}"
            dst = qs if kind == "q" else ks
            bias = bq_sb if kind == "q" else bk_sb
            for dt in range(NDT):
                def f(dt=dt, cbi=cbi, wsb=wsb, gw=gw, gw_sem=gw_sem, tc=tc,
                      kind=kind):
                    w0 = []
                    if dt == 0:
                        w0 = [(gw_sem, gw), (f"lq{cbi}", ld[tc])]
                        if st_fill["psF_war"]:
                            w0.append(("v", st_fill["psF_war"]))
                    MM(psF[:, 0:512], wsb[:, dt * 128:(dt + 1) * 128],
                       lnqk[cbi][:, dt * 512:(dt + 1) * 512],
                       start=(dt == 0), stop=(dt == NDT - 1), waits=w0,
                       inc=("t" if dt == NDT - 1 else None))
                    if dt == NDT - 1:
                        if kind == "k":
                            rel_lnqk[cbi] = cnt["t"]
                            if tc + NLQ < T // 512:
                                emit_ld(tc + NLQ)
                            if tc == T // 512 - 1:
                                rel_wk[bi] = cnt["t"]
                        elif tc == qc_n - 1:
                            rel_wq[bi] = cnt["t"]
                fl.append(f)

            def fevac(kind=kind, tc=tc, dst=dst, bias=bias):
                ts = TS(dst[:, tc * 512:(tc + 1) * 512], psF[:, 0:512],
                        bias[:, hp:hp + 1], None, ALU.add,
                        waits=[("t", cnt["t"]), ("s_tab", tab_s)], inc="v")
                st_fill["psF_war"] = ts
            fl.append(fevac)

        def mk_rope(kind, tc):
            src = qs if kind == "q" else ks

            def f(kind=kind, tc=tc, src=src):
                sl = slice(tc * 512, (tc + 1) * 512)
                w0 = [("g_tab", g_pm), ("v", st_fill["psF_war"])]
                tpm = MM(psF[:, 0:512], perm_sb[:, :], src[:, sl],
                         start=True, stop=True, waits=w0, inc="t")
                t1 = TT(rtmp[:, :], psF[:, 0:512], ss_sb[:, sl], ALU.mult,
                        waits=[("t", tpm), ("g_ss", a_ss), ("g_cs", g_cs)])
                st_fill["psF_war"] = t1
                TT(src[:, sl], src[:, sl], cs_sb[:, sl], ALU.mult)
                vr = TT(src[:, sl], src[:, sl], rtmp[:, :], ALU.add, inc="v")
                if kind == "k":
                    rope_maps[hp][tc] = vr
            fl.append(f)

        for tc in range(T // 512):
            if tc < qc_n:
                mk_proj("q", tc)
            mk_proj("k", tc)
            if tc < qc_n:
                mk_rope("q", tc)
            mk_rope("k", tc)
        return fl

    fill_q = []
    evac_prev = None
    for hp in range(8):
        if hp == 0:
            evac_prev = emit_proj_rope_legacy(0)
        par = hp % 2
        qr_sb, kr_sb = q_sbp[par], k_sbp[par]
        rope_k = rope_maps[hp]
        fill_q = mk_fillers(hp + 1) if hp < 7 else []

        for qc in range(qc_n):
            a_hist = []
            tpv = None

            def emit_S(kt):
                sbuf = psA if kt % 2 == 0 else psB
                w0 = [("v", rope_k[kt // 4])] + v_ready
                if kt >= 2:
                    w0.append(("a", a_hist[kt - 2]))
                MM(sbuf[:, 0:512], kr_sb[0:64, kt * 128:(kt + 1) * 128],
                   qr_sb[0:64, qc * 512:(qc + 1) * 512], start=True, stop=True,
                   waits=w0, tp=(0, 0))
                ts = MM(sbuf[:, 512:1024], kr_sb[64:128, kt * 128:(kt + 1) * 128],
                        qr_sb[64:128, qc * 512:(qc + 1) * 512], start=True, stop=True,
                        tp=(64, 0), inc="t")
                ae = ACTV(p2_sb[kt % 2][:, 0:1024], sbuf[:, :], AF.Exp, scale=0.125,
                          waits=[("t", ts)], inc="a")
                a_hist.append(ae)

            def emit_PV(kt):
                nonlocal tpv
                pb = p2_sb[kt % 2]
                vofs = kt * 1040 + 130 * hp
                wpv = [("a", a_hist[kt])]
                if kt == 0:
                    wpv.append(("v", evac_prev))
                MM(psC[0:65, 0:512], v_sb[:, vofs:vofs + 65], pb[:, 0:512],
                   start=(kt == 0), stop=(kt == NKT - 1), waits=wpv)
                tpv = MM(psD[0:65, 0:512], v_sb[:, vofs + 65:vofs + 130],
                         pb[:, 512:1024], start=(kt == 0), stop=(kt == NKT - 1),
                         inc=("t" if kt == NKT - 1 else None))

            def emit_norms():
                while nrm_q:
                    aofs_n, sl2_n, ev_n = nrm_q.pop(0)
                    tbN = MM(psE[:, 0:512], sel_sb[:, :], r2_sb[:, sl2_n],
                             start=True, stop=True,
                             waits=[("v", ev_n), ("pl", pl_r2), ("g_tab", g_bv)],
                             inc="t")
                    RECIP(rb_sb[:, :], psE[:, 0:512], waits=[("t", tbN)])
                    TT(ao_sb[0:64, aofs_n:aofs_n + 512],
                       ao_sb[0:64, aofs_n:aofs_n + 512], rb_sb[0:64, :], ALU.mult)
                    TT(ao_sb[64:128, aofs_n:aofs_n + 512],
                       ao_sb[64:128, aofs_n:aofs_n + 512], rb_sb[64:128, :],
                       ALU.mult, inc="v")

            emit_S(0)
            for kt in range(1, NKT):
                emit_S(kt)
                emit_PV(kt - 1)
                if fill_q:
                    fill_q.pop(0)()
                if kt == 4:
                    emit_norms()
            emit_PV(NKT - 1)
            sl2 = slice((qc % 2) * 512, (qc % 2) * 512 + 512)
            aofs = hp * TL + qc * 512
            CP("vector", r2_sb[0:1, sl2], psC[64:65, 0:512], waits=[("t", tpv)])
            CP("vector", r2_sb[64:65, sl2], psD[64:65, 0:512])
            CP("vector", ao_sb[0:64, aofs:aofs + 512], psC[0:64, 0:512])
            evac_prev = CP("vector", ao_sb[64:128, aofs:aofs + 512],
                           psD[0:64, 0:512], inc="v")
            nrm_q.append((aofs, sl2, evac_prev))
        while fill_q:
            fill_q.pop(0)()

    while nrm_q:
        aofs_n, sl2_n, ev_n = nrm_q.pop(0)
        tbN = MM(psE[:, 0:512], sel_sb[:, :], r2_sb[:, sl2_n], start=True,
                 stop=True, waits=[("v", ev_n), ("pl", pl_r2), ("g_tab", g_bv)],
                 inc="t")
        RECIP(rb_sb[:, :], psE[:, 0:512], waits=[("t", tbN)])
        TT(ao_sb[0:64, aofs_n:aofs_n + 512], ao_sb[0:64, aofs_n:aofs_n + 512],
           rb_sb[0:64, :], ALU.mult)
        TT(ao_sb[64:128, aofs_n:aofs_n + 512],
           ao_sb[64:128, aofs_n:aofs_n + 512], rb_sb[64:128, :],
           ALU.mult, inc="v")

    ao_done = cnt["v"]
    stt_h = [None, None]
    st_h = [None, None]
    x1_tc = {}
    b4_rel = [ao_done] * 4
    for i, (dc, tc) in enumerate([(d, t) for d in range(NDT)
                                  for t in range(qc_n)]):
        b = i % 2
        pbk4 = [psC[:, 0:512], psD[:, 0:512], psA[:, 0:512], psB[:, 0:512]][i % 4]
        sx = p.dma("scalar", xr_sb[b][:, :],
                   dx[dc * 128:(dc + 1) * 128, tc * 512:(tc + 1) * 512],
                   waits=[("v", stt_h[b] if stt_h[b] else ao_done)],
                   inc=f"a_xr{b}")
        for kt in range(NDT):
            w0 = []
            if kt == 0:
                w0 = [("v", b4_rel[i % 4]), ("g_wo", a_wo)]
            MM(pbk4,
               wo_sb[:, kt * 1024 + dc * 128: kt * 1024 + (dc + 1) * 128],
               ao_sb[:, kt * TL + tc * 512: kt * TL + (tc + 1) * 512],
               start=(kt == 0), stop=(kt == NDT - 1), waits=w0,
               inc=("t" if kt == NDT - 1 else None))
        vo = STT(o_sb[b][:, :], pbk4, bo_sb[:, dc:dc + 1], xr_sb[b][:, :],
                 ALU.add, ALU.add,
                 waits=[("t", cnt["t"]), (f"a_xr{b}", sx), ("s_tab", tab_s)] +
                       ([(f"x1s{b}", st_h[b])] if st_h[b] else []),
                 inc="v")
        b4_rel[i % 4] = vo
        stt_h[b] = vo
        st_h[b] = p.dma("sync", dx1[dc * 128:(dc + 1) * 128, tc * 512:(tc + 1) * 512],
                        o_sb[b][:, :], waits=[("v", vo)], inc=f"x1s{b}")
        if dc == NDT - 1:
            x1_tc[tc] = [("x1s0", cnt["x1s0"]), ("x1s1", cnt["x1s1"])]

    p.stage("ffn0")
    ar.reset()
    bC = mk_ln_bufs()
    ln1_ch = emit_ln(dx1, TL, dln1, bC,
                     src_ready_fn=lambda ch: x1_tc[ch])
    ln1_done = ln1_ch[-1]
    w1_sb = ar.alloc([128, NDT * 1024], BF16, "w1_sb")
    dsh_sb = ar.alloc([64, 1024], BF16, "dsh_sb")
    ps_sb = ar.alloc([128, 64], BF16, "ps_sb")
    lnc1 = bC["xcb"] + [ar.alloc([128, NDT * 512], BF16, f"lnc1x{i}")
                        for i in range(2)]
    pool_sb = ar.alloc([64, TL], BF16, "pool_sb")
    xc1_sb = [ar.alloc([128, 512], F32, f"xc1_sb{i}") for i in range(2)]
    oc1_sb = [ar.alloc([128, 512], F32, f"oc1_sb{i}") for i in range(2)]

    g_w1 = p.dma("gpsimd", w1_sb.rearrange("p (dt m) -> p dt m", dt=NDT),
                 dw1t.rearrange("(dt p) m -> p dt m", p=128), inc="g_tab")
    g_dsh = p.dma("gpsimd", dsh_sb[:, :], ddisha, inc="g_tab")
    g_ps = p.dma("gpsimd", ps_sb[:, :], dpsel, inc="g_tab")

    lc1_ld = [None] * qc_n
    for tc in range(qc_n):
        qd = "sync" if tc % 2 == 0 else "gpsimd"
        lc1_ld[tc] = p.dma(qd, lnc1[tc].rearrange("p (dt t) -> p dt t", dt=NDT),
                           dln1.rearrange("(dt p) t -> p dt t", p=128)[:, :, tc * 512:(tc + 1) * 512],
                           waits=[ln1_ch[tc + 2] if tc < 2 else ln1_ch[tc]],
                           inc=f"lc1{tc}")
        for dt in range(NDT):
            w0 = []
            if dt == 0:
                w0 = [(f"lc1{tc}", lc1_ld[tc]), ("g_tab", g_ps), ("v", cnt["v"])]
            MM(psE[0:64, 0:512], ps_sb[:, :], lnc1[tc][:, dt * 512:(dt + 1) * 512],
               start=(dt == 0), stop=(dt == NDT - 1), waits=w0,
               inc=("t" if dt == NDT - 1 else None))
        CP("vector", pool_sb[:, tc * 512:(tc + 1) * 512], psE[0:64, 0:512],
           waits=[("t", cnt["t"])])
    ctt_h = [None, None]
    cst_h = [None, None]
    x2_tc = {}
    c2_banks = [psC[:, 0:512], psD[:, 0:512], psA[:, 0:512], psB[:, 0:512]]
    c2_rel = [cnt["v"]] * 4
    for i, (tc, oc) in enumerate([(t, o) for t in range(qc_n)
                                  for o in range(NDT)]):
        b = i % 2
        pbk = c2_banks[i % 4]
        wx = (x1_tc[3] if i < 2 else [("v", ctt_h[b])])
        sx = p.dma("scalar", xc1_sb[b][:, :],
                   dx1[oc * 128:(oc + 1) * 128, tc * 512:(tc + 1) * 512],
                   waits=wx, inc=f"a_xr{b}")
        for dt in range(NDT):
            w0 = []
            if dt == 0:
                w0 = [(f"lc1{tc}", lc1_ld[tc]), ("g_tab", g_w1),
                      ("v", c2_rel[i % 4])]
            MM(pbk, w1_sb[:, dt * 1024 + oc * 128:dt * 1024 + (oc + 1) * 128],
               lnc1[tc][:, dt * 512:(dt + 1) * 512],
               start=(dt == 0), stop=False, waits=w0)
        MM(pbk, dsh_sb[0:64, oc * 128:(oc + 1) * 128],
           pool_sb[0:64, tc * 512:(tc + 1) * 512],
           start=False, stop=True, inc="t")
        vo = TT(oc1_sb[b][:, :], pbk, xc1_sb[b][:, :], ALU.add,
                waits=[("t", cnt["t"]), (f"a_xr{b}", sx)] +
                      ([(f"x2s{b}", cst_h[b])] if cst_h[b] else []),
                inc="v")
        c2_rel[i % 4] = vo
        ctt_h[b] = vo
        cst_h[b] = p.dma("sync", dx2[oc * 128:(oc + 1) * 128, tc * 512:(tc + 1) * 512],
                         oc1_sb[b][:, :], waits=[("v", vo)], inc=f"x2s{b}")
        if oc == NDT - 1:
            x2_tc[tc] = [("x2s0", cnt["x2s0"]), ("x2s1", cnt["x2s1"])]

    p.stage("l1attn")
    ar.reset()
    bD = mk_ln_bufs(dbl=False)
    ln2_ch = emit_ln(dx2, TL, dln2, bD,
                     src_ready_fn=lambda ch: x2_tc[ch])
    ln2_done = ln2_ch[-1]
    NHE = NDT
    wqk1_sb = ar.alloc([128, NDT * 64], BF16, "wqk1_sb")
    wv1_sb = ar.alloc([128, NDT * 1024], BF16, "wv1_sb")
    wo1_sb = ar.overlay("wv1_sb", 0, [128, NDT * 1024], BF16, "wo1_sb")
    sel16_sb = ar.alloc([16, 8 * 128], BF16, "sel16_sb")
    mask_sb = ar.alloc([128, 1], F32, "mask_sb")
    lnc2 = bD["xcb"] + [ar.alloc([128, NDT * 512], BF16, f"lnc2x{i}")
                        for i in range(2)]
    trig_tok = bD["sqb"]
    tqc = ar.alloc([16, TL], BF16, "tqc")
    tkc = ar.alloc([16, TL], BF16, "tkc")
    tqs = ar.alloc([16, TL], BF16, "tqs")
    tks = ar.alloc([16, TL], BF16, "tks")
    v1t_sb = ar.alloc([128, 512], F32, "v1t_sb")
    d4t_sb = ar.alloc([128, 512], F32, "d4t_sb")
    xc_sb = ar.alloc([128, NHE * TL], BF16, "xc_sb")
    xs_sb = ar.alloc([128, NHE * TL], BF16, "xs_sb")
    zer_sb = bD["lnc"][0]
    pub_sb = ar.alloc([128, 24], F32, "pub_sb")
    car_sb = ar.alloc([128, 24], F32, "car_sb")
    xdead = "xcf" + str(_lnbuf_n[0] - 2)
    kcc = ar.overlay(xdead, 0, [16, TL], BF16, "kcc")
    kcs = ar.overlay(xdead, TL * 2, [16, TL], BF16, "kcs")
    dent = ar.overlay(xdead, TL * 4, [16, TL], F32, "dent")
    dent2 = ar.overlay(xdead, TL * 12, [16, TL], BF16, "dent2")
    xdead2 = "xcf" + str(_lnbuf_n[0] - 1)
    a16 = ar.overlay(xdead2, 0, [16, TL], BF16, "a16")
    b16 = ar.overlay(xdead2, TL * 2, [16, TL], BF16, "b16")
    xr2_sb = [ar.overlay(xdead2, TL * 6, [128, 512], F32, "xr2_sb0"),
              ar.overlay(xdead2, TL * 7, [128, 512], F32, "xr2_sb1")]
    sqbD = "sqb" + str(_lnbuf_n[0] - 2)
    oc2_sb = [ar.overlay(sqbD, 0, [128, 512], F32, "oc2_sb0"),
              ar.overlay(sqbD, 2048, [128, 512], F32, "oc2_sb1")]

    g_wqk1 = p.dma("gpsimd", wqk1_sb.rearrange("p (dt m) -> p dt m", dt=NDT),
                   dwqk1.rearrange("(dt p) m -> p dt m", p=128), inc="g_tab")
    g_wv1 = p.dma("gpsimd", wv1_sb.rearrange("p (dt m) -> p dt m", dt=NDT),
                  dwv1.rearrange("(dt p) m -> p dt m", p=128), inc="g_tab")
    g_s16 = p.dma("gpsimd", sel16_sb[:, :], dsel16, inc="g_tab")
    g_msk = p.dma("gpsimd", mask_sb[:, :], dmask, inc="g_tab")
    MS(zer_sb[:, 0:TL], 0.0, waits=[ln2_done], inc="pl")
    zer_done = cnt["pl"]
    MS(pub_sb[:, :], 0.0, inc="pl")
    pub0_done = cnt["pl"]

    lc2_ld = [None] * qc_n
    for tc in range(qc_n):
        qd = "sync" if tc % 2 == 0 else "scalar"
        lc2_ld[tc] = p.dma(qd, lnc2[tc].rearrange("p (dt t) -> p dt t", dt=NDT),
                           dln2.rearrange("(dt p) t -> p dt t", p=128)[:, :, tc * 512:(tc + 1) * 512],
                           waits=[ln2_ch[tc + 2] if tc < 2 else ln2_ch[tc]],
                           inc=f"lc2{tc}")
    for tch in range(16):
        tc4 = tch // 4
        co = (tch % 4) * 128
        for dt in range(NDT):
            w0 = []
            if dt == 0:
                w0 = [(f"lc2{tc4}", lc2_ld[tc4]), ("g_tab", g_wqk1), ("a", cnt["a"])]
            MM(psE[:, 0:64], lnc2[tc4][:, dt * 512 + co:dt * 512 + co + 128],
               wqk1_sb[:, dt * 64:(dt + 1) * 64],
               start=(dt == 0), stop=(dt == NDT - 1), waits=w0,
               inc=("t" if dt == NDT - 1 else None))
        ACTV(trig_tok[:, tch * 128:tch * 128 + 48], psE[:, 0:48], AF.Tanh,
             waits=[("t", cnt["t"])])
        ACTV(trig_tok[:, tch * 128:tch * 128 + 48],
             trig_tok[:, tch * 128:tch * 128 + 48], AF.Sin,
             scale=float(np.pi / 4))
        ACTV(trig_tok[:, tch * 128 + 64:tch * 128 + 112], psE[:, 0:48], AF.Tanh)
        ACTV(trig_tok[:, tch * 128 + 64:tch * 128 + 112],
             trig_tok[:, tch * 128 + 64:tch * 128 + 112], AF.Sin,
             scale=float(np.pi / 4), bias=consts[:, 3:4], inc="a")
    av = cnt["a"]
    for tch in range(16):
        w0 = [("v", cnt["v"])] if tch else []
        if tch == 0:
            w0 = [("a", av), ("g_tab", g_id)]
        p.emit("tensor",
               (lambda e, o=psT[0:48, 0:128],
                i=trig_tok[:, tch * 128 + 64:tch * 128 + 112],
                idn=id_sb[:, :]: e.matmul(o, i, idn, start=True, stop=True)), waits=w0)
        p.emit("tensor",
               (lambda e, o=psT[0:48, 128:256],
                i=trig_tok[:, tch * 128:tch * 128 + 48],
                idn=id_sb[:, :]: e.matmul(o, i, idn, start=True, stop=True)), inc="t")
        tsl = slice(tch * 128, (tch + 1) * 128)
        CP("vector", tqc[:, tsl], psT[0:16, 0:128], waits=[("t", cnt["t"])])
        CP("vector", tkc[:, tsl], psT[32:48, 0:128])
        CP("vector", tqs[:, tsl], psT[0:16, 128:256])
        CP("vector", tks[:, tsl], psT[32:48, 128:256], inc="v")
    trig_done = cnt["v"]

    d2_vb = [psC[:, 0:512], psA[:, 0:512]]
    d2_bc = [(psE[:, 0:512], psD[:, 0:512]),
             (psB[:, 0:512], psB[:, 512:1024])]
    d2_rv = [cnt["v"]] * 2
    d2_rb = [cnt["v"]] * 2
    for kt in range(NHE):
        for tc in range(qc_n):
            j = (kt * qc_n + tc) % 2
            vb = d2_vb[j]
            bcA, bcB = d2_bc[j]
            for dt in range(NDT):
                w0 = []
                if dt == 0:
                    w0 = [(f"lc2{tc}", lc2_ld[tc]), ("g_tab", g_wv1),
                          ("v", d2_rv[j])]
                MM(vb, wv1_sb[:, dt * 1024 + kt * 128:dt * 1024 + (kt + 1) * 128],
                   lnc2[tc][:, dt * 512:(dt + 1) * 512],
                   start=(dt == 0), stop=(dt == NDT - 1), waits=w0,
                   inc=("t" if dt == NDT - 1 else None))
            MM(bcA, sel16_sb[:, kt * 128:(kt + 1) * 128],
               tkc[:, tc * 512:(tc + 1) * 512], start=True, stop=True,
               waits=[("v", trig_done), ("v", d2_rb[j]), ("g_tab", g_s16)])
            tbb = MM(bcB, sel16_sb[:, kt * 128:(kt + 1) * 128],
                     tks[:, tc * 512:(tc + 1) * 512], start=True, stop=True,
                     inc="t")
            CP("vector", v1t_sb[:, :], vb, waits=[("t", tbb)])
            d2_rv[j] = cnt["v"]
            TT(xc_sb[:, kt * TL + tc * 512:kt * TL + (tc + 1) * 512],
               v1t_sb[:, :], bcA, ALU.mult)
            d2_rb[j] = TT(xs_sb[:, kt * TL + tc * 512:kt * TL + (tc + 1) * 512],
                          v1t_sb[:, :], bcB, ALU.mult, inc="v")
        RED(pub_sb[:, kt:kt + 1], xc_sb[:, kt * TL:(kt + 1) * TL], ALU.add,
            waits=([("pl", pub0_done)] if kt == 0 else []))
        RED(pub_sb[:, 8 + kt:9 + kt], xs_sb[:, kt * TL:(kt + 1) * TL], ALU.add)
    d2_t = cnt["t"]
    wo13 = wo1_sb.rearrange("p (kt d) -> p kt d", kt=NDT)
    dwo13 = dwo1.rearrange("(kt p) d -> p kt d", p=128)
    g_wo1 = p.dma("gpsimd", wo13[:, 0:4, :], dwo13[:, 0:4, :],
                  waits=[("t", d2_t)], inc="g_wo1")
    s_wo1 = p.dma("gpsimd", wo13[:, 4:8, :], dwo13[:, 4:8, :],
                  waits=[("t", d2_t)], inc="s_wo1")
    RED(pub_sb[0:16, 16:17], tkc[:, :], ALU.add)
    vpub = RED(pub_sb[0:16, 17:18], tks[:, :], ALU.add, inc="v")
    s_pub = p.dma("sync", dcar_in, pub_sb[:, :], waits=[("v", vpub)], inc="s_fin")
    p.sem("cc")
    ccg = p.emit("gpsimd", (lambda e: e.collective_compute(
        "AllGather", ALU.bypass,
        replica_groups=[[2 * i, 2 * i + 1] for i in range(NCORES // 2)],
        ins=[dcar_in.opt()], outs=[dcar_out.opt()])),
        waits=[("s_fin", s_pub)], inc="cc", amt=1, dma=True)

    zcol = zer_sb[:, 0:1]
    v_pub = cnt["v"]
    SCAN(kcc[:, :], tkc[:, :], zer_sb[0:16, 0:TL], zer_sb[0:16, 0:1],
         waits=[("pl", zer_done)])
    SCAN(kcs[:, :], tks[:, :], zer_sb[0:16, 0:TL], zer_sb[0:16, 0:1])
    for kt in range(NHE):
        SCAN(xc_sb[:, kt * TL:(kt + 1) * TL], xc_sb[:, kt * TL:(kt + 1) * TL],
             zer_sb[:, 0:TL], zcol)
        SCAN(xs_sb[:, kt * TL:(kt + 1) * TL], xs_sb[:, kt * TL:(kt + 1) * TL],
             zer_sb[:, 0:TL], zcol)

    def read_partner(e, dst=car_sb, srcd=dcar_out):
        pid = e.partition_id()
        partner = nc.s_assert_within(
            e.compute_val(1 - pid % 2), 0, 1, skip_runtime_assert=True)
        return e.dma_start(dst[:, :],
                           srcd.rearrange("(r p) f -> r p f", r=2)[partner, :, :])
    p.sem("g_car")
    g_car = p.emit("gpsimd", read_partner, waits=[("cc", cnt["cc"])], inc="g_car",
                   amt=16, dma=True)
    vcar = TS(car_sb[:, :], car_sb[:, :], mask_sb[:, 0:1], None, ALU.mult,
              waits=[("g_car", g_car), ("g_tab", g_msk)], inc="v")

    STT(dent[:, :], kcc[:, :], car_sb[0:16, 16:17], tqc[:, :], ALU.add, ALU.mult,
        waits=[("v", vcar)])
    STT(dent2[:, :], kcs[:, :], car_sb[0:16, 17:18], tqs[:, :], ALU.add, ALU.mult)
    TT(dent[:, :], dent[:, :], dent2[:, :], ALU.add)
    TS(dent[:, :], dent[:, :], consts[0:16, 2:3], None, ALU.add)
    RECIP(dent2[:, :], dent[:, :])
    TT(a16[:, :], tqc[:, :], dent2[:, :], ALU.mult)
    vab = TT(b16[:, :], tqs[:, :], dent2[:, :], ALU.mult, inc="v")

    d4_rel = [None, None]
    for kt in range(NHE):
        for tc in range(qc_n):
            j = (kt * qc_n + tc) % 2
            bcA, bcB = d2_bc[j]
            sl = slice(tc * 512, (tc + 1) * 512)
            ksl = slice(kt * TL + tc * 512, kt * TL + (tc + 1) * 512)
            MM(bcA, sel16_sb[:, kt * 128:(kt + 1) * 128], a16[:, sl],
               start=True, stop=True,
               waits=[("v", d4_rel[j] if d4_rel[j] else cnt["v"])])
            tb2 = MM(bcB, sel16_sb[:, kt * 128:(kt + 1) * 128],
                     b16[:, sl], start=True, stop=True, inc="t")
            STT(v1t_sb[:, :], xc_sb[:, ksl], car_sb[:, kt:kt + 1], bcA,
                ALU.add, ALU.mult, waits=[("t", tb2)])
            STT(d4t_sb[:, :], xs_sb[:, ksl], car_sb[:, 8 + kt:9 + kt],
                bcB, ALU.add, ALU.mult)
            d4_rel[j] = TT(xc_sb[:, ksl], v1t_sb[:, :], d4t_sb[:, :], ALU.add,
                           inc="v")
    heads_done = cnt["v"]
    dtt_h = [None, None]
    dst_h = [None, None]
    x3_tc = {}
    for i, (dc, tc) in enumerate([(d, t) for d in range(NDT)
                                  for t in range(qc_n)]):
        b = i % 2
        sx = p.dma("scalar", xr2_sb[b][:, :],
                   dx2[dc * 128:(dc + 1) * 128, tc * 512:(tc + 1) * 512],
                   waits=[("v", dtt_h[b] if dtt_h[b] else heads_done)],
                   inc=f"a_xr{b}")
        for kt in range(NDT):
            w0 = []
            if kt == 0:
                w0 = [("v", heads_done), ("v", cnt["v"]),
                      ("g_wo1", g_wo1), ("s_wo1", s_wo1)]
            MM(psC[:, 0:512],
               wo1_sb[:, kt * 1024 + dc * 128:kt * 1024 + (dc + 1) * 128],
               xc_sb[:, kt * TL + tc * 512:kt * TL + (tc + 1) * 512],
               start=(kt == 0), stop=(kt == NDT - 1), waits=w0,
               inc=("t" if kt == NDT - 1 else None))
        vo = TT(oc2_sb[b][:, :], psC[:, 0:512], xr2_sb[b][:, :], ALU.add,
                waits=[("t", cnt["t"]), (f"a_xr{b}", sx)] +
                      ([(f"x3s{b}", dst_h[b])] if dst_h[b] else []),
                inc="v")
        dtt_h[b] = vo
        dst_h[b] = p.dma("sync", dx3[dc * 128:(dc + 1) * 128, tc * 512:(tc + 1) * 512],
                         oc2_sb[b][:, :], waits=[("v", vo)], inc=f"x3s{b}")
        if dc == NDT - 1:
            x3_tc[tc] = [("x3s0", cnt["x3s0"]), ("x3s1", cnt["x3s1"])]

    p.stage("moe")
    ar.reset()
    bE = mk_ln_bufs(dbl=False)
    _ie = _lnbuf_n[0] - 2
    lnc3 = bE["xcb"] + [ar.overlay(f"rows{_ie}", 0, [128, NDT * 512], BF16, "lnc3x0"),
                        ar.overlay(f"rows2_{_ie}", 0, [128, NDT * 512], BF16, "lnc3x1")]
    ng_sb = ar.alloc([128, 8], F32, "ng_sb")
    nb_sb = ar.alloc([128, 8], F32, "nb_sb")
    sg_sb = ar.alloc([128, 8], F32, "sg_sb")
    sbb_sb = ar.alloc([128, 8], F32, "sbb_sb")
    scol_sb = ar.alloc([128, 1], F32, "scol_sb")
    sc1_sb = ar.alloc([1, 1], F32, "sc1_sb")
    se_sb = ar.alloc([128, NDT * 1024], BF16, "se_sb")
    logF = ar.alloc([128, 8 * TL], F32, "logF")
    dD_sb = ar.alloc([128, 8 * TL], BF16, "dD_sb")
    mneg_sb = ar.alloc([128, 8], F32, "mneg_sb")
    mpos_sb = ar.alloc([128, 8], F32, "mpos_sb")
    sesum_sb = ar.alloc([128, 8], F32, "sesum_sb")
    cmneg_sb = ar.alloc([128, 1], F32, "cmneg_sb")
    crs_sb = ar.alloc([128, 1], F32, "crs_sb")
    bg_sb = ar.alloc([128, 32], F32, "bg_sb")
    bu_sb = ar.alloc([128, 32], F32, "bu_sb")
    bdr_sb = ar.alloc([1, 2048], BF16, "bdr_sb")
    xtok_sb = ar.overlay("logF", 0, [128, 16 * 1024], BF16, "xtok_sb")
    dtok_sb = ar.overlay("logF", 32768, [128, 16 * 1024], BF16, "dtok_sb")
    cfm_sb = ar.overlay("dD_sb", 0, [128, 8 * TL], BF16, "cfm_sb")
    wgx_sb = ar.overlay("logF", 0, [128, NDT * 2048], BF16, "wgx_sb")
    gt_sb = ar.overlay("logF", 32768, [128, 16 * 256], BF16, "gt_sb")
    gu_sb = ar.overlay("logF", 32768 + 8192, [128, 16 * 256], BF16, "gu_sb")
    slfm_sb = ar.overlay("logF", 32768 + 16384, [128, NDT * 512], BF16, "slfm_sb")
    bd_sb = ar.overlay("logF", 32768 + 24576, [128, 1024], F32, "bd_sb")
    eo_sb = ar.overlay("logF", 32768 + 28672, [128, 1024], BF16, "eo_sb")
    xdE = "xcf" + str(_lnbuf_n[0] - 2)
    eof_sb = ar.overlay(xdE, 0, [128, 8 * 1024], BF16, "eof_sb")
    xdE2 = "xcf" + str(_lnbuf_n[0] - 1)
    stg_sb = ar.overlay(xdE2, 0, [128, 1032], BF16, "stg_sb")
    cn_sb = ar.overlay(xdE2, 2080, [128, 1024], BF16, "cn_sb")
    prt2_sb = ar.overlay(xdE2, 4160, [128, 2080], BF16, "prt2")
    prt_sb = [prt2_sb[:, 0:1032], prt2_sb[:, 1040:2072]]
    combf_sb = ar.overlay(xdE2, 8352, [128, 1024], F32, "combf_sb")
    mnb_sb = ar.overlay(xdE2, 12448, [128, 8], BF16, "mnb_sb")
    mnr_sb = ar.overlay(xdE2, 12480, [128, 8], F32, "mnr_sb")
    sqbE = "sqb" + str(_lnbuf_n[0] - 2)
    xr3_sb = [ar.overlay(sqbE, 0, [128, 512], F32, "xr3_sb0"),
              ar.overlay(sqbE, 2048, [128, 512], F32, "xr3_sb1")]
    oc3_sb = [ar.overlay(sqbE, 4096, [128, 512], F32, "oc3_sb0"),
              ar.overlay(sqbE, 6144, [128, 512], F32, "oc3_sb1")]

    g_ng = p.dma("gpsimd", ng_sb[:, :], dng, inc="g_tab")
    p.dma("gpsimd", nb_sb[:, :], dnb, inc="g_tab")
    p.dma("gpsimd", sg_sb[:, :], dsg, inc="g_tab")
    p.dma("gpsimd", sbb_sb[:, :], dsbb, inc="g_tab")
    g_sc = p.dma("gpsimd", sc1_sb[:, :], dscale, inc="g_tab")
    MM(psE[:, 0:1], ones1x128[:, :], sc1_sb[:, :], start=True, stop=True,
       waits=[("g_tab", g_sc), ("v", cnt["v"])], inc="t")
    vsc = CP("vector", scol_sb[:, :], psE[:, 0:1], waits=[("t", cnt["t"])], inc="v")
    ln3_ch = emit_ln(dx3, TL, dln3, bE, gten=ng_sb, bten=nb_sb,
                     src_ready_fn=lambda ch: x3_tc[ch] + [("g_tab", g_ng)])
    ln3_done = ln3_ch[-1]
    seln_done = emit_ln(dse, 1024, dse_ln, bE, gten=sg_sb, bten=sbb_sb,
                        postscale=scol_sb[:, 0:1],
                        src_ready=[("t", cnt["t"]), ("v", cnt["v"]),
                                   ("a", cnt["a"])],
                        prev_st={0: ln3_ch[2], 1: ln3_ch[3]})[-1]
    g_se = p.dma("gpsimd", se_sb.rearrange("p (dt s) -> p dt s", dt=NDT),
                 dse_ln.rearrange("(dt p) s -> p dt s", p=128),
                 waits=[seln_done], inc="g_tab")
    g_bg = p.dma("gpsimd", bg_sb[:, :], dbgc, inc="g_tab")
    g_bu = p.dma("gpsimd", bu_sb[:, :], dbuc, inc="g_tab")
    g_bd = p.dma("gpsimd", bdr_sb.rearrange("o (e f) -> o e f", e=2),
                 dbdr.rearrange("(o e) f -> o e f", o=1), inc="g_tab")
    seln_t = cnt["t"]

    lc3_ld = [None] * qc_n
    for tc in range(qc_n):
        qd = "sync" if tc % 2 == 0 else "scalar"
        wl = [ln3_ch[tc], ("t", seln_t)]
        lc3_ld[tc] = p.dma(qd, lnc3[tc].rearrange("p (dt t) -> p dt t", dt=NDT),
                           dln3.rearrange("(dt p) t -> p dt t", p=128)[:, :, tc * 512:(tc + 1) * 512],
                           waits=wl, inc=f"lc3{tc}")
    for tc in range(qc_n):
        for st in range(NDT):
            pbk = psC if (st * qc_n + tc) % 2 == 0 else psD
            for dt in range(NDT):
                w0 = []
                if dt == 0:
                    w0 = [(f"lc3{tc}", lc3_ld[tc]), ("g_tab", g_se), ("v", cnt["v"])]
                MM(pbk[:, 0:512], se_sb[:, dt * 1024 + st * 128:dt * 1024 + (st + 1) * 128],
                   lnc3[tc][:, dt * 512:(dt + 1) * 512],
                   start=(dt == 0), stop=(dt == NDT - 1), waits=w0,
                   inc=("t" if dt == NDT - 1 else None))
            CP("vector", logF[:, st * TL + tc * 512:st * TL + (tc + 1) * 512],
               pbk[:, 0:512], waits=[("t", cnt["t"])], inc="v")
    for st in range(NDT):
        ssl = slice(st * TL, (st + 1) * TL)
        RED(mneg_sb[:, st:st + 1], logF[:, ssl], ALU.max, negate=True)
        CP("vector", mnb_sb[:, st:st + 1], mneg_sb[:, st:st + 1])
        va = CP("vector", mnr_sb[:, st:st + 1], mnb_sb[:, st:st + 1], inc="v")
        ACTV(dD_sb[:, ssl], logF[:, ssl], AF.Exp, bias=mnr_sb[:, st:st + 1],
             waits=[("v", va)], inc="a")
        RED(sesum_sb[:, st:st + 1], dD_sb[:, ssl], ALU.add,
            waits=[("a", cnt["a"])], inc="v")
    disp_done_a, disp_done_v = cnt["a"], cnt["v"]

    for tch in range(16):
        tc4, sub = tch // 4, tch % 4
        for g in range(2):
            for d4 in range(4):
                dt = g * 4 + d4
                w0 = [("v", cnt["v"])]
                if dt == 0:
                    w0 += [(f"lc3{tc4}", lc3_ld[tc4]), ("a", disp_done_a)]
                p.emit("tensor",
                       (lambda e, o=psT[:, d4 * 128:(d4 + 1) * 128],
                        i=lnc3[tc4][:, dt * 512 + sub * 128:dt * 512 + sub * 128 + 128],
                        idn=id_sb[:, :]: e.matmul(o, i, idn, start=True, stop=True)),
                       waits=w0, inc=("t" if d4 == 3 else None))
            CP("vector",
               xtok_sb[:, tch * 1024 + g * 512:tch * 1024 + (g + 1) * 512],
               psT[:, :],
               waits=[("t", cnt["t"]), ("v", disp_done_v), ("a", disp_done_a)],
               inc="v")
    for tch in range(16):
        for g in range(2):
            for s4 in range(4):
                st = g * 4 + s4
                w0 = [("v", cnt["v"])]
                p.emit("tensor",
                       (lambda e, o=psT[:, s4 * 128:(s4 + 1) * 128],
                        i=dD_sb[:, st * TL + tch * 128:st * TL + (tch + 1) * 128],
                        idn=id_sb[:, :]: e.matmul(o, i, idn, start=True, stop=True)),
                       waits=w0, inc=("t" if s4 == 3 else None))
            CP("vector",
               dtok_sb[:, tch * 1024 + g * 512:tch * 1024 + (g + 1) * 512],
               psT[:, :],
               waits=[("t", cnt["t"])], inc="v")
    tr_done = cnt["v"]

    MS(stg_sb[:, 1026:1032], 0.0, inc="pl")
    stg0 = cnt["pl"]
    for st in range(NDT):
        for tch in range(16):
            w0 = []
            if tch == 0:
                w0 = [("v", tr_done), ("v", cnt["v"])]
            MM(psA[:, 0:512], dtok_sb[:, tch * 1024 + st * 128:tch * 1024 + (st + 1) * 128],
               xtok_sb[:, tch * 1024:tch * 1024 + 512],
               start=(tch == 0), stop=(tch == 15), waits=w0)
            MM(psA[:, 512:1024], dtok_sb[:, tch * 1024 + st * 128:tch * 1024 + (st + 1) * 128],
               xtok_sb[:, tch * 1024 + 512:tch * 1024 + 1024],
               start=(tch == 0), stop=(tch == 15),
               inc=("t" if tch == 15 else None))
        t_slots = cnt["t"]
        CP("vector", stg_sb[:, 0:512], psA[:, 0:512],
           waits=[("t", cnt["t"]), ("pl", stg0)] +
                 ([("s_slag", cnt["s_slag"])] if cnt.get("s_slag") else []))
        CP("vector", stg_sb[:, 512:1024], psA[:, 512:1024])
        p.emit("vector", (lambda e, o=stg_sb[:, 1024:1025],
                          i=mnr_sb[:, st:st + 1]: e.tensor_scalar_mul(o, i, -1.0)))
        vstg = CP("vector", stg_sb[:, 1025:1026], sesum_sb[:, st:st + 1], inc="v")
        p.dma("sync", dsl_in[st * 128:(st + 1) * 128, :], stg_sb[:, :],
              waits=[("v", vstg)], inc="s_slag")
    p.sem("cc")
    cc_sl = p.emit("gpsimd", (lambda e: e.collective_compute(
        "AllGather", ALU.bypass, replica_groups=[list(range(NCORES))],
        ins=[dsl_in.opt()], outs=[dsl_out.opt()])),
        waits=[("s_slag", cnt["s_slag"])], inc="cc", amt=1, dma=True)

    p.sem("g_wx")
    p.sem("s_wx")

    def load_wx(dram_e, nch, war_t):
        dst = wgx_sb.rearrange("p (c f) -> p c f", c=nch)
        src = dram_e
        h = nch // 2
        wg_ = [("t", war_t)]
        if cnt["g_wx"]:
            wg_.append(("g_wx", cnt["g_wx"]))
        g = p.dma("gpsimd", dst[:, 0:h, :], src[:, 0:h, :], waits=wg_, inc="g_wx")
        ws_ = [("t", war_t)]
        if cnt["s_wx"]:
            ws_.append(("s_wx", cnt["s_wx"]))
        s = p.dma("gpsimd", dst[:, h:nch, :], src[:, h:nch, :], waits=ws_, inc="s_wx")
        return g, s, h

    wg0_pre = load_wx(dwg.rearrange("e (dt p) f -> e p dt f", p=128)[0],
                      NDT, t_slots)

    for tch in range(16):
        tc4, sub = tch // 4, tch % 4
        for dt in range(NDT):
            w0 = []
            if dt == 0:
                w0 = [(f"lc3{tc4}", lc3_ld[tc4]), ("v", cnt["v"]), ("a", cnt["a"])]
            MM(psB[:, 0:512], lnc3[tc4][:, dt * 512 + sub * 128:dt * 512 + sub * 128 + 128],
               se_sb[:, dt * 1024:dt * 1024 + 512],
               start=(dt == 0), stop=(dt == NDT - 1), waits=w0)
            MM(psB[:, 512:1024], lnc3[tc4][:, dt * 512 + sub * 128:dt * 512 + sub * 128 + 128],
               se_sb[:, dt * 1024 + 512:dt * 1024 + 1024],
               start=(dt == 0), stop=(dt == NDT - 1),
               inc=("t" if dt == NDT - 1 else None))
        RED(cmneg_sb[:, 0:1], psB[:, 0:1024], ALU.max, negate=True,
            waits=[("t", cnt["t"])])
        ACTV(cn_sb[:, :], psB[:, 0:1024], AF.Exp, bias=cmneg_sb[:, 0:1],
             waits=[("v", cnt["v"])], inc="a")
        RED(crs_sb[:, 0:1], cn_sb[:, :], ALU.add, waits=[("a", cnt["a"])])
        RECIP(crs_sb[:, 0:1], crs_sb[:, 0:1])
        TS(cn_sb[:, :], cn_sb[:, :], crs_sb[:, 0:1], None, ALU.mult, inc="v")
        for g in range(2):
            for s4 in range(4):
                st = g * 4 + s4
                w0 = [("v", cnt["v"])]
                p.emit("tensor",
                       (lambda e, o=psT[:, s4 * 128:(s4 + 1) * 128],
                        i=cn_sb[:, st * 128:(st + 1) * 128],
                        idn=id_sb[:, :]: e.matmul(o, i, idn, start=True, stop=True)),
                       waits=w0, inc=("t" if s4 == 3 else None))
            p.emit("vector",
                   (lambda e, tch=tch, g=g: e.tensor_copy(
                       cfm_sb.rearrange("p (st t) -> p st t", st=NDT)[:, 4 * g:4 * g + 4, tch * 128:(tch + 1) * 128],
                       psT.rearrange("p (s4 c) -> p s4 c", s4=4))),
                   waits=[("t", cnt["t"])], inc="v")
    cfm_done = cnt["v"]

    p.sem("g_sl")
    _dynidx = {}

    def get_me(e):
        if "me" not in _dynidx:
            pid = e.partition_id()
            _dynidx["me"] = nc.s_assert_within(pid, 0, 7, skip_runtime_assert=True)
        return _dynidx["me"]

    for b in range(4):
        def read_pair(e, b=b):
            me = get_me(e)
            return e.dma_start(
                prt2_sb.rearrange("p (r f) -> p r f", r=2)[:, :, 0:1032],
                dsl_out.rearrange("(r q p) f -> q p r f", q=NCORES, p=128)[me, :, 2 * b:2 * b + 2, :])
        p.emit("gpsimd", read_pair,
               waits=[("cc", cc_sl), ("g_sl", cnt["g_sl"])] +
                     ([("v", cnt["v"])] if b else []),
               inc="g_sl", amt=16, dma=True)
        g_pr = cnt["g_sl"]
        TT(cmneg_sb[:, 0:1], prt_sb[0][:, 1024:1025], prt_sb[1][:, 1024:1025],
           ALU.max, waits=[("g_sl", g_pr)])
        TT(crs_sb[:, 0:1], prt_sb[0][:, 1024:1025], cmneg_sb[:, 0:1], ALU.subtract)
        ACTV(crs_sb[:, 0:1], crs_sb[:, 0:1], AF.Exp, waits=[("v", cnt["v"])], inc="a")
        TT(mneg_sb[:, 0:1], prt_sb[1][:, 1024:1025], cmneg_sb[:, 0:1], ALU.subtract,
           waits=[("a", cnt["a"])])
        ACTV(mneg_sb[:, 0:1], mneg_sb[:, 0:1], AF.Exp, waits=[("v", cnt["v"])], inc="a")
        TS(mpos_sb[:, 0:1], prt_sb[0][:, 1025:1026], crs_sb[:, 0:1], None, ALU.mult,
           waits=[("a", cnt["a"])])
        STT(mpos_sb[:, 0:1], prt_sb[1][:, 1025:1026], mneg_sb[:, 0:1],
            mpos_sb[:, 0:1], ALU.mult, ALU.add)
        RECIP(mpos_sb[:, 0:1], mpos_sb[:, 0:1])
        TS(combf_sb[:, :], prt_sb[0][:, 0:1024], crs_sb[:, 0:1], None, ALU.mult)
        STT(combf_sb[:, :], prt_sb[1][:, 0:1024], mneg_sb[:, 0:1],
            combf_sb[:, :], ALU.mult, ALU.add)
        TS(cn_sb[:, :], combf_sb[:, :], mpos_sb[:, 0:1], None, ALU.mult, inc="v")
        for g in range(2):
            for d4 in range(4):
                dt = g * 4 + d4
                w0 = [("v", cnt["v"])]
                p.emit("tensor",
                       (lambda e, o=psT[:, d4 * 128:(d4 + 1) * 128],
                        i=cn_sb[:, dt * 128:(dt + 1) * 128],
                        idn=id_sb[:, :]: e.matmul(o, i, idn, start=True, stop=True)),
                       waits=w0, inc=("t" if d4 == 3 else None))
            vtr = p.emit("vector",
                         (lambda e, b=b, g=g: e.tensor_copy(
                             slfm_sb.rearrange("p (dt c) -> p dt c", dt=NDT)[:, 4 * g:4 * g + 4, b * 128:(b + 1) * 128],
                             psT.rearrange("p (d4 c) -> p d4 c", d4=4))),
                         waits=[("t", cnt["t"])], inc="v")
    slots_done = cnt["v"]

    for e_i in range(2):
        if e_i == 0:
            g_wg, s_wg, hh = wg0_pre
        else:
            g_wg, s_wg, hh = load_wx(
                dwg.rearrange("e (dt p) f -> e p dt f", p=128)[e_i], NDT, cnt["t"])
        sl_e = slfm_sb.rearrange("p (dt b s) -> p dt b s", dt=NDT, b=4)
        for fc in range(16):
            for dt in range(NDT):
                w0 = []
                if dt == 0:
                    w0 = [("a", cnt["a"])]
                    if fc == 0:
                        w0 += [("g_wx", g_wg), ("v", slots_done)]
                if fc == 0 and dt == hh:
                    w0 = w0 + [("s_wx", s_wg)]
                MM(psC[:, 0:256], wgx_sb[:, dt * 2048 + fc * 128:dt * 2048 + (fc + 1) * 128],
                   sl_e[:, dt, :, e_i * 64:(e_i + 1) * 64],
                   start=(dt == 0), stop=(dt == NDT - 1), waits=w0,
                   inc=("t" if dt == NDT - 1 else None))
            ACTV(gt_sb[:, fc * 256:(fc + 1) * 256], psC[:, 0:256], AF.Gelu,
                 bias=bg_sb[:, e_i * 16 + fc:e_i * 16 + fc + 1],
                 waits=[("t", cnt["t"]), ("g_tab", g_bg)], inc="a")
        gate_a = cnt["a"]
        g_wu, s_wu, _ = load_wx(dwu.rearrange("e (dt p) f -> e p dt f", p=128)[e_i],
                                NDT, cnt["t"])
        for fc in range(16):
            for dt in range(NDT):
                w0 = []
                if dt == 0:
                    w0 = [("v", cnt["v"])]
                    if fc == 0:
                        w0.append(("g_wx", g_wu))
                if fc == 0 and dt == hh:
                    w0 = w0 + [("s_wx", s_wu)]
                MM(psD[:, 0:256], wgx_sb[:, dt * 2048 + fc * 128:dt * 2048 + (fc + 1) * 128],
                   sl_e[:, dt, :, e_i * 64:(e_i + 1) * 64],
                   start=(dt == 0), stop=(dt == NDT - 1), waits=w0,
                   inc=("t" if dt == NDT - 1 else None))
            STT(gu_sb[:, fc * 256:(fc + 1) * 256], psD[:, 0:256],
                bu_sb[:, e_i * 16 + fc:e_i * 16 + fc + 1],
                gt_sb[:, fc * 256:(fc + 1) * 256], ALU.add, ALU.mult,
                waits=[("t", cnt["t"]), ("a", gate_a), ("g_tab", g_bu)], inc="v")
        gu_done = cnt["v"]
        g_wd, s_wd, hd = load_wx(dwd.rearrange("e (ft p) o -> e p ft o", p=128)[e_i],
                                 16, cnt["t"])
        MM(psB[:, 0:512], ones1x128b[:, :], bdr_sb[:, e_i * 1024:e_i * 1024 + 512],
           start=True, stop=True, waits=[("g_tab", g_bd), ("v", cnt["v"])])
        MM(psB[:, 512:1024], ones1x128b[:, :],
           bdr_sb[:, e_i * 1024 + 512:e_i * 1024 + 1024], start=True, stop=True,
           inc="t")
        CP("vector", bd_sb[:, :], psB[:, 0:1024], waits=[("t", cnt["t"])], inc="v")
        for sc in range(2):
            for ft in range(16):
                w0 = []
                if ft == 0:
                    w0 = [("v", gu_done), ("v", cnt["v"])]
                    if sc == 0:
                        w0.append(("g_wx", g_wd))
                if sc == 0 and ft == hd:
                    w0 = w0 + [("s_wx", s_wd)]
                MM(psA[:, 0:512], gu_sb[:, ft * 256 + sc * 128:ft * 256 + sc * 128 + 128],
                   wgx_sb[:, ft * 1024:ft * 1024 + 512],
                   start=(ft == 0), stop=(ft == 15), waits=w0)
                MM(psA[:, 512:1024], gu_sb[:, ft * 256 + sc * 128:ft * 256 + sc * 128 + 128],
                   wgx_sb[:, ft * 1024 + 512:ft * 1024 + 1024],
                   start=(ft == 0), stop=(ft == 15),
                   inc=("t" if ft == 15 else None))
            veo = TT(eo_sb[:, :], psA[:, 0:1024], bd_sb[:, :], ALU.add,
                     waits=[("t", cnt["t"])] +
                           ([("s_eoag", cnt["s_eoag"])] if cnt.get("s_eoag") else []),
                     inc="v")
            p.dma("sync",
                  deo_in[2 * sc * 128 + e_i * 64:2 * sc * 128 + (e_i + 1) * 64, :],
                  eo_sb[0:64, :], waits=[("v", veo)], inc="s_eoag")
            p.dma("sync",
                  deo_in[(2 * sc + 1) * 128 + e_i * 64:(2 * sc + 1) * 128 + (e_i + 1) * 64, :],
                  eo_sb[64:128, :], inc="s_eoag")
    cc_eo = p.emit("gpsimd", (lambda e: e.collective_compute(
        "AllGather", ALU.bypass, replica_groups=[list(range(NCORES))],
        ins=[deo_in.opt()], outs=[deo_out.opt()])),
        waits=[("s_eoag", cnt["s_eoag"])], inc="cc", amt=1, dma=True)

    p.sem("g_eo")

    def read_eo(e):
        pid = e.partition_id()
        bp = nc.s_assert_within(e.compute_val(pid // 2), 0, 3,
                                skip_runtime_assert=True)
        return e.dma_start(
            eof_sb.rearrange("p (r f) -> p r f", r=NCORES),
            deo_out.rearrange("(r q p) f -> q p r f", q=4, p=128)[bp, :, :, :])
    eof_done = p.emit("gpsimd", read_eo,
                      waits=[("cc", cc_eo), ("g_eo", cnt["g_eo"]),
                             ("a", disp_done_a)],
                      inc="g_eo", amt=16, dma=True)

    ett_h = [None, None]
    est_h = [None, None]
    e9_v0 = cnt["v"]
    for i, (dc, tc) in enumerate([(d, t) for d in range(NDT)
                                  for t in range(qc_n)]):
        b = i % 2
        sx = p.dma("scalar", xr3_sb[b][:, :],
                   dx3[dc * 128:(dc + 1) * 128, tc * 512:(tc + 1) * 512],
                   waits=[("v", ett_h[b] if ett_h[b] else e9_v0)],
                   inc=f"a_xr{b}")
        for st in range(NDT):
            w0 = []
            if st == 0:
                w0 = [("g_eo", eof_done), ("v", cfm_done), ("v", cnt["v"])]
            MM(psC[:, 0:512],
               eof_sb[:, st * 1024 + dc * 128:st * 1024 + (dc + 1) * 128],
               cfm_sb[:, st * TL + tc * 512:st * TL + (tc + 1) * 512],
               start=(st == 0), stop=(st == NDT - 1), waits=w0,
               inc=("t" if st == NDT - 1 else None))
        vo = TT(oc3_sb[b][:, :], psC[:, 0:512], xr3_sb[b][:, :], ALU.add,
                waits=[("t", cnt["t"]), (f"a_xr{b}", sx)] +
                      ([(f"os{b}", est_h[b])] if est_h[b] else []),
                inc="v")
        ett_h[b] = vo
        est_h[b] = p.dma("sync", dout[dc * 128:(dc + 1) * 128, tc * 512:(tc + 1) * 512],
                         oc3_sb[b][:, :], waits=[("v", vo)], inc=f"os{b}")
    p.emit("sync", lambda e: (e.wait_ge(p.sems["os0"], cnt["os0"]),
                              e.wait_ge(p.sems["os1"], cnt["os1"])) and None)

    if DEBUG_STAGE in (0, 4):
        p.replay()
        return nc
    p.stage("fin")
    ar.reset()
    fin_sb = ar.alloc([128, 2048], F32, "fin_sb")
    if DEBUG_STAGE == 1:
        src_d, sds = dx1, "s_x1"
    elif DEBUG_STAGE == 2:
        src_d, sds = dx2, "s_x2"
    else:
        src_d, sds = dx3, "s_x3"
    stream_done = [(sds, cnt[sds])]
    for dc in range(NDT):
        ld = p.dma("sync", fin_sb[:, :], src_d[dc * 128:(dc + 1) * 128, :],
                   waits=stream_done + ([("o", cnt["o"])] if dc else []), inc="s_fin")
        p.dma("sync", dout[dc * 128:(dc + 1) * 128, :], fin_sb[:, :],
              waits=[("s_fin", ld)], inc="o")
    p.emit("sync", lambda e: e.wait_ge(p.sems["o"], cnt["o"]) and None)

    p.replay()
    return nc


_CACHE = {}


def kernel(**inputs):
    if "nc" not in _CACHE:
        _CACHE["nc"] = build()
    nc = _CACHE["nc"]
    in_maps = _prep(inputs)
    res = run_bass_kernel_spmd(nc, in_maps, core_ids=list(range(NCORES)))
    out = np.empty((B, T, D), np.float32)
    for c in range(NCORES):
        b, half = c // 2, c % 2
        t0 = half * TL
        out[b, t0:t0 + TL, :] = res.results[c]["out_fm"].T
    return out
